# version 04
# baseline (speedup 1.0000x reference)
import numpy as np
from contextlib import ExitStack

import concourse.bass as bass
import concourse.mybir as mybir
from concourse.bass_utils import run_bass_kernel_spmd

F32 = mybir.dt.float32
BF16 = mybir.dt.bfloat16
I32 = mybir.dt.int32
AF = mybir.ActivationFunctionType
OP = mybir.AluOpType
AX = mybir.AxisListType

NCORES = 8
D = 1024
T = 2048
NK = 8
NH = 16
DH = 64
CHUNK = 512
NCH = T // CHUNK
HFF0 = 3072
NH0 = HFF0 // 128
HFF1 = 4096
NH1 = HFF1 // 128
NST = NH * 65
MAGIC = 12582912.0
PI = float(np.pi)
ANG_SCALE = PI / (2.0 * 8192.0)

ENGINES = ("sync", "scalar", "vector", "tensor", "gpsimd")
QSEM = {"sync": "dS", "scalar": "dA", "gpsimd": "dG"}


class Plan:
    def __init__(self):
        self.ops = {e: [] for e in ENGINES}
        self.cnt = {}
        self.sems = {}

    def newsem(self, *names):
        for n in names:
            self.cnt[n] = 0

    def v(self, name):
        return self.cnt[name]

    def emit(self, engine, fn, waits=(), incs=()):
        waits = tuple((s, v) for (s, v) in waits if v > 0)
        for (s, n) in incs:
            self.cnt[s] += n
        def run(eng, _fn=fn, _w=waits, _i=tuple(incs), _sems=self.sems):
            for (s, v) in _w:
                eng.wait_ge(_sems[s], v)
            inst = _fn(eng)
            for (s, n) in _i:
                inst = inst.then_inc(_sems[s], n)

        self.ops[engine].append(run)


class Banks:

    def __init__(self, slots, tok=("pe", 0)):
        self.slots = list(slots)
        self.tok = [tok] * len(slots)
        self.i = 0

    def get(self):
        j = self.i % len(self.slots)
        self.i += 1
        return j, self.slots[j], self.tok[j]

    def set_tok(self, j, tok):
        self.tok[j] = tok


class _EarlyStop(Exception):
    pass


def build_program(stage="full"):
    do_attn = stage in ("full", "l0", "l0_attn", "l0_hin") or stage.startswith("l1")
    do_ffn = stage in ("full", "l0", "l0_ffn", "l0_hid") or stage.startswith("l1")
    nc = bass.Bass("TRN2", target_bir_lowering=False, debug=False)
    P = Plan()
    P.newsem("dS", "dA", "dG", "dB", "pe", "ve", "ae", "ge", "cc", "st")

    x_h = nc.dram_tensor("x", [D, T], F32, kind="ExternalInput")
    out_h = nc.dram_tensor("out", [D, T], F32, kind="ExternalOutput")
    params_h = nc.dram_tensor("params", [128, 24], F32, kind="ExternalInput")
    wgT_h = nc.dram_tensor("wgT", [D, D], F32, kind="ExternalInput")
    wvT_h = nc.dram_tensor("wvT", [D, D], F32, kind="ExternalInput")
    wdT_h = nc.dram_tensor("wdT", [D, D], F32, kind="ExternalInput")
    bcols0_h = nc.dram_tensor("bcols0", [128, 24], F32, kind="ExternalInput")
    bcols1_h = nc.dram_tensor("bcols1", [128, 64], F32, kind="ExternalInput")
    fgT_h = nc.dram_tensor("fgT", [D, HFF0], I32, kind="ExternalInput")
    fuT_h = nc.dram_tensor("fuT", [D, HFF0], I32, kind="ExternalInput")
    fdT_h = nc.dram_tensor("fdT", [HFF0, D], I32, kind="ExternalInput")
    wq_h = nc.dram_tensor("wq", [D, D], F32, kind="ExternalInput")
    wk_h = nc.dram_tensor("wk", [D, D], F32, kind="ExternalInput")
    wv1_h = nc.dram_tensor("wv1", [D, D], F32, kind="ExternalInput")
    wo_h = nc.dram_tensor("wo", [D, D], F32, kind="ExternalInput")
    bk_h = nc.dram_tensor("bk", [1, D], F32, kind="ExternalInput")
    bv1_h = nc.dram_tensor("bv1", [1, D], F32, kind="ExternalInput")
    win_h = nc.dram_tensor("win", [D, HFF1], F32, kind="ExternalInput")
    wout_h = nc.dram_tensor("wout", [HFF1, D], F32, kind="ExternalInput")

    ag0_in = nc.dram_tensor("ag0_in", [1, 128 * 16], F32)
    ag0_out = nc.dram_tensor("ag0_out", [NCORES, 128 * 16], F32)
    ag1_in = nc.dram_tensor("ag1_in", [1, 128 * NST], F32)
    ag1_out = nc.dram_tensor("ag1_out", [NCORES, 128 * NST], F32)
    ag1b_in = nc.dram_tensor("ag1b_in", [1, 8], F32)
    ag1b_out = nc.dram_tensor("ag1b_out", [NCORES, 8], F32)
    ag2_in = nc.dram_tensor("ag2_in", [1, 8], F32)
    ag2_out = nc.dram_tensor("ag2_out", [NCORES, 8], F32)
    winq_d = nc.dram_tensor("winq_d", [D, HFF1], BF16)
    woutq_d = nc.dram_tensor("woutq_d", [HFF1, D], BF16)
    sinrow_d = nc.dram_tensor("sinrow_d", [1, HFF1], F32)
    hid_d = nc.dram_tensor("hid_d", [HFF1, T], BF16)

    es = ExitStack()

    def sbt(name, shape, dtype):
        return es.enter_context(nc.sbuf_tensor(name, shape, dtype))

    xres = sbt("xres", [128, NK * T], F32)
    xn = sbt("xn", [128, NK * T], BF16)
    bufA = sbt("bufA", [128, NK * T], BF16)
    bufB = sbt("bufB", [128, NK * T], BF16)
    ones_row = sbt("ones_row", [1, 128], F32)
    ones_col = sbt("ones_col", [128, 1], F32)
    prm = sbt("prm", [128, 24], F32)
    bcolL0 = sbt("bcolL0", [128, 3 * NK], F32)
    bcol1 = sbt("bcol1", [128, 64], F32)
    tri = sbt("tri", [128, 512], BF16)
    summ = sbt("summ", [128, 16], F32)
    hin = sbt("hin", [128, 8], F32)
    agbuf = sbt("agbuf", [128, 16 * NCORES], F32)
    red = sbt("red", [128, 144], F32)
    smax = sbt("smax", [8, 48], F32)
    identb = sbt("identb", [128, 128], BF16)
    ones_rb = sbt("ones_rb", [1, 128], BF16)

    ARENA_B = 45056
    arena = sbt("arena", [128, ARENA_B // 4], F32)
    abase = nc.lookup_mloc(arena).addr

    def aview(name, shape, dtype, off):
        assert off + shape[-1] * mybir.dt.size(dtype) * int(np.prod(shape[1:-1], initial=1)) <= ARENA_B
        return nc.alloc_sbuf_tensor_at(name, shape, dtype, offset=abase + off)

    KB = 1024
    lnsq = aview("lnsq", [128, 4096], F32, 0)
    lnrw = aview("lnrw", [128, 4096], F32, 16 * KB)
    a_row = aview("a_row", [128, T], F32, 0)
    xc_row = aview("xc_row", [128, T], F32, 8 * KB)
    g_row = aview("g_row", [128, T], F32, 16 * KB)
    vh_row = aview("vh_row", [128, T], F32, 24 * KB)
    wslA = aview("wslA", [128, 6144], BF16, 32 * KB)
    ffn_h = aview("ffn_h", [128, NH0 * 512], BF16, 0)
    ffn_gu = aview("ffn_gu", [128, 4096], BF16, 24 * KB)
    ffn_dn = aview("ffn_dn", [128, 2 * 1536], BF16, 32 * KB)
    gact = aview("gact", [128, 1024], F32, 38 * KB)
    corr_t = aview("corr_t", [128, T], F32, 0)

    PS0 = es.enter_context(nc.psum_tensor("PS0", [128, 1024], F32))
    PS1 = es.enter_context(nc.psum_tensor("PS1", [128, 1024], F32))
    PS2 = es.enter_context(nc.psum_tensor("PS2", [128, 1024], F32))
    PS3 = es.enter_context(nc.psum_tensor("PS3", [128, 512], F32))
    PS4 = es.enter_context(nc.psum_tensor("PS4", [128, 512], F32))

    for s in list(P.cnt.keys()):
        P.sems[s] = es.enter_context(nc.semaphore(s))

    def dma(q, out_ap, in_ap, waits=(), sem=None, slow=False):
        s = sem or QSEM[q]
        if s not in P.cnt:
            P.newsem(s)
            P.sems[s] = es.enter_context(nc.semaphore(s))
        P.emit(q, (lambda eng, o=out_ap, i=in_ap, sl=slow: eng.dma_start(
            out=o, in_=i, allow_slow_non_contiguous=sl)),
               waits=waits, incs=[(s, 16)])
        return (s, P.v(s))

    def mm(lhsT, rhs, out, start, stop, waits=(), inc=False):
        incs = [("pe", 1)] if inc else ()
        P.emit("tensor",
               (lambda eng, l=lhsT, r=rhs, o=out, s=start, p=stop:
                eng.matmul(o, l, r, start=s, stop=p)),
               waits=waits, incs=incs)
        return P.v("pe")

    def act(fn, waits=()):
        P.emit("scalar", fn, waits=waits, incs=[("ae", 1)])
        return P.v("ae")

    def vec(fn, waits=()):
        P.emit("vector", fn, waits=waits, incs=[("ve", 1)])
        return P.v("ve")

    def gp(fn, waits=()):
        P.emit("gpsimd", fn, waits=waits, incs=[("ge", 1)])
        return P.v("ge")

    def col_block_dma(dst_tile, src_h, col0, ncols, nk, waits=(), row0=0, sem=None):
        o = dst_tile.rearrange("p (k f) -> p k f", f=ncols)
        i = src_h.ap()[row0:row0 + nk * 128, col0:col0 + ncols].rearrange(
            "(k p) f -> p k f", p=128)
        return dma("gpsimd", o, i, waits, sem=sem)

    def xr(k, t0=0, t1=T):
        return xres[:, k * T + t0: k * T + t1]

    def xnv(k, t0=0, t1=T):
        return xn[:, k * T + t0: k * T + t1]

    setA = Banks([PS0[:, 0:512], PS0[:, 512:1024], PS1[:, 0:512], PS1[:, 512:1024]])
    setB = Banks([PS2[:, 0:512], PS2[:, 512:1024], PS3[:, 0:512], PS4[:, 0:512]])

    xw = []
    for k in range(NK):
        xw.append(dma("sync", xr(k), x_h.ap()[k * 128:(k + 1) * 128, :]))
    dma("sync", prm[:], params_h.ap())
    dma("sync", bcolL0[:], bcols0_h.ap())
    w_const = dma("sync", bcol1[:], bcols1_h.ap())

    vec(lambda eng: eng.memset(ones_row[:], 1.0))
    vec(lambda eng: eng.memset(ones_col[:], 1.0))
    v_tri = vec(lambda eng: eng.memset(tri[:], 1.0))
    g_tri = gp(lambda eng: eng.iota(
        agbuf[:, 0:128], pattern=[[-1, 128]], base=0, channel_multiplier=1,
        allow_small_or_imprecise_dtypes=True), waits=[("ve", v_tri)])
    vec(lambda eng: eng.tensor_single_scalar(
        tri[:, 0:128], agbuf[:, 0:128], 0.0, OP.is_le), waits=[("ge", g_tri)])

    def emit_layernorm(first_waits, stats_tok=("pe", 0)):
        bcR = lnrw[:, 0:512]
        bcM = lnrw[:, 512:1024]
        tmp = lnrw[:, 1024:1536]
        R0 = lnrw[0:1, 1536:2048]
        R1 = lnrw[0:1, 2048:2560]
        R2 = lnrw[0:1, 2560:3072]
        xn_done = 0
        for tt in range(4):
            t0 = tt * 512
            ae_sq = 0
            for k in range(NK):
                w = list(first_waits) if (tt == 0 and k == 0) else []
                ae_sq = act(lambda eng, k=k, t0=t0: eng.activation(
                    lnsq[:, k * 512:(k + 1) * 512], xr(k, t0, t0 + 512),
                    AF.Square), waits=w)
            first_w = [("ae", ae_sq)] + ([stats_tok] if tt == 0 else [])
            for k in range(NK):
                mm(ones_col[:], xr(k, t0, t0 + 512), PS3[0:1, 0:512],
                   start=(k == 0), stop=(k == NK - 1),
                   waits=(first_w if k == 0 else ()))
            pe_s = 0
            for k in range(NK):
                pe_s = mm(ones_col[:], lnsq[:, k * 512:(k + 1) * 512],
                          PS4[0:1, 0:512], start=(k == 0), stop=(k == NK - 1),
                          inc=(k == NK - 1))
            vec(lambda eng, R0=R0: eng.tensor_scalar(
                R0, PS3[0:1, 0:512], 1.0 / D, None, OP.mult),
                waits=[("pe", pe_s)])
            vec(lambda eng, R1=R1: eng.tensor_scalar(
                R1, PS4[0:1, 0:512], 1.0 / D, 1e-5, OP.mult, OP.add))
            vec(lambda eng, R0=R0, R2=R2: eng.scalar_tensor_tensor(
                R2, R0, -1.0, R0, OP.mult, OP.mult))
            ve_var = vec(lambda eng, R1=R1, R2=R2: eng.tensor_add(R1, R1, R2))
            ae_v = act(lambda eng, R1=R1, R2=R2: eng.activation(R2, R1, AF.Sqrt),
                       waits=[("ve", ve_var)])
            vec(lambda eng, R1=R1, R2=R2: eng.reciprocal(R1, R2),
                waits=[("ae", ae_v)])
            ve_r = vec(lambda eng, R0=R0, R1=R1: eng.scalar_tensor_tensor(
                R0, R0, -1.0, R1, OP.mult, OP.mult))
            mm(ones_row[0:1, 0:128], R1, PS4[:, 0:512], start=True, stop=True,
               waits=[("ve", ve_r)])
            pe_b = mm(ones_row[0:1, 0:128], R0, PS3[:, 0:512],
                      start=True, stop=True, inc=True)
            act(lambda eng, bcR=bcR: eng.activation(bcR, PS4[:, 0:512], AF.Copy),
                waits=[("pe", pe_b)])
            ae_b = act(lambda eng, bcM=bcM: eng.activation(bcM, PS3[:, 0:512], AF.Copy))
            for k in range(NK):
                vec(lambda eng, k=k, t0=t0, tmp=tmp, bcR=bcR: eng.tensor_mul(
                    tmp, xr(k, t0, t0 + 512), bcR),
                    waits=([("ae", ae_b)] if k == 0 else []))
                xn_done = vec(lambda eng, k=k, t0=t0, tmp=tmp, bcM=bcM: eng.tensor_add(
                    xnv(k, t0, t0 + 512), tmp, bcM))
        return xn_done

    xn_all = emit_layernorm(list(xw) + [w_const])
    setB.tok[2] = ("ae", P.v("ae"))
    setB.tok[3] = ("ae", P.v("ae"))

    def l0_wslot(m, b):
        c0 = (m * 2 + b) * 1024
        return wslA[:, c0:c0 + 1024]

    wslot_tok = {}
    ve_rows_done = 0
    for e in (range(NK) if do_attn else ()):
        b = e % 2
        dtok = {}
        for m, src in enumerate((wdT_h, wvT_h, wgT_h)):
            dtok[m] = col_block_dma(l0_wslot(m, b), src, e * 128, 128, NK,
                                    waits=[wslot_tok.get((m, b), ("pe", 0))],
                                    sem="wA%d" % (m * 2 + b))
        for m in range(3):
            bs = setA if (e * 3 + m) % 2 == 0 else setB
            wslot = l0_wslot(m, b)
            groups = {}
            for tt in range(4):
                j, ap, tok = bs.get()
                pw = [dtok[m], tok]
                if m == 0 and e == 0:
                    pw.append(("ve", xn_all))
                pe_last = 0
                for k in range(NK):
                    pe_last = mm(wslot[:, k * 128:(k + 1) * 128],
                                 xnv(k, tt * 512, tt * 512 + 512), ap,
                                 start=(k == 0), stop=(k == NK - 1),
                                 waits=(pw if k == 0 else ()),
                                 inc=(k == NK - 1))
                groups[tt] = (pe_last, ap, j)
            wslot_tok[(m, b)] = ("pe", groups[3][0])
            dst = (a_row, vh_row, g_row)[m]
            bofs = (2 * NK, NK, 0)[m]
            for tt in range(4):
                pe_g, ap, j = groups[tt]
                w = [("pe", pe_g)]
                if tt == 0 and m == 0:
                    w.append(("ve", ve_rows_done))
                if m == 1:
                    ve_t = vec(lambda eng, dst=dst, tt=tt, ap=ap, bo=bofs, e=e:
                               eng.tensor_scalar(
                                   dst[:, tt * 512:tt * 512 + 512], ap,
                                   bcolL0[:, bo + e:bo + e + 1], None, OP.add),
                               waits=w)
                    bs.set_tok(j, ("ve", ve_t))
                else:
                    ae_t = act(lambda eng, dst=dst, tt=tt, ap=ap, bo=bofs, e=e:
                               eng.activation(dst[:, tt * 512:tt * 512 + 512], ap,
                                              AF.Sigmoid,
                                              bias=bcolL0[:, bo + e:bo + e + 1]),
                               waits=w)
                    bs.set_tok(j, ("ae", ae_t))
        ae_evacs = P.v("ae")
        vec(lambda eng: eng.tensor_scalar(
            xc_row[:], a_row[:], -1.0, 1.0, OP.mult, OP.add),
            waits=[("ae", ae_evacs)])
        vec(lambda eng: eng.tensor_mul(xc_row[:], xc_row[:], vh_row[:]))
        vec(lambda eng: eng.tensor_tensor_scan(
            vh_row[:], a_row[:], xc_row[:], 0.0, OP.mult, OP.add))
        vec(lambda eng: eng.tensor_tensor_scan(
            xc_row[:], a_row[:], a_row[:], 1.0, OP.mult, OP.bypass))
        vec(lambda eng, e=e: eng.tensor_mul(bufA[:, e * T:(e + 1) * T], vh_row[:], g_row[:]))
        vec(lambda eng, e=e: eng.tensor_mul(
            bufB[:, e * T:(e + 1) * T], xc_row[:], g_row[:]))
        vec(lambda eng, e=e: eng.tensor_copy(summ[:, e:e + 1], xc_row[:, T - 1:T]))
        ve_rows_done = vec(lambda eng, e=e: eng.tensor_copy(
            summ[:, 8 + e:8 + e + 1], vh_row[:, T - 1:T]))
    ve_summary = ve_rows_done
    pe_a1_done = P.v("pe")

    if not do_attn:
        ve_summary = vec(lambda eng: eng.memset(summ[:], 0.0))
    g_ag0 = dma("gpsimd", ag0_in.ap().rearrange("a (p c) -> (a p) c", p=128),
                summ[:], waits=[("ve", ve_summary)])
    P.emit("gpsimd", lambda eng: eng.collective_compute(
        "AllGather", OP.bypass, replica_groups=[list(range(NCORES))],
        ins=[ag0_in.ap()], outs=[ag0_out.ap()]),
        waits=[("dG", g_ag0[1])], incs=[("cc", 1)])
    cc0 = P.v("cc")

    gu_tok = {}
    dn_tok = {}
    tok_g = tok_u = tok_d = ("ae", P.v("ae"))
    ve_ffn = 0
    pe_dn_last = 0
    first_gu_dma = True
    for tp in (range(1) if stage == "l0_hid" else (range(4) if do_ffn else ())):
        t0 = tp * 512
        h_last = 0
        for hch in range(NH0):
            sg = hch % 2
            wg = [gu_tok.get(("g", sg), ("pe", 0))]
            if first_gu_dma:
                wg.append(("ve", ve_summary))
                wg.append(("ve", xn_all))
                first_gu_dma = False
            dg = col_block_dma(ffn_gu[:, sg * 1024:(sg + 1) * 1024], fgT_h,
                               hch * 128, 128, NK, waits=wg, sem="fg%d" % sg)
            du = col_block_dma(ffn_gu[:, (2 + sg) * 1024:(3 + sg) * 1024], fuT_h,
                               hch * 128, 128, NK,
                               waits=[gu_tok.get(("u", sg), ("pe", 0))],
                               sem="fu%d" % sg)
            pw_g = [dg, tok_g]
            pw_u = [du, tok_u]
            if tp == 0 and hch == 0:
                pw_g = pw_g + [("ve", xn_all)]
            pe_g = pe_u = 0
            for k in range(NK):
                pe_g = mm(ffn_gu[:, sg * 1024 + k * 128: sg * 1024 + (k + 1) * 128],
                          xnv(k, t0, t0 + 512), PS0[:, 0:512],
                          start=(k == 0), stop=(k == NK - 1),
                          waits=(pw_g if k == 0 else ()), inc=(k == NK - 1))
            gu_tok[("g", sg)] = ("pe", pe_g)
            for k in range(NK):
                pe_u = mm(ffn_gu[:, (2 + sg) * 1024 + k * 128: (2 + sg) * 1024 + (k + 1) * 128],
                          xnv(k, t0, t0 + 512), PS1[:, 0:512],
                          start=(k == 0), stop=(k == NK - 1),
                          waits=(pw_u if k == 0 else ()), inc=(k == NK - 1))
            gu_tok[("u", sg)] = ("pe", pe_u)
            w_s = [("pe", pe_g), tok_g, ("ve", h_last)]
            if tp > 0 and hch == 0:
                w_s.append(("pe", pe_dn_last))
            ae_s = act(lambda eng: eng.activation(
                gact[:, 0:512], PS0[:, 0:512], AF.Silu, scale=1.0 / 127.0),
                waits=w_s)
            tok_g = ("ae", ae_s)
            h_last = vec(lambda eng, hch=hch: eng.scalar_tensor_tensor(
                ffn_h[:, hch * 512:(hch + 1) * 512], PS1[:, 0:512],
                1.0 / 127.0, gact[:, 0:512], OP.mult, OP.mult),
                waits=[("ae", ae_s), ("pe", pe_u)])
            tok_u = ("ve", h_last)
        for dch in range(NK):
            pe_d = 0
            for half in range(2):
                hb = half
                dn_w = [dn_tok.get(hb, ("pe", pe_a1_done))]
                if hb not in dn_tok:
                    dn_w.append(("ve", ve_summary))
                dd = col_block_dma(ffn_dn[:, hb * 1536:(hb + 1) * 1536], fdT_h,
                                   dch * 128, 128, 12,
                                   row0=half * 1536, waits=dn_w,
                                   sem="fd%d" % hb)
                for k in range(12):
                    kk = half * 12 + k
                    pe_d = mm(ffn_dn[:, hb * 1536 + k * 128: hb * 1536 + (k + 1) * 128],
                              ffn_h[:, kk * 512:(kk + 1) * 512],
                              PS2[:, 0:512],
                              start=(kk == 0), stop=(kk == NH0 - 1),
                              waits=([dd] +
                                     ([("ve", h_last), tok_d] if kk == 0 else []))
                              if k == 0 else (),
                              inc=(k == 11))
                dn_tok[hb] = ("pe", pe_d)
            pe_dn_last = pe_d
            ve_ffn = vec(lambda eng, dch=dch, t0=t0: eng.scalar_tensor_tensor(
                xr(dch, t0, t0 + 512), PS2[:, 0:512], 1.0 / 127.0,
                xr(dch, t0, t0 + 512), OP.mult, OP.add),
                waits=[("pe", pe_d)])
            tok_d = ("ve", ve_ffn)

    P.emit("gpsimd", (lambda eng: eng.dma_start(
        out=agbuf[:].rearrange("p (r c) -> p r c", c=16),
        in_=ag0_out.ap().rearrange("r (p c) -> p r c", p=128))),
        waits=[("cc", cc0)], incs=[("dB", 16)])
    g_back_v = P.v("dB")
    ve_prep = 0
    for r in range(NCORES):
        vec(lambda eng, r=r: eng.tensor_scalar(
            red[:, r * 8:(r + 1) * 8], agbuf[:, r * 16:r * 16 + 8],
            prm[:, r:r + 1], prm[:, 9 + r:10 + r], OP.mult, OP.add),
            waits=([("dB", g_back_v)] if r == 0 else ()))
        ve_prep = vec(lambda eng, r=r: eng.tensor_scalar(
            red[:, 64 + r * 8: 64 + (r + 1) * 8],
            agbuf[:, r * 16 + 8:r * 16 + 16],
            prm[:, r:r + 1], None, OP.mult))
    gp(lambda eng: eng.memset(hin[:], 0.0), waits=[("ve", ve_prep)])
    for r in range(NCORES):
        gp(lambda eng, r=r: eng.tensor_tensor(
            red[:, 128:136], red[:, r * 8:(r + 1) * 8], hin[:], OP.mult))
        gp(lambda eng, r=r: eng.tensor_tensor(
            hin[:], red[:, 128:136], red[:, 64 + r * 8: 64 + (r + 1) * 8], OP.add))
    ge_hin = P.v("ge")
    ve_corr = 0
    for e in (range(NK) if do_attn else ()):
        vec(lambda eng, e=e: eng.scalar_tensor_tensor(
            corr_t[:], bufB[:, e * T:(e + 1) * T], hin[:, e:e + 1],
            bufA[:, e * T:(e + 1) * T], OP.mult, OP.add),
            waits=([("ge", ge_hin), ("ve", ve_ffn)] if e == 0 else ()))
        ve_corr = vec(lambda eng, e=e: eng.tensor_add(
            xr(e), xr(e), corr_t[:]))

    if stage.startswith("l0"):
        last = None
        fin = max(ve_corr, ve_ffn, xn_all)
        if stage == "l0_hin":
            last = dma("gpsimd", out_h.ap()[0:128, 0:128], agbuf[:],
                       waits=[("ve", ve_corr)])
            last = dma("gpsimd", out_h.ap()[0:128, 128:136], hin[:],
                       waits=[("ve", ve_corr)])
            last = dma("gpsimd", out_h.ap()[0:128, 136:152], summ[:],
                       waits=[("ve", ve_corr)])
            last = dma("gpsimd", out_h.ap()[0:128, 152:168], prm[:],
                       waits=[("ve", ve_corr)])
            last = dma("gpsimd", out_h.ap()[0:128, 168:360],
                       bufA[:, 0:192], waits=[("ve", ve_corr)])
            P.emit("gpsimd", lambda eng: None, waits=[("dG", last[1])])
        elif stage == "l0_hid":
            for k in range(6):
                last = dma("gpsimd", out_h.ap()[k * 128:(k + 1) * 128, :],
                           ffn_h[:, k * 2048:(k + 1) * 2048],
                           waits=[("ve", max(ve_ffn, 1))])
            P.emit("gpsimd", lambda eng: None, waits=[("dG", last[1])])
        elif stage == "l0_xn":
            for k in range(NK):
                last = dma("gpsimd", out_h.ap()[k * 128:(k + 1) * 128, :], xnv(k),
                           waits=[("ve", fin)])
            P.emit("gpsimd", lambda eng: None, waits=[("dG", last[1])])
        else:
            for k in range(NK):
                last = dma("sync", out_h.ap()[k * 128:(k + 1) * 128, :], xr(k),
                           waits=[("ve", fin)])
            P.emit("sync", lambda eng: None, waits=[("dS", last[1])])
        emit_all(nc, P)
        es.close()
        return nc

    pe_l0 = P.v("pe")
    ae_l0 = P.v("ae")

    xn_all = emit_layernorm([("ve", ve_corr), ("pe", pe_l0), ("ae", ae_l0)],
                            stats_tok=("ve", ve_corr))
    ae_ln1 = P.v("ae")


    def early_out(waits):
        last = None
        for k in range(NK):
            last = dma("sync", out_h.ap()[k * 128:(k + 1) * 128, :], xr(k),
                       waits=waits)
        P.emit("sync", lambda eng: None, waits=[("dS", last[1])])
        emit_all(nc, P)
        es.close()
        return nc

    brow1 = aview("brow1", [1, 2048], BF16, 0)
    cosfc = aview("cosfc", [128, 512], BF16, 4096)
    sinfc = aview("sinfc", [128, 512], BF16, 5120)
    cscol = aview("cscol", [128, 32], F32, 6144)
    qp_t = aview("qp_t", [128, 512], BF16, 6272)
    prQc = aview("prQc", [128, 512], BF16, 7296)
    prQs = aview("prQs", [128, 512], BF16, 8320)
    prKc = aview("prKc", [128, 512], BF16, 9344)
    prKs = aview("prKs", [128, 512], BF16, 10368)
    st_t = aview("st_t", [128, 512], BF16, 11392)
    krelu = aview("krelu", [128, 1024], F32, 12416)
    sden = aview("sden", [1, 512], F32, 12416)
    iotat = aview("iotat", [128, 512], F32, 14464)
    vtok = aview("vtok", [128, 4 * NST], BF16, 16512)
    sprefb = aview("sprefb", [128, 4 * NST], BF16, 24832)
    srun1 = aview("srun1", [128, NST], F32, 33152)
    sptot = aview("sptot", [128, NST], BF16, 37312)
    sshift = aview("sshift", [128, NST], BF16, 39392)
    onesT = aview("onesT", [128, 128], F32, 41472)
    wvsl = aview("wvsl", [128, 1024], BF16, 41984)
    at64 = aview("at64", [128, 512], BF16, 44032)
    xq_t = aview("xq_t", [128, 4096], BF16, 0)
    w1sl = aview("w1sl", [128, 2048], BF16, 8192)
    w2sl = aview("w2sl", [128, 2048], BF16, 12288)
    hidt = aview("hidt", [128, 1024], BF16, 16384)
    qtf = aview("qtf", [128, 1024], F32, 18432)
    wprep = aview("wprep", [128, 4096], BF16, 22528)
    wpabs = aview("wpabs", [128, 512], F32, 30720)
    wpq = aview("wpq", [128, 512], F32, 32768)
    amx32 = aview("amx32", [128, 32], F32, 34816)
    sc32 = aview("sc32", [128, 32], F32, 34944)
    rs32 = aview("rs32", [128, 32], F32, 35072)
    rso32 = aview("rso32", [128, 32], F32, 35200)
    cmb32 = aview("cmb32", [128, 32], F32, 35328)
    srow_t = aview("srow_t", [1, 512], BF16, 35456)

    g_id = gp(lambda eng: eng.iota(
        red[:, 0:128] if False else red[:, 16:144], pattern=[[-1, 128]], base=0,
        channel_multiplier=1, allow_small_or_imprecise_dtypes=True))
    ve_id = vec(lambda eng: eng.tensor_single_scalar(
        identb[:], red[:, 16:144], 0.0, OP.is_equal), waits=[("ge", g_id)])
    ve_orb = vec(lambda eng: eng.memset(ones_rb[:], 1.0), waits=[("ve", ve_corr)])

    for k in range(NK):
        vec(lambda eng, k=k: eng.reduce_max(
            red[:, 8 + k:9 + k], xnv(k),
            axis=AX.X, apply_absolute_value=True))
    ve_am = vec(lambda eng: eng.reduce_max(
        summ[:, 0:1], red[:, 8:16], axis=AX.X))
    ve_amb = vec(lambda eng: eng.tensor_copy(tri[:, 0:1] if False else summ.bitcast(BF16)[:, 2:3], summ[:, 0:1]),
                 waits=[("ve", ve_am)])
    P.emit("tensor", (lambda eng: eng.transpose(
        PS4.bitcast(BF16)[0:1, 0:128], summ.bitcast(BF16)[:, 2:3], identb[:])),
        waits=[("ve", ve_amb), ("ve", ve_id)], incs=[("pe", 1)])
    pe_am = P.v("pe")
    ve_am2 = vec(lambda eng: eng.reduce_max(
        smax[0:1, 0:1], PS4.bitcast(BF16)[0:1, 0:128], axis=AX.X),
        waits=[("pe", pe_am)])

    if stage == "l1_ln":
        return early_out([("ve", xn_all), ("ve", ve_am2), ("ve", ve_corr)])
    d_brow = dma("gpsimd", brow1[0:1, 0:D], bk_h.ap(), sem="dR",
                 waits=[("ve", xn_all)])
    d_brow = dma("gpsimd", brow1[0:1, D:2 * D], bv1_h.ap(), sem="dR")
    g_cs = gp(lambda eng: eng.iota(
        iotat[:, 0:16], pattern=[[128, 16]], base=0, channel_multiplier=1,
        allow_small_or_imprecise_dtypes=True), waits=[("ve", xn_all)])
    ae_cs = act(lambda eng: eng.activation(
        cscol[:, 0:16], iotat[:, 0:16], AF.Sin, scale=ANG_SCALE,
        bias=prm[:, 17:18]), waits=[("ge", g_cs)])
    ae_cs = act(lambda eng: eng.activation(
        cscol[:, 16:32], iotat[:, 0:16], AF.Sin, scale=ANG_SCALE,
        bias=prm[:, 8:9]))
    ve_vi = vec(lambda eng: eng.memset(vtok[:], 0.0), waits=[("ve", xn_all)])
    ve_vi = vec(lambda eng: eng.memset(
        vtok[:].rearrange("p (u h) -> p u h", h=65)[:, :, 64:65], 1.0))

    wk_last = wv_last = None
    for k in range(NK):
        wk_last = dma("gpsimd", bufB[:, k * 1024:(k + 1) * 1024],
                      wk_h.ap()[k * 128:(k + 1) * 128, :], sem="wWk",
                      waits=([("ve", ve_corr)] if k == 0 else ()))
    for k in range(NK):
        wv_last = dma("gpsimd", bufB[:, (NK + k) * 1024:(NK + k + 1) * 1024],
                      wv1_h.ap()[k * 128:(k + 1) * 128, :], sem="wWv",
                      waits=([("ve", ve_corr)] if k == 0 else ()))

    def wkv(which, k):
        base = 0 if which == "k" else NK * 1024
        return bufB[:, base + k * 1024: base + (k + 1) * 1024]

    ve_state_done = 0
    tokB = {("st", 0): ("pe", 0), ("st", 1): ("ve", ve_am2)}
    for c in range(NCH):
        if c == 0:
            ve_sp = vec(lambda eng, c=c: eng.memset(
                sprefb[:, c * NST:(c + 1) * NST], 0.0))
            ve_sp = vec(lambda eng: eng.memset(srun1[:], 0.0))
        else:
            ve_sp = vec(lambda eng, c=c: eng.tensor_copy(
                sprefb[:, c * NST:(c + 1) * NST], srun1[:]),
                waits=[("ve", ve_state_done)])
        for u in range(4):
            tt = c * 4 + u
            t0 = tt * 128
            pe_k = 0
            for k in range(NK):
                for hf in range(2):
                    pw = []
                    if k == 0 and hf == 0:
                        pw = [tokB.get("ps0", ("pe", 0))]
                        if c == 0 and u == 0:
                            pw += [wk_last, ("ve", xn_all)]
                    pe_k = mm(xnv(k, t0, t0 + 128),
                              wkv("k", k)[:, hf * 512:(hf + 1) * 512],
                              PS0[:, hf * 512:hf * 512 + 512],
                              start=(k == 0), stop=False, waits=pw)
            for hf in range(2):
                pe_k = mm(ones_rb[0:1, 0:128], brow1[0:1, hf * 512:(hf + 1) * 512],
                          PS0[:, hf * 512:hf * 512 + 512], start=False, stop=True,
                          waits=([d_brow, ("ve", ve_orb)]
                                 if (c == 0 and u == 0 and hf == 0) else ()),
                          inc=(hf == 1))
            ae_k = act(lambda eng: eng.activation(
                krelu[:, 0:512], PS0[:, 0:512], AF.Relu),
                waits=[("pe", pe_k), tokB.get("krelu", ("pe", 0))])
            ae_k2 = act(lambda eng: eng.activation(
                krelu[:, 512:1024], PS0[:, 512:1024], AF.Relu))
            tokB["ps0"] = ("ae", ae_k2)
            kt = bufA[:, tt * 2048:(tt + 1) * 2048] if False else bufA[:, u * 2048:(u + 1) * 2048]
            ve_kt = 0
            for hf in range(2):
                src = krelu[:, hf * 512:(hf + 1) * 512]
                ve_kt = vec(lambda eng, kt=kt, src=src, hf=hf, tt=tt: eng.tensor_scalar(
                    kt.rearrange("p (h z) -> p h z", z=128)[:, hf * 8:(hf + 1) * 8, 0:64],
                    src.rearrange("p (h z) -> p h z", z=64),
                    cscol[:, tt:tt + 1], None, OP.mult),
                    waits=([("ae", ae_k2), tokB.get("kt", ("pe", 0))] if hf == 0 else ()))
                ve_kt = vec(lambda eng, kt=kt, src=src, hf=hf, tt=tt: eng.tensor_scalar(
                    kt.rearrange("p (h z) -> p h z", z=128)[:, hf * 8:(hf + 1) * 8, 64:128],
                    src.rearrange("p (h z) -> p h z", z=64),
                    cscol[:, 16 + tt:17 + tt], None, OP.mult))
            tokB["krelu"] = ("ve", ve_kt)
            pe_v = 0
            for k in range(NK):
                for hf in range(2):
                    pw = []
                    if k == 0 and hf == 0:
                        pw = [tokB.get("ps1", ("pe", 0))]
                        if c == 0 and u == 0:
                            pw += [wv_last]
                    pe_v = mm(xnv(k, t0, t0 + 128),
                              wkv("v", k)[:, hf * 512:(hf + 1) * 512],
                              PS1[:, hf * 512:hf * 512 + 512],
                              start=(k == 0), stop=False, waits=pw)
            for hf in range(2):
                pe_v = mm(ones_rb[0:1, 0:128], brow1[0:1, D + hf * 512:D + (hf + 1) * 512],
                          PS1[:, hf * 512:hf * 512 + 512], start=False, stop=True,
                          inc=(hf == 1))
            vt = vtok[:, u * NST:(u + 1) * NST]
            ae_vv = 0
            for hf in range(2):
                w0 = []
                if hf == 0:
                    w0 = [("pe", pe_v), tokB.get("vt", ("pe", 0))]
                    if c == 0 and u == 0:
                        w0.append(("ve", ve_vi))
                ae_vv = act(lambda eng, vt=vt, hf=hf: eng.activation(
                    vt.rearrange("p (h z) -> p h z", z=65)[:, hf * 8:(hf + 1) * 8, 0:64],
                    PS1[:, hf * 512:(hf + 1) * 512].rearrange("p (h z) -> p h z", z=64),
                    AF.Copy), waits=w0)
            tokB["ps1"] = ("ae", ae_vv)
            for h in range(NH):
                ps = PS3 if h % 2 == 0 else PS4
                pe_s = mm(kt[:, h * 128:(h + 1) * 128], vt[:, h * 65:(h + 1) * 65],
                          ps[:, 0:65], start=True, stop=True,
                          waits=[("ve", ve_kt), ("ae", ae_vv), tokB[("st", h % 2)]],
                          inc=True)
                ve_state_done = vec(lambda eng, h=h, ps=ps: eng.scalar_tensor_tensor(
                    srun1[:, h * 65:(h + 1) * 65], ps[:, 0:65], 1.0,
                    srun1[:, h * 65:(h + 1) * 65], OP.mult, OP.add),
                    waits=[("pe", pe_s)])
                tokB[("st", h % 2)] = ("ve", ve_state_done)
            tokB["vt"] = ("pe", P.v("pe"))
            tokB["kt"] = ("pe", P.v("pe"))

    g_s1 = dma("gpsimd", ag1_in.ap().rearrange("a (p c) -> (a p) c", p=128),
               srun1[:], waits=[("ve", ve_state_done)], sem="dB")
    g_s2 = dma("gpsimd", ag1b_in.ap()[0:1, 0:1], smax[0:1, 0:1],
               waits=[("ve", ve_am2)], sem="dB")
    P.emit("gpsimd", lambda eng: eng.collective_compute(
        "AllGather", OP.bypass, replica_groups=[list(range(NCORES))],
        ins=[ag1_in.ap()], outs=[ag1_out.ap()]),
        waits=[g_s1, g_s2], incs=[("cc", 1)])
    P.emit("gpsimd", lambda eng: eng.collective_compute(
        "AllGather", OP.bypass, replica_groups=[list(range(NCORES))],
        ins=[ag1b_in.ap()], outs=[ag1b_out.ap()]), incs=[("cc", 1)])
    cc1 = P.v("cc")

    ve_sx = vec(lambda eng: eng.memset(srun1[:], 0.0),
                waits=[("cc", cc1), ("ve", ve_state_done)])
    for r in range(NCORES):
        d_sr = dma("gpsimd", sptot[:],
                   ag1_out.ap()[r:r + 1, :].rearrange("a (p c) -> (a p) c", p=128),
                   sem="dB", waits=[("ve", ve_sx)] if r == 0 else [("ve", ve_sx)])
        ve_m1 = vec(lambda eng, r=r: eng.tensor_scalar(
            sshift[:], sptot[:], prm[:, r:r + 1], None, OP.mult),
            waits=[d_sr])
        ve_sx = vec(lambda eng: eng.tensor_add(srun1[:], srun1[:], sshift[:]))
    ve_scross = ve_sx
    d_sb = dma("gpsimd", smax[0:8, 8:9], ag1b_out.ap()[:, 0:1], sem="dB",
               waits=[("cc", cc1)], slow=True)
    d_fold = dma("gpsimd", smax[0:1, 24:32], smax[0:8, 8:9], sem="dB",
                 waits=[d_sb], slow=True)
    ve_g1 = vec(lambda eng: eng.reduce_max(
        smax[0:1, 16:17], smax[0:1, 24:32], axis=AX.X), waits=[d_fold])
    ve_g1 = vec(lambda eng: eng.tensor_scalar(
        smax[0:1, 17:18], smax[0:1, 16:17], 1.0 / 32767.0, None, OP.mult))
    pe_bc = mm(ones_row[0:1, 0:128], smax[0:1, 16:18], PS4[:, 0:2],
               start=True, stop=True,
               waits=[("ve", ve_g1), ("ve", ve_state_done)], inc=True)
    ve_sc = vec(lambda eng: eng.reciprocal(red[:, 32:33], PS4[:, 0:1]),
                waits=[("pe", pe_bc)])
    ve_sc = vec(lambda eng: eng.tensor_copy(red[:, 34:35], PS4[:, 1:2]))
    ve_sc = vec(lambda eng: eng.tensor_scalar(
        red[:, 33:34], red[:, 32:33], 32767.0, None, OP.mult))
    ve_scales = ve_sc

    if stage == "l1_b1":
        return early_out([("ve", ve_scross), ("ve", ve_scales), ("ve", ve_corr)])
    lvl = 99
    if stage.startswith("l1_b2@"):
        lvl = int(stage.split("@")[1])
    pe_b1 = P.v("pe")
    wq_last = None
    for k in range(NK):
        wq_last = dma("gpsimd", bufB[:, (NK + k) * 1024:(NK + k + 1) * 1024],
                      wq_h.ap()[k * 128:(k + 1) * 128, :], sem="wWq",
                      waits=([("pe", pe_b1)] if k == 0 else ()))
    ve_o1 = vec(lambda eng: eng.memset(onesT[:], 1.0))

    tokC = {"ps4": ("ve", ve_scales), ("st", 0): ("pe", 0),
            "ps0": tokB["ps0"], "ps1": tokB["ps1"],
            "ps3": tokB[("st", 0)], "ps2": ("ve", ve_ffn)}
    ve_attn_done = 0
    dH_attn = 0
    for c in range(NCH):
        t0 = c * CHUNK
        ve_tot = vec(lambda eng, c=c: eng.tensor_add(
            sptot[:], sprefb[:, c * NST:(c + 1) * NST], srun1[:]),
            waits=[("ve", ve_scross), tokC.get("sptot", ("pe", 0))])
        d_sh1 = dma("sync", sshift[0:64, :], sptot[64:128, :], sem="dH",
                    waits=[("ve", ve_tot), ("dH", dH_attn)])
        d_sh2 = dma("sync", sshift[64:128, :], sptot[0:64, :], sem="dH")
        g_i2 = gp(lambda eng, c=c: eng.iota(
            iotat[:], pattern=[[1, 512]], base=c * 512, channel_multiplier=0,
            allow_small_or_imprecise_dtypes=True),
            waits=[tokC.get("iotat", ("pe", 0))])
        ae_c1 = act(lambda eng: eng.activation(
            cosfc[:], iotat[:], AF.Sin, scale=ANG_SCALE, bias=prm[:, 17:18]),
            waits=[("ge", g_i2), tokC.get("csf", ("pe", 0))])
        ae_c2 = act(lambda eng: eng.activation(
            sinfc[:], iotat[:], AF.Sin, scale=ANG_SCALE, bias=prm[:, 8:9]))
        tokC["iotat"] = ("ae", ae_c2)
        for u in range(4):
            tu0 = t0 + u * 128
            pe_v2 = 0
            for k in range(NK):
                wv2 = dma("gpsimd", wvsl[:],
                          wv1_h.ap()[k * 128:(k + 1) * 128, :], sem="wV2",
                          waits=[tokC.get("wvsl", ("pe", 0))])
                for hf in range(2):
                    pw = [wv2]
                    if k == 0 and hf == 0:
                        pw.append(tokC.get("ps1", ("pe", 0)))
                    pe_v2 = mm(xnv(k, tu0, tu0 + 128),
                               wvsl[:, hf * 512:(hf + 1) * 512],
                               PS1[:, hf * 512:hf * 512 + 512],
                               start=(k == 0), stop=False, waits=pw,
                               inc=(hf == 1))
                tokC["wvsl"] = ("pe", pe_v2)
            for hf in range(2):
                pe_v2 = mm(ones_rb[0:1, 0:128],
                           brow1[0:1, D + hf * 512:D + (hf + 1) * 512],
                           PS1[:, hf * 512:hf * 512 + 512], start=False,
                           stop=True, inc=(hf == 1))
            vt = vtok[:, u * NST:(u + 1) * NST]
            ae_vv2 = 0
            for hf in range(2):
                w0 = []
                if hf == 0:
                    w0 = [("pe", pe_v2), tokC.get("vt", ("pe", 0))]
                ae_vv2 = act(lambda eng, vt=vt, hf=hf: eng.activation(
                    vt.rearrange("p (h z) -> p h z", z=65)[:, hf * 8:(hf + 1) * 8, 0:64],
                    PS1[:, hf * 512:(hf + 1) * 512].rearrange("p (h z) -> p h z", z=64),
                    AF.Copy), waits=w0)
            tokC["ps1"] = ("ae", ae_vv2)
        tokC["vprj"] = ("ae", ae_vv2)
        for p in (range(NK) if lvl >= 1 else ()):
            pe_q = 0
            for k in range(NK):
                pw = []
                if k == 0:
                    pw = [tokC.get("ps0", ("pe", 0))]
                    if c == 0 and p == 0:
                        pw.append(wq_last)
                pe_q = mm(bufB[:, (NK + k) * 1024 + p * 128:(NK + k) * 1024 + (p + 1) * 128],
                          xnv(k, t0, t0 + CHUNK), PS0[:, 0:512],
                          start=(k == 0), stop=(k == NK - 1), waits=pw,
                          inc=(k == NK - 1))
            ae_q = act(lambda eng, p=p: eng.activation(
                qp_t[:], PS0[:, 0:512], AF.Relu, bias=bcol1[:, p:p + 1]),
                waits=[("pe", pe_q), tokC.get("qp", ("pe", 0)), ("ae", ae_c2)])
            ve_qc = vec(lambda eng: eng.tensor_mul(prQc[:], qp_t[:], cosfc[:]),
                        waits=[("ae", ae_q), tokC.get("prq", ("pe", 0))])
            ve_qs = vec(lambda eng: eng.tensor_mul(prQs[:], qp_t[:], sinfc[:]))
            tokC["ps0"] = ("ae", ae_q)
            pe_k2 = 0
            for k in range(NK):
                pw = [("ve", ve_qs), tokC["ps0"]] if k == 0 else []
                pe_k2 = mm(bufB[:, k * 1024 + p * 128: k * 1024 + (p + 1) * 128],
                           xnv(k, t0, t0 + CHUNK), PS0[:, 0:512],
                           start=(k == 0), stop=(k == NK - 1), waits=pw,
                           inc=(k == NK - 1))
            ae_k3 = act(lambda eng, p=p: eng.activation(
                qp_t[:], PS0[:, 0:512], AF.Relu, bias=bcol1[:, NK + p:NK + p + 1]),
                waits=[("pe", pe_k2), ("ve", ve_qs)])
            ve_kc = vec(lambda eng: eng.tensor_mul(prKc[:], qp_t[:], cosfc[:]),
                        waits=[("ae", ae_k3), tokC.get("prk", ("pe", 0))])
            ve_ks = vec(lambda eng: eng.tensor_mul(prKs[:], qp_t[:], sinfc[:]))
            tokC["ps0"] = ("ae", ae_k3)
            for hh in (range(2) if lvl >= 2 else ()):
                h = 2 * p + hh
                pb = hh * 64
                w_int = [tokC.get("ps2", ("pe", 0)), ("dH", d_sh2[1]),
                         ("ve", ve_ks), tokC["vprj"]]
                if c == 0 and p == 0 and hh == 0:
                    w_int.append(("ve", ve_o1))
                lhs_cos = sptot[0:64, h * 65:(h + 1) * 65] if hh == 0 \
                    else sshift[64:128, h * 65:(h + 1) * 65]
                lhs_sin = sshift[0:64, h * 65:(h + 1) * 65] if hh == 0 \
                    else sptot[64:128, h * 65:(h + 1) * 65]
                pe_at = mm(lhs_cos, prQc[pb:pb + 64, :], PS2[0:65, 0:512],
                           start=True, stop=False, waits=w_int)
                pe_at = mm(lhs_sin, prQs[pb:pb + 64, :], PS2[0:65, 0:512],
                           start=False, stop=(lvl < 5), inc=(lvl < 5))
                for u in (range(4) if lvl >= 3 else ()):
                    w_u = 512 - u * 128
                    pe_sc2 = mm(prKc[pb:pb + 64, u * 128:(u + 1) * 128],
                                prQc[pb:pb + 64, u * 128:512],
                                PS3[:, 0:w_u], start=True, stop=False,
                                waits=[tokC.get("ps3", ("pe", 0))])
                    pe_sc2 = mm(prKs[pb:pb + 64, u * 128:(u + 1) * 128],
                                prQs[pb:pb + 64, u * 128:512],
                                PS3[:, 0:w_u], start=False, stop=True, inc=True)
                    if lvl == 7:
                        ae_m = act(lambda eng, w_u=w_u: eng.activation(
                            st_t[:, 0:w_u], PS3[:, 0:w_u], AF.Copy),
                            waits=[("pe", pe_sc2)])
                        ve_m = ae_m
                        tokC["ps3"] = ("ae", ae_m)
                        tokC[("st", 0)] = ("ae", ae_m)
                        continue
                    ve_m = vec(lambda eng, w_u=w_u: eng.tensor_mul(
                        st_t[:, 0:w_u], PS3[:, 0:w_u], tri[:, 0:w_u]),
                        waits=[("pe", pe_sc2), tokC[("st", 0)],
                               ("dH", dH_attn)])
                    tokC["ps3"] = ("ve", ve_m)
                    wait_st = [("ve", ve_m)]
                    if lvl >= 5 and lvl not in (7, 8):
                        pe_at = mm(vtok[:, u * NST + h * 65: u * NST + (h + 1) * 65],
                                   st_t[:, 0:w_u], PS2[0:65, u * 128:512],
                                   start=False, stop=(u == 3),
                                   waits=wait_st, inc=(u == 3))
                        tokC[("st", 0)] = ("pe", pe_at)
                    else:
                        tokC[("st", 0)] = wait_st[0]
                if lvl < 6 or lvl in (7, 8):
                    tokC["ps2"] = ("pe", pe_at)
                    continue
                ve_d1 = vec(lambda eng: eng.tensor_scalar(
                    iotat[64:65, 0:512], PS2[64:65, 0:512], 1e-6, None, OP.max),
                    waits=[("pe", pe_at), tokC.get("iden", ("pe", 0))])
                ve_d2 = vec(lambda eng: eng.reciprocal(
                    iotat[64:65, 0:512], iotat[64:65, 0:512]))
                tokC["iden"] = ("ve", ve_d2)
                pe_bc2 = mm(onesT[64:65, 0:64], iotat[64:65, 0:512],
                            PS4[0:64, 0:512], start=True, stop=True,
                            waits=[("ve", ve_d2), tokC["ps4"]], inc=True)
                tokC["iotat"] = ("pe", pe_bc2)
                ae_n = act(lambda eng: eng.activation(
                    at64[0:64, :], PS2[0:64, 0:512], AF.Copy),
                    waits=[("pe", pe_at), tokC.get("at64", ("pe", 0))])
                if stage == "l1_dbg" and c == 0 and p == 7 and hh == 1:
                    dma("gpsimd", out_h.ap()[0:128, 0:512], qp_t[:],
                        waits=[("ve", P.v("ve")), ("ae", P.v("ae"))], sem="dD")
                    dma("gpsimd", out_h.ap()[0:128, 512:1024], prQc[:],
                        waits=[("ve", P.v("ve"))], sem="dD")
                    dma("gpsimd", out_h.ap()[0:128, 1024:1536], prKc[:],
                        waits=[("ve", P.v("ve"))], sem="dD")
                    dma("gpsimd", out_h.ap()[128:256, 0:512], st_t[:],
                        waits=[("ve", P.v("ve"))], sem="dD")
                    dma("gpsimd", out_h.ap()[128:256, 512:1024], at64[:],
                        waits=[("ae", P.v("ae"))], sem="dD")
                    dma("gpsimd", out_h.ap()[384:385, 0:512], iotat[64:65, 0:512],
                        waits=[("ve", P.v("ve"))], sem="dD")
                    dma("gpsimd", out_h.ap()[385:386, 0:512], cosfc[0:1, :],
                        waits=[("ve", P.v("ve"))], sem="dD")
                    dma("gpsimd", out_h.ap()[386:387, 0:512], sinfc[0:1, :],
                        waits=[("ve", P.v("ve"))], sem="dD")
                    dma("gpsimd", out_h.ap()[256:384, 0:512],
                        vtok[:, 0:512],
                        waits=[("ve", P.v("ve")), ("ae", P.v("ae"))], sem="dD")
                    dma("gpsimd", out_h.ap()[512:640, 0:512], xnv(0, 0, 512),
                        waits=[("ve", P.v("ve"))], sem="dD")
                    dma("gpsimd", out_h.ap()[640:768, 0:512],
                        bufB[:, NK * 1024:NK * 1024 + 512],
                        waits=[("ve", P.v("ve"))], sem="dD")
                    raise _EarlyStop(nc, P, es, P.v("dD"))
                tokC["ps2"] = ("ae", ae_n)
                at_tile = bufA[:, p * 2048 + t0: p * 2048 + t0 + 512]
                if hh == 0:
                    ve_attn_done = vec(lambda eng, at_tile=at_tile: eng.tensor_mul(
                        at_tile[0:64, :], at64[0:64, :], PS4[0:64, 0:512]),
                        waits=[("pe", pe_bc2), ("ae", ae_n)])
                    tokC["at64"] = ("ve", ve_attn_done)
                    tokC["ps4"] = ("ve", ve_attn_done)
                else:
                    ve_sh = vec(lambda eng: eng.tensor_mul(
                        st_t[0:64, :], at64[0:64, :], PS4[0:64, 0:512]),
                        waits=[("pe", pe_bc2), ("ae", ae_n), ("dH", dH_attn)])
                    d_sh3 = dma("sync", at_tile[64:128, :], st_t[0:64, :],
                                sem="dH", waits=[("ve", ve_sh)])
                    dH_attn = d_sh3[1]
                    tokC["at64"] = ("ve", ve_sh)
                    tokC["ps4"] = ("ve", ve_sh)
                    tokC[("st", 0)] = ("ve", ve_sh)
            tokC["qp"] = tokC["ps4"]
            tokC["prq"] = tokC[("st", 0)]
            tokC["prk"] = tokC[("st", 0)]
        tokC["sptot"] = ("pe", P.v("pe"))
        tokC["csf"] = tokC["ps0"]
    ve_attn_last = P.v("ve")

    pe_all_b2 = P.v("pe")
    tok_wo = ("pe", pe_all_b2)
    tok_evac = ("ve", 0)
    pe_wo_done = 0
    for dch in range(NK):
        d_wo = col_block_dma(wvsl[:, 0:1024], wo_h, dch * 128, 128, NK,
                             sem="wV2", waits=[tok_wo])
        pe_o = 0
        for k in range(NK):
            for tt in range(4):
                pw = []
                if k == 0 and tt == 0:
                    pw = [d_wo, ("ve", ve_attn_last), ("dH", dH_attn), tok_evac]
                pe_o = mm(wvsl[:, k * 128:(k + 1) * 128],
                          bufA[:, k * 2048 + tt * 512: k * 2048 + (tt + 1) * 512],
                          (PS0 if tt < 2 else PS1)[:, (tt % 2) * 512:(tt % 2) * 512 + 512],
                          start=(k == 0), stop=(k == NK - 1), waits=pw,
                          inc=(k == NK - 1))
        tok_wo = ("pe", pe_o)
        ve_wo = 0
        for tt in range(4):
            ve_wo = vec(lambda eng, dch=dch, tt=tt: eng.scalar_tensor_tensor(
                xr(dch, tt * 512, (tt + 1) * 512),
                (PS0 if tt < 2 else PS1)[:, (tt % 2) * 512:(tt % 2) * 512 + 512],
                bcol1[:, 16 + dch:17 + dch], xr(dch, tt * 512, (tt + 1) * 512),
                OP.add, OP.add), waits=[("pe", pe_o)] if tt == 0 else ())
        tok_evac = ("ve", ve_wo)
        pe_wo_done = pe_o
    ve_wo_done = tok_evac[1]

    if stage == "l1_atdump":
        last = None
        for k in range(NK):
            last = dma("gpsimd", out_h.ap()[k * 128:(k + 1) * 128, :],
                       bufA[:, k * 2048:(k + 1) * 2048],
                       waits=[("ve", ve_wo_done)], sem="dS")
        P.emit("sync", lambda eng: None, waits=[("dS", last[1])])
        emit_all(nc, P)
        es.close()
        return nc
    if stage.startswith("l1_b2"):
        return early_out([("ve", ve_wo_done), ("pe", P.v("pe")),
                          ("ve", P.v("ve")), ("ae", P.v("ae")),
                          ("ge", P.v("ge"))])
    pe_prep0 = pe_wo_done

    def fence(n=1):
        v = 0
        for _ in range(n):
            v = vec(lambda eng: eng.tensor_copy(red[:, 80:144], red[:, 16:80]))
        return v

    for hch in range(NH1):
        sl = hch % 2
        d_w2 = dma("gpsimd", w2sl[:, sl * 1024:(sl + 1) * 1024],
                   wout_h.ap()[hch * 128:(hch + 1) * 128, :], sem="pW%d" % sl,
                   waits=[("pe", pe_prep0)] if hch < 2 else
                   [("ve", P.v("ve"))])
        ve_a2 = vec(lambda eng, sl=sl, hch=hch: eng.reduce_max(
            rso32[:, hch:hch + 1], w2sl[:, sl * 1024:(sl + 1) * 1024],
            axis=AX.X, apply_absolute_value=True),
            waits=[d_w2, ("ve", ve_wo_done) if hch == 0 else ("ve", 0)])
    ve_amo = fence(1)
    ve_amo = vec(lambda eng: eng.tensor_scalar(
        rso32[:], rso32[:], 1e-8, None, OP.max))
    ve_amo = fence(1)
    ve_rq = vec(lambda eng: eng.reciprocal(amx32[:], rso32[:]))
    ve_rq = fence(1)
    ve_rq = vec(lambda eng: eng.tensor_scalar(
        sc32[:], amx32[:], 127.0, None, OP.mult))
    ve_rso = vec(lambda eng: eng.tensor_scalar(
        rso32[:], rso32[:], 1.0 / 127.0, None, OP.mult))
    ve_rq2 = fence(1)
    tok_p0 = tok_p1 = ("ve", ve_rq2)
    for hch in range(NH1):
        sl = hch % 2
        d_w2 = dma("gpsimd", w2sl[:, sl * 1024:(sl + 1) * 1024],
                   wout_h.ap()[hch * 128:(hch + 1) * 128, :], sem="pW%d" % sl,
                   waits=[tok_p0 if sl == 0 else tok_p1])
        ae_q1 = act(lambda eng, sl=sl, hch=hch: eng.activation(
            wpq[:, 0:512], w2sl[:, sl * 1024:sl * 1024 + 512],
            AF.Copy, scale=sc32[:, hch:hch + 1], bias=MAGIC),
            waits=[d_w2, ("ve", ve_rq2)])
        ae_q2 = act(lambda eng, sl=sl, hch=hch: eng.activation(
            wpabs[:, 0:512], w2sl[:, sl * 1024 + 512:(sl + 1) * 1024],
            AF.Copy, scale=sc32[:, hch:hch + 1], bias=MAGIC))
        ve_w1 = vec(lambda eng, sl=sl: eng.tensor_scalar(
            w2sl[:, sl * 1024:sl * 1024 + 512], wpq[:, 0:512], -MAGIC, None,
            OP.add), waits=[("ae", ae_q2)])
        ve_w2 = vec(lambda eng, sl=sl: eng.tensor_scalar(
            w2sl[:, sl * 1024 + 512:(sl + 1) * 1024], wpabs[:, 0:512], -MAGIC,
            None, OP.add))
        d_w2o = dma("sync", woutq_d.ap()[hch * 128:(hch + 1) * 128, :],
                    w2sl[:, sl * 1024:(sl + 1) * 1024], sem="pO%d" % sl,
                    waits=[("ve", ve_w2)])
        if sl == 0:
            tok_p0 = d_w2o
        else:
            tok_p1 = d_w2o

    sc_done = ("pW0", 0)
    ae_tr_tok = ("ae", P.v("ae"))
    for blk in range(8):
        c0 = blk * 512
        d_wi = dma("gpsimd",
                   wprep[:].rearrange("p (k f) -> p k f", f=512),
                   win_h.ap()[:, c0:c0 + 512].rearrange("(k p) f -> p k f", p=128),
                   sem="pI0", waits=[sc_done if blk else ("pe", pe_prep0)])
        ve_ab = vec(lambda eng: eng.tensor_tensor(
            wpabs[:], wprep[:, 0:512], wprep[:, 512:1024], OP.max),
            waits=[d_wi])
        for k in range(2, NK):
            ve_ab = vec(lambda eng, k=k: eng.tensor_tensor(
                wpabs[:], wpabs[:], wprep[:, k * 512:(k + 1) * 512], OP.max))
        ve_ab = vec(lambda eng: eng.tensor_tensor(
            wpq[:], wprep[:, 0:512], wprep[:, 512:1024], OP.min))
        for k in range(2, NK):
            ve_ab = vec(lambda eng, k=k: eng.tensor_tensor(
                wpq[:], wpq[:], wprep[:, k * 512:(k + 1) * 512], OP.min))
        ve_ab = vec(lambda eng: eng.tensor_scalar(
            wpq[:], wpq[:], -1.0, None, OP.mult))
        ve_ab = vec(lambda eng: eng.tensor_tensor(
            wpabs[:], wpabs[:], wpq[:], OP.max))
        ve_ab = vec(lambda eng: eng.tensor_copy(st_t[:, 0:512], wpabs[:]))
        pe_tr = 0
        for q in range(4):
            P.emit("tensor", (lambda eng, q=q: eng.transpose(
                PS4.bitcast(BF16)[:, q * 128:(q + 1) * 128],
                st_t[:, q * 128:(q + 1) * 128], identb[:])),
                waits=([("ve", ve_ab), ("ve", ve_id),
                        ("pe", pe_prep0) if blk == 0 else ("pe", 0),
                        ae_tr_tok] if q == 0 else ()),
                incs=[("pe", 1)])
        pe_tr = P.v("pe")
        ve_mx = 0
        for q in range(4):
            ve_mx = vec(lambda eng, blk=blk, q=q: eng.reduce_max(
                sc32[:, blk * 4 + q:blk * 4 + q + 1],
                PS4.bitcast(BF16)[:, q * 128:(q + 1) * 128], axis=AX.X),
                waits=([("pe", pe_tr)] if q == 0 else ()))
        ae_tr_tok = ("ve", ve_mx)
        sc_done = ("ve", ve_mx)
    ve_f = fence(1)
    ve_f = vec(lambda eng: eng.tensor_scalar(
        sc32[:], sc32[:], 1e-8, None, OP.max))
    ve_f = fence(1)
    ve_rs = vec(lambda eng: eng.tensor_scalar(
        rs32[:], sc32[:], 1.0 / 127.0, None, OP.mult),
        waits=[("ve", ve_scales)])
    ve_rs = vec(lambda eng: eng.tensor_scalar(
        rs32[:], rs32[:], red[:, 34:35], None, OP.mult))
    ve_f = fence(1)
    ve_rp = vec(lambda eng: eng.reciprocal(amx32[:], sc32[:]))
    ve_rp = fence(1)
    ve_rp = vec(lambda eng: eng.tensor_scalar(
        sc32[:], amx32[:], 127.0, None, OP.mult))
    q_done = ("ve", ve_rp)
    tok_srow = ("pe", 0)
    for blk in range(8):
        c0 = blk * 512
        d_wi = dma("gpsimd",
                   wprep[:].rearrange("p (k f) -> p k f", f=512),
                   win_h.ap()[:, c0:c0 + 512].rearrange("(k p) f -> p k f", p=128),
                   sem="pI0", waits=[q_done])
        ve_cp = vec(lambda eng, blk=blk: eng.tensor_copy(
            st_t[:, 0:4], sc32[:, blk * 4:(blk + 1) * 4]),
            waits=[("ve", ve_rp)] if blk == 0 else [])
        P.emit("tensor", (lambda eng: eng.transpose(
            PS4.bitcast(BF16)[0:4, 0:128], st_t[:, 0:4], identb[:])),
            waits=[("ve", ve_cp), tok_srow], incs=[("pe", 1)])
        pe_t2 = P.v("pe")
        ae_sr = act(lambda eng: eng.activation(
            st_t[0:4, 0:128], PS4.bitcast(BF16)[0:4, 0:128], AF.Copy),
            waits=[("pe", pe_t2)])
        d_fl = dma("sync", srow_t[0:1, :].rearrange("a (k f) -> a k f", f=128),
                   st_t[0:4, 0:128].rearrange("k f -> () k f") if False else
                   st_t[0:4, 0:128], sem="pF", waits=[("ae", ae_sr)])
        pe_b3 = mm(ones_rb[0:1, 0:128], srow_t[0:1, :], PS3[:, 0:512],
                   start=True, stop=True, waits=[d_fl, q_done], inc=True)
        tok_srow = ("pe", pe_b3)
        for k in range(NK):
            ve_q1 = vec(lambda eng, k=k: eng.tensor_tensor(
                wpq[:], wprep[:, k * 512:(k + 1) * 512], PS3[:, 0:512], OP.mult),
                waits=([("pe", pe_b3)] if k == 0 else []))
            ve_q2 = vec(lambda eng: eng.tensor_scalar(
                wpq[:], wpq[:], MAGIC, -MAGIC, OP.add, OP.add))
            ve_q3 = vec(lambda eng, k=k: eng.tensor_copy(
                wpabs.bitcast(BF16)[:, 0:512], wpq[:]))
            d_qo = dma("sync", winq_d.ap()[k * 128:(k + 1) * 128, c0:c0 + 512],
                       wpabs.bitcast(BF16)[:, 0:512], sem="pF",
                       waits=[("ve", ve_q3)])
            q_done = d_qo
    pe_prep_done = P.v("pe")
    ae_prep_done = P.v("ae")

    if stage == "l1_prep":
        return early_out([("pe", pe_prep_done), ("ae", ae_prep_done),
                          ("ve", P.v("ve")), (QSEM["sync"], P.v("dS")),
                          ("pF", P.v("pF"))])
    hT = {}
    ve_fold = 0
    for tt2 in range(4):
        t0 = tt2 * 512
        ve_xq = 0
        for k in range(NK):
            ve_x1 = vec(lambda eng, k=k, t0=t0: eng.tensor_scalar(
                qtf[:, 0:512], xnv(k, t0, t0 + 512), red[:, 33:34], MAGIC,
                OP.mult, OP.add),
                waits=([("ve", ve_scales), ("pe", pe_prep_done),
                        ("ae", ae_prep_done)] if (tt2 == 0 and k == 0) else []))
            ve_xq = vec(lambda eng, k=k: eng.tensor_scalar(
                xq_t[:, k * 512:(k + 1) * 512], qtf[:, 0:512], -MAGIC, None,
                OP.add))
        for hch in range(NH1):
            sl = hch % 2
            d_w1 = col_block_dma(w1sl[:, sl * 1024:(sl + 1) * 1024], winq_d,
                                 hch * 128, 128, NK, sem="fI%d" % sl,
                                 waits=[hT.get(("w", sl), ("pe", pe_prep_done))])
            pe_h = 0
            for k in range(NK):
                pw = []
                if k == 0:
                    pw = [d_w1, hT.get(("ps", sl), ("ae", 0)), ("ve", ve_xq)]
                pe_h = mm(w1sl[:, sl * 1024 + k * 128: sl * 1024 + (k + 1) * 128],
                          xq_t[:, k * 512:(k + 1) * 512],
                          (PS0 if sl == 0 else PS1)[:, 0:512],
                          start=(k == 0), stop=(k == NK - 1), waits=pw,
                          inc=(k == NK - 1))
            hT[("w", sl)] = ("pe", pe_h)
            ae_h = act(lambda eng, sl=sl, hch=hch: eng.activation(
                hidt[:, sl * 512:(sl + 1) * 512],
                (PS0 if sl == 0 else PS1)[:, 0:512],
                AF.Gelu, scale=rs32[:, hch:hch + 1],
                bias=bcol1[:, 32 + hch:33 + hch]),
                waits=[("pe", pe_h), hT.get(("hd", sl), ("pe", 0)),
                       hT.get(("am", sl), ("pe", 0))])
            hT[("ps", sl)] = ("ae", ae_h)
            ve_hm = vec(lambda eng, sl=sl, hch=hch: eng.reduce_max(
                amx32[:, hch:hch + 1], hidt[:, sl * 512:(sl + 1) * 512],
                axis=AX.X, apply_absolute_value=True), waits=[("ae", ae_h)])
            hT[("am", sl)] = ("ve", ve_hm)
            d_ho = dma("sync", hid_d.ap()[hch * 128:(hch + 1) * 128, t0:t0 + 512],
                       hidt[:, sl * 512:(sl + 1) * 512], sem="hd%d" % sl,
                       waits=[("ve", ve_hm)])
            hT[("hd", sl)] = d_ho
        ve_fold = fence(1)
        ve_fold = vec(lambda eng, tt2=tt2: eng.reduce_max(
            red[:, 40 + tt2:41 + tt2], amx32[:], axis=AX.X))
    ve_fold = fence(1)
    ve_fold = vec(lambda eng: eng.reduce_max(
        red[:, 44:45], red[:, 40:44], axis=AX.X))
    ve_fold = fence(1)
    ve_fb = vec(lambda eng: eng.tensor_copy(summ.bitcast(BF16)[:, 2:3],
                                            red[:, 44:45]))
    P.emit("tensor", (lambda eng: eng.transpose(
        PS4.bitcast(BF16)[0:1, 0:128], summ.bitcast(BF16)[:, 2:3], identb[:])),
        waits=[("ve", ve_fb), tok_srow, ("ae", P.v("ae"))], incs=[("pe", 1)])
    pe_am2 = P.v("pe")
    ve_am3 = vec(lambda eng: eng.reduce_max(
        smax[0:1, 2:3], PS4.bitcast(BF16)[0:1, 0:128], axis=AX.X),
        waits=[("pe", pe_am2)])
    g_a2 = dma("gpsimd", ag2_in.ap()[0:1, 0:1], smax[0:1, 2:3],
               waits=[("ve", ve_am3)], sem="dB")
    P.emit("gpsimd", lambda eng: eng.collective_compute(
        "AllGather", OP.bypass, replica_groups=[list(range(NCORES))],
        ins=[ag2_in.ap()], outs=[ag2_out.ap()]),
        waits=[g_a2], incs=[("cc", 1)])
    cc2 = P.v("cc")
    d_a2 = dma("gpsimd", smax[0:8, 10:11], ag2_out.ap()[:, 0:1], sem="dB",
               waits=[("cc", cc2)], slow=True)
    d_a2f = dma("gpsimd", smax[0:1, 32:40], smax[0:8, 10:11], sem="dB",
                waits=[d_a2], slow=True)
    ve_g2 = vec(lambda eng: eng.reduce_max(
        smax[0:1, 3:4], smax[0:1, 32:40], axis=AX.X), waits=[d_a2f])
    ve_g2 = vec(lambda eng: eng.tensor_scalar(
        smax[0:1, 4:5], smax[0:1, 3:4], 1.0 / 32767.0, None, OP.mult))
    pe_bc3 = mm(ones_row[0:1, 0:128], smax[0:1, 3:5], PS4[:, 0:2],
                start=True, stop=True,
                waits=[("ve", ve_g2), ("ve", ve_am3)], inc=True)
    ve_s2 = vec(lambda eng: eng.reciprocal(red[:, 45:46], PS4[:, 0:1]),
                waits=[("pe", pe_bc3)])
    ve_s2 = vec(lambda eng: eng.tensor_copy(red[:, 47:48], PS4[:, 1:2]))
    ve_s2 = vec(lambda eng: eng.tensor_scalar(
        red[:, 46:47], red[:, 45:46], 32767.0, None, OP.mult))
    ve_cmb = vec(lambda eng: eng.tensor_scalar(
        cmb32[:], rso32[:], red[:, 47:48], None, OP.mult),
        waits=[("ve", ve_rso)])
    ve_sc2done = ve_cmb

    if stage == "l1_f1":
        return early_out([("ve", ve_sc2done)])
    f2 = {}
    ve_final = 0
    psl = [PS0[:, 0:512], PS0[:, 512:1024], PS1[:, 0:512], PS1[:, 512:1024],
           PS2[:, 0:512], PS2[:, 512:1024], PS3[:, 0:512], PS4[:, 0:512]]
    for tt2 in range(4):
        t0 = tt2 * 512
        pe_d2 = [0] * NK
        for hch in range(NH1):
            sl = hch % 2
            d_w2 = dma("sync", w2sl[:, sl * 1024:(sl + 1) * 1024],
                       woutq_d.ap()[hch * 128:(hch + 1) * 128, :],
                       sem="f2W%d" % sl,
                       waits=[f2.get(("w", sl), ("pe", pe_am2))])
            d_h2 = dma("scalar", hidt[:, sl * 512:(sl + 1) * 512],
                       hid_d.ap()[hch * 128:(hch + 1) * 128, t0:t0 + 512],
                       sem="f2H%d" % sl,
                       waits=[f2.get(("h", sl), ("ve", ve_sc2done))])
            ve_hq = vec(lambda eng, sl=sl: eng.tensor_scalar(
                qtf[:, sl * 512:(sl + 1) * 512],
                hidt[:, sl * 512:(sl + 1) * 512],
                red[:, 46:47], MAGIC, OP.mult, OP.add),
                waits=[d_h2] + ([("ve", ve_sc2done)] if (tt2 == 0 and hch < 2) else []))
            ve_hq2 = vec(lambda eng, sl=sl, hch=hch: eng.tensor_scalar(
                hidt[:, sl * 512:(sl + 1) * 512],
                qtf[:, sl * 512:(sl + 1) * 512],
                -MAGIC, cmb32[:, hch:hch + 1], OP.add, OP.mult))
            for dch in range(NK):
                pw = []
                if dch == 0:
                    pw = [d_w2, ("ve", ve_hq2)]
                    if hch == 0:
                        pw.append(f2.get("ps_all", ("pe", pe_am2)))
                pe_d2[dch] = mm(w2sl[:, sl * 1024 + dch * 128: sl * 1024 + (dch + 1) * 128],
                                hidt[:, sl * 512:(sl + 1) * 512], psl[dch],
                                start=(hch == 0), stop=(hch == NH1 - 1),
                                waits=pw, inc=True)
            f2[("w", sl)] = ("pe", pe_d2[NK - 1])
            f2[("h", sl)] = ("pe", pe_d2[NK - 1])
        for dch in range(NK):
            ve_final = vec(lambda eng, dch=dch, t0=t0: eng.scalar_tensor_tensor(
                xr(dch, t0, t0 + 512), psl[dch], bcol1[:, 24 + dch:25 + dch],
                xr(dch, t0, t0 + 512), OP.add, OP.add),
                waits=[("pe", pe_d2[dch])])
        f2["ps_all"] = ("ve", ve_final)

    last = None
    for k in range(NK):
        last = dma("sync", out_h.ap()[k * 128:(k + 1) * 128, :], xr(k),
                   waits=[("ve", ve_final)])
    P.emit("sync", lambda eng: None, waits=[("dS", last[1])])
    emit_all(nc, P)
    es.close()
    return nc


def emit_all(nc, P):
    with nc.Block() as block:
        @block.sync
        def _(eng):
            for f in P.ops["sync"]:
                f(eng)

        @block.scalar
        def _(eng):
            for f in P.ops["scalar"]:
                f(eng)

        @block.vector
        def _(eng):
            for f in P.ops["vector"]:
                f(eng)

        @block.tensor
        def _(eng):
            for f in P.ops["tensor"]:
                f(eng)

        @block.gpsimd
        def _(eng):
            for f in P.ops["gpsimd"]:
                f(eng)
    return nc


def make_inputs(inp):
    C = np.ascontiguousarray

    def A(v):
        return np.asarray(v)

    x = A(inp["L0_attn_x"])
    shared = {
        "wgT": C(A(inp["L0_attn_Wg"]).T), "wvT": C(A(inp["L0_attn_Wv"]).T),
        "wdT": C(A(inp["L0_attn_Wd"]).T),
        "bcols0": np.concatenate([
            A(inp["L0_attn_bg"]).reshape(8, 128).T,
            A(inp["L0_attn_bv"]).reshape(8, 128).T,
            A(inp["L0_attn_bd"]).reshape(8, 128).T], axis=1).copy(),
        "bcols1": np.concatenate([
            A(inp["L1_attn_bq"]).reshape(8, 128).T,
            A(inp["L1_attn_bk"]).reshape(8, 128).T,
            A(inp["L1_attn_bo"]).reshape(8, 128).T,
            A(inp["L1_ffn_b_out"]).reshape(8, 128).T,
            A(inp["L1_ffn_b_in"]).reshape(32, 128).T], axis=1).copy(),
        "fgT": C(A(inp["L0_ffn_w_gate"]).T), "fuT": C(A(inp["L0_ffn_w_up"]).T),
        "fdT": C(A(inp["L0_ffn_w_down"]).T),
        "wq": C(A(inp["L1_attn_Wq"])), "wk": C(A(inp["L1_attn_Wk"])),
        "wv1": C(A(inp["L1_attn_Wv"])), "wo": C(A(inp["L1_attn_Wo"])),
        "bk": A(inp["L1_attn_bk"]).reshape(1, -1),
        "bv1": A(inp["L1_attn_bv"]).reshape(1, -1),
        "win": C(A(inp["L1_ffn_W_in"])), "wout": C(A(inp["L1_ffn_W_out"])),
    }
    maps = []
    for core in range(NCORES):
        b, j = divmod(core, 4)
        xs = C(x[b, j * T:(j + 1) * T, :].T)
        params = np.zeros((128, 24), np.float32)
        for r in range(NCORES):
            rb, rj = divmod(r, 4)
            params[:, r] = 1.0 if (rb == b and rj < j) else 0.0
            params[:, 9 + r] = 1.0 - params[:, r]
        params[:, 8] = (j * T) * ANG_SCALE
        params[:, 17] = (j * T) * ANG_SCALE + np.pi / 2.0
        m = dict(shared)
        m["x"] = xs
        m["params"] = params
        maps.append(m)
    return maps


def _run(stage, inp):
    maps = make_inputs(inp)
    try:
        nc = build_program(stage)
    except _EarlyStop as e:
        nc, P, es, dD = e.args
        P.emit("sync", lambda eng: None, waits=[("dD", dD)])
        emit_all(nc, P)
        es.close()
    res = run_bass_kernel_spmd(nc, maps, core_ids=list(range(NCORES)))
    out = np.empty((2, 8192, 1024), np.float32)
    for core in range(NCORES):
        b, j = divmod(core, 4)
        out[b, j * T:(j + 1) * T, :] = res.results[core]["out"].T
    return out


def kernel(**inp):
    return _run("full", inp)


# version 06
# speedup vs baseline: 1.1760x; 1.1760x over previous
import numpy as np
from contextlib import ExitStack

import concourse.bass as bass
import concourse.mybir as mybir
from concourse.bass_utils import run_bass_kernel_spmd

F32 = mybir.dt.float32
BF16 = mybir.dt.bfloat16
I32 = mybir.dt.int32
AF = mybir.ActivationFunctionType
OP = mybir.AluOpType
AX = mybir.AxisListType

NCORES = 8
D = 1024
T = 2048
NK = 8
NH = 16
DH = 64
CHUNK = 512
NCH = T // CHUNK
HFF0 = 3072
NH0 = HFF0 // 128
HFF1 = 4096
NH1 = HFF1 // 128
NST = NH * 65
MAGIC = 12582912.0
PI = float(np.pi)
ANG_SCALE = PI / (2.0 * 8192.0)

ENGINES = ("sync", "scalar", "vector", "tensor", "gpsimd")
QSEM = {"sync": "dS", "scalar": "dA", "gpsimd": "dG"}


class Plan:
    def __init__(self):
        self.ops = {e: [] for e in ENGINES}
        self.cnt = {}
        self.sems = {}

    def newsem(self, *names):
        for n in names:
            self.cnt[n] = 0

    def v(self, name):
        return self.cnt[name]

    def emit(self, engine, fn, waits=(), incs=()):
        waits = tuple((s, v) for (s, v) in waits if v > 0)
        for (s, n) in incs:
            self.cnt[s] += n
        def run(eng, _fn=fn, _w=waits, _i=tuple(incs), _sems=self.sems):
            for (s, v) in _w:
                eng.wait_ge(_sems[s], v)
            inst = _fn(eng)
            for (s, n) in _i:
                inst = inst.then_inc(_sems[s], n)

        self.ops[engine].append(run)


class Banks:

    def __init__(self, slots, tok=("pe", 0)):
        self.slots = list(slots)
        self.tok = [tok] * len(slots)
        self.i = 0

    def get(self):
        j = self.i % len(self.slots)
        self.i += 1
        return j, self.slots[j], self.tok[j]

    def set_tok(self, j, tok):
        self.tok[j] = tok


class _EarlyStop(Exception):
    pass


def build_program(stage="full"):
    do_attn = stage in ("full", "l0", "l0_attn", "l0_hin") or stage.startswith("l1")
    do_ffn = stage in ("full", "l0", "l0_ffn", "l0_hid") or stage.startswith("l1")
    nc = bass.Bass("TRN2", target_bir_lowering=False, debug=False)
    P = Plan()
    P.newsem("dS", "dA", "dG", "dB", "pe", "ve", "ae", "ge", "cc", "st")

    x_h = nc.dram_tensor("x", [D, T], F32, kind="ExternalInput")
    out_h = nc.dram_tensor("out", [D, T], F32, kind="ExternalOutput")
    params_h = nc.dram_tensor("params", [128, 24], F32, kind="ExternalInput")
    wgT_h = nc.dram_tensor("wgT", [D, D], F32, kind="ExternalInput")
    wvT_h = nc.dram_tensor("wvT", [D, D], F32, kind="ExternalInput")
    wdT_h = nc.dram_tensor("wdT", [D, D], F32, kind="ExternalInput")
    bcols0_h = nc.dram_tensor("bcols0", [128, 24], F32, kind="ExternalInput")
    bcols1_h = nc.dram_tensor("bcols1", [128, 64], F32, kind="ExternalInput")
    fgT_h = nc.dram_tensor("fgT", [D, HFF0], I32, kind="ExternalInput")
    fuT_h = nc.dram_tensor("fuT", [D, HFF0], I32, kind="ExternalInput")
    fdT_h = nc.dram_tensor("fdT", [HFF0, D], I32, kind="ExternalInput")
    wq_h = nc.dram_tensor("wq", [D, D], F32, kind="ExternalInput")
    wk_h = nc.dram_tensor("wk", [D, D], F32, kind="ExternalInput")
    wv1_h = nc.dram_tensor("wv1", [D, D], F32, kind="ExternalInput")
    wo_h = nc.dram_tensor("wo", [D, D], F32, kind="ExternalInput")
    bk_h = nc.dram_tensor("bk", [1, D], F32, kind="ExternalInput")
    bv1_h = nc.dram_tensor("bv1", [1, D], F32, kind="ExternalInput")
    win_h = nc.dram_tensor("win", [D, HFF1], F32, kind="ExternalInput")
    wout_h = nc.dram_tensor("wout", [HFF1, D], F32, kind="ExternalInput")

    ag0_in = nc.dram_tensor("ag0_in", [1, 128 * 16], F32)
    ag0_out = nc.dram_tensor("ag0_out", [NCORES, 128 * 16], F32)
    ag1_in = nc.dram_tensor("ag1_in", [1, 128 * NST], F32)
    ag1_out = nc.dram_tensor("ag1_out", [NCORES, 128 * NST], F32)
    ag1b_in = nc.dram_tensor("ag1b_in", [1, 8], F32)
    ag1b_out = nc.dram_tensor("ag1b_out", [NCORES, 8], F32)
    ag2_in = nc.dram_tensor("ag2_in", [1, 8], F32)
    ag2_out = nc.dram_tensor("ag2_out", [NCORES, 8], F32)
    winq_d = nc.dram_tensor("winq_d", [D, HFF1], BF16)
    woutq_d = nc.dram_tensor("woutq_d", [HFF1, D], BF16)
    sinrow_d = nc.dram_tensor("sinrow_d", [1, HFF1], F32)
    hid_d = nc.dram_tensor("hid_d", [HFF1, T], BF16)

    es = ExitStack()

    def sbt(name, shape, dtype):
        return es.enter_context(nc.sbuf_tensor(name, shape, dtype))

    xres = sbt("xres", [128, NK * T], F32)
    xn = sbt("xn", [128, NK * T], BF16)
    bufA = sbt("bufA", [128, NK * T], BF16)
    bufB = sbt("bufB", [128, NK * T], BF16)
    ones_row = sbt("ones_row", [1, 128], F32)
    ones_col = sbt("ones_col", [128, 1], F32)
    prm = sbt("prm", [128, 24], F32)
    bcolL0 = sbt("bcolL0", [128, 3 * NK], F32)
    bcol1 = sbt("bcol1", [128, 64], F32)
    tri = sbt("tri", [128, 512], BF16)
    summ = sbt("summ", [128, 16], F32)
    hin = sbt("hin", [128, 8], F32)
    agbuf = sbt("agbuf", [128, 16 * NCORES], F32)
    red = sbt("red", [128, 144], F32)
    smax = sbt("smax", [8, 48], F32)
    identb = sbt("identb", [128, 128], BF16)
    ones_rb = sbt("ones_rb", [1, 128], BF16)

    ARENA_B = 45056
    arena = sbt("arena", [128, ARENA_B // 4], F32)
    abase = nc.lookup_mloc(arena).addr

    def aview(name, shape, dtype, off):
        assert off + shape[-1] * mybir.dt.size(dtype) * int(np.prod(shape[1:-1], initial=1)) <= ARENA_B
        return nc.alloc_sbuf_tensor_at(name, shape, dtype, offset=abase + off)

    KB = 1024
    lnsq = aview("lnsq", [128, 4096], F32, 0)
    lnrw = aview("lnrw", [128, 4096], F32, 16 * KB)
    a_row = aview("a_row", [128, T], F32, 0)
    xc_row = aview("xc_row", [128, T], F32, 8 * KB)
    g_row = aview("g_row", [128, T], F32, 16 * KB)
    vh_row = aview("vh_row", [128, T], F32, 24 * KB)
    wslA = aview("wslA", [128, 6144], BF16, 32 * KB)
    ffn_h = aview("ffn_h", [128, NH0 * 512], BF16, 0)
    ffn_gu = aview("ffn_gu", [128, 4096], BF16, 24 * KB)
    ffn_dn = aview("ffn_dn", [128, 2 * 1536], BF16, 32 * KB)
    gact = aview("gact", [128, 1024], F32, 38 * KB)
    corr_t = aview("corr_t", [128, T], F32, 0)

    PS0 = es.enter_context(nc.psum_tensor("PS0", [128, 1024], F32))
    PS1 = es.enter_context(nc.psum_tensor("PS1", [128, 1024], F32))
    PS2 = es.enter_context(nc.psum_tensor("PS2", [128, 1024], F32))
    PS3 = es.enter_context(nc.psum_tensor("PS3", [128, 512], F32))
    PS4 = es.enter_context(nc.psum_tensor("PS4", [128, 512], F32))

    for s in list(P.cnt.keys()):
        P.sems[s] = es.enter_context(nc.semaphore(s))

    def dma(q, out_ap, in_ap, waits=(), sem=None, slow=False):
        s = sem or QSEM[q]
        if s not in P.cnt:
            P.newsem(s)
            P.sems[s] = es.enter_context(nc.semaphore(s))
        P.emit(q, (lambda eng, o=out_ap, i=in_ap, sl=slow: eng.dma_start(
            out=o, in_=i, allow_slow_non_contiguous=sl)),
               waits=waits, incs=[(s, 16)])
        return (s, P.v(s))

    def mm(lhsT, rhs, out, start, stop, waits=(), inc=False):
        incs = [("pe", 1)] if inc else ()
        P.emit("tensor",
               (lambda eng, l=lhsT, r=rhs, o=out, s=start, p=stop:
                eng.matmul(o, l, r, start=s, stop=p)),
               waits=waits, incs=incs)
        return P.v("pe")

    def act(fn, waits=()):
        P.emit("scalar", fn, waits=waits, incs=[("ae", 1)])
        return P.v("ae")

    def vec(fn, waits=()):
        P.emit("vector", fn, waits=waits, incs=[("ve", 1)])
        return P.v("ve")

    def gp(fn, waits=()):
        P.emit("gpsimd", fn, waits=waits, incs=[("ge", 1)])
        return P.v("ge")

    def col_block_dma(dst_tile, src_h, col0, ncols, nk, waits=(), row0=0, sem=None):
        o = dst_tile.rearrange("p (k f) -> p k f", f=ncols)
        i = src_h.ap()[row0:row0 + nk * 128, col0:col0 + ncols].rearrange(
            "(k p) f -> p k f", p=128)
        return dma("gpsimd", o, i, waits, sem=sem)

    def xr(k, t0=0, t1=T):
        return xres[:, k * T + t0: k * T + t1]

    def xnv(k, t0=0, t1=T):
        return xn[:, k * T + t0: k * T + t1]

    setA = Banks([PS0[:, 0:512], PS0[:, 512:1024], PS1[:, 0:512], PS1[:, 512:1024]])
    setB = Banks([PS2[:, 0:512], PS2[:, 512:1024], PS3[:, 0:512], PS4[:, 0:512]])

    xw = []
    for k in range(NK):
        xw.append(dma("sync", xr(k), x_h.ap()[k * 128:(k + 1) * 128, :]))
    dma("sync", prm[:], params_h.ap())
    dma("sync", bcolL0[:], bcols0_h.ap())
    w_const = dma("sync", bcol1[:], bcols1_h.ap())

    vec(lambda eng: eng.memset(ones_row[:], 1.0))
    vec(lambda eng: eng.memset(ones_col[:], 1.0))
    v_tri = vec(lambda eng: eng.memset(tri[:], 1.0))
    g_tri = gp(lambda eng: eng.iota(
        agbuf[:, 0:128], pattern=[[-1, 128]], base=0, channel_multiplier=1,
        allow_small_or_imprecise_dtypes=True), waits=[("ve", v_tri)])
    vec(lambda eng: eng.tensor_single_scalar(
        tri[:, 0:128], agbuf[:, 0:128], 0.0, OP.is_le), waits=[("ge", g_tri)])

    def emit_layernorm(first_waits, stats_tok=("pe", 0)):
        bcR = lnrw[:, 0:512]
        bcM = lnrw[:, 512:1024]
        tmp = lnrw[:, 1024:1536]
        R0 = lnrw[0:1, 1536:2048]
        R1 = lnrw[0:1, 2048:2560]
        R2 = lnrw[0:1, 2560:3072]
        xn_done = 0
        for tt in range(4):
            t0 = tt * 512
            ae_sq = 0
            for k in range(NK):
                w = list(first_waits) if (tt == 0 and k == 0) else []
                ae_sq = act(lambda eng, k=k, t0=t0: eng.activation(
                    lnsq[:, k * 512:(k + 1) * 512], xr(k, t0, t0 + 512),
                    AF.Square), waits=w)
            first_w = [("ae", ae_sq)] + ([stats_tok] if tt == 0 else [])
            for k in range(NK):
                mm(ones_col[:], xr(k, t0, t0 + 512), PS3[0:1, 0:512],
                   start=(k == 0), stop=(k == NK - 1),
                   waits=(first_w if k == 0 else ()))
            pe_s = 0
            for k in range(NK):
                pe_s = mm(ones_col[:], lnsq[:, k * 512:(k + 1) * 512],
                          PS4[0:1, 0:512], start=(k == 0), stop=(k == NK - 1),
                          inc=(k == NK - 1))
            vec(lambda eng, R0=R0: eng.tensor_scalar(
                R0, PS3[0:1, 0:512], 1.0 / D, None, OP.mult),
                waits=[("pe", pe_s)])
            vec(lambda eng, R1=R1: eng.tensor_scalar(
                R1, PS4[0:1, 0:512], 1.0 / D, 1e-5, OP.mult, OP.add))
            vec(lambda eng, R0=R0, R2=R2: eng.scalar_tensor_tensor(
                R2, R0, -1.0, R0, OP.mult, OP.mult))
            ve_var = vec(lambda eng, R1=R1, R2=R2: eng.tensor_add(R1, R1, R2))
            ae_v = act(lambda eng, R1=R1, R2=R2: eng.activation(R2, R1, AF.Sqrt),
                       waits=[("ve", ve_var)])
            vec(lambda eng, R1=R1, R2=R2: eng.reciprocal(R1, R2),
                waits=[("ae", ae_v)])
            ve_r = vec(lambda eng, R0=R0, R1=R1: eng.scalar_tensor_tensor(
                R0, R0, -1.0, R1, OP.mult, OP.mult))
            mm(ones_row[0:1, 0:128], R1, PS4[:, 0:512], start=True, stop=True,
               waits=[("ve", ve_r)])
            pe_b = mm(ones_row[0:1, 0:128], R0, PS3[:, 0:512],
                      start=True, stop=True, inc=True)
            act(lambda eng, bcR=bcR: eng.activation(bcR, PS4[:, 0:512], AF.Copy),
                waits=[("pe", pe_b)])
            ae_b = act(lambda eng, bcM=bcM: eng.activation(bcM, PS3[:, 0:512], AF.Copy))
            for k in range(NK):
                vec(lambda eng, k=k, t0=t0, tmp=tmp, bcR=bcR: eng.tensor_mul(
                    tmp, xr(k, t0, t0 + 512), bcR),
                    waits=([("ae", ae_b)] if k == 0 else []))
                xn_done = vec(lambda eng, k=k, t0=t0, tmp=tmp, bcM=bcM: eng.tensor_add(
                    xnv(k, t0, t0 + 512), tmp, bcM))
        return xn_done

    xn_all = emit_layernorm(list(xw) + [w_const])
    setB.tok[2] = ("ae", P.v("ae"))
    setB.tok[3] = ("ae", P.v("ae"))

    def l0_wslot(m, b):
        c0 = (m * 2 + b) * 1024
        return wslA[:, c0:c0 + 1024]

    wslot_tok = {}
    ve_rows_done = 0
    for e in (range(NK) if do_attn else ()):
        b = e % 2
        dtok = {}
        for m, src in enumerate((wdT_h, wvT_h, wgT_h)):
            dtok[m] = col_block_dma(l0_wslot(m, b), src, e * 128, 128, NK,
                                    waits=[wslot_tok.get((m, b), ("pe", 0))],
                                    sem="wA%d" % (m * 2 + b))
        for m in range(3):
            bs = setA if (e * 3 + m) % 2 == 0 else setB
            wslot = l0_wslot(m, b)
            groups = {}
            for tt in range(4):
                j, ap, tok = bs.get()
                pw = [dtok[m], tok]
                if m == 0 and e == 0:
                    pw.append(("ve", xn_all))
                pe_last = 0
                for k in range(NK):
                    pe_last = mm(wslot[:, k * 128:(k + 1) * 128],
                                 xnv(k, tt * 512, tt * 512 + 512), ap,
                                 start=(k == 0), stop=(k == NK - 1),
                                 waits=(pw if k == 0 else ()),
                                 inc=(k == NK - 1))
                groups[tt] = (pe_last, ap, j)
            wslot_tok[(m, b)] = ("pe", groups[3][0])
            dst = (a_row, vh_row, g_row)[m]
            bofs = (2 * NK, NK, 0)[m]
            for tt in range(4):
                pe_g, ap, j = groups[tt]
                w = [("pe", pe_g)]
                if tt == 0 and m == 0:
                    w.append(("ve", ve_rows_done))
                if m == 1:
                    ve_t = vec(lambda eng, dst=dst, tt=tt, ap=ap, bo=bofs, e=e:
                               eng.tensor_scalar(
                                   dst[:, tt * 512:tt * 512 + 512], ap,
                                   bcolL0[:, bo + e:bo + e + 1], None, OP.add),
                               waits=w)
                    bs.set_tok(j, ("ve", ve_t))
                else:
                    ae_t = act(lambda eng, dst=dst, tt=tt, ap=ap, bo=bofs, e=e:
                               eng.activation(dst[:, tt * 512:tt * 512 + 512], ap,
                                              AF.Sigmoid,
                                              bias=bcolL0[:, bo + e:bo + e + 1]),
                               waits=w)
                    bs.set_tok(j, ("ae", ae_t))
        ae_evacs = P.v("ae")
        vec(lambda eng: eng.tensor_scalar(
            xc_row[:], a_row[:], -1.0, 1.0, OP.mult, OP.add),
            waits=[("ae", ae_evacs)])
        vec(lambda eng: eng.tensor_mul(xc_row[:], xc_row[:], vh_row[:]))
        vec(lambda eng: eng.tensor_tensor_scan(
            vh_row[:], a_row[:], xc_row[:], 0.0, OP.mult, OP.add))
        vec(lambda eng: eng.tensor_tensor_scan(
            xc_row[:], a_row[:], a_row[:], 1.0, OP.mult, OP.bypass))
        vec(lambda eng, e=e: eng.tensor_mul(bufA[:, e * T:(e + 1) * T], vh_row[:], g_row[:]))
        vec(lambda eng, e=e: eng.tensor_mul(
            bufB[:, e * T:(e + 1) * T], xc_row[:], g_row[:]))
        vec(lambda eng, e=e: eng.tensor_copy(summ[:, e:e + 1], xc_row[:, T - 1:T]))
        ve_rows_done = vec(lambda eng, e=e: eng.tensor_copy(
            summ[:, 8 + e:8 + e + 1], vh_row[:, T - 1:T]))
    ve_summary = ve_rows_done
    pe_a1_done = P.v("pe")

    if not do_attn:
        ve_summary = vec(lambda eng: eng.memset(summ[:], 0.0))
    g_ag0 = dma("gpsimd", ag0_in.ap().rearrange("a (p c) -> (a p) c", p=128),
                summ[:], waits=[("ve", ve_summary)])
    P.emit("gpsimd", lambda eng: eng.collective_compute(
        "AllGather", OP.bypass, replica_groups=[list(range(NCORES))],
        ins=[ag0_in.ap()], outs=[ag0_out.ap()]),
        waits=[("dG", g_ag0[1])], incs=[("cc", 1)])
    cc0 = P.v("cc")

    gu_tok = {}
    dn_tok = {}
    tok_g = tok_u = tok_d = ("ae", P.v("ae"))
    ve_ffn = 0
    pe_dn_last = 0
    first_gu_dma = True
    for tp in (range(1) if stage == "l0_hid" else (range(4) if do_ffn else ())):
        t0 = tp * 512
        h_last = 0
        for hch in range(NH0):
            sg = hch % 2
            wg = [gu_tok.get(("g", sg), ("pe", 0))]
            if first_gu_dma:
                wg.append(("ve", ve_summary))
                wg.append(("ve", xn_all))
                first_gu_dma = False
            dg = col_block_dma(ffn_gu[:, sg * 1024:(sg + 1) * 1024], fgT_h,
                               hch * 128, 128, NK, waits=wg, sem="fg%d" % sg)
            du = col_block_dma(ffn_gu[:, (2 + sg) * 1024:(3 + sg) * 1024], fuT_h,
                               hch * 128, 128, NK,
                               waits=[gu_tok.get(("u", sg), ("pe", 0))],
                               sem="fu%d" % sg)
            pw_g = [dg, tok_g]
            pw_u = [du, tok_u]
            if tp == 0 and hch == 0:
                pw_g = pw_g + [("ve", xn_all)]
            pe_g = pe_u = 0
            for k in range(NK):
                pe_g = mm(ffn_gu[:, sg * 1024 + k * 128: sg * 1024 + (k + 1) * 128],
                          xnv(k, t0, t0 + 512), PS0[:, 0:512],
                          start=(k == 0), stop=(k == NK - 1),
                          waits=(pw_g if k == 0 else ()), inc=(k == NK - 1))
            gu_tok[("g", sg)] = ("pe", pe_g)
            for k in range(NK):
                pe_u = mm(ffn_gu[:, (2 + sg) * 1024 + k * 128: (2 + sg) * 1024 + (k + 1) * 128],
                          xnv(k, t0, t0 + 512), PS1[:, 0:512],
                          start=(k == 0), stop=(k == NK - 1),
                          waits=(pw_u if k == 0 else ()), inc=(k == NK - 1))
            gu_tok[("u", sg)] = ("pe", pe_u)
            w_s = [("pe", pe_g), tok_g, ("ve", h_last)]
            if tp > 0 and hch == 0:
                w_s.append(("pe", pe_dn_last))
            ae_s = act(lambda eng: eng.activation(
                gact[:, 0:512], PS0[:, 0:512], AF.Silu, scale=1.0 / 127.0),
                waits=w_s)
            tok_g = ("ae", ae_s)
            h_last = vec(lambda eng, hch=hch: eng.scalar_tensor_tensor(
                ffn_h[:, hch * 512:(hch + 1) * 512], PS1[:, 0:512],
                1.0 / 127.0, gact[:, 0:512], OP.mult, OP.mult),
                waits=[("ae", ae_s), ("pe", pe_u)])
            tok_u = ("ve", h_last)
        for dch in range(NK):
            pe_d = 0
            for half in range(2):
                hb = half
                dn_w = [dn_tok.get(hb, ("pe", pe_a1_done))]
                if hb not in dn_tok:
                    dn_w.append(("ve", ve_summary))
                dd = col_block_dma(ffn_dn[:, hb * 1536:(hb + 1) * 1536], fdT_h,
                                   dch * 128, 128, 12,
                                   row0=half * 1536, waits=dn_w,
                                   sem="fd%d" % hb)
                for k in range(12):
                    kk = half * 12 + k
                    pe_d = mm(ffn_dn[:, hb * 1536 + k * 128: hb * 1536 + (k + 1) * 128],
                              ffn_h[:, kk * 512:(kk + 1) * 512],
                              PS2[:, 0:512],
                              start=(kk == 0), stop=(kk == NH0 - 1),
                              waits=([dd] +
                                     ([("ve", h_last), tok_d] if kk == 0 else []))
                              if k == 0 else (),
                              inc=(k == 11))
                dn_tok[hb] = ("pe", pe_d)
            pe_dn_last = pe_d
            ve_ffn = vec(lambda eng, dch=dch, t0=t0: eng.scalar_tensor_tensor(
                xr(dch, t0, t0 + 512), PS2[:, 0:512], 1.0 / 127.0,
                xr(dch, t0, t0 + 512), OP.mult, OP.add),
                waits=[("pe", pe_d)])
            tok_d = ("ve", ve_ffn)

    P.emit("gpsimd", (lambda eng: eng.dma_start(
        out=agbuf[:].rearrange("p (r c) -> p r c", c=16),
        in_=ag0_out.ap().rearrange("r (p c) -> p r c", p=128))),
        waits=[("cc", cc0)], incs=[("dB", 16)])
    g_back_v = P.v("dB")
    ve_prep = 0
    for r in range(NCORES):
        vec(lambda eng, r=r: eng.tensor_scalar(
            red[:, r * 8:(r + 1) * 8], agbuf[:, r * 16:r * 16 + 8],
            prm[:, r:r + 1], prm[:, 9 + r:10 + r], OP.mult, OP.add),
            waits=([("dB", g_back_v)] if r == 0 else ()))
        ve_prep = vec(lambda eng, r=r: eng.tensor_scalar(
            red[:, 64 + r * 8: 64 + (r + 1) * 8],
            agbuf[:, r * 16 + 8:r * 16 + 16],
            prm[:, r:r + 1], None, OP.mult))
    gp(lambda eng: eng.memset(hin[:], 0.0), waits=[("ve", ve_prep)])
    for r in range(NCORES):
        gp(lambda eng, r=r: eng.tensor_tensor(
            red[:, 128:136], red[:, r * 8:(r + 1) * 8], hin[:], OP.mult))
        gp(lambda eng, r=r: eng.tensor_tensor(
            hin[:], red[:, 128:136], red[:, 64 + r * 8: 64 + (r + 1) * 8], OP.add))
    ge_hin = P.v("ge")
    ve_corr = 0
    for e in (range(NK) if do_attn else ()):
        vec(lambda eng, e=e: eng.scalar_tensor_tensor(
            corr_t[:], bufB[:, e * T:(e + 1) * T], hin[:, e:e + 1],
            bufA[:, e * T:(e + 1) * T], OP.mult, OP.add),
            waits=([("ge", ge_hin), ("ve", ve_ffn)] if e == 0 else ()))
        ve_corr = vec(lambda eng, e=e: eng.tensor_add(
            xr(e), xr(e), corr_t[:]))

    if stage.startswith("l0"):
        last = None
        fin = max(ve_corr, ve_ffn, xn_all)
        if stage == "l0_hin":
            last = dma("gpsimd", out_h.ap()[0:128, 0:128], agbuf[:],
                       waits=[("ve", ve_corr)])
            last = dma("gpsimd", out_h.ap()[0:128, 128:136], hin[:],
                       waits=[("ve", ve_corr)])
            last = dma("gpsimd", out_h.ap()[0:128, 136:152], summ[:],
                       waits=[("ve", ve_corr)])
            last = dma("gpsimd", out_h.ap()[0:128, 152:168], prm[:],
                       waits=[("ve", ve_corr)])
            last = dma("gpsimd", out_h.ap()[0:128, 168:360],
                       bufA[:, 0:192], waits=[("ve", ve_corr)])
            P.emit("gpsimd", lambda eng: None, waits=[("dG", last[1])])
        elif stage == "l0_hid":
            for k in range(6):
                last = dma("gpsimd", out_h.ap()[k * 128:(k + 1) * 128, :],
                           ffn_h[:, k * 2048:(k + 1) * 2048],
                           waits=[("ve", max(ve_ffn, 1))])
            P.emit("gpsimd", lambda eng: None, waits=[("dG", last[1])])
        elif stage == "l0_xn":
            for k in range(NK):
                last = dma("gpsimd", out_h.ap()[k * 128:(k + 1) * 128, :], xnv(k),
                           waits=[("ve", fin)])
            P.emit("gpsimd", lambda eng: None, waits=[("dG", last[1])])
        else:
            for k in range(NK):
                last = dma("sync", out_h.ap()[k * 128:(k + 1) * 128, :], xr(k),
                           waits=[("ve", fin)])
            P.emit("sync", lambda eng: None, waits=[("dS", last[1])])
        emit_all(nc, P)
        es.close()
        return nc

    pe_l0 = P.v("pe")
    ae_l0 = P.v("ae")

    xn_all = emit_layernorm([("ve", ve_corr), ("pe", pe_l0), ("ae", ae_l0)],
                            stats_tok=("ve", ve_corr))
    ae_ln1 = P.v("ae")


    def early_out(waits):
        last = None
        for k in range(NK):
            last = dma("sync", out_h.ap()[k * 128:(k + 1) * 128, :], xr(k),
                       waits=waits)
        P.emit("sync", lambda eng: None, waits=[("dS", last[1])])
        emit_all(nc, P)
        es.close()
        return nc

    brow1 = aview("brow1", [1, 2048], BF16, 0)
    cosfc = aview("cosfc", [128, 512], BF16, 4096)
    sinfc = aview("sinfc", [128, 512], BF16, 5120)
    cscol = aview("cscol", [128, 32], F32, 6144)
    qp_t = aview("qp_t", [128, 512], BF16, 6272)
    prQc = aview("prQc", [128, 512], BF16, 7296)
    prQs = aview("prQs", [128, 512], BF16, 8320)
    prKc = aview("prKc", [128, 512], BF16, 9344)
    prKs = aview("prKs", [128, 512], BF16, 10368)
    st_t = aview("st_t", [128, 512], BF16, 11392)
    krelu = aview("krelu", [128, 1024], F32, 12416)
    sden = aview("sden", [1, 512], F32, 12416)
    iotat = aview("iotat", [128, 512], F32, 14464)
    vtok = aview("vtok", [128, 4 * NST], BF16, 16512)
    sprefb = aview("sprefb", [128, 4 * NST], BF16, 24832)
    srun1 = aview("srun1", [128, NST], F32, 33152)
    sptot = aview("sptot", [128, NST], BF16, 37312)
    sshift = aview("sshift", [128, NST], BF16, 39392)
    onesT = aview("onesT", [128, 128], F32, 41472)
    wvsl = aview("wvsl", [128, 1024], BF16, 41984)
    at64 = aview("at64", [128, 512], BF16, 44032)
    xq_t = aview("xq_t", [128, 4096], BF16, 0)
    w1sl = aview("w1sl", [128, 2048], BF16, 8192)
    w2sl = aview("w2sl", [128, 2048], BF16, 12288)
    hidt = aview("hidt", [128, 1024], BF16, 16384)
    qtf = aview("qtf", [128, 1024], F32, 18432)
    wprep = aview("wprep", [128, 4096], BF16, 22528)
    wpabs = aview("wpabs", [128, 512], F32, 30720)
    wpq = aview("wpq", [128, 512], F32, 32768)
    amx32 = aview("amx32", [128, 32], F32, 34816)
    sc32 = aview("sc32", [128, 32], F32, 34944)
    rs32 = aview("rs32", [128, 32], F32, 35072)
    rso32 = aview("rso32", [128, 32], F32, 35200)
    cmb32 = aview("cmb32", [128, 32], F32, 35328)
    srow_t = aview("srow_t", [1, 512], BF16, 35456)

    g_id = gp(lambda eng: eng.iota(
        red[:, 0:128] if False else red[:, 16:144], pattern=[[-1, 128]], base=0,
        channel_multiplier=1, allow_small_or_imprecise_dtypes=True))
    ve_id = vec(lambda eng: eng.tensor_single_scalar(
        identb[:], red[:, 16:144], 0.0, OP.is_equal), waits=[("ge", g_id)])
    ve_orb = vec(lambda eng: eng.memset(ones_rb[:], 1.0), waits=[("ve", ve_corr)])

    for k in range(NK):
        vec(lambda eng, k=k: eng.reduce_max(
            red[:, 8 + k:9 + k], xnv(k),
            axis=AX.X, apply_absolute_value=True))
    ve_am = vec(lambda eng: eng.reduce_max(
        summ[:, 0:1], red[:, 8:16], axis=AX.X))
    ve_amb = vec(lambda eng: eng.tensor_copy(tri[:, 0:1] if False else summ.bitcast(BF16)[:, 2:3], summ[:, 0:1]),
                 waits=[("ve", ve_am)])
    P.emit("tensor", (lambda eng: eng.transpose(
        PS4.bitcast(BF16)[0:1, 0:128], summ.bitcast(BF16)[:, 2:3], identb[:])),
        waits=[("ve", ve_amb), ("ve", ve_id)], incs=[("pe", 1)])
    pe_am = P.v("pe")
    ve_am2 = vec(lambda eng: eng.reduce_max(
        smax[0:1, 0:1], PS4.bitcast(BF16)[0:1, 0:128], axis=AX.X),
        waits=[("pe", pe_am)])

    if stage == "l1_ln":
        return early_out([("ve", xn_all), ("ve", ve_am2), ("ve", ve_corr)])
    d_brow = dma("gpsimd", brow1[0:1, 0:D], bk_h.ap(), sem="dR",
                 waits=[("ve", xn_all)])
    d_brow = dma("gpsimd", brow1[0:1, D:2 * D], bv1_h.ap(), sem="dR")
    g_cs = gp(lambda eng: eng.iota(
        iotat[:, 0:16], pattern=[[128, 16]], base=0, channel_multiplier=1,
        allow_small_or_imprecise_dtypes=True), waits=[("ve", xn_all)])
    ae_cs = act(lambda eng: eng.activation(
        cscol[:, 0:16], iotat[:, 0:16], AF.Sin, scale=ANG_SCALE,
        bias=prm[:, 17:18]), waits=[("ge", g_cs)])
    ae_cs = act(lambda eng: eng.activation(
        cscol[:, 16:32], iotat[:, 0:16], AF.Sin, scale=ANG_SCALE,
        bias=prm[:, 8:9]))
    ve_vi = vec(lambda eng: eng.memset(vtok[:], 0.0), waits=[("ve", xn_all)])
    ve_vi = vec(lambda eng: eng.memset(
        vtok[:].rearrange("p (u h) -> p u h", h=65)[:, :, 64:65], 1.0))

    wk_last = wv_last = None
    for k in range(NK):
        wk_last = dma("gpsimd", bufB[:, k * 1024:(k + 1) * 1024],
                      wk_h.ap()[k * 128:(k + 1) * 128, :], sem="wWk",
                      waits=([("ve", ve_corr)] if k == 0 else ()))
    for k in range(NK):
        wv_last = dma("gpsimd", bufB[:, (NK + k) * 1024:(NK + k + 1) * 1024],
                      wv1_h.ap()[k * 128:(k + 1) * 128, :], sem="wWv",
                      waits=([("ve", ve_corr)] if k == 0 else ()))

    def wkv(which, k):
        base = 0 if which == "k" else NK * 1024
        return bufB[:, base + k * 1024: base + (k + 1) * 1024]

    ve_state_done = 0
    tokB = {("st", 0): ("pe", 0), ("st", 1): ("ve", ve_am2)}
    for c in range(NCH):
        if c == 0:
            ve_sp = vec(lambda eng, c=c: eng.memset(
                sprefb[:, c * NST:(c + 1) * NST], 0.0))
            ve_sp = vec(lambda eng: eng.memset(srun1[:], 0.0))
        else:
            ve_sp = vec(lambda eng, c=c: eng.tensor_copy(
                sprefb[:, c * NST:(c + 1) * NST], srun1[:]),
                waits=[("ve", ve_state_done)])
        for u in range(4):
            tt = c * 4 + u
            t0 = tt * 128
            pe_k = 0
            for k in range(NK):
                for hf in range(2):
                    pw = []
                    if k == 0 and hf == 0:
                        pw = [tokB.get("ps0", ("pe", 0))]
                        if c == 0 and u == 0:
                            pw += [wk_last, ("ve", xn_all)]
                    pe_k = mm(xnv(k, t0, t0 + 128),
                              wkv("k", k)[:, hf * 512:(hf + 1) * 512],
                              PS0[:, hf * 512:hf * 512 + 512],
                              start=(k == 0), stop=False, waits=pw)
            for hf in range(2):
                pe_k = mm(ones_rb[0:1, 0:128], brow1[0:1, hf * 512:(hf + 1) * 512],
                          PS0[:, hf * 512:hf * 512 + 512], start=False, stop=True,
                          waits=([d_brow, ("ve", ve_orb)]
                                 if (c == 0 and u == 0 and hf == 0) else ()),
                          inc=(hf == 1))
            ae_k = act(lambda eng: eng.activation(
                krelu[:, 0:512], PS0[:, 0:512], AF.Relu),
                waits=[("pe", pe_k), tokB.get("krelu", ("pe", 0))])
            ae_k2 = act(lambda eng: eng.activation(
                krelu[:, 512:1024], PS0[:, 512:1024], AF.Relu))
            tokB["ps0"] = ("ae", ae_k2)
            kt = bufA[:, tt * 2048:(tt + 1) * 2048] if False else bufA[:, u * 2048:(u + 1) * 2048]
            ve_kt = 0
            for hf in range(2):
                src = krelu[:, hf * 512:(hf + 1) * 512]
                ve_kt = vec(lambda eng, kt=kt, src=src, hf=hf, tt=tt: eng.tensor_scalar(
                    kt.rearrange("p (h z) -> p h z", z=128)[:, hf * 8:(hf + 1) * 8, 0:64],
                    src.rearrange("p (h z) -> p h z", z=64),
                    cscol[:, tt:tt + 1], None, OP.mult),
                    waits=([("ae", ae_k2), tokB.get("kt", ("pe", 0))] if hf == 0 else ()))
                ve_kt = vec(lambda eng, kt=kt, src=src, hf=hf, tt=tt: eng.tensor_scalar(
                    kt.rearrange("p (h z) -> p h z", z=128)[:, hf * 8:(hf + 1) * 8, 64:128],
                    src.rearrange("p (h z) -> p h z", z=64),
                    cscol[:, 16 + tt:17 + tt], None, OP.mult))
            tokB["krelu"] = ("ve", ve_kt)
            pe_v = 0
            for k in range(NK):
                for hf in range(2):
                    pw = []
                    if k == 0 and hf == 0:
                        pw = [tokB.get("ps1", ("pe", 0))]
                        if c == 0 and u == 0:
                            pw += [wv_last]
                    pe_v = mm(xnv(k, t0, t0 + 128),
                              wkv("v", k)[:, hf * 512:(hf + 1) * 512],
                              PS1[:, hf * 512:hf * 512 + 512],
                              start=(k == 0), stop=False, waits=pw)
            for hf in range(2):
                pe_v = mm(ones_rb[0:1, 0:128], brow1[0:1, D + hf * 512:D + (hf + 1) * 512],
                          PS1[:, hf * 512:hf * 512 + 512], start=False, stop=True,
                          inc=(hf == 1))
            vt = vtok[:, u * NST:(u + 1) * NST]
            ae_vv = 0
            for hf in range(2):
                w0 = []
                if hf == 0:
                    w0 = [("pe", pe_v), tokB.get("vt", ("pe", 0))]
                    if c == 0 and u == 0:
                        w0.append(("ve", ve_vi))
                ae_vv = act(lambda eng, vt=vt, hf=hf: eng.activation(
                    vt.rearrange("p (h z) -> p h z", z=65)[:, hf * 8:(hf + 1) * 8, 0:64],
                    PS1[:, hf * 512:(hf + 1) * 512].rearrange("p (h z) -> p h z", z=64),
                    AF.Copy), waits=w0)
            tokB["ps1"] = ("ae", ae_vv)
            tokB[("ktv", u)] = (("ve", ve_kt), ("ae", ae_vv))
        for h in range(NH):
            ps = PS3 if h % 2 == 0 else PS4
            pe_s = 0
            for u in range(4):
                kt = bufA[:, u * 2048:(u + 1) * 2048]
                vt = vtok[:, u * NST:(u + 1) * NST]
                pw = []
                if u == 0:
                    pw = [tokB[("st", h % 2)]]
                    if h < 2:
                        pw += [tokB[("ktv", 3)][0], tokB[("ktv", 3)][1]]
                pe_s = mm(kt[:, h * 128:(h + 1) * 128], vt[:, h * 65:(h + 1) * 65],
                          ps[:, 0:65], start=(u == 0), stop=(u == 3),
                          waits=pw, inc=(u == 3))
            ve_state_done = vec(lambda eng, h=h, ps=ps: eng.scalar_tensor_tensor(
                srun1[:, h * 65:(h + 1) * 65], ps[:, 0:65], 1.0,
                srun1[:, h * 65:(h + 1) * 65], OP.mult, OP.add),
                waits=[("pe", pe_s), ("ve", ve_sp) if h == 0 else ("ve", 0)])
            tokB[("st", h % 2)] = ("ve", ve_state_done)
        tokB["vt"] = ("pe", P.v("pe"))
        tokB["kt"] = ("pe", P.v("pe"))

    g_s1 = dma("gpsimd", ag1_in.ap().rearrange("a (p c) -> (a p) c", p=128),
               srun1[:], waits=[("ve", ve_state_done)], sem="dB")
    g_s2 = dma("gpsimd", ag1b_in.ap()[0:1, 0:1], smax[0:1, 0:1],
               waits=[("ve", ve_am2)], sem="dB")
    P.emit("gpsimd", lambda eng: eng.collective_compute(
        "AllGather", OP.bypass, replica_groups=[list(range(NCORES))],
        ins=[ag1_in.ap()], outs=[ag1_out.ap()]),
        waits=[g_s1, g_s2], incs=[("cc", 1)])
    P.emit("gpsimd", lambda eng: eng.collective_compute(
        "AllGather", OP.bypass, replica_groups=[list(range(NCORES))],
        ins=[ag1b_in.ap()], outs=[ag1b_out.ap()]), incs=[("cc", 1)])
    cc1 = P.v("cc")

    ve_sx = vec(lambda eng: eng.memset(srun1[:], 0.0),
                waits=[("cc", cc1), ("ve", ve_state_done)])
    for r in range(NCORES):
        d_sr = dma("gpsimd", sptot[:],
                   ag1_out.ap()[r:r + 1, :].rearrange("a (p c) -> (a p) c", p=128),
                   sem="dB", waits=[("ve", ve_sx)] if r == 0 else [("ve", ve_sx)])
        ve_m1 = vec(lambda eng, r=r: eng.tensor_scalar(
            sshift[:], sptot[:], prm[:, r:r + 1], None, OP.mult),
            waits=[d_sr])
        ve_sx = vec(lambda eng: eng.tensor_add(srun1[:], srun1[:], sshift[:]))
    ve_scross = ve_sx
    d_sb = dma("gpsimd", smax[0:8, 8:9], ag1b_out.ap()[:, 0:1], sem="dB",
               waits=[("cc", cc1)], slow=True)
    d_fold = dma("gpsimd", smax[0:1, 24:32], smax[0:8, 8:9], sem="dB",
                 waits=[d_sb], slow=True)
    ve_g1 = vec(lambda eng: eng.reduce_max(
        smax[0:1, 16:17], smax[0:1, 24:32], axis=AX.X), waits=[d_fold])
    ve_g1 = vec(lambda eng: eng.tensor_scalar(
        smax[0:1, 17:18], smax[0:1, 16:17], 1.0 / 32767.0, None, OP.mult))
    pe_bc = mm(ones_row[0:1, 0:128], smax[0:1, 16:18], PS4[:, 0:2],
               start=True, stop=True,
               waits=[("ve", ve_g1), ("ve", ve_state_done)], inc=True)
    ve_sc = vec(lambda eng: eng.reciprocal(red[:, 32:33], PS4[:, 0:1]),
                waits=[("pe", pe_bc)])
    ve_sc = vec(lambda eng: eng.tensor_copy(red[:, 34:35], PS4[:, 1:2]))
    ve_sc = vec(lambda eng: eng.tensor_scalar(
        red[:, 33:34], red[:, 32:33], 32767.0, None, OP.mult))
    ve_scales = ve_sc

    if stage == "l1_b1":
        return early_out([("ve", ve_scross), ("ve", ve_scales), ("ve", ve_corr)])
    lvl = 99
    if stage.startswith("l1_b2@"):
        lvl = int(stage.split("@")[1])
    pe_b1 = P.v("pe")
    ve_o1 = vec(lambda eng: eng.memset(onesT[:], 1.0))

    tokC = {"ps4": ("ve", ve_scales), ("st", 0): ("pe", 0),
            "ps0": tokB["ps0"], "ps0b": tokB["ps0"], "ps1": tokB["ps1"],
            "ps1b": tokB["ps1"], "ps3": tokB[("st", 0)],
            "ps2": ("ve", ve_ffn), "ps2b": ("ve", ve_ffn)}
    ve_attn_done = 0
    dH_attn = 0
    for c in range(NCH):
        t0 = c * CHUNK
        ve_tot = vec(lambda eng, c=c: eng.tensor_add(
            sptot[:], sprefb[:, c * NST:(c + 1) * NST], srun1[:]),
            waits=[("ve", ve_scross), tokC.get("sptot", ("pe", 0))])
        d_sh1 = dma("sync", sshift[0:64, :], sptot[64:128, :], sem="dH",
                    waits=[("ve", ve_tot), ("dH", dH_attn)])
        d_sh2 = dma("sync", sshift[64:128, :], sptot[0:64, :], sem="dH")
        g_i2 = gp(lambda eng, c=c: eng.iota(
            iotat[:], pattern=[[1, 512]], base=c * 512, channel_multiplier=0,
            allow_small_or_imprecise_dtypes=True),
            waits=[tokC.get("iotat", ("pe", 0))])
        ae_c1 = act(lambda eng: eng.activation(
            cosfc[:], iotat[:], AF.Sin, scale=ANG_SCALE, bias=prm[:, 17:18]),
            waits=[("ge", g_i2), tokC.get("csf", ("pe", 0))])
        ae_c2 = act(lambda eng: eng.activation(
            sinfc[:], iotat[:], AF.Sin, scale=ANG_SCALE, bias=prm[:, 8:9]))
        tokC["iotat"] = ("ae", ae_c2)
        for u in range(4):
            tu0 = t0 + u * 128
            pe_v2 = 0
            for k in range(NK):
                for hf in range(2):
                    pw = []
                    if k == 0 and hf == 0:
                        pw = [tokC.get("ps1", ("pe", 0)), tokC.get("ps1b", ("pe", 0))]
                    pe_v2 = mm(xnv(k, tu0, tu0 + 128),
                               wkv("v", k)[:, hf * 512:(hf + 1) * 512],
                               PS1[:, hf * 512:hf * 512 + 512],
                               start=(k == 0), stop=False, waits=pw)
            for hf in range(2):
                pe_v2 = mm(ones_rb[0:1, 0:128],
                           brow1[0:1, D + hf * 512:D + (hf + 1) * 512],
                           PS1[:, hf * 512:hf * 512 + 512], start=False,
                           stop=True, inc=(hf == 1))
            vt = vtok[:, u * NST:(u + 1) * NST]
            ae_vv2 = 0
            for hf in range(2):
                w0 = []
                if hf == 0:
                    w0 = [("pe", pe_v2), tokC.get("vt", ("pe", 0))]
                ae_vv2 = act(lambda eng, vt=vt, hf=hf: eng.activation(
                    vt.rearrange("p (h z) -> p h z", z=65)[:, hf * 8:(hf + 1) * 8, 0:64],
                    PS1[:, hf * 512:(hf + 1) * 512].rearrange("p (h z) -> p h z", z=64),
                    AF.Copy), waits=w0)
            tokC["ps1"] = ("ae", ae_vv2)
        tokC["vprj"] = ("ae", ae_vv2)
        for p in (range(NK) if lvl >= 1 else ()):
            d_wq = col_block_dma(wvsl[:], wq_h, p * 128, 128, NK, sem="wV2",
                                 waits=[tokC.get("wvsl", ("pe", pe_b1))])
            pe_q = 0
            for k in range(NK):
                pw = []
                if k == 0:
                    pw = [tokC.get("ps0", ("pe", 0)), d_wq]
                pe_q = mm(wvsl[:, k * 128:(k + 1) * 128],
                          xnv(k, t0, t0 + CHUNK), PS0[:, 0:512],
                          start=(k == 0), stop=(k == NK - 1), waits=pw,
                          inc=(k == NK - 1))
            tokC["wvsl"] = ("pe", pe_q)
            ae_q = act(lambda eng, p=p: eng.activation(
                qp_t[:], PS0[:, 0:512], AF.Relu, bias=bcol1[:, p:p + 1]),
                waits=[("pe", pe_q), tokC.get("qp", ("pe", 0)), ("ae", ae_c2)])
            ve_qc = vec(lambda eng: eng.tensor_mul(prQc[:], qp_t[:], cosfc[:]),
                        waits=[("ae", ae_q), tokC.get("prq", ("pe", 0))])
            ve_qs = vec(lambda eng: eng.tensor_mul(prQs[:], qp_t[:], sinfc[:]))
            tokC["ps0"] = ("ae", ae_q)
            pe_k2 = 0
            for k in range(NK):
                pw = [tokC.get("ps0b", ("pe", 0))] if k == 0 else []
                pe_k2 = mm(bufB[:, k * 1024 + p * 128: k * 1024 + (p + 1) * 128],
                           xnv(k, t0, t0 + CHUNK), PS0[:, 512:1024],
                           start=(k == 0), stop=(k == NK - 1), waits=pw,
                           inc=(k == NK - 1))
            ae_k3 = act(lambda eng, p=p: eng.activation(
                qp_t[:], PS0[:, 512:1024], AF.Relu, bias=bcol1[:, NK + p:NK + p + 1]),
                waits=[("pe", pe_k2), ("ve", ve_qs)])
            tokC["ps0b"] = ("ae", ae_k3)
            ve_kc = vec(lambda eng: eng.tensor_mul(prKc[:], qp_t[:], cosfc[:]),
                        waits=[("ae", ae_k3), tokC.get("prk", ("pe", 0))])
            ve_ks = vec(lambda eng: eng.tensor_mul(prKs[:], qp_t[:], sinfc[:]))
            for hh in (range(2) if lvl >= 2 else ()):
                h = 2 * p + hh
                pb = hh * 64
                ps2h = PS2[0:65, 0:512] if hh == 0 else PS2[0:65, 512:1024]
                w_int = [tokC.get("ps2" if hh == 0 else "ps2b", ("pe", 0)),
                         ("dH", d_sh2[1]), ("ve", ve_ks), tokC["vprj"]]
                if c == 0 and p == 0 and hh == 0:
                    w_int.append(("ve", ve_o1))
                lhs_cos = sptot[0:64, h * 65:(h + 1) * 65] if hh == 0 \
                    else sshift[64:128, h * 65:(h + 1) * 65]
                lhs_sin = sshift[0:64, h * 65:(h + 1) * 65] if hh == 0 \
                    else sptot[64:128, h * 65:(h + 1) * 65]
                pe_at = mm(lhs_cos, prQc[pb:pb + 64, :], ps2h,
                           start=True, stop=False, waits=w_int)
                pe_at = mm(lhs_sin, prQs[pb:pb + 64, :], ps2h,
                           start=False, stop=(lvl < 5), inc=(lvl < 5))
                for u in (range(4) if lvl >= 3 else ()):
                    w_u = 512 - u * 128
                    pe_sc2 = mm(prKc[pb:pb + 64, u * 128:(u + 1) * 128],
                                prQc[pb:pb + 64, u * 128:512],
                                PS3[:, 0:w_u], start=True, stop=False,
                                waits=[tokC.get("ps3", ("pe", 0))])
                    pe_sc2 = mm(prKs[pb:pb + 64, u * 128:(u + 1) * 128],
                                prQs[pb:pb + 64, u * 128:512],
                                PS3[:, 0:w_u], start=False, stop=True, inc=True)
                    if lvl == 7:
                        ae_m = act(lambda eng, w_u=w_u: eng.activation(
                            st_t[:, 0:w_u], PS3[:, 0:w_u], AF.Copy),
                            waits=[("pe", pe_sc2)])
                        ve_m = ae_m
                        tokC["ps3"] = ("ae", ae_m)
                        tokC[("st", 0)] = ("ae", ae_m)
                        continue
                    ve_m = vec(lambda eng, w_u=w_u: eng.tensor_mul(
                        st_t[:, 0:w_u], PS3[:, 0:w_u], tri[:, 0:w_u]),
                        waits=[("pe", pe_sc2), tokC[("st", 0)],
                               ("dH", dH_attn)])
                    tokC["ps3"] = ("ve", ve_m)
                    wait_st = [("ve", ve_m)]
                    if lvl >= 5 and lvl not in (7, 8):
                        pe_at = mm(vtok[:, u * NST + h * 65: u * NST + (h + 1) * 65],
                                   st_t[:, 0:w_u],
                                   ps2h.tensor[0:65, (hh * 512) + u * 128: hh * 512 + 512]
                                   if False else
                                   (PS2[0:65, u * 128:512] if hh == 0 else
                                    PS2[0:65, 512 + u * 128:1024]),
                                   start=False, stop=(u == 3),
                                   waits=wait_st, inc=(u == 3))
                        tokC[("st", 0)] = ("pe", pe_at)
                    else:
                        tokC[("st", 0)] = wait_st[0]
                if lvl < 6 or lvl in (7, 8):
                    tokC["ps2"] = ("pe", pe_at)
                    continue
                ve_d1 = vec(lambda eng: eng.tensor_scalar(
                    iotat[64:65, 0:512], PS2[64:65, 0:512], 1e-6, None, OP.max),
                    waits=[("pe", pe_at), tokC.get("iden", ("pe", 0))])
                ve_d2 = vec(lambda eng: eng.reciprocal(
                    iotat[64:65, 0:512], iotat[64:65, 0:512]))
                tokC["iden"] = ("ve", ve_d2)
                pe_bc2 = mm(onesT[64:65, 0:64], iotat[64:65, 0:512],
                            PS4[0:64, 0:512], start=True, stop=True,
                            waits=[("ve", ve_d2), tokC["ps4"]], inc=True)
                tokC["iotat"] = ("pe", pe_bc2)
                ae_n = act(lambda eng: eng.activation(
                    at64[0:64, :], PS2[0:64, 0:512], AF.Copy),
                    waits=[("pe", pe_at), tokC.get("at64", ("pe", 0))])
                if stage == "l1_dbg" and c == 0 and p == 7 and hh == 1:
                    dma("gpsimd", out_h.ap()[0:128, 0:512], qp_t[:],
                        waits=[("ve", P.v("ve")), ("ae", P.v("ae"))], sem="dD")
                    dma("gpsimd", out_h.ap()[0:128, 512:1024], prQc[:],
                        waits=[("ve", P.v("ve"))], sem="dD")
                    dma("gpsimd", out_h.ap()[0:128, 1024:1536], prKc[:],
                        waits=[("ve", P.v("ve"))], sem="dD")
                    dma("gpsimd", out_h.ap()[128:256, 0:512], st_t[:],
                        waits=[("ve", P.v("ve"))], sem="dD")
                    dma("gpsimd", out_h.ap()[128:256, 512:1024], at64[:],
                        waits=[("ae", P.v("ae"))], sem="dD")
                    dma("gpsimd", out_h.ap()[384:385, 0:512], iotat[64:65, 0:512],
                        waits=[("ve", P.v("ve"))], sem="dD")
                    dma("gpsimd", out_h.ap()[385:386, 0:512], cosfc[0:1, :],
                        waits=[("ve", P.v("ve"))], sem="dD")
                    dma("gpsimd", out_h.ap()[386:387, 0:512], sinfc[0:1, :],
                        waits=[("ve", P.v("ve"))], sem="dD")
                    dma("gpsimd", out_h.ap()[256:384, 0:512],
                        vtok[:, 0:512],
                        waits=[("ve", P.v("ve")), ("ae", P.v("ae"))], sem="dD")
                    dma("gpsimd", out_h.ap()[512:640, 0:512], xnv(0, 0, 512),
                        waits=[("ve", P.v("ve"))], sem="dD")
                    dma("gpsimd", out_h.ap()[640:768, 0:512],
                        bufB[:, NK * 1024:NK * 1024 + 512],
                        waits=[("ve", P.v("ve"))], sem="dD")
                    raise _EarlyStop(nc, P, es, P.v("dD"))
                tokC["ps2"] = ("ae", ae_n)
                at_tile = bufA[:, p * 2048 + t0: p * 2048 + t0 + 512]
                if hh == 0:
                    ve_attn_done = vec(lambda eng, at_tile=at_tile: eng.tensor_mul(
                        at_tile[0:64, :], at64[0:64, :], PS4[0:64, 0:512]),
                        waits=[("pe", pe_bc2), ("ae", ae_n)])
                    tokC["at64"] = ("ve", ve_attn_done)
                    tokC["ps4"] = ("ve", ve_attn_done)
                else:
                    ve_sh = vec(lambda eng: eng.tensor_mul(
                        st_t[0:64, :], at64[0:64, :], PS4[0:64, 0:512]),
                        waits=[("pe", pe_bc2), ("ae", ae_n), ("dH", dH_attn)])
                    d_sh3 = dma("sync", at_tile[64:128, :], st_t[0:64, :],
                                sem="dH", waits=[("ve", ve_sh)])
                    dH_attn = d_sh3[1]
                    tokC["at64"] = ("ve", ve_sh)
                    tokC["ps4"] = ("ve", ve_sh)
                    tokC[("st", 0)] = ("ve", ve_sh)
            tokC["qp"] = tokC["ps4"]
            tokC["prq"] = tokC[("st", 0)]
            tokC["prk"] = tokC[("st", 0)]
        tokC["sptot"] = ("pe", P.v("pe"))
        tokC["csf"] = tokC["ps0"]
    ve_attn_last = P.v("ve")

    pe_all_b2 = P.v("pe")
    tok_wo = ("pe", pe_all_b2)
    tok_evac = ("ve", 0)
    pe_wo_done = 0
    for dch in range(NK):
        d_wo = col_block_dma(wvsl[:, 0:1024], wo_h, dch * 128, 128, NK,
                             sem="wV2", waits=[tok_wo])
        pe_o = 0
        for k in range(NK):
            for tt in range(4):
                pw = []
                if k == 0 and tt == 0:
                    pw = [d_wo, ("ve", ve_attn_last), ("dH", dH_attn), tok_evac]
                pe_o = mm(wvsl[:, k * 128:(k + 1) * 128],
                          bufA[:, k * 2048 + tt * 512: k * 2048 + (tt + 1) * 512],
                          (PS0 if tt < 2 else PS1)[:, (tt % 2) * 512:(tt % 2) * 512 + 512],
                          start=(k == 0), stop=(k == NK - 1), waits=pw,
                          inc=(k == NK - 1))
        tok_wo = ("pe", pe_o)
        ve_wo = 0
        for tt in range(4):
            ve_wo = vec(lambda eng, dch=dch, tt=tt: eng.scalar_tensor_tensor(
                xr(dch, tt * 512, (tt + 1) * 512),
                (PS0 if tt < 2 else PS1)[:, (tt % 2) * 512:(tt % 2) * 512 + 512],
                bcol1[:, 16 + dch:17 + dch], xr(dch, tt * 512, (tt + 1) * 512),
                OP.add, OP.add), waits=[("pe", pe_o)] if tt == 0 else ())
        tok_evac = ("ve", ve_wo)
        pe_wo_done = pe_o
    ve_wo_done = tok_evac[1]

    if stage == "l1_atdump":
        last = None
        for k in range(NK):
            last = dma("gpsimd", out_h.ap()[k * 128:(k + 1) * 128, :],
                       bufA[:, k * 2048:(k + 1) * 2048],
                       waits=[("ve", ve_wo_done)], sem="dS")
        P.emit("sync", lambda eng: None, waits=[("dS", last[1])])
        emit_all(nc, P)
        es.close()
        return nc
    if stage.startswith("l1_b2"):
        return early_out([("ve", ve_wo_done), ("pe", P.v("pe")),
                          ("ve", P.v("ve")), ("ae", P.v("ae")),
                          ("ge", P.v("ge"))])
    pe_prep0 = pe_wo_done

    def fence(n=1):
        v = 0
        for _ in range(n):
            v = vec(lambda eng: eng.tensor_copy(red[:, 80:144], red[:, 16:80]))
        return v

    for hch in range(NH1):
        sl = hch % 2
        d_w2 = dma("gpsimd", w2sl[:, sl * 1024:(sl + 1) * 1024],
                   wout_h.ap()[hch * 128:(hch + 1) * 128, :], sem="pW%d" % sl,
                   waits=[("pe", pe_prep0)] if hch < 2 else
                   [("ve", P.v("ve"))])
        ve_a2 = vec(lambda eng, sl=sl, hch=hch: eng.reduce_max(
            rso32[:, hch:hch + 1], w2sl[:, sl * 1024:(sl + 1) * 1024],
            axis=AX.X, apply_absolute_value=True),
            waits=[d_w2, ("ve", ve_wo_done) if hch == 0 else ("ve", 0)])
    ve_amo = fence(1)
    ve_amo = vec(lambda eng: eng.tensor_scalar(
        rso32[:], rso32[:], 1e-8, None, OP.max))
    ve_amo = fence(1)
    ve_rq = vec(lambda eng: eng.reciprocal(amx32[:], rso32[:]))
    ve_rq = fence(1)
    ve_rq = vec(lambda eng: eng.tensor_scalar(
        sc32[:], amx32[:], 127.0, None, OP.mult))
    ve_rso = vec(lambda eng: eng.tensor_scalar(
        rso32[:], rso32[:], 1.0 / 127.0, None, OP.mult))
    ve_rq2 = fence(1)
    tok_p0 = tok_p1 = ("ve", ve_rq2)
    for hch in range(NH1):
        sl = hch % 2
        d_w2 = dma("gpsimd", w2sl[:, sl * 1024:(sl + 1) * 1024],
                   wout_h.ap()[hch * 128:(hch + 1) * 128, :], sem="pW%d" % sl,
                   waits=[tok_p0 if sl == 0 else tok_p1])
        ae_q1 = act(lambda eng, sl=sl, hch=hch: eng.activation(
            wpq[:, 0:512], w2sl[:, sl * 1024:sl * 1024 + 512],
            AF.Copy, scale=sc32[:, hch:hch + 1], bias=MAGIC),
            waits=[d_w2, ("ve", ve_rq2)])
        ae_q2 = act(lambda eng, sl=sl, hch=hch: eng.activation(
            wpabs[:, 0:512], w2sl[:, sl * 1024 + 512:(sl + 1) * 1024],
            AF.Copy, scale=sc32[:, hch:hch + 1], bias=MAGIC))
        ve_w1 = vec(lambda eng, sl=sl: eng.tensor_scalar(
            w2sl[:, sl * 1024:sl * 1024 + 512], wpq[:, 0:512], -MAGIC, None,
            OP.add), waits=[("ae", ae_q2)])
        ve_w2 = vec(lambda eng, sl=sl: eng.tensor_scalar(
            w2sl[:, sl * 1024 + 512:(sl + 1) * 1024], wpabs[:, 0:512], -MAGIC,
            None, OP.add))
        d_w2o = dma("sync", woutq_d.ap()[hch * 128:(hch + 1) * 128, :],
                    w2sl[:, sl * 1024:(sl + 1) * 1024], sem="pO%d" % sl,
                    waits=[("ve", ve_w2)])
        if sl == 0:
            tok_p0 = d_w2o
        else:
            tok_p1 = d_w2o

    sc_done = ("pW0", 0)
    ae_tr_tok = ("ae", P.v("ae"))
    for blk in range(8):
        c0 = blk * 512
        d_wi = dma("gpsimd",
                   wprep[:].rearrange("p (k f) -> p k f", f=512),
                   win_h.ap()[:, c0:c0 + 512].rearrange("(k p) f -> p k f", p=128),
                   sem="pI0", waits=[sc_done if blk else ("pe", pe_prep0)])
        ve_ab = vec(lambda eng: eng.tensor_tensor(
            wpabs[:], wprep[:, 0:512], wprep[:, 512:1024], OP.max),
            waits=[d_wi])
        for k in range(2, NK):
            ve_ab = vec(lambda eng, k=k: eng.tensor_tensor(
                wpabs[:], wpabs[:], wprep[:, k * 512:(k + 1) * 512], OP.max))
        ve_ab = vec(lambda eng: eng.tensor_tensor(
            wpq[:], wprep[:, 0:512], wprep[:, 512:1024], OP.min))
        for k in range(2, NK):
            ve_ab = vec(lambda eng, k=k: eng.tensor_tensor(
                wpq[:], wpq[:], wprep[:, k * 512:(k + 1) * 512], OP.min))
        ve_ab = vec(lambda eng: eng.tensor_scalar(
            wpq[:], wpq[:], -1.0, None, OP.mult))
        ve_ab = vec(lambda eng: eng.tensor_tensor(
            wpabs[:], wpabs[:], wpq[:], OP.max))
        ve_ab = vec(lambda eng: eng.tensor_copy(st_t[:, 0:512], wpabs[:]))
        pe_tr = 0
        for q in range(4):
            P.emit("tensor", (lambda eng, q=q: eng.transpose(
                PS4.bitcast(BF16)[:, q * 128:(q + 1) * 128],
                st_t[:, q * 128:(q + 1) * 128], identb[:])),
                waits=([("ve", ve_ab), ("ve", ve_id),
                        ("pe", pe_prep0) if blk == 0 else ("pe", 0),
                        ae_tr_tok] if q == 0 else ()),
                incs=[("pe", 1)])
        pe_tr = P.v("pe")
        ve_mx = 0
        for q in range(4):
            ve_mx = vec(lambda eng, blk=blk, q=q: eng.reduce_max(
                sc32[:, blk * 4 + q:blk * 4 + q + 1],
                PS4.bitcast(BF16)[:, q * 128:(q + 1) * 128], axis=AX.X),
                waits=([("pe", pe_tr)] if q == 0 else ()))
        ae_tr_tok = ("ve", ve_mx)
        sc_done = ("ve", ve_mx)
    ve_f = fence(1)
    ve_f = vec(lambda eng: eng.tensor_scalar(
        sc32[:], sc32[:], 1e-8, None, OP.max))
    ve_f = fence(1)
    ve_rs = vec(lambda eng: eng.tensor_scalar(
        rs32[:], sc32[:], 1.0 / 127.0, None, OP.mult),
        waits=[("ve", ve_scales)])
    ve_rs = vec(lambda eng: eng.tensor_scalar(
        rs32[:], rs32[:], red[:, 34:35], None, OP.mult))
    ve_f = fence(1)
    ve_rp = vec(lambda eng: eng.reciprocal(amx32[:], sc32[:]))
    ve_rp = fence(1)
    ve_rp = vec(lambda eng: eng.tensor_scalar(
        sc32[:], amx32[:], 127.0, None, OP.mult))
    q_done = ("ve", ve_rp)
    tok_srow = ("pe", 0)
    for blk in range(8):
        c0 = blk * 512
        d_wi = dma("gpsimd",
                   wprep[:].rearrange("p (k f) -> p k f", f=512),
                   win_h.ap()[:, c0:c0 + 512].rearrange("(k p) f -> p k f", p=128),
                   sem="pI0", waits=[q_done])
        ve_cp = vec(lambda eng, blk=blk: eng.tensor_copy(
            st_t[:, 0:4], sc32[:, blk * 4:(blk + 1) * 4]),
            waits=[("ve", ve_rp)] if blk == 0 else [])
        P.emit("tensor", (lambda eng: eng.transpose(
            PS4.bitcast(BF16)[0:4, 0:128], st_t[:, 0:4], identb[:])),
            waits=[("ve", ve_cp), tok_srow], incs=[("pe", 1)])
        pe_t2 = P.v("pe")
        ae_sr = act(lambda eng: eng.activation(
            st_t[0:4, 0:128], PS4.bitcast(BF16)[0:4, 0:128], AF.Copy),
            waits=[("pe", pe_t2)])
        d_fl = dma("sync", srow_t[0:1, :].rearrange("a (k f) -> a k f", f=128),
                   st_t[0:4, 0:128].rearrange("k f -> () k f") if False else
                   st_t[0:4, 0:128], sem="pF", waits=[("ae", ae_sr)])
        pe_b3 = mm(ones_rb[0:1, 0:128], srow_t[0:1, :], PS3[:, 0:512],
                   start=True, stop=True, waits=[d_fl, q_done], inc=True)
        tok_srow = ("pe", pe_b3)
        for k in range(NK):
            ve_q1 = vec(lambda eng, k=k: eng.tensor_tensor(
                wpq[:], wprep[:, k * 512:(k + 1) * 512], PS3[:, 0:512], OP.mult),
                waits=([("pe", pe_b3)] if k == 0 else []))
            ve_q2 = vec(lambda eng: eng.tensor_scalar(
                wpq[:], wpq[:], MAGIC, -MAGIC, OP.add, OP.add))
            ve_q3 = vec(lambda eng, k=k: eng.tensor_copy(
                wpabs.bitcast(BF16)[:, 0:512], wpq[:]))
            d_qo = dma("sync", winq_d.ap()[k * 128:(k + 1) * 128, c0:c0 + 512],
                       wpabs.bitcast(BF16)[:, 0:512], sem="pF",
                       waits=[("ve", ve_q3)])
            q_done = d_qo
    pe_prep_done = P.v("pe")
    ae_prep_done = P.v("ae")

    if stage == "l1_prep":
        return early_out([("pe", pe_prep_done), ("ae", ae_prep_done),
                          ("ve", P.v("ve")), (QSEM["sync"], P.v("dS")),
                          ("pF", P.v("pF"))])
    hT = {}
    ae_ffn0 = P.v("ae")
    ve_fold = 0
    for tt2 in range(4):
        t0 = tt2 * 512
        ve_xq = 0
        for k in range(NK):
            ve_x1 = vec(lambda eng, k=k, t0=t0: eng.tensor_scalar(
                qtf[:, 0:512], xnv(k, t0, t0 + 512), red[:, 33:34], MAGIC,
                OP.mult, OP.add),
                waits=([("ve", ve_scales), ("pe", pe_prep_done),
                        ("ae", ae_prep_done)] if (tt2 == 0 and k == 0) else []))
            ve_xq = vec(lambda eng, k=k: eng.tensor_scalar(
                xq_t[:, k * 512:(k + 1) * 512], qtf[:, 0:512], -MAGIC, None,
                OP.add))
        for hch in range(NH1):
            sl = hch % 2
            d_w1 = col_block_dma(w1sl[:, sl * 1024:(sl + 1) * 1024], winq_d,
                                 hch * 128, 128, NK, sem="fI%d" % sl,
                                 waits=[hT.get(("w", sl), ("pe", pe_prep_done))])
            pe_h = 0
            for k in range(NK):
                pw = []
                if k == 0:
                    pw = [d_w1, hT.get(("ps", sl), ("ae", ae_ffn0)), ("ve", ve_xq)]
                pe_h = mm(w1sl[:, sl * 1024 + k * 128: sl * 1024 + (k + 1) * 128],
                          xq_t[:, k * 512:(k + 1) * 512],
                          (PS0 if sl == 0 else PS1)[:, 0:512],
                          start=(k == 0), stop=(k == NK - 1), waits=pw,
                          inc=(k == NK - 1))
            hT[("w", sl)] = ("pe", pe_h)
            ae_h = act(lambda eng, sl=sl, hch=hch: eng.activation(
                hidt[:, sl * 512:(sl + 1) * 512],
                (PS0 if sl == 0 else PS1)[:, 0:512],
                AF.Gelu, scale=rs32[:, hch:hch + 1],
                bias=bcol1[:, 32 + hch:33 + hch]),
                waits=[("pe", pe_h), hT.get(("hd", sl), ("pe", 0)),
                       hT.get(("am", sl), ("pe", 0))])
            hT[("ps", sl)] = ("ae", ae_h)
            ve_hm = vec(lambda eng, sl=sl, hch=hch: eng.reduce_max(
                amx32[:, hch:hch + 1], hidt[:, sl * 512:(sl + 1) * 512],
                axis=AX.X, apply_absolute_value=True), waits=[("ae", ae_h)])
            hT[("am", sl)] = ("ve", ve_hm)
            d_ho = dma("sync", hid_d.ap()[hch * 128:(hch + 1) * 128, t0:t0 + 512],
                       hidt[:, sl * 512:(sl + 1) * 512], sem="hd%d" % sl,
                       waits=[("ve", ve_hm)])
            hT[("hd", sl)] = d_ho
        ve_fold = fence(1)
        ve_fold = vec(lambda eng, tt2=tt2: eng.reduce_max(
            red[:, 40 + tt2:41 + tt2], amx32[:], axis=AX.X))
    ve_fold = fence(1)
    ve_fold = vec(lambda eng: eng.reduce_max(
        red[:, 44:45], red[:, 40:44], axis=AX.X))
    ve_fold = fence(1)
    ve_fb = vec(lambda eng: eng.tensor_copy(summ.bitcast(BF16)[:, 2:3],
                                            red[:, 44:45]))
    P.emit("tensor", (lambda eng: eng.transpose(
        PS4.bitcast(BF16)[0:1, 0:128], summ.bitcast(BF16)[:, 2:3], identb[:])),
        waits=[("ve", ve_fb), tok_srow, ("ae", P.v("ae"))], incs=[("pe", 1)])
    pe_am2 = P.v("pe")
    ve_am3 = vec(lambda eng: eng.reduce_max(
        smax[0:1, 2:3], PS4.bitcast(BF16)[0:1, 0:128], axis=AX.X),
        waits=[("pe", pe_am2)])
    g_a2 = dma("gpsimd", ag2_in.ap()[0:1, 0:1], smax[0:1, 2:3],
               waits=[("ve", ve_am3)], sem="dB")
    P.emit("gpsimd", lambda eng: eng.collective_compute(
        "AllGather", OP.bypass, replica_groups=[list(range(NCORES))],
        ins=[ag2_in.ap()], outs=[ag2_out.ap()]),
        waits=[g_a2], incs=[("cc", 1)])
    cc2 = P.v("cc")
    d_a2 = dma("gpsimd", smax[0:8, 10:11], ag2_out.ap()[:, 0:1], sem="dB",
               waits=[("cc", cc2)], slow=True)
    d_a2f = dma("gpsimd", smax[0:1, 32:40], smax[0:8, 10:11], sem="dB",
                waits=[d_a2], slow=True)
    ve_g2 = vec(lambda eng: eng.reduce_max(
        smax[0:1, 3:4], smax[0:1, 32:40], axis=AX.X), waits=[d_a2f])
    ve_g2 = vec(lambda eng: eng.tensor_scalar(
        smax[0:1, 4:5], smax[0:1, 3:4], 1.0 / 32767.0, None, OP.mult))
    pe_bc3 = mm(ones_row[0:1, 0:128], smax[0:1, 3:5], PS4[:, 0:2],
                start=True, stop=True,
                waits=[("ve", ve_g2), ("ve", ve_am3)], inc=True)
    ve_s2 = vec(lambda eng: eng.reciprocal(red[:, 45:46], PS4[:, 0:1]),
                waits=[("pe", pe_bc3)])
    ve_s2 = vec(lambda eng: eng.tensor_copy(red[:, 47:48], PS4[:, 1:2]))
    ve_s2 = vec(lambda eng: eng.tensor_scalar(
        red[:, 46:47], red[:, 45:46], 32767.0, None, OP.mult))
    ve_cmb = vec(lambda eng: eng.tensor_scalar(
        cmb32[:], rso32[:], red[:, 47:48], None, OP.mult),
        waits=[("ve", ve_rso)])
    ve_sc2done = ve_cmb

    if stage == "l1_f1":
        return early_out([("ve", ve_sc2done)])
    f2 = {}
    ve_final = 0
    psl = [PS0[:, 0:512], PS0[:, 512:1024], PS1[:, 0:512], PS1[:, 512:1024],
           PS2[:, 0:512], PS2[:, 512:1024], PS3[:, 0:512], PS4[:, 0:512]]
    for tt2 in range(4):
        t0 = tt2 * 512
        pe_d2 = [0] * NK
        for hch in range(NH1):
            sl = hch % 2
            d_w2 = dma("sync", w2sl[:, sl * 1024:(sl + 1) * 1024],
                       woutq_d.ap()[hch * 128:(hch + 1) * 128, :],
                       sem="f2W%d" % sl,
                       waits=[f2.get(("w", sl), ("pe", pe_am2))])
            d_h2 = dma("scalar", hidt[:, sl * 512:(sl + 1) * 512],
                       hid_d.ap()[hch * 128:(hch + 1) * 128, t0:t0 + 512],
                       sem="f2H%d" % sl,
                       waits=[f2.get(("h", sl), ("ve", ve_sc2done))])
            ve_hq = vec(lambda eng, sl=sl: eng.tensor_scalar(
                qtf[:, sl * 512:(sl + 1) * 512],
                hidt[:, sl * 512:(sl + 1) * 512],
                red[:, 46:47], MAGIC, OP.mult, OP.add),
                waits=[d_h2] + ([("ve", ve_sc2done)] if (tt2 == 0 and hch < 2) else []))
            ve_hq2 = vec(lambda eng, sl=sl, hch=hch: eng.tensor_scalar(
                hidt[:, sl * 512:(sl + 1) * 512],
                qtf[:, sl * 512:(sl + 1) * 512],
                -MAGIC, cmb32[:, hch:hch + 1], OP.add, OP.mult))
            for dch in range(NK):
                pw = []
                if dch == 0:
                    pw = [d_w2, ("ve", ve_hq2)]
                    if hch == 0:
                        pw.append(f2.get("ps_all", ("pe", pe_am2)))
                pe_d2[dch] = mm(w2sl[:, sl * 1024 + dch * 128: sl * 1024 + (dch + 1) * 128],
                                hidt[:, sl * 512:(sl + 1) * 512], psl[dch],
                                start=(hch == 0), stop=(hch == NH1 - 1),
                                waits=pw, inc=True)
            f2[("w", sl)] = ("pe", pe_d2[NK - 1])
            f2[("h", sl)] = ("pe", pe_d2[NK - 1])
        for dch in range(NK):
            ve_final = vec(lambda eng, dch=dch, t0=t0: eng.scalar_tensor_tensor(
                xr(dch, t0, t0 + 512), psl[dch], bcol1[:, 24 + dch:25 + dch],
                xr(dch, t0, t0 + 512), OP.add, OP.add),
                waits=[("pe", pe_d2[dch])])
        f2["ps_all"] = ("ve", ve_final)

    last = None
    for k in range(NK):
        last = dma("sync", out_h.ap()[k * 128:(k + 1) * 128, :], xr(k),
                   waits=[("ve", ve_final)])
    P.emit("sync", lambda eng: None, waits=[("dS", last[1])])
    emit_all(nc, P)
    es.close()
    return nc


def emit_all(nc, P):
    with nc.Block() as block:
        @block.sync
        def _(eng):
            for f in P.ops["sync"]:
                f(eng)

        @block.scalar
        def _(eng):
            for f in P.ops["scalar"]:
                f(eng)

        @block.vector
        def _(eng):
            for f in P.ops["vector"]:
                f(eng)

        @block.tensor
        def _(eng):
            for f in P.ops["tensor"]:
                f(eng)

        @block.gpsimd
        def _(eng):
            for f in P.ops["gpsimd"]:
                f(eng)
    return nc


def make_inputs(inp):
    C = np.ascontiguousarray

    def A(v):
        return np.asarray(v)

    x = A(inp["L0_attn_x"])
    shared = {
        "wgT": C(A(inp["L0_attn_Wg"]).T), "wvT": C(A(inp["L0_attn_Wv"]).T),
        "wdT": C(A(inp["L0_attn_Wd"]).T),
        "bcols0": np.concatenate([
            A(inp["L0_attn_bg"]).reshape(8, 128).T,
            A(inp["L0_attn_bv"]).reshape(8, 128).T,
            A(inp["L0_attn_bd"]).reshape(8, 128).T], axis=1).copy(),
        "bcols1": np.concatenate([
            A(inp["L1_attn_bq"]).reshape(8, 128).T,
            A(inp["L1_attn_bk"]).reshape(8, 128).T,
            A(inp["L1_attn_bo"]).reshape(8, 128).T,
            A(inp["L1_ffn_b_out"]).reshape(8, 128).T,
            A(inp["L1_ffn_b_in"]).reshape(32, 128).T], axis=1).copy(),
        "fgT": C(A(inp["L0_ffn_w_gate"]).T), "fuT": C(A(inp["L0_ffn_w_up"]).T),
        "fdT": C(A(inp["L0_ffn_w_down"]).T),
        "wq": C(A(inp["L1_attn_Wq"])), "wk": C(A(inp["L1_attn_Wk"])),
        "wv1": C(A(inp["L1_attn_Wv"])), "wo": C(A(inp["L1_attn_Wo"])),
        "bk": A(inp["L1_attn_bk"]).reshape(1, -1),
        "bv1": A(inp["L1_attn_bv"]).reshape(1, -1),
        "win": C(A(inp["L1_ffn_W_in"])), "wout": C(A(inp["L1_ffn_W_out"])),
    }
    maps = []
    for core in range(NCORES):
        b, j = divmod(core, 4)
        xs = C(x[b, j * T:(j + 1) * T, :].T)
        params = np.zeros((128, 24), np.float32)
        for r in range(NCORES):
            rb, rj = divmod(r, 4)
            params[:, r] = 1.0 if (rb == b and rj < j) else 0.0
            params[:, 9 + r] = 1.0 - params[:, r]
        params[:, 8] = (j * T) * ANG_SCALE
        params[:, 17] = (j * T) * ANG_SCALE + np.pi / 2.0
        m = dict(shared)
        m["x"] = xs
        m["params"] = params
        maps.append(m)
    return maps


def _run(stage, inp):
    maps = make_inputs(inp)
    try:
        nc = build_program(stage)
    except _EarlyStop as e:
        nc, P, es, dD = e.args
        P.emit("sync", lambda eng: None, waits=[("dD", dD)])
        emit_all(nc, P)
        es.close()
    res = run_bass_kernel_spmd(nc, maps, core_ids=list(range(NCORES)))
    out = np.empty((2, 8192, 1024), np.float32)
    for core in range(NCORES):
        b, j = divmod(core, 4)
        out[b, j * T:(j + 1) * T, :] = res.results[core]["out"].T
    return out


def kernel(**inp):
    return _run("full", inp)


# version 08
# speedup vs baseline: 1.1987x; 1.0193x over previous
import numpy as np
from contextlib import ExitStack

import concourse.bass as bass
import concourse.mybir as mybir
from concourse.bass_utils import run_bass_kernel_spmd

F32 = mybir.dt.float32
BF16 = mybir.dt.bfloat16
I32 = mybir.dt.int32
AF = mybir.ActivationFunctionType
OP = mybir.AluOpType
AX = mybir.AxisListType

NCORES = 8
D = 1024
T = 2048
NK = 8
NH = 16
DH = 64
CHUNK = 512
NCH = T // CHUNK
HFF0 = 3072
NH0 = HFF0 // 128
HFF1 = 4096
NH1 = HFF1 // 128
NST = NH * 65
MAGIC = 12582912.0
PI = float(np.pi)
ANG_SCALE = PI / (2.0 * 8192.0)

ENGINES = ("sync", "scalar", "vector", "tensor", "gpsimd")
QSEM = {"sync": "dS", "scalar": "dA", "gpsimd": "dG"}


class Plan:
    def __init__(self):
        self.ops = {e: [] for e in ENGINES}
        self.cnt = {}
        self.sems = {}

    def newsem(self, *names):
        for n in names:
            self.cnt[n] = 0

    def v(self, name):
        return self.cnt[name]

    def emit(self, engine, fn, waits=(), incs=()):
        waits = tuple((s, v) for (s, v) in waits if v > 0)
        for (s, n) in incs:
            self.cnt[s] += n
        def run(eng, _fn=fn, _w=waits, _i=tuple(incs), _sems=self.sems):
            for (s, v) in _w:
                eng.wait_ge(_sems[s], v)
            inst = _fn(eng)
            for (s, n) in _i:
                inst = inst.then_inc(_sems[s], n)

        self.ops[engine].append(run)


class Banks:

    def __init__(self, slots, tok=("pe", 0)):
        self.slots = list(slots)
        self.tok = [tok] * len(slots)
        self.i = 0

    def get(self):
        j = self.i % len(self.slots)
        self.i += 1
        return j, self.slots[j], self.tok[j]

    def set_tok(self, j, tok):
        self.tok[j] = tok


class _EarlyStop(Exception):
    pass


def build_program(stage="full"):
    do_attn = stage in ("full", "l0", "l0_attn", "l0_hin") or stage.startswith("l1")
    do_ffn = stage in ("full", "l0", "l0_ffn", "l0_hid") or stage.startswith("l1")
    nc = bass.Bass("TRN2", target_bir_lowering=False, debug=False)
    P = Plan()
    P.newsem("dS", "dA", "dG", "dB", "pe", "ve", "ae", "ge", "cc", "st")

    x_h = nc.dram_tensor("x", [D, T], F32, kind="ExternalInput")
    out_h = nc.dram_tensor("out", [D, T], F32, kind="ExternalOutput")
    params_h = nc.dram_tensor("params", [128, 24], F32, kind="ExternalInput")
    wgT_h = nc.dram_tensor("wgT", [D, D], F32, kind="ExternalInput")
    wvT_h = nc.dram_tensor("wvT", [D, D], F32, kind="ExternalInput")
    wdT_h = nc.dram_tensor("wdT", [D, D], F32, kind="ExternalInput")
    bcols0_h = nc.dram_tensor("bcols0", [128, 24], F32, kind="ExternalInput")
    bcols1_h = nc.dram_tensor("bcols1", [128, 64], F32, kind="ExternalInput")
    fgT_h = nc.dram_tensor("fgT", [D, HFF0], I32, kind="ExternalInput")
    fuT_h = nc.dram_tensor("fuT", [D, HFF0], I32, kind="ExternalInput")
    fdT_h = nc.dram_tensor("fdT", [HFF0, D], I32, kind="ExternalInput")
    wq_h = nc.dram_tensor("wq", [D, D], F32, kind="ExternalInput")
    wk_h = nc.dram_tensor("wk", [D, D], F32, kind="ExternalInput")
    wv1_h = nc.dram_tensor("wv1", [D, D], F32, kind="ExternalInput")
    wo_h = nc.dram_tensor("wo", [D, D], F32, kind="ExternalInput")
    bk_h = nc.dram_tensor("bk", [1, D], F32, kind="ExternalInput")
    bv1_h = nc.dram_tensor("bv1", [1, D], F32, kind="ExternalInput")
    win_h = nc.dram_tensor("win", [D, HFF1], F32, kind="ExternalInput")
    wout_h = nc.dram_tensor("wout", [HFF1, D], F32, kind="ExternalInput")

    ag0_in = nc.dram_tensor("ag0_in", [1, 128 * 16], F32)
    ag0_out = nc.dram_tensor("ag0_out", [NCORES, 128 * 16], F32)
    ag1_in = nc.dram_tensor("ag1_in", [1, 128 * NST], F32)
    ag1_out = nc.dram_tensor("ag1_out", [NCORES, 128 * NST], F32)
    ag1b_in = nc.dram_tensor("ag1b_in", [1, 8], F32)
    ag1b_out = nc.dram_tensor("ag1b_out", [NCORES, 8], F32)
    ag2_in = nc.dram_tensor("ag2_in", [1, 8], F32)
    ag2_out = nc.dram_tensor("ag2_out", [NCORES, 8], F32)
    winq_d = nc.dram_tensor("winq_d", [D, HFF1], BF16)
    woutq_d = nc.dram_tensor("woutq_d", [HFF1, D], BF16)
    sinrow_d = nc.dram_tensor("sinrow_d", [1, HFF1], F32)
    hid_d = nc.dram_tensor("hid_d", [HFF1, T], BF16)

    es = ExitStack()

    def sbt(name, shape, dtype):
        return es.enter_context(nc.sbuf_tensor(name, shape, dtype))

    xres = sbt("xres", [128, NK * T], F32)
    xn = sbt("xn", [128, NK * T], BF16)
    bufA = sbt("bufA", [128, NK * T], BF16)
    bufB = sbt("bufB", [128, NK * T], BF16)
    ones_row = sbt("ones_row", [1, 128], F32)
    ones_col = sbt("ones_col", [128, 1], F32)
    prm = sbt("prm", [128, 24], F32)
    bcolL0 = sbt("bcolL0", [128, 3 * NK], F32)
    bcol1 = sbt("bcol1", [128, 64], F32)
    tri = sbt("tri", [128, 512], BF16)
    summ = sbt("summ", [128, 16], F32)
    hin = sbt("hin", [128, 8], F32)
    agbuf = sbt("agbuf", [128, 16 * NCORES], F32)
    red = sbt("red", [128, 144], F32)
    smax = sbt("smax", [8, 48], F32)
    identb = sbt("identb", [128, 128], BF16)
    ones_rb = sbt("ones_rb", [1, 128], BF16)

    ARENA_B = 45056
    arena = sbt("arena", [128, ARENA_B // 4], F32)
    abase = nc.lookup_mloc(arena).addr

    def aview(name, shape, dtype, off):
        assert off + shape[-1] * mybir.dt.size(dtype) * int(np.prod(shape[1:-1], initial=1)) <= ARENA_B
        return nc.alloc_sbuf_tensor_at(name, shape, dtype, offset=abase + off)

    KB = 1024
    lnsq = aview("lnsq", [128, 4096], F32, 0)
    lnrw = aview("lnrw", [128, 4096], F32, 16 * KB)
    a_row = aview("a_row", [128, T], F32, 0)
    xc_row = aview("xc_row", [128, T], F32, 8 * KB)
    g_row = aview("g_row", [128, T], F32, 16 * KB)
    vh_row = aview("vh_row", [128, T], F32, 24 * KB)
    wslA = aview("wslA", [128, 6144], BF16, 32 * KB)
    ffn_h = aview("ffn_h", [128, NH0 * 512], BF16, 0)
    ffn_gu = aview("ffn_gu", [128, 4096], BF16, 24 * KB)
    ffn_dn = aview("ffn_dn", [128, 2 * 1536], BF16, 32 * KB)
    gact = aview("gact", [128, 1024], F32, 38 * KB)
    corr_t = aview("corr_t", [128, T], F32, 0)

    PS0 = es.enter_context(nc.psum_tensor("PS0", [128, 1024], F32))
    PS1 = es.enter_context(nc.psum_tensor("PS1", [128, 1024], F32))
    PS2 = es.enter_context(nc.psum_tensor("PS2", [128, 1024], F32))
    PS3 = es.enter_context(nc.psum_tensor("PS3", [128, 512], F32))
    PS4 = es.enter_context(nc.psum_tensor("PS4", [128, 512], F32))

    for s in list(P.cnt.keys()):
        P.sems[s] = es.enter_context(nc.semaphore(s))

    def dma(q, out_ap, in_ap, waits=(), sem=None, slow=False):
        s = sem or QSEM[q]
        if s not in P.cnt:
            P.newsem(s)
            P.sems[s] = es.enter_context(nc.semaphore(s))
        P.emit(q, (lambda eng, o=out_ap, i=in_ap, sl=slow: eng.dma_start(
            out=o, in_=i, allow_slow_non_contiguous=sl)),
               waits=waits, incs=[(s, 16)])
        return (s, P.v(s))

    def mm(lhsT, rhs, out, start, stop, waits=(), inc=False):
        incs = [("pe", 1)] if inc else ()
        P.emit("tensor",
               (lambda eng, l=lhsT, r=rhs, o=out, s=start, p=stop:
                eng.matmul(o, l, r, start=s, stop=p)),
               waits=waits, incs=incs)
        return P.v("pe")

    def act(fn, waits=()):
        P.emit("scalar", fn, waits=waits, incs=[("ae", 1)])
        return P.v("ae")

    def vec(fn, waits=()):
        P.emit("vector", fn, waits=waits, incs=[("ve", 1)])
        return P.v("ve")

    def gp(fn, waits=()):
        P.emit("gpsimd", fn, waits=waits, incs=[("ge", 1)])
        return P.v("ge")

    def col_block_dma(dst_tile, src_h, col0, ncols, nk, waits=(), row0=0, sem=None):
        o = dst_tile.rearrange("p (k f) -> p k f", f=ncols)
        i = src_h.ap()[row0:row0 + nk * 128, col0:col0 + ncols].rearrange(
            "(k p) f -> p k f", p=128)
        return dma("gpsimd", o, i, waits, sem=sem)

    def xr(k, t0=0, t1=T):
        return xres[:, k * T + t0: k * T + t1]

    def xnv(k, t0=0, t1=T):
        return xn[:, k * T + t0: k * T + t1]

    setA = Banks([PS0[:, 0:512], PS0[:, 512:1024], PS1[:, 0:512], PS1[:, 512:1024]])
    setB = Banks([PS2[:, 0:512], PS2[:, 512:1024], PS3[:, 0:512], PS4[:, 0:512]])

    xw = []
    for k in range(NK):
        xw.append(dma("sync", xr(k), x_h.ap()[k * 128:(k + 1) * 128, :]))
    dma("sync", prm[:], params_h.ap())
    dma("sync", bcolL0[:], bcols0_h.ap())
    w_const = dma("sync", bcol1[:], bcols1_h.ap())

    vec(lambda eng: eng.memset(ones_row[:], 1.0))
    vec(lambda eng: eng.memset(ones_col[:], 1.0))
    v_tri = vec(lambda eng: eng.memset(tri[:], 1.0))
    g_tri = gp(lambda eng: eng.iota(
        agbuf[:, 0:128], pattern=[[-1, 128]], base=0, channel_multiplier=1,
        allow_small_or_imprecise_dtypes=True), waits=[("ve", v_tri)])
    vec(lambda eng: eng.tensor_single_scalar(
        tri[:, 0:128], agbuf[:, 0:128], 0.0, OP.is_le), waits=[("ge", g_tri)])

    def emit_layernorm(first_waits, stats_tok=("pe", 0)):
        bcR = lnrw[:, 0:512]
        bcM = lnrw[:, 512:1024]
        tmp = lnrw[:, 1024:1536]
        R0 = lnrw[0:1, 1536:2048]
        R1 = lnrw[0:1, 2048:2560]
        R2 = lnrw[0:1, 2560:3072]
        xn_done = 0
        for tt in range(4):
            t0 = tt * 512
            ae_sq = 0
            for k in range(NK):
                w = list(first_waits) if (tt == 0 and k == 0) else []
                ae_sq = act(lambda eng, k=k, t0=t0: eng.activation(
                    lnsq[:, k * 512:(k + 1) * 512], xr(k, t0, t0 + 512),
                    AF.Square), waits=w)
            first_w = [("ae", ae_sq)] + ([stats_tok] if tt == 0 else [])
            for k in range(NK):
                mm(ones_col[:], xr(k, t0, t0 + 512), PS3[0:1, 0:512],
                   start=(k == 0), stop=(k == NK - 1),
                   waits=(first_w if k == 0 else ()))
            pe_s = 0
            for k in range(NK):
                pe_s = mm(ones_col[:], lnsq[:, k * 512:(k + 1) * 512],
                          PS4[0:1, 0:512], start=(k == 0), stop=(k == NK - 1),
                          inc=(k == NK - 1))
            vec(lambda eng, R0=R0: eng.tensor_scalar(
                R0, PS3[0:1, 0:512], 1.0 / D, None, OP.mult),
                waits=[("pe", pe_s)])
            vec(lambda eng, R1=R1: eng.tensor_scalar(
                R1, PS4[0:1, 0:512], 1.0 / D, 1e-5, OP.mult, OP.add))
            vec(lambda eng, R0=R0, R2=R2: eng.scalar_tensor_tensor(
                R2, R0, -1.0, R0, OP.mult, OP.mult))
            ve_var = vec(lambda eng, R1=R1, R2=R2: eng.tensor_add(R1, R1, R2))
            ae_v = act(lambda eng, R1=R1, R2=R2: eng.activation(R2, R1, AF.Sqrt),
                       waits=[("ve", ve_var)])
            vec(lambda eng, R1=R1, R2=R2: eng.reciprocal(R1, R2),
                waits=[("ae", ae_v)])
            ve_r = vec(lambda eng, R0=R0, R1=R1: eng.scalar_tensor_tensor(
                R0, R0, -1.0, R1, OP.mult, OP.mult))
            mm(ones_row[0:1, 0:128], R1, PS4[:, 0:512], start=True, stop=True,
               waits=[("ve", ve_r)])
            pe_b = mm(ones_row[0:1, 0:128], R0, PS3[:, 0:512],
                      start=True, stop=True, inc=True)
            act(lambda eng, bcR=bcR: eng.activation(bcR, PS4[:, 0:512], AF.Copy),
                waits=[("pe", pe_b)])
            ae_b = act(lambda eng, bcM=bcM: eng.activation(bcM, PS3[:, 0:512], AF.Copy))
            for k in range(NK):
                vec(lambda eng, k=k, t0=t0, tmp=tmp, bcR=bcR: eng.tensor_mul(
                    tmp, xr(k, t0, t0 + 512), bcR),
                    waits=([("ae", ae_b)] if k == 0 else []))
                xn_done = vec(lambda eng, k=k, t0=t0, tmp=tmp, bcM=bcM: eng.tensor_add(
                    xnv(k, t0, t0 + 512), tmp, bcM))
        return xn_done

    xn_all = emit_layernorm(list(xw) + [w_const])
    setB.tok[2] = ("ae", P.v("ae"))
    setB.tok[3] = ("ae", P.v("ae"))

    def l0_wslot(m, b):
        c0 = (m * 2 + b) * 1024
        return wslA[:, c0:c0 + 1024]

    wslot_tok = {}
    ve_rows_done = 0
    for e in (range(NK) if do_attn else ()):
        b = e % 2
        dtok = {}
        for m, src in enumerate((wdT_h, wvT_h, wgT_h)):
            dtok[m] = col_block_dma(l0_wslot(m, b), src, e * 128, 128, NK,
                                    waits=[wslot_tok.get((m, b), ("pe", 0))],
                                    sem="wA%d" % (m * 2 + b))
        for m in range(3):
            bs = setA if (e * 3 + m) % 2 == 0 else setB
            wslot = l0_wslot(m, b)
            groups = {}
            for tt in range(4):
                j, ap, tok = bs.get()
                pw = [dtok[m], tok]
                if m == 0 and e == 0:
                    pw.append(("ve", xn_all))
                pe_last = 0
                for k in range(NK):
                    pe_last = mm(wslot[:, k * 128:(k + 1) * 128],
                                 xnv(k, tt * 512, tt * 512 + 512), ap,
                                 start=(k == 0), stop=(k == NK - 1),
                                 waits=(pw if k == 0 else ()),
                                 inc=(k == NK - 1))
                groups[tt] = (pe_last, ap, j)
            wslot_tok[(m, b)] = ("pe", groups[3][0])
            dst = (a_row, vh_row, g_row)[m]
            bofs = (2 * NK, NK, 0)[m]
            for tt in range(4):
                pe_g, ap, j = groups[tt]
                w = [("pe", pe_g)]
                if tt == 0 and m == 0:
                    w.append(("ve", ve_rows_done))
                if m == 1:
                    ve_t = vec(lambda eng, dst=dst, tt=tt, ap=ap, bo=bofs, e=e:
                               eng.tensor_scalar(
                                   dst[:, tt * 512:tt * 512 + 512], ap,
                                   bcolL0[:, bo + e:bo + e + 1], None, OP.add),
                               waits=w)
                    bs.set_tok(j, ("ve", ve_t))
                else:
                    ae_t = act(lambda eng, dst=dst, tt=tt, ap=ap, bo=bofs, e=e:
                               eng.activation(dst[:, tt * 512:tt * 512 + 512], ap,
                                              AF.Sigmoid,
                                              bias=bcolL0[:, bo + e:bo + e + 1]),
                               waits=w)
                    bs.set_tok(j, ("ae", ae_t))
        ae_evacs = P.v("ae")
        vec(lambda eng: eng.tensor_scalar(
            xc_row[:], a_row[:], -1.0, 1.0, OP.mult, OP.add),
            waits=[("ae", ae_evacs)])
        vec(lambda eng: eng.tensor_mul(xc_row[:], xc_row[:], vh_row[:]))
        vec(lambda eng: eng.tensor_tensor_scan(
            vh_row[:], a_row[:], xc_row[:], 0.0, OP.mult, OP.add))
        vec(lambda eng: eng.tensor_tensor_scan(
            xc_row[:], a_row[:], a_row[:], 1.0, OP.mult, OP.bypass))
        vec(lambda eng, e=e: eng.tensor_mul(bufA[:, e * T:(e + 1) * T], vh_row[:], g_row[:]))
        vec(lambda eng, e=e: eng.tensor_mul(
            bufB[:, e * T:(e + 1) * T], xc_row[:], g_row[:]))
        vec(lambda eng, e=e: eng.tensor_copy(summ[:, e:e + 1], xc_row[:, T - 1:T]))
        ve_rows_done = vec(lambda eng, e=e: eng.tensor_copy(
            summ[:, 8 + e:8 + e + 1], vh_row[:, T - 1:T]))
    ve_summary = ve_rows_done
    pe_a1_done = P.v("pe")

    if not do_attn:
        ve_summary = vec(lambda eng: eng.memset(summ[:], 0.0))
    g_ag0 = dma("gpsimd", ag0_in.ap().rearrange("a (p c) -> (a p) c", p=128),
                summ[:], waits=[("ve", ve_summary)])
    P.emit("gpsimd", lambda eng: eng.collective_compute(
        "AllGather", OP.bypass, replica_groups=[list(range(NCORES))],
        ins=[ag0_in.ap()], outs=[ag0_out.ap()]),
        waits=[("dG", g_ag0[1])], incs=[("cc", 1)])
    cc0 = P.v("cc")

    gu_tok = {}
    dn_tok = {}
    tok_g = tok_u = tok_d = ("ae", P.v("ae"))
    ve_ffn = 0
    pe_dn_last = 0
    first_gu_dma = True
    for tp in (range(1) if stage == "l0_hid" else (range(4) if do_ffn else ())):
        t0 = tp * 512
        h_last = 0
        for hch in range(NH0):
            sg = hch % 2
            wg = [gu_tok.get(("g", sg), ("pe", 0))]
            if first_gu_dma:
                wg.append(("ve", ve_summary))
                wg.append(("ve", xn_all))
                first_gu_dma = False
            dg = col_block_dma(ffn_gu[:, sg * 1024:(sg + 1) * 1024], fgT_h,
                               hch * 128, 128, NK, waits=wg, sem="fg%d" % sg)
            du = col_block_dma(ffn_gu[:, (2 + sg) * 1024:(3 + sg) * 1024], fuT_h,
                               hch * 128, 128, NK,
                               waits=[gu_tok.get(("u", sg), ("pe", 0))],
                               sem="fu%d" % sg)
            pw_g = [dg, tok_g]
            pw_u = [du, tok_u]
            if tp == 0 and hch == 0:
                pw_g = pw_g + [("ve", xn_all)]
            pe_g = pe_u = 0
            for k in range(NK):
                pe_g = mm(ffn_gu[:, sg * 1024 + k * 128: sg * 1024 + (k + 1) * 128],
                          xnv(k, t0, t0 + 512), PS0[:, 0:512],
                          start=(k == 0), stop=(k == NK - 1),
                          waits=(pw_g if k == 0 else ()), inc=(k == NK - 1))
            gu_tok[("g", sg)] = ("pe", pe_g)
            for k in range(NK):
                pe_u = mm(ffn_gu[:, (2 + sg) * 1024 + k * 128: (2 + sg) * 1024 + (k + 1) * 128],
                          xnv(k, t0, t0 + 512), PS1[:, 0:512],
                          start=(k == 0), stop=(k == NK - 1),
                          waits=(pw_u if k == 0 else ()), inc=(k == NK - 1))
            gu_tok[("u", sg)] = ("pe", pe_u)
            w_s = [("pe", pe_g), tok_g, ("ve", h_last)]
            if tp > 0 and hch == 0:
                w_s.append(("pe", pe_dn_last))
            ae_s = act(lambda eng: eng.activation(
                gact[:, 0:512], PS0[:, 0:512], AF.Silu, scale=1.0 / 127.0),
                waits=w_s)
            tok_g = ("ae", ae_s)
            h_last = vec(lambda eng, hch=hch: eng.scalar_tensor_tensor(
                ffn_h[:, hch * 512:(hch + 1) * 512], PS1[:, 0:512],
                1.0 / 127.0, gact[:, 0:512], OP.mult, OP.mult),
                waits=[("ae", ae_s), ("pe", pe_u)])
            tok_u = ("ve", h_last)
        for dch in range(NK):
            pe_d = 0
            for half in range(2):
                hb = half
                dn_w = [dn_tok.get(hb, ("pe", pe_a1_done))]
                if hb not in dn_tok:
                    dn_w.append(("ve", ve_summary))
                dd = col_block_dma(ffn_dn[:, hb * 1536:(hb + 1) * 1536], fdT_h,
                                   dch * 128, 128, 12,
                                   row0=half * 1536, waits=dn_w,
                                   sem="fd%d" % hb)
                for k in range(12):
                    kk = half * 12 + k
                    pe_d = mm(ffn_dn[:, hb * 1536 + k * 128: hb * 1536 + (k + 1) * 128],
                              ffn_h[:, kk * 512:(kk + 1) * 512],
                              PS2[:, 0:512],
                              start=(kk == 0), stop=(kk == NH0 - 1),
                              waits=([dd] +
                                     ([("ve", h_last), tok_d] if kk == 0 else []))
                              if k == 0 else (),
                              inc=(k == 11))
                dn_tok[hb] = ("pe", pe_d)
            pe_dn_last = pe_d
            ve_ffn = vec(lambda eng, dch=dch, t0=t0: eng.scalar_tensor_tensor(
                xr(dch, t0, t0 + 512), PS2[:, 0:512], 1.0 / 127.0,
                xr(dch, t0, t0 + 512), OP.mult, OP.add),
                waits=[("pe", pe_d)])
            tok_d = ("ve", ve_ffn)

    P.emit("gpsimd", (lambda eng: eng.dma_start(
        out=agbuf[:].rearrange("p (r c) -> p r c", c=16),
        in_=ag0_out.ap().rearrange("r (p c) -> p r c", p=128))),
        waits=[("cc", cc0)], incs=[("dB", 16)])
    g_back_v = P.v("dB")
    ve_prep = 0
    for r in range(NCORES):
        vec(lambda eng, r=r: eng.tensor_scalar(
            red[:, r * 8:(r + 1) * 8], agbuf[:, r * 16:r * 16 + 8],
            prm[:, r:r + 1], prm[:, 9 + r:10 + r], OP.mult, OP.add),
            waits=([("dB", g_back_v)] if r == 0 else ()))
        ve_prep = vec(lambda eng, r=r: eng.tensor_scalar(
            red[:, 64 + r * 8: 64 + (r + 1) * 8],
            agbuf[:, r * 16 + 8:r * 16 + 16],
            prm[:, r:r + 1], None, OP.mult))
    gp(lambda eng: eng.memset(hin[:], 0.0), waits=[("ve", ve_prep)])
    for r in range(NCORES):
        gp(lambda eng, r=r: eng.tensor_tensor(
            red[:, 128:136], red[:, r * 8:(r + 1) * 8], hin[:], OP.mult))
        gp(lambda eng, r=r: eng.tensor_tensor(
            hin[:], red[:, 128:136], red[:, 64 + r * 8: 64 + (r + 1) * 8], OP.add))
    ge_hin = P.v("ge")
    ve_corr = 0
    for e in (range(NK) if do_attn else ()):
        vec(lambda eng, e=e: eng.scalar_tensor_tensor(
            corr_t[:], bufB[:, e * T:(e + 1) * T], hin[:, e:e + 1],
            bufA[:, e * T:(e + 1) * T], OP.mult, OP.add),
            waits=([("ge", ge_hin), ("ve", ve_ffn)] if e == 0 else ()))
        ve_corr = vec(lambda eng, e=e: eng.tensor_add(
            xr(e), xr(e), corr_t[:]))

    if stage.startswith("l0"):
        last = None
        fin = max(ve_corr, ve_ffn, xn_all)
        if stage == "l0_hin":
            last = dma("gpsimd", out_h.ap()[0:128, 0:128], agbuf[:],
                       waits=[("ve", ve_corr)])
            last = dma("gpsimd", out_h.ap()[0:128, 128:136], hin[:],
                       waits=[("ve", ve_corr)])
            last = dma("gpsimd", out_h.ap()[0:128, 136:152], summ[:],
                       waits=[("ve", ve_corr)])
            last = dma("gpsimd", out_h.ap()[0:128, 152:168], prm[:],
                       waits=[("ve", ve_corr)])
            last = dma("gpsimd", out_h.ap()[0:128, 168:360],
                       bufA[:, 0:192], waits=[("ve", ve_corr)])
            P.emit("gpsimd", lambda eng: None, waits=[("dG", last[1])])
        elif stage == "l0_hid":
            for k in range(6):
                last = dma("gpsimd", out_h.ap()[k * 128:(k + 1) * 128, :],
                           ffn_h[:, k * 2048:(k + 1) * 2048],
                           waits=[("ve", max(ve_ffn, 1))])
            P.emit("gpsimd", lambda eng: None, waits=[("dG", last[1])])
        elif stage == "l0_xn":
            for k in range(NK):
                last = dma("gpsimd", out_h.ap()[k * 128:(k + 1) * 128, :], xnv(k),
                           waits=[("ve", fin)])
            P.emit("gpsimd", lambda eng: None, waits=[("dG", last[1])])
        else:
            for k in range(NK):
                last = dma("sync", out_h.ap()[k * 128:(k + 1) * 128, :], xr(k),
                           waits=[("ve", fin)])
            P.emit("sync", lambda eng: None, waits=[("dS", last[1])])
        emit_all(nc, P)
        es.close()
        return nc

    pe_l0 = P.v("pe")
    ae_l0 = P.v("ae")

    xn_all = emit_layernorm([("ve", ve_corr), ("pe", pe_l0), ("ae", ae_l0)],
                            stats_tok=("ve", ve_corr))
    ae_ln1 = P.v("ae")


    def early_out(waits):
        last = None
        for k in range(NK):
            last = dma("sync", out_h.ap()[k * 128:(k + 1) * 128, :], xr(k),
                       waits=waits)
        P.emit("sync", lambda eng: None, waits=[("dS", last[1])])
        emit_all(nc, P)
        es.close()
        return nc

    brow1 = aview("brow1", [1, 2048], BF16, 0)
    cosfc = aview("cosfc", [128, 512], BF16, 4096)
    sinfc = aview("sinfc", [128, 512], BF16, 5120)
    qp_t = aview("qp_t", [128, 512], BF16, 6272)
    prQc = aview("prQc", [128, 512], BF16, 7296)
    prQs = aview("prQs", [128, 512], BF16, 8320)
    prKc = aview("prKc", [128, 512], BF16, 9344)
    prKs = aview("prKs", [128, 512], BF16, 10368)
    st_t = aview("st_t", [128, 512], BF16, 11392)
    krelu = aview("krelu", [128, 1024], F32, 12416)
    sden = aview("sden", [1, 512], F32, 12416)
    iotat = aview("iotat", [128, 512], F32, 14464)
    vtok = aview("vtok", [128, 4 * NST], BF16, 16512)
    sprefb = aview("sprefb", [128, 4 * NST], BF16, 24832)
    srun1 = aview("srun1", [128, NST], F32, 33152)
    sptot = aview("sptot", [128, NST], BF16, 37312)
    sshift = aview("sshift", [128, NST], BF16, 39392)
    onesT = aview("onesT", [128, 128], F32, 41472)
    wvsl = aview("wvsl", [128, 1024], BF16, 41984)
    at64 = aview("at64", [128, 512], BF16, 44032)
    xq_t = aview("xq_t", [128, 4096], BF16, 0)
    w1sl = aview("w1sl", [128, 2048], BF16, 8192)
    w2sl = aview("w2sl", [128, 2048], BF16, 12288)
    hidt = aview("hidt", [128, 1024], BF16, 16384)
    qtf = aview("qtf", [128, 1024], F32, 18432)
    wprep = aview("wprep", [128, 4096], BF16, 22528)
    wpabs = aview("wpabs", [128, 512], F32, 30720)
    wpq = aview("wpq", [128, 512], F32, 32768)
    amx32 = aview("amx32", [128, 32], F32, 34816)
    sc32 = aview("sc32", [128, 32], F32, 34944)
    rs32 = aview("rs32", [128, 32], F32, 35072)
    rso32 = aview("rso32", [128, 32], F32, 35200)
    cmb32 = aview("cmb32", [128, 32], F32, 35328)
    srow_t = aview("srow_t", [1, 512], BF16, 35456)

    g_id = gp(lambda eng: eng.iota(
        red[:, 0:128] if False else red[:, 16:144], pattern=[[-1, 128]], base=0,
        channel_multiplier=1, allow_small_or_imprecise_dtypes=True))
    ve_id = vec(lambda eng: eng.tensor_single_scalar(
        identb[:], red[:, 16:144], 0.0, OP.is_equal), waits=[("ge", g_id)])
    ve_orb = vec(lambda eng: eng.memset(ones_rb[:], 1.0), waits=[("ve", ve_corr)])

    for k in range(NK):
        vec(lambda eng, k=k: eng.reduce_max(
            red[:, 8 + k:9 + k], xnv(k),
            axis=AX.X, apply_absolute_value=True))
    ve_am = vec(lambda eng: eng.reduce_max(
        summ[:, 0:1], red[:, 8:16], axis=AX.X))
    ve_amb = vec(lambda eng: eng.tensor_copy(tri[:, 0:1] if False else summ.bitcast(BF16)[:, 2:3], summ[:, 0:1]),
                 waits=[("ve", ve_am)])
    P.emit("tensor", (lambda eng: eng.transpose(
        PS4.bitcast(BF16)[0:1, 0:128], summ.bitcast(BF16)[:, 2:3], identb[:])),
        waits=[("ve", ve_amb), ("ve", ve_id)], incs=[("pe", 1)])
    pe_am = P.v("pe")
    ve_am2 = vec(lambda eng: eng.reduce_max(
        smax[0:1, 0:1], PS4.bitcast(BF16)[0:1, 0:128], axis=AX.X),
        waits=[("pe", pe_am)])

    if stage == "l1_ln":
        return early_out([("ve", xn_all), ("ve", ve_am2), ("ve", ve_corr)])
    d_brow = dma("gpsimd", brow1[0:1, 0:D], bk_h.ap(), sem="dR",
                 waits=[("ve", xn_all)])
    d_brow = dma("gpsimd", brow1[0:1, D:2 * D], bv1_h.ap(), sem="dR")
    g_cs = gp(lambda eng: eng.iota(
        iotat[:, 0:16], pattern=[[128, 16]], base=0, channel_multiplier=1,
        allow_small_or_imprecise_dtypes=True), waits=[("ve", xn_all)])
    cscol = red[:, 112:144]
    ae_cs = act(lambda eng: eng.activation(
        cscol[:, 0:16], iotat[:, 0:16], AF.Sin, scale=ANG_SCALE,
        bias=prm[:, 17:18]), waits=[("ge", g_cs)])
    ae_cs = act(lambda eng: eng.activation(
        cscol[:, 16:32], iotat[:, 0:16], AF.Sin, scale=ANG_SCALE,
        bias=prm[:, 8:9]))
    ve_vi = vec(lambda eng: eng.memset(vtok[:], 0.0), waits=[("ve", xn_all)])
    ve_vi = vec(lambda eng: eng.memset(
        vtok[:].rearrange("p (u h) -> p u h", h=65)[:, :, 64:65], 1.0))


    def fence(buf=None):
        b = agbuf if buf is None else buf
        return vec(lambda eng, b=b: eng.tensor_copy(b[:, 0:64], b[:, 64:128]))

    w2slB = aview("w2slB", [128, 2048], BF16, 4096)
    wpqB = aview("wpqB", [128, 512], F32, 8320)
    rsoC = red[:, 48:80]
    scoC = red[:, 80:112]
    pw_tok = {}
    for hch in range(NH1):
        sl = hch % 2
        d_w2 = dma("gpsimd", w2slB[:, sl * 1024:(sl + 1) * 1024],
                   wout_h.ap()[hch * 128:(hch + 1) * 128, :], sem="pW%d" % sl,
                   waits=[pw_tok.get(sl, ("ve", xn_all))])
        ve_a2 = vec(lambda eng, sl=sl, hch=hch: eng.reduce_max(
            rsoC[:, hch:hch + 1], w2slB[:, sl * 1024:(sl + 1) * 1024],
            axis=AX.X, apply_absolute_value=True), waits=[d_w2])
        pw_tok[sl] = ("ve", ve_a2)
    ve_f = fence(w2slB)
    ve_f = vec(lambda eng: eng.tensor_scalar(
        rsoC[:], rsoC[:], 1e-8, None, OP.max))
    ve_f = fence(w2slB)
    ve_rq = vec(lambda eng: eng.reciprocal(scoC[:], rsoC[:]))
    ve_rso = vec(lambda eng: eng.tensor_scalar(
        rsoC[:], rsoC[:], 1.0 / 127.0, None, OP.mult))
    ve_f = fence(w2slB)
    ve_rq = vec(lambda eng: eng.tensor_scalar(
        scoC[:], scoC[:], 127.0, None, OP.mult))
    ve_f = fence(w2slB)
    q_tok = {0: ("ve", ve_f), 1: ("ve", ve_f)}
    for hch in range(NH1):
        sl = hch % 2
        d_w2 = dma("gpsimd", w2slB[:, sl * 1024:(sl + 1) * 1024],
                   wout_h.ap()[hch * 128:(hch + 1) * 128, :], sem="pW%d" % sl,
                   waits=[q_tok[sl]])
        ae_q1 = act(lambda eng, sl=sl, hch=hch: eng.activation(
            wpqB[:, 0:512], w2slB[:, sl * 1024:sl * 1024 + 512],
            AF.Copy, scale=scoC[:, hch:hch + 1], bias=MAGIC),
            waits=[d_w2, ("ae", pw_tok.get("ae", 0))])
        ae_q2 = act(lambda eng, sl=sl, hch=hch: eng.activation(
            w2slB.bitcast(F32)[:, sl * 512 + 0:sl * 512 + 256] if False else wpqB[:, 0:512],
            w2slB[:, sl * 1024 + 512:(sl + 1) * 1024],
            AF.Copy, scale=scoC[:, hch:hch + 1], bias=MAGIC)) if False else None
        ve_w1 = vec(lambda eng, sl=sl: eng.tensor_scalar(
            w2slB[:, sl * 1024:sl * 1024 + 512], wpqB[:, 0:512], -MAGIC, None,
            OP.add), waits=[("ae", ae_q1)])
        ae_q3 = act(lambda eng, sl=sl, hch=hch: eng.activation(
            wpqB[:, 0:512], w2slB[:, sl * 1024 + 512:(sl + 1) * 1024],
            AF.Copy, scale=scoC[:, hch:hch + 1], bias=MAGIC),
            waits=[("ve", ve_w1)])
        pw_tok["ae"] = ae_q3
        ve_w2 = vec(lambda eng, sl=sl: eng.tensor_scalar(
            w2slB[:, sl * 1024 + 512:(sl + 1) * 1024], wpqB[:, 0:512], -MAGIC,
            None, OP.add), waits=[("ae", ae_q3)])
        d_w2o = dma("sync", woutq_d.ap()[hch * 128:(hch + 1) * 128, :],
                    w2slB[:, sl * 1024:(sl + 1) * 1024], sem="pO%d" % sl,
                    waits=[("ve", ve_w2)])
        q_tok[sl] = d_w2o
    ve_woutprep = P.v("ve")
    pO_done = [q_tok[0], q_tok[1]]

    wk_last = wv_last = None
    for k in range(NK):
        wk_last = dma("gpsimd", bufB[:, k * 1024:(k + 1) * 1024],
                      wk_h.ap()[k * 128:(k + 1) * 128, :], sem="wWk",
                      waits=([("ve", ve_corr)] if k == 0 else ()))
    for k in range(NK):
        wv_last = dma("gpsimd", bufB[:, (NK + k) * 1024:(NK + k + 1) * 1024],
                      wv1_h.ap()[k * 128:(k + 1) * 128, :], sem="wWv",
                      waits=([("ve", ve_corr)] if k == 0 else ()))

    def wkv(which, k):
        base = 0 if which == "k" else NK * 1024
        return bufB[:, base + k * 1024: base + (k + 1) * 1024]

    ve_state_done = 0
    tokB = {("st", 0): ("pe", 0), ("st", 1): ("ve", ve_am2)}
    for c in range(NCH):
        if c == 0:
            ve_sp = vec(lambda eng, c=c: eng.memset(
                sprefb[:, c * NST:(c + 1) * NST], 0.0))
            ve_sp = vec(lambda eng: eng.memset(srun1[:], 0.0))
        else:
            ve_sp = vec(lambda eng, c=c: eng.tensor_copy(
                sprefb[:, c * NST:(c + 1) * NST], srun1[:]),
                waits=[("ve", ve_state_done)])
        for u in range(4):
            tt = c * 4 + u
            t0 = tt * 128
            pe_k = 0
            for k in range(NK):
                for hf in range(2):
                    pw = []
                    if k == 0 and hf == 0:
                        pw = [tokB.get("ps0", ("pe", 0))]
                        if c == 0 and u == 0:
                            pw += [wk_last, ("ve", xn_all)]
                    pe_k = mm(xnv(k, t0, t0 + 128),
                              wkv("k", k)[:, hf * 512:(hf + 1) * 512],
                              PS0[:, hf * 512:hf * 512 + 512],
                              start=(k == 0), stop=False, waits=pw)
            for hf in range(2):
                pe_k = mm(ones_rb[0:1, 0:128], brow1[0:1, hf * 512:(hf + 1) * 512],
                          PS0[:, hf * 512:hf * 512 + 512], start=False, stop=True,
                          waits=([d_brow, ("ve", ve_orb)]
                                 if (c == 0 and u == 0 and hf == 0) else ()),
                          inc=(hf == 1))
            ae_k = act(lambda eng: eng.activation(
                krelu[:, 0:512], PS0[:, 0:512], AF.Relu),
                waits=[("pe", pe_k), tokB.get("krelu", ("pe", 0))])
            ae_k2 = act(lambda eng: eng.activation(
                krelu[:, 512:1024], PS0[:, 512:1024], AF.Relu))
            tokB["ps0"] = ("ae", ae_k2)
            kt = bufA[:, tt * 2048:(tt + 1) * 2048] if False else bufA[:, u * 2048:(u + 1) * 2048]
            ve_kt = 0
            for hf in range(2):
                src = krelu[:, hf * 512:(hf + 1) * 512]
                ve_kt = vec(lambda eng, kt=kt, src=src, hf=hf, tt=tt: eng.tensor_scalar(
                    kt.rearrange("p (h z) -> p h z", z=128)[:, hf * 8:(hf + 1) * 8, 0:64],
                    src.rearrange("p (h z) -> p h z", z=64),
                    cscol[:, tt:tt + 1], None, OP.mult),
                    waits=([("ae", ae_k2), tokB.get("kt", ("pe", 0))] if hf == 0 else ()))
                ve_kt = vec(lambda eng, kt=kt, src=src, hf=hf, tt=tt: eng.tensor_scalar(
                    kt.rearrange("p (h z) -> p h z", z=128)[:, hf * 8:(hf + 1) * 8, 64:128],
                    src.rearrange("p (h z) -> p h z", z=64),
                    cscol[:, 16 + tt:17 + tt], None, OP.mult))
            tokB["krelu"] = ("ve", ve_kt)
            pe_v = 0
            for k in range(NK):
                for hf in range(2):
                    pw = []
                    if k == 0 and hf == 0:
                        pw = [tokB.get("ps1", ("pe", 0))]
                        if c == 0 and u == 0:
                            pw += [wv_last]
                    pe_v = mm(xnv(k, t0, t0 + 128),
                              wkv("v", k)[:, hf * 512:(hf + 1) * 512],
                              PS1[:, hf * 512:hf * 512 + 512],
                              start=(k == 0), stop=False, waits=pw)
            for hf in range(2):
                pe_v = mm(ones_rb[0:1, 0:128], brow1[0:1, D + hf * 512:D + (hf + 1) * 512],
                          PS1[:, hf * 512:hf * 512 + 512], start=False, stop=True,
                          inc=(hf == 1))
            vt = vtok[:, u * NST:(u + 1) * NST]
            ae_vv = 0
            for hf in range(2):
                w0 = []
                if hf == 0:
                    w0 = [("pe", pe_v), tokB.get("vt", ("pe", 0))]
                    if c == 0 and u == 0:
                        w0.append(("ve", ve_vi))
                ae_vv = act(lambda eng, vt=vt, hf=hf: eng.activation(
                    vt.rearrange("p (h z) -> p h z", z=65)[:, hf * 8:(hf + 1) * 8, 0:64],
                    PS1[:, hf * 512:(hf + 1) * 512].rearrange("p (h z) -> p h z", z=64),
                    AF.Copy), waits=w0)
            tokB["ps1"] = ("ae", ae_vv)
            tokB[("ktv", u)] = (("ve", ve_kt), ("ae", ae_vv))
        for h in range(NH):
            ps = PS3 if h % 2 == 0 else PS4
            pe_s = 0
            for u in range(4):
                kt = bufA[:, u * 2048:(u + 1) * 2048]
                vt = vtok[:, u * NST:(u + 1) * NST]
                pw = []
                if u == 0:
                    pw = [tokB[("st", h % 2)]]
                    if h < 2:
                        pw += [tokB[("ktv", 3)][0], tokB[("ktv", 3)][1]]
                pe_s = mm(kt[:, h * 128:(h + 1) * 128], vt[:, h * 65:(h + 1) * 65],
                          ps[:, 0:65], start=(u == 0), stop=(u == 3),
                          waits=pw, inc=(u == 3))
            ve_state_done = vec(lambda eng, h=h, ps=ps: eng.scalar_tensor_tensor(
                srun1[:, h * 65:(h + 1) * 65], ps[:, 0:65], 1.0,
                srun1[:, h * 65:(h + 1) * 65], OP.mult, OP.add),
                waits=[("pe", pe_s), ("ve", ve_sp) if h == 0 else ("ve", 0)])
            tokB[("st", h % 2)] = ("ve", ve_state_done)
        tokB["vt"] = ("pe", P.v("pe"))
        tokB["kt"] = ("pe", P.v("pe"))

    g_s1 = dma("gpsimd", ag1_in.ap().rearrange("a (p c) -> (a p) c", p=128),
               srun1[:], waits=[("ve", ve_state_done)], sem="dB")
    g_s2 = dma("gpsimd", ag1b_in.ap()[0:1, 0:1], smax[0:1, 0:1],
               waits=[("ve", ve_am2)], sem="dB")
    P.emit("gpsimd", lambda eng: eng.collective_compute(
        "AllGather", OP.bypass, replica_groups=[list(range(NCORES))],
        ins=[ag1_in.ap()], outs=[ag1_out.ap()]),
        waits=[g_s1, g_s2], incs=[("cc", 1)])
    P.emit("gpsimd", lambda eng: eng.collective_compute(
        "AllGather", OP.bypass, replica_groups=[list(range(NCORES))],
        ins=[ag1b_in.ap()], outs=[ag1b_out.ap()]), incs=[("cc", 1)])
    cc1 = P.v("cc")

    ve_sx = vec(lambda eng: eng.memset(srun1[:], 0.0),
                waits=[("cc", cc1), ("ve", ve_state_done)])
    for r in range(NCORES):
        d_sr = dma("gpsimd", sptot[:],
                   ag1_out.ap()[r:r + 1, :].rearrange("a (p c) -> (a p) c", p=128),
                   sem="dB", waits=[("ve", ve_sx)] if r == 0 else [("ve", ve_sx)])
        ve_m1 = vec(lambda eng, r=r: eng.tensor_scalar(
            sshift[:], sptot[:], prm[:, r:r + 1], None, OP.mult),
            waits=[d_sr])
        ve_sx = vec(lambda eng: eng.tensor_add(srun1[:], srun1[:], sshift[:]))
    ve_scross = ve_sx
    d_sb = dma("gpsimd", smax[0:8, 8:9], ag1b_out.ap()[:, 0:1], sem="dB",
               waits=[("cc", cc1)], slow=True)
    d_fold = dma("gpsimd", smax[0:1, 24:32], smax[0:8, 8:9], sem="dB",
                 waits=[d_sb], slow=True)
    ve_g1 = vec(lambda eng: eng.reduce_max(
        smax[0:1, 16:17], smax[0:1, 24:32], axis=AX.X), waits=[d_fold])
    ve_g1 = vec(lambda eng: eng.tensor_scalar(
        smax[0:1, 17:18], smax[0:1, 16:17], 1.0 / 32767.0, None, OP.mult))
    pe_bc = mm(ones_row[0:1, 0:128], smax[0:1, 16:18], PS4[:, 0:2],
               start=True, stop=True,
               waits=[("ve", ve_g1), ("ve", ve_state_done)], inc=True)
    ve_sc = vec(lambda eng: eng.reciprocal(red[:, 32:33], PS4[:, 0:1]),
                waits=[("pe", pe_bc)])
    ve_sc = vec(lambda eng: eng.tensor_copy(red[:, 34:35], PS4[:, 1:2]))
    ve_sc = vec(lambda eng: eng.tensor_scalar(
        red[:, 33:34], red[:, 32:33], 32767.0, None, OP.mult))
    ve_scales = ve_sc

    if stage == "l1_b1":
        return early_out([("ve", ve_scross), ("ve", ve_scales), ("ve", ve_corr)])
    lvl = 99
    if stage.startswith("l1_b2@"):
        lvl = int(stage.split("@")[1])
    pe_b1 = P.v("pe")
    ve_o1 = vec(lambda eng: eng.memset(onesT[:], 1.0))

    tokC = {"ps4": ("ve", ve_scales), ("st", 0): ("pe", 0),
            "ps0": tokB["ps0"], "ps0b": tokB["ps0"], "ps1": tokB["ps1"],
            "ps1b": tokB["ps1"], "ps3": tokB[("st", 0)],
            "ps2": ("ve", ve_ffn), "ps2b": ("ve", ve_ffn)}
    ve_attn_done = 0
    dH_attn = 0
    for c in range(NCH):
        t0 = c * CHUNK
        ve_tot = vec(lambda eng, c=c: eng.tensor_add(
            sptot[:], sprefb[:, c * NST:(c + 1) * NST], srun1[:]),
            waits=[("ve", ve_scross), tokC.get("sptot", ("pe", 0))])
        d_sh1 = dma("sync", sshift[0:64, :], sptot[64:128, :], sem="dH",
                    waits=[("ve", ve_tot), ("dH", dH_attn)])
        d_sh2 = dma("sync", sshift[64:128, :], sptot[0:64, :], sem="dH")
        g_i2 = gp(lambda eng, c=c: eng.iota(
            iotat[:], pattern=[[1, 512]], base=c * 512, channel_multiplier=0,
            allow_small_or_imprecise_dtypes=True),
            waits=[tokC.get("iotat", ("pe", 0))])
        ae_c1 = act(lambda eng: eng.activation(
            cosfc[:], iotat[:], AF.Sin, scale=ANG_SCALE, bias=prm[:, 17:18]),
            waits=([("ge", g_i2), tokC.get("csf", ("pe", 0))] +
                   ([pO_done[0], pO_done[1], ("ve", ve_woutprep)] if c == 0 else [])))
        ae_c2 = act(lambda eng: eng.activation(
            sinfc[:], iotat[:], AF.Sin, scale=ANG_SCALE, bias=prm[:, 8:9]))
        tokC["iotat"] = ("ae", ae_c2)
        for u in range(4):
            tu0 = t0 + u * 128
            pe_v2 = 0
            for k in range(NK):
                for hf in range(2):
                    pw = []
                    if k == 0 and hf == 0:
                        pw = [tokC.get("ps1", ("pe", 0)), tokC.get("ps1b", ("pe", 0))]
                    pe_v2 = mm(xnv(k, tu0, tu0 + 128),
                               wkv("v", k)[:, hf * 512:(hf + 1) * 512],
                               PS1[:, hf * 512:hf * 512 + 512],
                               start=(k == 0), stop=False, waits=pw)
            for hf in range(2):
                pe_v2 = mm(ones_rb[0:1, 0:128],
                           brow1[0:1, D + hf * 512:D + (hf + 1) * 512],
                           PS1[:, hf * 512:hf * 512 + 512], start=False,
                           stop=True, inc=(hf == 1))
            vt = vtok[:, u * NST:(u + 1) * NST]
            ae_vv2 = 0
            for hf in range(2):
                w0 = []
                if hf == 0:
                    w0 = [("pe", pe_v2), tokC.get("vt", ("pe", 0))]
                ae_vv2 = act(lambda eng, vt=vt, hf=hf: eng.activation(
                    vt.rearrange("p (h z) -> p h z", z=65)[:, hf * 8:(hf + 1) * 8, 0:64],
                    PS1[:, hf * 512:(hf + 1) * 512].rearrange("p (h z) -> p h z", z=64),
                    AF.Copy), waits=w0)
            tokC["ps1"] = ("ae", ae_vv2)
        tokC["vprj"] = ("ae", ae_vv2)
        for p in (range(NK) if lvl >= 1 else ()):
            d_wq = col_block_dma(wvsl[:], wq_h, p * 128, 128, NK, sem="wV2",
                                 waits=[tokC.get("wvsl", ("pe", pe_b1))])
            pe_q = 0
            for k in range(NK):
                pw = []
                if k == 0:
                    pw = [tokC.get("ps0", ("pe", 0)), d_wq]
                pe_q = mm(wvsl[:, k * 128:(k + 1) * 128],
                          xnv(k, t0, t0 + CHUNK), PS0[:, 0:512],
                          start=(k == 0), stop=(k == NK - 1), waits=pw,
                          inc=(k == NK - 1))
            tokC["wvsl"] = ("pe", pe_q)
            ae_q = act(lambda eng, p=p: eng.activation(
                qp_t[:], PS0[:, 0:512], AF.Relu, bias=bcol1[:, p:p + 1]),
                waits=[("pe", pe_q), tokC.get("qp", ("pe", 0)), ("ae", ae_c2)])
            ve_qc = vec(lambda eng: eng.tensor_mul(prQc[:], qp_t[:], cosfc[:]),
                        waits=[("ae", ae_q), tokC.get("prq", ("pe", 0))])
            ve_qs = vec(lambda eng: eng.tensor_mul(prQs[:], qp_t[:], sinfc[:]))
            tokC["ps0"] = ("ae", ae_q)
            pe_k2 = 0
            for k in range(NK):
                pw = [tokC.get("ps0b", ("pe", 0))] if k == 0 else []
                pe_k2 = mm(bufB[:, k * 1024 + p * 128: k * 1024 + (p + 1) * 128],
                           xnv(k, t0, t0 + CHUNK), PS0[:, 512:1024],
                           start=(k == 0), stop=(k == NK - 1), waits=pw,
                           inc=(k == NK - 1))
            ae_k3 = act(lambda eng, p=p: eng.activation(
                qp_t[:], PS0[:, 512:1024], AF.Relu, bias=bcol1[:, NK + p:NK + p + 1]),
                waits=[("pe", pe_k2), ("ve", ve_qs)])
            tokC["ps0b"] = ("ae", ae_k3)
            ve_kc = vec(lambda eng: eng.tensor_mul(prKc[:], qp_t[:], cosfc[:]),
                        waits=[("ae", ae_k3), tokC.get("prk", ("pe", 0))])
            ve_ks = vec(lambda eng: eng.tensor_mul(prKs[:], qp_t[:], sinfc[:]))
            for hh in (range(2) if lvl >= 2 else ()):
                h = 2 * p + hh
                pb = hh * 64
                ps2h = PS2[0:65, 0:512] if hh == 0 else PS2[0:65, 512:1024]
                w_int = [tokC.get("ps2" if hh == 0 else "ps2b", ("pe", 0)),
                         ("dH", d_sh2[1]), ("ve", ve_ks), tokC["vprj"]]
                if c == 0 and p == 0 and hh == 0:
                    w_int.append(("ve", ve_o1))
                lhs_cos = sptot[0:64, h * 65:(h + 1) * 65] if hh == 0 \
                    else sshift[64:128, h * 65:(h + 1) * 65]
                lhs_sin = sshift[0:64, h * 65:(h + 1) * 65] if hh == 0 \
                    else sptot[64:128, h * 65:(h + 1) * 65]
                pe_at = mm(lhs_cos, prQc[pb:pb + 64, :], ps2h,
                           start=True, stop=False, waits=w_int)
                pe_at = mm(lhs_sin, prQs[pb:pb + 64, :], ps2h,
                           start=False, stop=(lvl < 5), inc=(lvl < 5))
                for u in (range(4) if lvl >= 3 else ()):
                    w_u = 512 - u * 128
                    pe_sc2 = mm(prKc[pb:pb + 64, u * 128:(u + 1) * 128],
                                prQc[pb:pb + 64, u * 128:512],
                                PS3[:, 0:w_u], start=True, stop=False,
                                waits=[tokC.get("ps3", ("pe", 0))])
                    pe_sc2 = mm(prKs[pb:pb + 64, u * 128:(u + 1) * 128],
                                prQs[pb:pb + 64, u * 128:512],
                                PS3[:, 0:w_u], start=False, stop=True, inc=True)
                    if lvl == 7:
                        ae_m = act(lambda eng, w_u=w_u: eng.activation(
                            st_t[:, 0:w_u], PS3[:, 0:w_u], AF.Copy),
                            waits=[("pe", pe_sc2)])
                        ve_m = ae_m
                        tokC["ps3"] = ("ae", ae_m)
                        tokC[("st", 0)] = ("ae", ae_m)
                        continue
                    ve_m = vec(lambda eng, w_u=w_u: eng.tensor_mul(
                        st_t[:, 0:w_u], PS3[:, 0:w_u], tri[:, 0:w_u]),
                        waits=[("pe", pe_sc2), tokC[("st", 0)],
                               ("dH", dH_attn)])
                    tokC["ps3"] = ("ve", ve_m)
                    wait_st = [("ve", ve_m)]
                    if lvl >= 5 and lvl not in (7, 8):
                        pe_at = mm(vtok[:, u * NST + h * 65: u * NST + (h + 1) * 65],
                                   st_t[:, 0:w_u],
                                   ps2h.tensor[0:65, (hh * 512) + u * 128: hh * 512 + 512]
                                   if False else
                                   (PS2[0:65, u * 128:512] if hh == 0 else
                                    PS2[0:65, 512 + u * 128:1024]),
                                   start=False, stop=(u == 3),
                                   waits=wait_st, inc=(u == 3))
                        tokC[("st", 0)] = ("pe", pe_at)
                    else:
                        tokC[("st", 0)] = wait_st[0]
                if lvl < 6 or lvl in (7, 8):
                    tokC["ps2"] = ("pe", pe_at)
                    continue
                ve_d1 = vec(lambda eng: eng.tensor_scalar(
                    iotat[64:65, 0:512], PS2[64:65, 0:512], 1e-6, None, OP.max),
                    waits=[("pe", pe_at), tokC.get("iden", ("pe", 0))])
                ve_d2 = vec(lambda eng: eng.reciprocal(
                    iotat[64:65, 0:512], iotat[64:65, 0:512]))
                tokC["iden"] = ("ve", ve_d2)
                pe_bc2 = mm(onesT[64:65, 0:64], iotat[64:65, 0:512],
                            PS4[0:64, 0:512], start=True, stop=True,
                            waits=[("ve", ve_d2), tokC["ps4"]], inc=True)
                tokC["iotat"] = ("pe", pe_bc2)
                ae_n = act(lambda eng: eng.activation(
                    at64[0:64, :], PS2[0:64, 0:512], AF.Copy),
                    waits=[("pe", pe_at), tokC.get("at64", ("pe", 0))])
                if stage == "l1_dbg" and c == 0 and p == 7 and hh == 1:
                    dma("gpsimd", out_h.ap()[0:128, 0:512], qp_t[:],
                        waits=[("ve", P.v("ve")), ("ae", P.v("ae"))], sem="dD")
                    dma("gpsimd", out_h.ap()[0:128, 512:1024], prQc[:],
                        waits=[("ve", P.v("ve"))], sem="dD")
                    dma("gpsimd", out_h.ap()[0:128, 1024:1536], prKc[:],
                        waits=[("ve", P.v("ve"))], sem="dD")
                    dma("gpsimd", out_h.ap()[128:256, 0:512], st_t[:],
                        waits=[("ve", P.v("ve"))], sem="dD")
                    dma("gpsimd", out_h.ap()[128:256, 512:1024], at64[:],
                        waits=[("ae", P.v("ae"))], sem="dD")
                    dma("gpsimd", out_h.ap()[384:385, 0:512], iotat[64:65, 0:512],
                        waits=[("ve", P.v("ve"))], sem="dD")
                    dma("gpsimd", out_h.ap()[385:386, 0:512], cosfc[0:1, :],
                        waits=[("ve", P.v("ve"))], sem="dD")
                    dma("gpsimd", out_h.ap()[386:387, 0:512], sinfc[0:1, :],
                        waits=[("ve", P.v("ve"))], sem="dD")
                    dma("gpsimd", out_h.ap()[256:384, 0:512],
                        vtok[:, 0:512],
                        waits=[("ve", P.v("ve")), ("ae", P.v("ae"))], sem="dD")
                    dma("gpsimd", out_h.ap()[512:640, 0:512], xnv(0, 0, 512),
                        waits=[("ve", P.v("ve"))], sem="dD")
                    dma("gpsimd", out_h.ap()[640:768, 0:512],
                        bufB[:, NK * 1024:NK * 1024 + 512],
                        waits=[("ve", P.v("ve"))], sem="dD")
                    raise _EarlyStop(nc, P, es, P.v("dD"))
                tokC["ps2"] = ("ae", ae_n)
                at_tile = bufA[:, p * 2048 + t0: p * 2048 + t0 + 512]
                if hh == 0:
                    ve_attn_done = vec(lambda eng, at_tile=at_tile: eng.tensor_mul(
                        at_tile[0:64, :], at64[0:64, :], PS4[0:64, 0:512]),
                        waits=[("pe", pe_bc2), ("ae", ae_n)])
                    tokC["at64"] = ("ve", ve_attn_done)
                    tokC["ps4"] = ("ve", ve_attn_done)
                else:
                    ve_sh = vec(lambda eng: eng.tensor_mul(
                        st_t[0:64, :], at64[0:64, :], PS4[0:64, 0:512]),
                        waits=[("pe", pe_bc2), ("ae", ae_n), ("dH", dH_attn)])
                    d_sh3 = dma("sync", at_tile[64:128, :], st_t[0:64, :],
                                sem="dH", waits=[("ve", ve_sh)])
                    dH_attn = d_sh3[1]
                    tokC["at64"] = ("ve", ve_sh)
                    tokC["ps4"] = ("ve", ve_sh)
                    tokC[("st", 0)] = ("ve", ve_sh)
            tokC["qp"] = tokC["ps4"]
            tokC["prq"] = tokC[("st", 0)]
            tokC["prk"] = tokC[("st", 0)]
        tokC["sptot"] = ("pe", P.v("pe"))
        tokC["csf"] = tokC["ps0"]
    ve_attn_last = P.v("ve")

    pe_all_b2 = P.v("pe")
    tok_wo = ("pe", pe_all_b2)
    tok_evac = ("ve", 0)
    pe_wo_done = 0
    for dch in range(NK):
        d_wo = col_block_dma(wvsl[:, 0:1024], wo_h, dch * 128, 128, NK,
                             sem="wV2", waits=[tok_wo])
        pe_o = 0
        for k in range(NK):
            for tt in range(4):
                pw = []
                if k == 0 and tt == 0:
                    pw = [d_wo, ("ve", ve_attn_last), ("dH", dH_attn), tok_evac]
                pe_o = mm(wvsl[:, k * 128:(k + 1) * 128],
                          bufA[:, k * 2048 + tt * 512: k * 2048 + (tt + 1) * 512],
                          (PS0 if tt < 2 else PS1)[:, (tt % 2) * 512:(tt % 2) * 512 + 512],
                          start=(k == 0), stop=(k == NK - 1), waits=pw,
                          inc=(k == NK - 1))
        tok_wo = ("pe", pe_o)
        ve_wo = 0
        for tt in range(4):
            ve_wo = vec(lambda eng, dch=dch, tt=tt: eng.scalar_tensor_tensor(
                xr(dch, tt * 512, (tt + 1) * 512),
                (PS0 if tt < 2 else PS1)[:, (tt % 2) * 512:(tt % 2) * 512 + 512],
                bcol1[:, 16 + dch:17 + dch], xr(dch, tt * 512, (tt + 1) * 512),
                OP.add, OP.add), waits=[("pe", pe_o)] if tt == 0 else ())
        tok_evac = ("ve", ve_wo)
        pe_wo_done = pe_o
    ve_wo_done = tok_evac[1]

    if stage == "l1_atdump":
        last = None
        for k in range(NK):
            last = dma("gpsimd", out_h.ap()[k * 128:(k + 1) * 128, :],
                       bufA[:, k * 2048:(k + 1) * 2048],
                       waits=[("ve", ve_wo_done)], sem="dS")
        P.emit("sync", lambda eng: None, waits=[("dS", last[1])])
        emit_all(nc, P)
        es.close()
        return nc
    if stage.startswith("l1_b2"):
        return early_out([("ve", ve_wo_done), ("pe", P.v("pe")),
                          ("ve", P.v("ve")), ("ae", P.v("ae")),
                          ("ge", P.v("ge"))])
    pe_prep0 = pe_wo_done

    sc_done = ("pW0", 0)
    ae_tr_tok = ("ae", P.v("ae"))
    for blk in range(8):
        c0 = blk * 512
        d_wi = dma("gpsimd",
                   wprep[:].rearrange("p (k f) -> p k f", f=512),
                   win_h.ap()[:, c0:c0 + 512].rearrange("(k p) f -> p k f", p=128),
                   sem="pI0", waits=[sc_done if blk else ("pe", pe_prep0)])
        ve_ab = vec(lambda eng: eng.tensor_tensor(
            wpabs[:], wprep[:, 0:512], wprep[:, 512:1024], OP.max),
            waits=[d_wi])
        for k in range(2, NK):
            ve_ab = vec(lambda eng, k=k: eng.tensor_tensor(
                wpabs[:], wpabs[:], wprep[:, k * 512:(k + 1) * 512], OP.max))
        ve_ab = vec(lambda eng: eng.tensor_tensor(
            wpq[:], wprep[:, 0:512], wprep[:, 512:1024], OP.min))
        for k in range(2, NK):
            ve_ab = vec(lambda eng, k=k: eng.tensor_tensor(
                wpq[:], wpq[:], wprep[:, k * 512:(k + 1) * 512], OP.min))
        ve_ab = vec(lambda eng: eng.tensor_scalar(
            wpq[:], wpq[:], -1.0, None, OP.mult))
        ve_ab = vec(lambda eng: eng.tensor_tensor(
            wpabs[:], wpabs[:], wpq[:], OP.max))
        ve_ab = vec(lambda eng: eng.tensor_copy(st_t[:, 0:512], wpabs[:]))
        pe_tr = 0
        for q in range(4):
            P.emit("tensor", (lambda eng, q=q: eng.transpose(
                PS4.bitcast(BF16)[:, q * 128:(q + 1) * 128],
                st_t[:, q * 128:(q + 1) * 128], identb[:])),
                waits=([("ve", ve_ab), ("ve", ve_id),
                        ("pe", pe_prep0) if blk == 0 else ("pe", 0),
                        ae_tr_tok] if q == 0 else ()),
                incs=[("pe", 1)])
        pe_tr = P.v("pe")
        ve_mx = 0
        for q in range(4):
            ve_mx = vec(lambda eng, blk=blk, q=q: eng.reduce_max(
                sc32[:, blk * 4 + q:blk * 4 + q + 1],
                PS4.bitcast(BF16)[:, q * 128:(q + 1) * 128], axis=AX.X),
                waits=([("pe", pe_tr)] if q == 0 else ()))
        ae_tr_tok = ("ve", ve_mx)
        sc_done = ("ve", ve_mx)
    ve_f = fence()
    ve_f = vec(lambda eng: eng.tensor_scalar(
        sc32[:], sc32[:], 1e-8, None, OP.max))
    ve_f = fence()
    ve_rs = vec(lambda eng: eng.tensor_scalar(
        rs32[:], sc32[:], 1.0 / 127.0, None, OP.mult),
        waits=[("ve", ve_scales)])
    ve_rs = vec(lambda eng: eng.tensor_scalar(
        rs32[:], rs32[:], red[:, 34:35], None, OP.mult))
    ve_f = fence()
    ve_rp = vec(lambda eng: eng.reciprocal(amx32[:], sc32[:]))
    ve_rp = fence()
    ve_rp = vec(lambda eng: eng.tensor_scalar(
        sc32[:], amx32[:], 127.0, None, OP.mult))
    q_done = ("ve", ve_rp)
    tok_srow = ("pe", 0)
    for blk in range(8):
        c0 = blk * 512
        d_wi = dma("gpsimd",
                   wprep[:].rearrange("p (k f) -> p k f", f=512),
                   win_h.ap()[:, c0:c0 + 512].rearrange("(k p) f -> p k f", p=128),
                   sem="pI0", waits=[q_done])
        ve_cp = vec(lambda eng, blk=blk: eng.tensor_copy(
            st_t[:, 0:4], sc32[:, blk * 4:(blk + 1) * 4]),
            waits=[("ve", ve_rp)] if blk == 0 else [])
        P.emit("tensor", (lambda eng: eng.transpose(
            PS4.bitcast(BF16)[0:4, 0:128], st_t[:, 0:4], identb[:])),
            waits=[("ve", ve_cp), tok_srow], incs=[("pe", 1)])
        pe_t2 = P.v("pe")
        ae_sr = act(lambda eng: eng.activation(
            st_t[0:4, 0:128], PS4.bitcast(BF16)[0:4, 0:128], AF.Copy),
            waits=[("pe", pe_t2)])
        d_fl = dma("sync", srow_t[0:1, :].rearrange("a (k f) -> a k f", f=128),
                   st_t[0:4, 0:128].rearrange("k f -> () k f") if False else
                   st_t[0:4, 0:128], sem="pF", waits=[("ae", ae_sr)])
        pe_b3 = mm(ones_rb[0:1, 0:128], srow_t[0:1, :], PS3[:, 0:512],
                   start=True, stop=True, waits=[d_fl, q_done], inc=True)
        tok_srow = ("pe", pe_b3)
        for k in range(NK):
            ve_q1 = vec(lambda eng, k=k: eng.tensor_tensor(
                wpq[:], wprep[:, k * 512:(k + 1) * 512], PS3[:, 0:512], OP.mult),
                waits=([("pe", pe_b3)] if k == 0 else []))
            ve_q2 = vec(lambda eng: eng.tensor_scalar(
                wpq[:], wpq[:], MAGIC, -MAGIC, OP.add, OP.add))
            ve_q3 = vec(lambda eng, k=k: eng.tensor_copy(
                wpabs.bitcast(BF16)[:, 0:512], wpq[:]))
            d_qo = dma("sync", winq_d.ap()[k * 128:(k + 1) * 128, c0:c0 + 512],
                       wpabs.bitcast(BF16)[:, 0:512], sem="pF",
                       waits=[("ve", ve_q3)])
            q_done = d_qo
    pe_prep_done = P.v("pe")
    ae_prep_done = P.v("ae")

    if stage == "l1_prep":
        return early_out([("pe", pe_prep_done), ("ae", ae_prep_done),
                          ("ve", P.v("ve")), (QSEM["sync"], P.v("dS")),
                          ("pF", P.v("pF"))])
    hT = {}
    ae_ffn0 = P.v("ae")
    ve_fold = 0
    for tt2 in range(4):
        t0 = tt2 * 512
        ve_xq = 0
        for k in range(NK):
            ve_x1 = vec(lambda eng, k=k, t0=t0: eng.tensor_scalar(
                qtf[:, 0:512], xnv(k, t0, t0 + 512), red[:, 33:34], MAGIC,
                OP.mult, OP.add),
                waits=([("ve", ve_scales), ("pe", pe_prep_done),
                        ("ae", ae_prep_done)] if (tt2 == 0 and k == 0) else []))
            ve_xq = vec(lambda eng, k=k: eng.tensor_scalar(
                xq_t[:, k * 512:(k + 1) * 512], qtf[:, 0:512], -MAGIC, None,
                OP.add))
        for hch in range(NH1):
            sl = hch % 2
            d_w1 = col_block_dma(w1sl[:, sl * 1024:(sl + 1) * 1024], winq_d,
                                 hch * 128, 128, NK, sem="fI%d" % sl,
                                 waits=[hT.get(("w", sl), ("pe", pe_prep_done))])
            pe_h = 0
            for k in range(NK):
                pw = []
                if k == 0:
                    pw = [d_w1, hT.get(("ps", sl), ("ae", ae_ffn0)), ("ve", ve_xq)]
                pe_h = mm(w1sl[:, sl * 1024 + k * 128: sl * 1024 + (k + 1) * 128],
                          xq_t[:, k * 512:(k + 1) * 512],
                          (PS0 if sl == 0 else PS1)[:, 0:512],
                          start=(k == 0), stop=(k == NK - 1), waits=pw,
                          inc=(k == NK - 1))
            hT[("w", sl)] = ("pe", pe_h)
            ae_h = act(lambda eng, sl=sl, hch=hch: eng.activation(
                hidt[:, sl * 512:(sl + 1) * 512],
                (PS0 if sl == 0 else PS1)[:, 0:512],
                AF.Gelu, scale=rs32[:, hch:hch + 1],
                bias=bcol1[:, 32 + hch:33 + hch]),
                waits=[("pe", pe_h), hT.get(("hd", sl), ("pe", 0)),
                       hT.get(("am", sl), ("pe", 0))])
            hT[("ps", sl)] = ("ae", ae_h)
            ve_hm = vec(lambda eng, sl=sl, hch=hch: eng.reduce_max(
                amx32[:, hch:hch + 1], hidt[:, sl * 512:(sl + 1) * 512],
                axis=AX.X, apply_absolute_value=True), waits=[("ae", ae_h)])
            hT[("am", sl)] = ("ve", ve_hm)
            d_ho = dma("sync", hid_d.ap()[hch * 128:(hch + 1) * 128, t0:t0 + 512],
                       hidt[:, sl * 512:(sl + 1) * 512], sem="hd%d" % sl,
                       waits=[("ve", ve_hm)])
            hT[("hd", sl)] = d_ho
        ve_fold = fence()
        ve_fold = vec(lambda eng, tt2=tt2: eng.reduce_max(
            red[:, 40 + tt2:41 + tt2], amx32[:], axis=AX.X))
    ve_fold = fence()
    ve_fold = vec(lambda eng: eng.reduce_max(
        red[:, 44:45], red[:, 40:44], axis=AX.X))
    ve_fold = fence()
    ve_fb = vec(lambda eng: eng.tensor_copy(summ.bitcast(BF16)[:, 2:3],
                                            red[:, 44:45]))
    P.emit("tensor", (lambda eng: eng.transpose(
        PS4.bitcast(BF16)[0:1, 0:128], summ.bitcast(BF16)[:, 2:3], identb[:])),
        waits=[("ve", ve_fb), tok_srow, ("ae", P.v("ae"))], incs=[("pe", 1)])
    pe_am2 = P.v("pe")
    ve_am3 = vec(lambda eng: eng.reduce_max(
        smax[0:1, 2:3], PS4.bitcast(BF16)[0:1, 0:128], axis=AX.X),
        waits=[("pe", pe_am2)])
    g_a2 = dma("gpsimd", ag2_in.ap()[0:1, 0:1], smax[0:1, 2:3],
               waits=[("ve", ve_am3)], sem="dB")
    P.emit("gpsimd", lambda eng: eng.collective_compute(
        "AllGather", OP.bypass, replica_groups=[list(range(NCORES))],
        ins=[ag2_in.ap()], outs=[ag2_out.ap()]),
        waits=[g_a2], incs=[("cc", 1)])
    cc2 = P.v("cc")
    d_a2 = dma("gpsimd", smax[0:8, 10:11], ag2_out.ap()[:, 0:1], sem="dB",
               waits=[("cc", cc2)], slow=True)
    d_a2f = dma("gpsimd", smax[0:1, 32:40], smax[0:8, 10:11], sem="dB",
                waits=[d_a2], slow=True)
    ve_g2 = vec(lambda eng: eng.reduce_max(
        smax[0:1, 3:4], smax[0:1, 32:40], axis=AX.X), waits=[d_a2f])
    ve_g2 = vec(lambda eng: eng.tensor_scalar(
        smax[0:1, 4:5], smax[0:1, 3:4], 1.0 / 32767.0, None, OP.mult))
    pe_bc3 = mm(ones_row[0:1, 0:128], smax[0:1, 3:5], PS4[:, 0:2],
                start=True, stop=True,
                waits=[("ve", ve_g2), ("ve", ve_am3)], inc=True)
    ve_s2 = vec(lambda eng: eng.reciprocal(red[:, 45:46], PS4[:, 0:1]),
                waits=[("pe", pe_bc3)])
    ve_s2 = vec(lambda eng: eng.tensor_copy(red[:, 47:48], PS4[:, 1:2]))
    ve_s2 = vec(lambda eng: eng.tensor_scalar(
        red[:, 46:47], red[:, 45:46], 32767.0, None, OP.mult))
    ve_cmb = vec(lambda eng: eng.tensor_scalar(
        cmb32[:], rsoC[:], red[:, 47:48], None, OP.mult),
        waits=[("ve", ve_rso)])
    ve_sc2done = ve_cmb

    if stage == "l1_f1":
        return early_out([("ve", ve_sc2done)])
    f2 = {}
    ve_final = 0
    psl = [PS0[:, 0:512], PS0[:, 512:1024], PS1[:, 0:512], PS1[:, 512:1024],
           PS2[:, 0:512], PS2[:, 512:1024], PS3[:, 0:512], PS4[:, 0:512]]
    for tt2 in range(4):
        t0 = tt2 * 512
        pe_d2 = [0] * NK
        for hch in range(NH1):
            sl = hch % 2
            d_w2 = dma("sync", w2sl[:, sl * 1024:(sl + 1) * 1024],
                       woutq_d.ap()[hch * 128:(hch + 1) * 128, :],
                       sem="f2W%d" % sl,
                       waits=[f2.get(("w", sl), ("pe", pe_am2))])
            d_h2 = dma("scalar", hidt[:, sl * 512:(sl + 1) * 512],
                       hid_d.ap()[hch * 128:(hch + 1) * 128, t0:t0 + 512],
                       sem="f2H%d" % sl,
                       waits=[f2.get(("h", sl), ("ve", ve_sc2done))])
            ve_hq = vec(lambda eng, sl=sl: eng.tensor_scalar(
                qtf[:, sl * 512:(sl + 1) * 512],
                hidt[:, sl * 512:(sl + 1) * 512],
                red[:, 46:47], MAGIC, OP.mult, OP.add),
                waits=[d_h2] + ([("ve", ve_sc2done)] if (tt2 == 0 and hch < 2) else []))
            ve_hq2 = vec(lambda eng, sl=sl, hch=hch: eng.tensor_scalar(
                hidt[:, sl * 512:(sl + 1) * 512],
                qtf[:, sl * 512:(sl + 1) * 512],
                -MAGIC, cmb32[:, hch:hch + 1], OP.add, OP.mult))
            for dch in range(NK):
                pw = []
                if dch == 0:
                    pw = [d_w2, ("ve", ve_hq2)]
                    if hch == 0:
                        pw.append(f2.get("ps_all", ("pe", pe_am2)))
                pe_d2[dch] = mm(w2sl[:, sl * 1024 + dch * 128: sl * 1024 + (dch + 1) * 128],
                                hidt[:, sl * 512:(sl + 1) * 512], psl[dch],
                                start=(hch == 0), stop=(hch == NH1 - 1),
                                waits=pw, inc=True)
            f2[("w", sl)] = ("pe", pe_d2[NK - 1])
            f2[("h", sl)] = ("pe", pe_d2[NK - 1])
        for dch in range(NK):
            ve_final = vec(lambda eng, dch=dch, t0=t0: eng.scalar_tensor_tensor(
                xr(dch, t0, t0 + 512), psl[dch], bcol1[:, 24 + dch:25 + dch],
                xr(dch, t0, t0 + 512), OP.add, OP.add),
                waits=[("pe", pe_d2[dch])])
        f2["ps_all"] = ("ve", ve_final)

    last = None
    for k in range(NK):
        last = dma("sync", out_h.ap()[k * 128:(k + 1) * 128, :], xr(k),
                   waits=[("ve", ve_final)])
    P.emit("sync", lambda eng: None, waits=[("dS", last[1])])
    emit_all(nc, P)
    es.close()
    return nc


def emit_all(nc, P):
    with nc.Block() as block:
        @block.sync
        def _(eng):
            for f in P.ops["sync"]:
                f(eng)

        @block.scalar
        def _(eng):
            for f in P.ops["scalar"]:
                f(eng)

        @block.vector
        def _(eng):
            for f in P.ops["vector"]:
                f(eng)

        @block.tensor
        def _(eng):
            for f in P.ops["tensor"]:
                f(eng)

        @block.gpsimd
        def _(eng):
            for f in P.ops["gpsimd"]:
                f(eng)
    return nc


def make_inputs(inp):
    C = np.ascontiguousarray

    def A(v):
        return np.asarray(v)

    x = A(inp["L0_attn_x"])
    shared = {
        "wgT": C(A(inp["L0_attn_Wg"]).T), "wvT": C(A(inp["L0_attn_Wv"]).T),
        "wdT": C(A(inp["L0_attn_Wd"]).T),
        "bcols0": np.concatenate([
            A(inp["L0_attn_bg"]).reshape(8, 128).T,
            A(inp["L0_attn_bv"]).reshape(8, 128).T,
            A(inp["L0_attn_bd"]).reshape(8, 128).T], axis=1).copy(),
        "bcols1": np.concatenate([
            A(inp["L1_attn_bq"]).reshape(8, 128).T,
            A(inp["L1_attn_bk"]).reshape(8, 128).T,
            A(inp["L1_attn_bo"]).reshape(8, 128).T,
            A(inp["L1_ffn_b_out"]).reshape(8, 128).T,
            A(inp["L1_ffn_b_in"]).reshape(32, 128).T], axis=1).copy(),
        "fgT": C(A(inp["L0_ffn_w_gate"]).T), "fuT": C(A(inp["L0_ffn_w_up"]).T),
        "fdT": C(A(inp["L0_ffn_w_down"]).T),
        "wq": C(A(inp["L1_attn_Wq"])), "wk": C(A(inp["L1_attn_Wk"])),
        "wv1": C(A(inp["L1_attn_Wv"])), "wo": C(A(inp["L1_attn_Wo"])),
        "bk": A(inp["L1_attn_bk"]).reshape(1, -1),
        "bv1": A(inp["L1_attn_bv"]).reshape(1, -1),
        "win": C(A(inp["L1_ffn_W_in"])), "wout": C(A(inp["L1_ffn_W_out"])),
    }
    maps = []
    for core in range(NCORES):
        b, j = divmod(core, 4)
        xs = C(x[b, j * T:(j + 1) * T, :].T)
        params = np.zeros((128, 24), np.float32)
        for r in range(NCORES):
            rb, rj = divmod(r, 4)
            params[:, r] = 1.0 if (rb == b and rj < j) else 0.0
            params[:, 9 + r] = 1.0 - params[:, r]
        params[:, 8] = (j * T) * ANG_SCALE
        params[:, 17] = (j * T) * ANG_SCALE + np.pi / 2.0
        m = dict(shared)
        m["x"] = xs
        m["params"] = params
        maps.append(m)
    return maps


def _run(stage, inp):
    maps = make_inputs(inp)
    try:
        nc = build_program(stage)
    except _EarlyStop as e:
        nc, P, es, dD = e.args
        P.emit("sync", lambda eng: None, waits=[("dD", dD)])
        emit_all(nc, P)
        es.close()
    res = run_bass_kernel_spmd(nc, maps, core_ids=list(range(NCORES)))
    out = np.empty((2, 8192, 1024), np.float32)
    for core in range(NCORES):
        b, j = divmod(core, 4)
        out[b, j * T:(j + 1) * T, :] = res.results[core]["out"].T
    return out


def kernel(**inp):
    return _run("full", inp)


# version 12
# speedup vs baseline: 1.2024x; 1.0031x over previous
import numpy as np
from contextlib import ExitStack

import concourse.bass as bass
import concourse.mybir as mybir
from concourse.bass_utils import run_bass_kernel_spmd

F32 = mybir.dt.float32
BF16 = mybir.dt.bfloat16
I32 = mybir.dt.int32
AF = mybir.ActivationFunctionType
OP = mybir.AluOpType
AX = mybir.AxisListType

NCORES = 8
D = 1024
T = 2048
NK = 8
NH = 16
DH = 64
CHUNK = 512
NCH = T // CHUNK
HFF0 = 3072
NH0 = HFF0 // 128
HFF1 = 4096
NH1 = HFF1 // 128
NST = NH * 65
MAGIC = 12582912.0
PI = float(np.pi)
ANG_SCALE = PI / (2.0 * 8192.0)

ENGINES = ("sync", "scalar", "vector", "tensor", "gpsimd")
QSEM = {"sync": "dS", "scalar": "dA", "gpsimd": "dG"}


class Plan:
    def __init__(self):
        self.ops = {e: [] for e in ENGINES}
        self.cnt = {}
        self.sems = {}

    def newsem(self, *names):
        for n in names:
            self.cnt[n] = 0

    def v(self, name):
        return self.cnt[name]

    def emit(self, engine, fn, waits=(), incs=()):
        waits = tuple((s, v) for (s, v) in waits if v > 0)
        for (s, n) in incs:
            self.cnt[s] += n
        def run(eng, _fn=fn, _w=waits, _i=tuple(incs), _sems=self.sems):
            for (s, v) in _w:
                eng.wait_ge(_sems[s], v)
            inst = _fn(eng)
            for (s, n) in _i:
                inst = inst.then_inc(_sems[s], n)

        self.ops[engine].append(run)


class Banks:

    def __init__(self, slots, tok=("pe", 0)):
        self.slots = list(slots)
        self.tok = [tok] * len(slots)
        self.i = 0

    def get(self):
        j = self.i % len(self.slots)
        self.i += 1
        return j, self.slots[j], self.tok[j]

    def set_tok(self, j, tok):
        self.tok[j] = tok


class _EarlyStop(Exception):
    pass


def build_program(stage="full"):
    do_attn = stage in ("full", "l0", "l0_attn", "l0_hin") or stage.startswith("l1")
    do_ffn = stage in ("full", "l0", "l0_ffn", "l0_hid") or stage.startswith("l1")
    nc = bass.Bass("TRN2", target_bir_lowering=False, debug=False)
    P = Plan()
    P.newsem("dS", "dA", "dG", "dB", "pe", "ve", "ae", "ge", "cc", "st")

    x_h = nc.dram_tensor("x", [D, T], F32, kind="ExternalInput")
    out_h = nc.dram_tensor("out", [D, T], F32, kind="ExternalOutput")
    params_h = nc.dram_tensor("params", [128, 24], F32, kind="ExternalInput")
    wgT_h = nc.dram_tensor("wgT", [D, D], F32, kind="ExternalInput")
    wvT_h = nc.dram_tensor("wvT", [D, D], F32, kind="ExternalInput")
    wdT_h = nc.dram_tensor("wdT", [D, D], F32, kind="ExternalInput")
    bcols0_h = nc.dram_tensor("bcols0", [128, 24], F32, kind="ExternalInput")
    bcols1_h = nc.dram_tensor("bcols1", [128, 64], F32, kind="ExternalInput")
    fgT_h = nc.dram_tensor("fgT", [D, HFF0], I32, kind="ExternalInput")
    fuT_h = nc.dram_tensor("fuT", [D, HFF0], I32, kind="ExternalInput")
    fdT_h = nc.dram_tensor("fdT", [HFF0, D], I32, kind="ExternalInput")
    wq_h = nc.dram_tensor("wq", [D, D], F32, kind="ExternalInput")
    wk_h = nc.dram_tensor("wk", [D, D], F32, kind="ExternalInput")
    wv1_h = nc.dram_tensor("wv1", [D, D], F32, kind="ExternalInput")
    wo_h = nc.dram_tensor("wo", [D, D], F32, kind="ExternalInput")
    bk_h = nc.dram_tensor("bk", [1, D], F32, kind="ExternalInput")
    bv1_h = nc.dram_tensor("bv1", [1, D], F32, kind="ExternalInput")
    win_h = nc.dram_tensor("win", [D, HFF1], F32, kind="ExternalInput")
    wout_h = nc.dram_tensor("wout", [HFF1, D], F32, kind="ExternalInput")

    ag0_in = nc.dram_tensor("ag0_in", [1, 128 * 16], F32)
    ag0_out = nc.dram_tensor("ag0_out", [NCORES, 128 * 16], F32)
    ag1_in = nc.dram_tensor("ag1_in", [1, 128 * NST], F32)
    ag1_out = nc.dram_tensor("ag1_out", [NCORES, 128 * NST], F32)
    ag1b_in = nc.dram_tensor("ag1b_in", [1, 8], F32)
    ag1b_out = nc.dram_tensor("ag1b_out", [NCORES, 8], F32)
    ag2_in = nc.dram_tensor("ag2_in", [1, 8], F32)
    ag2_out = nc.dram_tensor("ag2_out", [NCORES, 8], F32)
    winq_d = nc.dram_tensor("winq_d", [D, HFF1], BF16)
    woutq_d = nc.dram_tensor("woutq_d", [HFF1, D], BF16)
    sinrow_d = nc.dram_tensor("sinrow_d", [1, HFF1], F32)
    hid_d = nc.dram_tensor("hid_d", [HFF1, T], BF16)

    es = ExitStack()

    def sbt(name, shape, dtype):
        return es.enter_context(nc.sbuf_tensor(name, shape, dtype))

    xres = sbt("xres", [128, NK * T], F32)
    xn = sbt("xn", [128, NK * T], BF16)
    bufA = sbt("bufA", [128, NK * T], BF16)
    bufB = sbt("bufB", [128, NK * T], BF16)
    ones_row = sbt("ones_row", [1, 128], F32)
    ones_col = sbt("ones_col", [128, 1], F32)
    prm = sbt("prm", [128, 24], F32)
    bcolL0 = sbt("bcolL0", [128, 3 * NK], F32)
    bcol1 = sbt("bcol1", [128, 64], F32)
    tri = sbt("tri", [128, 512], BF16)
    summ = sbt("summ", [128, 16], F32)
    hin = sbt("hin", [128, 8], F32)
    agbuf = sbt("agbuf", [128, 16 * NCORES], F32)
    red = sbt("red", [128, 144], F32)
    smax = sbt("smax", [8, 48], F32)
    identb = sbt("identb", [128, 128], BF16)
    ones_rb = sbt("ones_rb", [1, 128], BF16)

    ARENA_B = 45056
    arena = sbt("arena", [128, ARENA_B // 4], F32)
    abase = nc.lookup_mloc(arena).addr

    def aview(name, shape, dtype, off):
        assert off + shape[-1] * mybir.dt.size(dtype) * int(np.prod(shape[1:-1], initial=1)) <= ARENA_B
        return nc.alloc_sbuf_tensor_at(name, shape, dtype, offset=abase + off)

    KB = 1024
    lnsq = aview("lnsq", [128, 4096], F32, 0)
    lnrw = aview("lnrw", [128, 4096], F32, 16 * KB)
    a_row = aview("a_row", [128, T], F32, 0)
    xc_row = aview("xc_row", [128, T], F32, 8 * KB)
    g_row = aview("g_row", [128, T], F32, 16 * KB)
    vh_row = aview("vh_row", [128, T], F32, 24 * KB)
    wslA = aview("wslA", [128, 6144], BF16, 32 * KB)
    ffn_h = aview("ffn_h", [128, NH0 * 512], BF16, 0)
    ffn_gu = aview("ffn_gu", [128, 4096], BF16, 24 * KB)
    ffn_dn = aview("ffn_dn", [128, 2 * 1536], BF16, 32 * KB)
    gact = aview("gact", [128, 1024], F32, 38 * KB)
    corr_t = aview("corr_t", [128, T], F32, 0)

    PS0 = es.enter_context(nc.psum_tensor("PS0", [128, 1024], F32))
    PS1 = es.enter_context(nc.psum_tensor("PS1", [128, 1024], F32))
    PS2 = es.enter_context(nc.psum_tensor("PS2", [128, 1024], F32))
    PS3 = es.enter_context(nc.psum_tensor("PS3", [128, 512], F32))
    PS4 = es.enter_context(nc.psum_tensor("PS4", [128, 512], F32))

    for s in list(P.cnt.keys()):
        P.sems[s] = es.enter_context(nc.semaphore(s))

    def dma(q, out_ap, in_ap, waits=(), sem=None, slow=False):
        s = sem or QSEM[q]
        if s not in P.cnt:
            P.newsem(s)
            P.sems[s] = es.enter_context(nc.semaphore(s))
        P.emit(q, (lambda eng, o=out_ap, i=in_ap, sl=slow: eng.dma_start(
            out=o, in_=i, allow_slow_non_contiguous=sl)),
               waits=waits, incs=[(s, 16)])
        return (s, P.v(s))

    def mm(lhsT, rhs, out, start, stop, waits=(), inc=False):
        incs = [("pe", 1)] if inc else ()
        P.emit("tensor",
               (lambda eng, l=lhsT, r=rhs, o=out, s=start, p=stop:
                eng.matmul(o, l, r, start=s, stop=p)),
               waits=waits, incs=incs)
        return P.v("pe")

    def act(fn, waits=()):
        P.emit("scalar", fn, waits=waits, incs=[("ae", 1)])
        return P.v("ae")

    def vec(fn, waits=()):
        P.emit("vector", fn, waits=waits, incs=[("ve", 1)])
        return P.v("ve")

    def gp(fn, waits=()):
        P.emit("gpsimd", fn, waits=waits, incs=[("ge", 1)])
        return P.v("ge")

    def col_block_dma(dst_tile, src_h, col0, ncols, nk, waits=(), row0=0,
                      sem=None, q="gpsimd"):
        o = dst_tile.rearrange("p (k f) -> p k f", f=ncols)
        i = src_h.ap()[row0:row0 + nk * 128, col0:col0 + ncols].rearrange(
            "(k p) f -> p k f", p=128)
        return dma(q, o, i, waits, sem=sem)

    def xr(k, t0=0, t1=T):
        return xres[:, k * T + t0: k * T + t1]

    def xnv(k, t0=0, t1=T):
        return xn[:, k * T + t0: k * T + t1]

    setA = Banks([PS0[:, 0:512], PS0[:, 512:1024], PS1[:, 0:512], PS1[:, 512:1024]])
    setB = Banks([PS2[:, 0:512], PS2[:, 512:1024], PS3[:, 0:512], PS4[:, 0:512]])

    xw = []
    for k in range(NK):
        xw.append(dma("sync", xr(k), x_h.ap()[k * 128:(k + 1) * 128, :]))
    dma("sync", prm[:], params_h.ap())
    dma("sync", bcolL0[:], bcols0_h.ap())
    w_const = dma("sync", bcol1[:], bcols1_h.ap())

    vec(lambda eng: eng.memset(ones_row[:], 1.0))
    vec(lambda eng: eng.memset(ones_col[:], 1.0))
    v_tri = vec(lambda eng: eng.memset(tri[:], 1.0))
    g_tri = gp(lambda eng: eng.iota(
        agbuf[:, 0:128], pattern=[[-1, 128]], base=0, channel_multiplier=1,
        allow_small_or_imprecise_dtypes=True), waits=[("ve", v_tri)])
    vec(lambda eng: eng.tensor_single_scalar(
        tri[:, 0:128], agbuf[:, 0:128], 0.0, OP.is_le), waits=[("ge", g_tri)])

    def emit_layernorm(first_waits, stats_tok=("pe", 0)):
        bcR = lnrw[:, 0:512]
        bcM = lnrw[:, 512:1024]
        tmp = lnrw[:, 1024:1536]
        R0 = lnrw[0:1, 1536:2048]
        R1 = lnrw[0:1, 2048:2560]
        R2 = lnrw[0:1, 2560:3072]
        xn_done = 0
        for tt in range(4):
            t0 = tt * 512
            ae_sq = 0
            for k in range(NK):
                w = list(first_waits) if (tt == 0 and k == 0) else []
                ae_sq = act(lambda eng, k=k, t0=t0: eng.activation(
                    lnsq[:, k * 512:(k + 1) * 512], xr(k, t0, t0 + 512),
                    AF.Square), waits=w)
            first_w = [("ae", ae_sq)] + ([stats_tok] if tt == 0 else [])
            for k in range(NK):
                mm(ones_col[:], xr(k, t0, t0 + 512), PS3[0:1, 0:512],
                   start=(k == 0), stop=(k == NK - 1),
                   waits=(first_w if k == 0 else ()))
            pe_s = 0
            for k in range(NK):
                pe_s = mm(ones_col[:], lnsq[:, k * 512:(k + 1) * 512],
                          PS4[0:1, 0:512], start=(k == 0), stop=(k == NK - 1),
                          inc=(k == NK - 1))
            vec(lambda eng, R0=R0: eng.tensor_scalar(
                R0, PS3[0:1, 0:512], 1.0 / D, None, OP.mult),
                waits=[("pe", pe_s)])
            vec(lambda eng, R1=R1: eng.tensor_scalar(
                R1, PS4[0:1, 0:512], 1.0 / D, 1e-5, OP.mult, OP.add))
            vec(lambda eng, R0=R0, R2=R2: eng.scalar_tensor_tensor(
                R2, R0, -1.0, R0, OP.mult, OP.mult))
            ve_var = vec(lambda eng, R1=R1, R2=R2: eng.tensor_add(R1, R1, R2))
            ae_v = act(lambda eng, R1=R1, R2=R2: eng.activation(R2, R1, AF.Sqrt),
                       waits=[("ve", ve_var)])
            vec(lambda eng, R1=R1, R2=R2: eng.reciprocal(R1, R2),
                waits=[("ae", ae_v)])
            ve_r = vec(lambda eng, R0=R0, R1=R1: eng.scalar_tensor_tensor(
                R0, R0, -1.0, R1, OP.mult, OP.mult))
            mm(ones_row[0:1, 0:128], R1, PS4[:, 0:512], start=True, stop=True,
               waits=[("ve", ve_r)])
            pe_b = mm(ones_row[0:1, 0:128], R0, PS3[:, 0:512],
                      start=True, stop=True, inc=True)
            act(lambda eng, bcR=bcR: eng.activation(bcR, PS4[:, 0:512], AF.Copy),
                waits=[("pe", pe_b)])
            ae_b = act(lambda eng, bcM=bcM: eng.activation(bcM, PS3[:, 0:512], AF.Copy))
            for k in range(NK):
                vec(lambda eng, k=k, t0=t0, tmp=tmp, bcR=bcR: eng.tensor_mul(
                    tmp, xr(k, t0, t0 + 512), bcR),
                    waits=([("ae", ae_b)] if k == 0 else []))
                xn_done = vec(lambda eng, k=k, t0=t0, tmp=tmp, bcM=bcM: eng.tensor_add(
                    xnv(k, t0, t0 + 512), tmp, bcM))
        return xn_done

    xn_all = emit_layernorm(list(xw) + [w_const])
    setB.tok[2] = ("ae", P.v("ae"))
    setB.tok[3] = ("ae", P.v("ae"))

    def l0_wslot(m, b):
        c0 = (m * 2 + b) * 1024
        return wslA[:, c0:c0 + 1024]

    wslot_tok = {}
    ve_rows_done = 0
    for e in (range(NK) if do_attn else ()):
        b = e % 2
        dtok = {}
        for m, src in enumerate((wdT_h, wvT_h, wgT_h)):
            dtok[m] = col_block_dma(l0_wslot(m, b), src, e * 128, 128, NK,
                                    waits=[wslot_tok.get((m, b), ("pe", 0))],
                                    sem="wA%d" % (m * 2 + b))
        for m in range(3):
            bs = setA if (e * 3 + m) % 2 == 0 else setB
            wslot = l0_wslot(m, b)
            groups = {}
            for tt in range(4):
                j, ap, tok = bs.get()
                pw = [dtok[m], tok]
                if m == 0 and e == 0:
                    pw.append(("ve", xn_all))
                pe_last = 0
                for k in range(NK):
                    pe_last = mm(wslot[:, k * 128:(k + 1) * 128],
                                 xnv(k, tt * 512, tt * 512 + 512), ap,
                                 start=(k == 0), stop=(k == NK - 1),
                                 waits=(pw if k == 0 else ()),
                                 inc=(k == NK - 1))
                groups[tt] = (pe_last, ap, j)
            wslot_tok[(m, b)] = ("pe", groups[3][0])
            dst = (a_row, vh_row, g_row)[m]
            bofs = (2 * NK, NK, 0)[m]
            for tt in range(4):
                pe_g, ap, j = groups[tt]
                w = [("pe", pe_g)]
                if tt == 0 and m == 0:
                    w.append(("ve", ve_rows_done))
                if m == 1:
                    ve_t = vec(lambda eng, dst=dst, tt=tt, ap=ap, bo=bofs, e=e:
                               eng.tensor_scalar(
                                   dst[:, tt * 512:tt * 512 + 512], ap,
                                   bcolL0[:, bo + e:bo + e + 1], None, OP.add),
                               waits=w)
                    bs.set_tok(j, ("ve", ve_t))
                else:
                    ae_t = act(lambda eng, dst=dst, tt=tt, ap=ap, bo=bofs, e=e:
                               eng.activation(dst[:, tt * 512:tt * 512 + 512], ap,
                                              AF.Sigmoid,
                                              bias=bcolL0[:, bo + e:bo + e + 1]),
                               waits=w)
                    bs.set_tok(j, ("ae", ae_t))
        ae_evacs = P.v("ae")
        vec(lambda eng: eng.tensor_scalar(
            xc_row[:], a_row[:], -1.0, 1.0, OP.mult, OP.add),
            waits=[("ae", ae_evacs)])
        vec(lambda eng: eng.tensor_mul(xc_row[:], xc_row[:], vh_row[:]))
        vec(lambda eng: eng.tensor_tensor_scan(
            vh_row[:], a_row[:], xc_row[:], 0.0, OP.mult, OP.add))
        vec(lambda eng: eng.tensor_tensor_scan(
            xc_row[:], a_row[:], a_row[:], 1.0, OP.mult, OP.bypass))
        vec(lambda eng, e=e: eng.tensor_mul(bufA[:, e * T:(e + 1) * T], vh_row[:], g_row[:]))
        vec(lambda eng, e=e: eng.tensor_mul(
            bufB[:, e * T:(e + 1) * T], xc_row[:], g_row[:]))
        vec(lambda eng, e=e: eng.tensor_copy(summ[:, e:e + 1], xc_row[:, T - 1:T]))
        ve_rows_done = vec(lambda eng, e=e: eng.tensor_copy(
            summ[:, 8 + e:8 + e + 1], vh_row[:, T - 1:T]))
    ve_summary = ve_rows_done
    pe_a1_done = P.v("pe")

    if not do_attn:
        ve_summary = vec(lambda eng: eng.memset(summ[:], 0.0))
    g_ag0 = dma("gpsimd", ag0_in.ap().rearrange("a (p c) -> (a p) c", p=128),
                summ[:], waits=[("ve", ve_summary)])
    P.emit("gpsimd", lambda eng: eng.collective_compute(
        "AllGather", OP.bypass, replica_groups=[list(range(NCORES))],
        ins=[ag0_in.ap()], outs=[ag0_out.ap()]),
        waits=[("dG", g_ag0[1])], incs=[("cc", 1)])
    cc0 = P.v("cc")

    gu_tok = {}
    dn_tok = {}
    tok_g = tok_u = tok_d = ("ae", P.v("ae"))
    ve_ffn = 0
    pe_dn_last = 0
    first_gu_dma = True
    for tp in (range(1) if stage == "l0_hid" else (range(4) if do_ffn else ())):
        t0 = tp * 512
        h_last = 0
        for hch in range(NH0):
            sg = hch % 2
            wg = [gu_tok.get(("g", sg), ("pe", 0))]
            if first_gu_dma:
                wg.append(("ve", ve_summary))
                wg.append(("ve", xn_all))
                first_gu_dma = False
            dg = col_block_dma(ffn_gu[:, sg * 1024:(sg + 1) * 1024], fgT_h,
                               hch * 128, 128, NK, waits=wg, sem="fg%d" % sg)
            du = col_block_dma(ffn_gu[:, (2 + sg) * 1024:(3 + sg) * 1024], fuT_h,
                               hch * 128, 128, NK,
                               waits=[gu_tok.get(("u", sg), ("pe", 0))],
                               sem="fu%d" % sg)
            pw_g = [dg, tok_g]
            pw_u = [du, tok_u]
            if tp == 0 and hch == 0:
                pw_g = pw_g + [("ve", xn_all)]
            pe_g = pe_u = 0
            for k in range(NK):
                pe_g = mm(ffn_gu[:, sg * 1024 + k * 128: sg * 1024 + (k + 1) * 128],
                          xnv(k, t0, t0 + 512), PS0[:, 0:512],
                          start=(k == 0), stop=(k == NK - 1),
                          waits=(pw_g if k == 0 else ()), inc=(k == NK - 1))
            gu_tok[("g", sg)] = ("pe", pe_g)
            for k in range(NK):
                pe_u = mm(ffn_gu[:, (2 + sg) * 1024 + k * 128: (2 + sg) * 1024 + (k + 1) * 128],
                          xnv(k, t0, t0 + 512), PS1[:, 0:512],
                          start=(k == 0), stop=(k == NK - 1),
                          waits=(pw_u if k == 0 else ()), inc=(k == NK - 1))
            gu_tok[("u", sg)] = ("pe", pe_u)
            w_s = [("pe", pe_g), tok_g, ("ve", h_last)]
            if tp > 0 and hch == 0:
                w_s.append(("pe", pe_dn_last))
            ae_s = act(lambda eng: eng.activation(
                gact[:, 0:512], PS0[:, 0:512], AF.Silu, scale=1.0 / 127.0),
                waits=w_s)
            tok_g = ("ae", ae_s)
            h_last = vec(lambda eng, hch=hch: eng.scalar_tensor_tensor(
                ffn_h[:, hch * 512:(hch + 1) * 512], PS1[:, 0:512],
                1.0 / 127.0, gact[:, 0:512], OP.mult, OP.mult),
                waits=[("ae", ae_s), ("pe", pe_u)])
            tok_u = ("ve", h_last)
        for dch in range(NK):
            pe_d = 0
            for half in range(2):
                hb = half
                dn_w = [dn_tok.get(hb, ("pe", pe_a1_done))]
                if hb not in dn_tok:
                    dn_w.append(("ve", ve_summary))
                dd = col_block_dma(ffn_dn[:, hb * 1536:(hb + 1) * 1536], fdT_h,
                                   dch * 128, 128, 12,
                                   row0=half * 1536, waits=dn_w,
                                   sem="fd%d" % hb)
                for k in range(12):
                    kk = half * 12 + k
                    pe_d = mm(ffn_dn[:, hb * 1536 + k * 128: hb * 1536 + (k + 1) * 128],
                              ffn_h[:, kk * 512:(kk + 1) * 512],
                              PS2[:, 0:512],
                              start=(kk == 0), stop=(kk == NH0 - 1),
                              waits=([dd] +
                                     ([("ve", h_last), tok_d] if kk == 0 else []))
                              if k == 0 else (),
                              inc=(k == 11))
                dn_tok[hb] = ("pe", pe_d)
            pe_dn_last = pe_d
            ve_ffn = vec(lambda eng, dch=dch, t0=t0: eng.scalar_tensor_tensor(
                xr(dch, t0, t0 + 512), PS2[:, 0:512], 1.0 / 127.0,
                xr(dch, t0, t0 + 512), OP.mult, OP.add),
                waits=[("pe", pe_d)])
            tok_d = ("ve", ve_ffn)

    P.emit("gpsimd", (lambda eng: eng.dma_start(
        out=agbuf[:].rearrange("p (r c) -> p r c", c=16),
        in_=ag0_out.ap().rearrange("r (p c) -> p r c", p=128))),
        waits=[("cc", cc0)], incs=[("dB", 16)])
    g_back_v = P.v("dB")
    ve_prep = 0
    for r in range(NCORES):
        vec(lambda eng, r=r: eng.tensor_scalar(
            red[:, r * 8:(r + 1) * 8], agbuf[:, r * 16:r * 16 + 8],
            prm[:, r:r + 1], prm[:, 9 + r:10 + r], OP.mult, OP.add),
            waits=([("dB", g_back_v)] if r == 0 else ()))
        ve_prep = vec(lambda eng, r=r: eng.tensor_scalar(
            red[:, 64 + r * 8: 64 + (r + 1) * 8],
            agbuf[:, r * 16 + 8:r * 16 + 16],
            prm[:, r:r + 1], None, OP.mult))
    gp(lambda eng: eng.memset(hin[:], 0.0), waits=[("ve", ve_prep)])
    for r in range(NCORES):
        gp(lambda eng, r=r: eng.tensor_tensor(
            red[:, 128:136], red[:, r * 8:(r + 1) * 8], hin[:], OP.mult))
        gp(lambda eng, r=r: eng.tensor_tensor(
            hin[:], red[:, 128:136], red[:, 64 + r * 8: 64 + (r + 1) * 8], OP.add))
    ge_hin = P.v("ge")
    ve_corr = 0
    for e in (range(NK) if do_attn else ()):
        vec(lambda eng, e=e: eng.scalar_tensor_tensor(
            corr_t[:], bufB[:, e * T:(e + 1) * T], hin[:, e:e + 1],
            bufA[:, e * T:(e + 1) * T], OP.mult, OP.add),
            waits=([("ge", ge_hin), ("ve", ve_ffn)] if e == 0 else ()))
        ve_corr = vec(lambda eng, e=e: eng.tensor_add(
            xr(e), xr(e), corr_t[:]))

    if stage.startswith("l0"):
        last = None
        fin = max(ve_corr, ve_ffn, xn_all)
        if stage == "l0_hin":
            last = dma("gpsimd", out_h.ap()[0:128, 0:128], agbuf[:],
                       waits=[("ve", ve_corr)])
            last = dma("gpsimd", out_h.ap()[0:128, 128:136], hin[:],
                       waits=[("ve", ve_corr)])
            last = dma("gpsimd", out_h.ap()[0:128, 136:152], summ[:],
                       waits=[("ve", ve_corr)])
            last = dma("gpsimd", out_h.ap()[0:128, 152:168], prm[:],
                       waits=[("ve", ve_corr)])
            last = dma("gpsimd", out_h.ap()[0:128, 168:360],
                       bufA[:, 0:192], waits=[("ve", ve_corr)])
            P.emit("gpsimd", lambda eng: None, waits=[("dG", last[1])])
        elif stage == "l0_hid":
            for k in range(6):
                last = dma("gpsimd", out_h.ap()[k * 128:(k + 1) * 128, :],
                           ffn_h[:, k * 2048:(k + 1) * 2048],
                           waits=[("ve", max(ve_ffn, 1))])
            P.emit("gpsimd", lambda eng: None, waits=[("dG", last[1])])
        elif stage == "l0_xn":
            for k in range(NK):
                last = dma("gpsimd", out_h.ap()[k * 128:(k + 1) * 128, :], xnv(k),
                           waits=[("ve", fin)])
            P.emit("gpsimd", lambda eng: None, waits=[("dG", last[1])])
        else:
            for k in range(NK):
                last = dma("sync", out_h.ap()[k * 128:(k + 1) * 128, :], xr(k),
                           waits=[("ve", fin)])
            P.emit("sync", lambda eng: None, waits=[("dS", last[1])])
        emit_all(nc, P)
        es.close()
        return nc

    pe_l0 = P.v("pe")
    ae_l0 = P.v("ae")

    xn_all = emit_layernorm([("ve", ve_corr), ("pe", pe_l0), ("ae", ae_l0)],
                            stats_tok=("ve", ve_corr))
    ae_ln1 = P.v("ae")


    def early_out(waits):
        last = None
        for k in range(NK):
            last = dma("sync", out_h.ap()[k * 128:(k + 1) * 128, :], xr(k),
                       waits=waits)
        P.emit("sync", lambda eng: None, waits=[("dS", last[1])])
        emit_all(nc, P)
        es.close()
        return nc

    brow1 = aview("brow1", [1, 2048], BF16, 0)
    cosfc = aview("cosfc", [128, 512], BF16, 4096)
    sinfc = aview("sinfc", [128, 512], BF16, 5120)
    qp_t = aview("qp_t", [128, 512], BF16, 6272)
    prQc = aview("prQc", [128, 512], BF16, 7296)
    prQs = aview("prQs", [128, 512], BF16, 8320)
    prKc = aview("prKc", [128, 512], BF16, 9344)
    prKs = aview("prKs", [128, 512], BF16, 10368)
    st_t = aview("st_t", [128, 512], BF16, 11392)
    krelu = aview("krelu", [128, 1024], F32, 12416)
    sden = aview("sden", [1, 512], F32, 12416)
    iotat = aview("iotat", [128, 512], F32, 14464)
    vtok = aview("vtok", [128, 4 * NST], BF16, 16512)
    sprefb = aview("sprefb", [128, 4 * NST], BF16, 24832)
    srun1 = aview("srun1", [128, NST], F32, 33152)
    sptot = aview("sptot", [128, NST], BF16, 37312)
    sshift = aview("sshift", [128, NST], BF16, 39392)
    onesT = aview("onesT", [128, 128], F32, 41472)
    wvsl = aview("wvsl", [128, 1024], BF16, 41984)
    at64 = aview("at64", [128, 512], BF16, 44032)
    xq_t = aview("xq_t", [128, 4096], BF16, 0)
    w1sl = aview("w1sl", [128, 2048], BF16, 8192)
    w2sl = aview("w2sl", [128, 2048], BF16, 12288)
    hidt = aview("hidt", [128, 1024], BF16, 16384)
    qtf = aview("qtf", [128, 1024], F32, 18432)
    wprepD = [aview("wprepA", [128, 2048], BF16, 22528),
              aview("wprepB", [128, 2048], BF16, 26624),
              aview("wprepC", [128, 2048], BF16, 30720),
              aview("wprepE", [128, 2048], BF16, 34816)]
    wpabs = aview("wpabs", [128, 512], F32, 38912)
    wpq = aview("wpq", [128, 512], F32, 40960)
    amx32 = agbuf[:, 0:32]
    sc32 = agbuf[:, 32:64]
    rs32 = agbuf[:, 64:96]
    cmb32 = agbuf[:, 96:128]
    srow_t = aview("srow_t", [1, 512], BF16, 43008)
    xq4 = aview("xq4", [128, 16384], BF16, 0)
    w1slb = aview("w1slb", [128, 2048], BF16, 32768)
    hidt1 = aview("hidt1", [128, 1024], BF16, 36864)

    g_id = gp(lambda eng: eng.iota(
        red[:, 0:128] if False else red[:, 16:144], pattern=[[-1, 128]], base=0,
        channel_multiplier=1, allow_small_or_imprecise_dtypes=True))
    ve_id = vec(lambda eng: eng.tensor_single_scalar(
        identb[:], red[:, 16:144], 0.0, OP.is_equal), waits=[("ge", g_id)])
    ve_orb = vec(lambda eng: eng.memset(ones_rb[:], 1.0), waits=[("ve", ve_corr)])

    for k in range(NK):
        vec(lambda eng, k=k: eng.reduce_max(
            red[:, 8 + k:9 + k], xnv(k),
            axis=AX.X, apply_absolute_value=True))
    ve_am = vec(lambda eng: eng.reduce_max(
        summ[:, 0:1], red[:, 8:16], axis=AX.X))
    ve_amb = vec(lambda eng: eng.tensor_copy(tri[:, 0:1] if False else summ.bitcast(BF16)[:, 2:3], summ[:, 0:1]),
                 waits=[("ve", ve_am)])
    P.emit("tensor", (lambda eng: eng.transpose(
        PS4.bitcast(BF16)[0:1, 0:128], summ.bitcast(BF16)[:, 2:3], identb[:])),
        waits=[("ve", ve_amb), ("ve", ve_id)], incs=[("pe", 1)])
    pe_am = P.v("pe")
    ve_am2 = vec(lambda eng: eng.reduce_max(
        smax[0:1, 0:1], PS4.bitcast(BF16)[0:1, 0:128], axis=AX.X),
        waits=[("pe", pe_am)])

    if stage == "l1_ln":
        return early_out([("ve", xn_all), ("ve", ve_am2), ("ve", ve_corr)])
    d_brow = dma("gpsimd", brow1[0:1, 0:D], bk_h.ap(), sem="dR",
                 waits=[("ve", xn_all)])
    d_brow = dma("gpsimd", brow1[0:1, D:2 * D], bv1_h.ap(), sem="dR")
    g_cs = gp(lambda eng: eng.iota(
        iotat[:, 0:16], pattern=[[128, 16]], base=0, channel_multiplier=1,
        allow_small_or_imprecise_dtypes=True), waits=[("ve", xn_all)])
    cscol = red[:, 112:144]
    ae_cs = act(lambda eng: eng.activation(
        cscol[:, 0:16], iotat[:, 0:16], AF.Sin, scale=ANG_SCALE,
        bias=prm[:, 17:18]), waits=[("ge", g_cs)])
    ae_cs = act(lambda eng: eng.activation(
        cscol[:, 16:32], iotat[:, 0:16], AF.Sin, scale=ANG_SCALE,
        bias=prm[:, 8:9]))
    ve_vi = vec(lambda eng: eng.memset(vtok[:], 0.0), waits=[("ve", xn_all)])
    ve_vi = vec(lambda eng: eng.memset(
        vtok[:].rearrange("p (u h) -> p u h", h=65)[:, :, 64:65], 1.0))


    def fence(buf=None):
        b = bufB if buf is None else buf
        return vec(lambda eng, b=b: eng.tensor_copy(b[:, 0:512], b[:, 512:1024]))

    w2slB = aview("w2slB", [128, 2048], BF16, 4096)
    wpqB = aview("wpqB", [128, 512], F32, 8320)
    rsoC = red[:, 48:80]
    scoC = red[:, 80:112]
    pw_tok = {}
    for hch in range(NH1):
        sl = hch % 2
        d_w2 = dma("gpsimd", w2slB[:, sl * 1024:(sl + 1) * 1024],
                   wout_h.ap()[hch * 128:(hch + 1) * 128, :], sem="pW%d" % sl,
                   waits=[pw_tok.get(sl, ("ve", xn_all))])
        ve_a2 = vec(lambda eng, sl=sl, hch=hch: eng.reduce_max(
            rsoC[:, hch:hch + 1], w2slB[:, sl * 1024:(sl + 1) * 1024],
            axis=AX.X, apply_absolute_value=True), waits=[d_w2])
        pw_tok[sl] = ("ve", ve_a2)
    ve_f = fence(w2slB)
    ve_f = vec(lambda eng: eng.tensor_scalar(
        rsoC[:], rsoC[:], 1e-8, None, OP.max))
    ve_f = fence(w2slB)
    ve_rq = vec(lambda eng: eng.reciprocal(scoC[:], rsoC[:]))
    ve_rso = vec(lambda eng: eng.tensor_scalar(
        rsoC[:], rsoC[:], 1.0 / 127.0, None, OP.mult))
    ve_f = fence(w2slB)
    ve_rq = vec(lambda eng: eng.tensor_scalar(
        scoC[:], scoC[:], 127.0, None, OP.mult))
    ve_f = fence(w2slB)
    q_tok = {0: ("ve", ve_f), 1: ("ve", ve_f)}
    for hch in range(NH1):
        sl = hch % 2
        d_w2 = dma("gpsimd", w2slB[:, sl * 1024:(sl + 1) * 1024],
                   wout_h.ap()[hch * 128:(hch + 1) * 128, :], sem="pW%d" % sl,
                   waits=[q_tok[sl]])
        ae_q1 = act(lambda eng, sl=sl, hch=hch: eng.activation(
            wpqB[:, 0:512], w2slB[:, sl * 1024:sl * 1024 + 512],
            AF.Copy, scale=scoC[:, hch:hch + 1], bias=MAGIC),
            waits=[d_w2, ("ae", pw_tok.get("ae", 0))])
        ae_q2 = act(lambda eng, sl=sl, hch=hch: eng.activation(
            w2slB.bitcast(F32)[:, sl * 512 + 0:sl * 512 + 256] if False else wpqB[:, 0:512],
            w2slB[:, sl * 1024 + 512:(sl + 1) * 1024],
            AF.Copy, scale=scoC[:, hch:hch + 1], bias=MAGIC)) if False else None
        ve_w1 = vec(lambda eng, sl=sl: eng.tensor_scalar(
            w2slB[:, sl * 1024:sl * 1024 + 512], wpqB[:, 0:512], -MAGIC, None,
            OP.add), waits=[("ae", ae_q1)])
        ae_q3 = act(lambda eng, sl=sl, hch=hch: eng.activation(
            wpqB[:, 0:512], w2slB[:, sl * 1024 + 512:(sl + 1) * 1024],
            AF.Copy, scale=scoC[:, hch:hch + 1], bias=MAGIC),
            waits=[("ve", ve_w1)])
        pw_tok["ae"] = ae_q3
        ve_w2 = vec(lambda eng, sl=sl: eng.tensor_scalar(
            w2slB[:, sl * 1024 + 512:(sl + 1) * 1024], wpqB[:, 0:512], -MAGIC,
            None, OP.add), waits=[("ae", ae_q3)])
        d_w2o = dma("sync", woutq_d.ap()[hch * 128:(hch + 1) * 128, :],
                    w2slB[:, sl * 1024:(sl + 1) * 1024], sem="pO%d" % sl,
                    waits=[("ve", ve_w2)])
        q_tok[sl] = d_w2o
    ve_woutprep = P.v("ve")
    pO_done = [q_tok[0], q_tok[1]]

    wk_last = wv_last = None
    for k in range(NK):
        wk_last = dma("gpsimd", bufB[:, k * 1024:(k + 1) * 1024],
                      wk_h.ap()[k * 128:(k + 1) * 128, :], sem="wWk",
                      waits=([("ve", ve_corr)] if k == 0 else ()))
    for k in range(NK):
        wv_last = dma("gpsimd", bufB[:, (NK + k) * 1024:(NK + k + 1) * 1024],
                      wv1_h.ap()[k * 128:(k + 1) * 128, :], sem="wWv",
                      waits=([("ve", ve_corr)] if k == 0 else ()))

    def wkv(which, k):
        base = 0 if which == "k" else NK * 1024
        return bufB[:, base + k * 1024: base + (k + 1) * 1024]

    ve_state_done = 0
    tokB = {("st", 0): ("pe", 0), ("st", 1): ("ve", ve_am2)}
    for c in range(NCH):
        if c == 0:
            ve_sp = vec(lambda eng, c=c: eng.memset(
                sprefb[:, c * NST:(c + 1) * NST], 0.0))
            ve_sp = vec(lambda eng: eng.memset(srun1[:], 0.0))
        else:
            ve_sp = vec(lambda eng, c=c: eng.tensor_copy(
                sprefb[:, c * NST:(c + 1) * NST], srun1[:]),
                waits=[("ve", ve_state_done)])
        for u in range(4):
            tt = c * 4 + u
            t0 = tt * 128
            pe_k = 0
            for k in range(NK):
                for hf in range(2):
                    pw = []
                    if k == 0 and hf == 0:
                        pw = [tokB.get("ps0", ("pe", 0))]
                        if c == 0 and u == 0:
                            pw += [wk_last, ("ve", xn_all)]
                    pe_k = mm(xnv(k, t0, t0 + 128),
                              wkv("k", k)[:, hf * 512:(hf + 1) * 512],
                              PS0[:, hf * 512:hf * 512 + 512],
                              start=(k == 0), stop=False, waits=pw)
            for hf in range(2):
                pe_k = mm(ones_rb[0:1, 0:128], brow1[0:1, hf * 512:(hf + 1) * 512],
                          PS0[:, hf * 512:hf * 512 + 512], start=False, stop=True,
                          waits=([d_brow, ("ve", ve_orb)]
                                 if (c == 0 and u == 0 and hf == 0) else ()),
                          inc=(hf == 1))
            ae_k = act(lambda eng: eng.activation(
                krelu[:, 0:512], PS0[:, 0:512], AF.Relu),
                waits=[("pe", pe_k), tokB.get("krelu", ("pe", 0))])
            ae_k2 = act(lambda eng: eng.activation(
                krelu[:, 512:1024], PS0[:, 512:1024], AF.Relu))
            tokB["ps0"] = ("ae", ae_k2)
            kt = bufA[:, tt * 2048:(tt + 1) * 2048] if False else bufA[:, u * 2048:(u + 1) * 2048]
            ve_kt = 0
            for hf in range(2):
                src = krelu[:, hf * 512:(hf + 1) * 512]
                ve_kt = vec(lambda eng, kt=kt, src=src, hf=hf, tt=tt: eng.tensor_scalar(
                    kt.rearrange("p (h z) -> p h z", z=128)[:, hf * 8:(hf + 1) * 8, 0:64],
                    src.rearrange("p (h z) -> p h z", z=64),
                    cscol[:, tt:tt + 1], None, OP.mult),
                    waits=([("ae", ae_k2), tokB.get("kt", ("pe", 0))] if hf == 0 else ()))
                ve_kt = vec(lambda eng, kt=kt, src=src, hf=hf, tt=tt: eng.tensor_scalar(
                    kt.rearrange("p (h z) -> p h z", z=128)[:, hf * 8:(hf + 1) * 8, 64:128],
                    src.rearrange("p (h z) -> p h z", z=64),
                    cscol[:, 16 + tt:17 + tt], None, OP.mult))
            tokB["krelu"] = ("ve", ve_kt)
            pe_v = 0
            for k in range(NK):
                for hf in range(2):
                    pw = []
                    if k == 0 and hf == 0:
                        pw = [tokB.get("ps1", ("pe", 0))]
                        if c == 0 and u == 0:
                            pw += [wv_last]
                    pe_v = mm(xnv(k, t0, t0 + 128),
                              wkv("v", k)[:, hf * 512:(hf + 1) * 512],
                              PS1[:, hf * 512:hf * 512 + 512],
                              start=(k == 0), stop=False, waits=pw)
            for hf in range(2):
                pe_v = mm(ones_rb[0:1, 0:128], brow1[0:1, D + hf * 512:D + (hf + 1) * 512],
                          PS1[:, hf * 512:hf * 512 + 512], start=False, stop=True,
                          inc=(hf == 1))
            vt = vtok[:, u * NST:(u + 1) * NST]
            ae_vv = 0
            for hf in range(2):
                w0 = []
                if hf == 0:
                    w0 = [("pe", pe_v), tokB.get("vt", ("pe", 0))]
                    if c == 0 and u == 0:
                        w0.append(("ve", ve_vi))
                ae_vv = act(lambda eng, vt=vt, hf=hf: eng.activation(
                    vt.rearrange("p (h z) -> p h z", z=65)[:, hf * 8:(hf + 1) * 8, 0:64],
                    PS1[:, hf * 512:(hf + 1) * 512].rearrange("p (h z) -> p h z", z=64),
                    AF.Copy), waits=w0)
            tokB["ps1"] = ("ae", ae_vv)
            tokB[("ktv", u)] = (("ve", ve_kt), ("ae", ae_vv))
        for h in range(NH):
            ps = PS3 if h % 2 == 0 else PS4
            pe_s = 0
            for u in range(4):
                kt = bufA[:, u * 2048:(u + 1) * 2048]
                vt = vtok[:, u * NST:(u + 1) * NST]
                pw = []
                if u == 0:
                    pw = [tokB[("st", h % 2)]]
                    if h < 2:
                        pw += [tokB[("ktv", 3)][0], tokB[("ktv", 3)][1]]
                pe_s = mm(kt[:, h * 128:(h + 1) * 128], vt[:, h * 65:(h + 1) * 65],
                          ps[:, 0:65], start=(u == 0), stop=(u == 3),
                          waits=pw, inc=(u == 3))
            ve_state_done = vec(lambda eng, h=h, ps=ps: eng.scalar_tensor_tensor(
                srun1[:, h * 65:(h + 1) * 65], ps[:, 0:65], 1.0,
                srun1[:, h * 65:(h + 1) * 65], OP.mult, OP.add),
                waits=[("pe", pe_s), ("ve", ve_sp) if h == 0 else ("ve", 0)])
            tokB[("st", h % 2)] = ("ve", ve_state_done)
        tokB["vt"] = ("pe", P.v("pe"))
        tokB["kt"] = ("pe", P.v("pe"))

    g_s1 = dma("gpsimd", ag1_in.ap().rearrange("a (p c) -> (a p) c", p=128),
               srun1[:], waits=[("ve", ve_state_done)], sem="dB")
    g_s2 = dma("gpsimd", ag1b_in.ap()[0:1, 0:1], smax[0:1, 0:1],
               waits=[("ve", ve_am2)], sem="dB")
    P.emit("gpsimd", lambda eng: eng.collective_compute(
        "AllGather", OP.bypass, replica_groups=[list(range(NCORES))],
        ins=[ag1_in.ap()], outs=[ag1_out.ap()]),
        waits=[g_s1, g_s2], incs=[("cc", 1)])
    P.emit("gpsimd", lambda eng: eng.collective_compute(
        "AllGather", OP.bypass, replica_groups=[list(range(NCORES))],
        ins=[ag1b_in.ap()], outs=[ag1b_out.ap()]), incs=[("cc", 1)])
    cc1 = P.v("cc")

    ve_sx = vec(lambda eng: eng.memset(srun1[:], 0.0),
                waits=[("cc", cc1), ("ve", ve_state_done)])
    for r in range(NCORES):
        d_sr = dma("gpsimd", sptot[:],
                   ag1_out.ap()[r:r + 1, :].rearrange("a (p c) -> (a p) c", p=128),
                   sem="dB", waits=[("ve", ve_sx)] if r == 0 else [("ve", ve_sx)])
        ve_m1 = vec(lambda eng, r=r: eng.tensor_scalar(
            sshift[:], sptot[:], prm[:, r:r + 1], None, OP.mult),
            waits=[d_sr])
        ve_sx = vec(lambda eng: eng.tensor_add(srun1[:], srun1[:], sshift[:]))
    ve_scross = ve_sx
    d_sb = dma("gpsimd", smax[0:8, 8:9], ag1b_out.ap()[:, 0:1], sem="dB",
               waits=[("cc", cc1)], slow=True)
    d_fold = dma("gpsimd", smax[0:1, 24:32], smax[0:8, 8:9], sem="dB",
                 waits=[d_sb], slow=True)
    ve_g1 = vec(lambda eng: eng.reduce_max(
        smax[0:1, 16:17], smax[0:1, 24:32], axis=AX.X), waits=[d_fold])
    ve_g1 = vec(lambda eng: eng.tensor_scalar(
        smax[0:1, 17:18], smax[0:1, 16:17], 1.0 / 32767.0, None, OP.mult))
    pe_bc = mm(ones_row[0:1, 0:128], smax[0:1, 16:18], PS4[:, 0:2],
               start=True, stop=True,
               waits=[("ve", ve_g1), ("ve", ve_state_done)], inc=True)
    ve_sc = vec(lambda eng: eng.reciprocal(red[:, 32:33], PS4[:, 0:1]),
                waits=[("pe", pe_bc)])
    ve_sc = vec(lambda eng: eng.tensor_copy(red[:, 34:35], PS4[:, 1:2]))
    ve_sc = vec(lambda eng: eng.tensor_scalar(
        red[:, 33:34], red[:, 32:33], 32767.0, None, OP.mult))
    ve_scales = ve_sc

    if stage == "l1_b1":
        return early_out([("ve", ve_scross), ("ve", ve_scales), ("ve", ve_corr)])
    lvl = 99
    if stage.startswith("l1_b2@"):
        lvl = int(stage.split("@")[1])
    pe_b1 = P.v("pe")
    ve_o1 = vec(lambda eng: eng.memset(onesT[:], 1.0))

    tokC = {"ps4s": ("ve", ve_scales), ("st", 0): ("pe", 0),
            "ps0": tokB["ps0"], "ps0b": tokB["ps0"], "ps1": tokB["ps1"],
            "ps1b": tokB["ps1"], "ps3": tokB[("st", 0)],
            "ps2": ("ve", ve_ffn), "ps2b": ("ve", ve_ffn)}
    ve_attn_done = 0
    dH_attn = 0
    for c in range(NCH):
        t0 = c * CHUNK
        ve_tot = vec(lambda eng, c=c: eng.tensor_add(
            sptot[:], sprefb[:, c * NST:(c + 1) * NST], srun1[:]),
            waits=[("ve", ve_scross), tokC.get("sptot", ("pe", 0))])
        d_sh1 = dma("sync", sshift[0:64, :], sptot[64:128, :], sem="dH",
                    waits=[("ve", ve_tot), ("dH", dH_attn)])
        d_sh2 = dma("sync", sshift[64:128, :], sptot[0:64, :], sem="dH")
        g_i2 = gp(lambda eng, c=c: eng.iota(
            iotat[:], pattern=[[1, 512]], base=c * 512, channel_multiplier=0,
            allow_small_or_imprecise_dtypes=True),
            waits=[tokC.get("iotat", ("pe", 0))])
        ae_c1 = act(lambda eng: eng.activation(
            cosfc[:], iotat[:], AF.Sin, scale=ANG_SCALE, bias=prm[:, 17:18]),
            waits=([("ge", g_i2), tokC.get("csf", ("pe", 0))] +
                   ([pO_done[0], pO_done[1], ("ve", ve_woutprep)] if c == 0 else [])))
        ae_c2 = act(lambda eng: eng.activation(
            sinfc[:], iotat[:], AF.Sin, scale=ANG_SCALE, bias=prm[:, 8:9]))
        tokC["iotat"] = ("ae", ae_c2)
        for u in range(4):
            tu0 = t0 + u * 128
            pe_v2 = 0
            for k in range(NK):
                for hf in range(2):
                    pw = []
                    if k == 0 and hf == 0:
                        pw = [tokC.get("ps1", ("pe", 0)), tokC.get("ps1b", ("pe", 0))]
                    pe_v2 = mm(xnv(k, tu0, tu0 + 128),
                               wkv("v", k)[:, hf * 512:(hf + 1) * 512],
                               PS1[:, hf * 512:hf * 512 + 512],
                               start=(k == 0), stop=False, waits=pw)
            for hf in range(2):
                pe_v2 = mm(ones_rb[0:1, 0:128],
                           brow1[0:1, D + hf * 512:D + (hf + 1) * 512],
                           PS1[:, hf * 512:hf * 512 + 512], start=False,
                           stop=True, inc=(hf == 1))
            vt = vtok[:, u * NST:(u + 1) * NST]
            ae_vv2 = 0
            for hf in range(2):
                w0 = []
                if hf == 0:
                    w0 = [("pe", pe_v2), tokC.get("vt", ("pe", 0))]
                ae_vv2 = act(lambda eng, vt=vt, hf=hf: eng.activation(
                    vt.rearrange("p (h z) -> p h z", z=65)[:, hf * 8:(hf + 1) * 8, 0:64],
                    PS1[:, hf * 512:(hf + 1) * 512].rearrange("p (h z) -> p h z", z=64),
                    AF.Copy), waits=w0)
            tokC["ps1"] = ("ae", ae_vv2)
        tokC["vprj"] = ("ae", ae_vv2)
        for p in (range(NK) if lvl >= 1 else ()):
            d_wq = col_block_dma(wvsl[:], wq_h, p * 128, 128, NK, sem="wV2",
                                 waits=[tokC.get("wvsl", ("pe", pe_b1))])
            pe_q = 0
            for k in range(NK):
                pw = []
                if k == 0:
                    pw = [tokC.get("ps0", ("pe", 0)), d_wq]
                pe_q = mm(wvsl[:, k * 128:(k + 1) * 128],
                          xnv(k, t0, t0 + CHUNK), PS0[:, 0:512],
                          start=(k == 0), stop=(k == NK - 1), waits=pw,
                          inc=(k == NK - 1))
            tokC["wvsl"] = ("pe", pe_q)
            ae_q = act(lambda eng, p=p: eng.activation(
                qp_t[:], PS0[:, 0:512], AF.Relu, bias=bcol1[:, p:p + 1]),
                waits=[("pe", pe_q), tokC.get("qp", ("pe", 0)), ("ae", ae_c2)])
            ve_qc = vec(lambda eng: eng.tensor_mul(prQc[:], qp_t[:], cosfc[:]),
                        waits=[("ae", ae_q), tokC.get("prq", ("pe", 0))])
            ve_qs = vec(lambda eng: eng.tensor_mul(prQs[:], qp_t[:], sinfc[:]))
            tokC["ps0"] = ("ae", ae_q)
            pe_k2 = 0
            for k in range(NK):
                pw = [tokC.get("ps0b", ("pe", 0))] if k == 0 else []
                pe_k2 = mm(bufB[:, k * 1024 + p * 128: k * 1024 + (p + 1) * 128],
                           xnv(k, t0, t0 + CHUNK), PS0[:, 512:1024],
                           start=(k == 0), stop=(k == NK - 1), waits=pw,
                           inc=(k == NK - 1))
            ae_k3 = act(lambda eng, p=p: eng.activation(
                qp_t[:], PS0[:, 512:1024], AF.Relu, bias=bcol1[:, NK + p:NK + p + 1]),
                waits=[("pe", pe_k2), ("ve", ve_qs)])
            tokC["ps0b"] = ("ae", ae_k3)
            ve_kc = vec(lambda eng: eng.tensor_mul(prKc[:], qp_t[:], cosfc[:]),
                        waits=[("ae", ae_k3), tokC.get("prk", ("pe", 0))])
            ve_ks = vec(lambda eng: eng.tensor_mul(prKs[:], qp_t[:], sinfc[:]))
            for hh in (range(2) if lvl >= 2 else ()):
                h = 2 * p + hh
                pb = hh * 64
                ps2h = PS2[0:65, 0:512] if hh == 0 else PS2[0:65, 512:1024]
                w_int = [tokC.get("ps2" if hh == 0 else "ps2b", ("pe", 0)),
                         ("dH", d_sh2[1]), ("ve", ve_ks), tokC["vprj"]]
                if c == 0 and p == 0 and hh == 0:
                    w_int.append(("ve", ve_o1))
                lhs_cos = sptot[0:64, h * 65:(h + 1) * 65] if hh == 0 \
                    else sshift[64:128, h * 65:(h + 1) * 65]
                lhs_sin = sshift[0:64, h * 65:(h + 1) * 65] if hh == 0 \
                    else sptot[64:128, h * 65:(h + 1) * 65]
                pe_at = mm(lhs_cos, prQc[pb:pb + 64, :], ps2h,
                           start=True, stop=False, waits=w_int)
                pe_at = mm(lhs_sin, prQs[pb:pb + 64, :], ps2h,
                           start=False, stop=(lvl < 5), inc=(lvl < 5))
                for u in (range(4) if lvl >= 3 else ()):
                    w_u = 512 - u * 128
                    ps_s = PS3 if u % 2 == 0 else PS4
                    pe_sc2 = mm(prKc[pb:pb + 64, u * 128:(u + 1) * 128],
                                prQc[pb:pb + 64, u * 128:512],
                                ps_s[:, 0:w_u], start=True, stop=False,
                                waits=[tokC.get("ps3" if u % 2 == 0 else "ps4s",
                                                ("pe", 0))])
                    pe_sc2 = mm(prKs[pb:pb + 64, u * 128:(u + 1) * 128],
                                prQs[pb:pb + 64, u * 128:512],
                                ps_s[:, 0:w_u], start=False, stop=True, inc=True)
                    if lvl == 7:
                        ae_m = act(lambda eng, w_u=w_u: eng.activation(
                            st_t[:, 0:w_u], PS3[:, 0:w_u], AF.Copy),
                            waits=[("pe", pe_sc2)])
                        ve_m = ae_m
                        tokC["ps3"] = ("ae", ae_m)
                        tokC[("st", 0)] = ("ae", ae_m)
                        continue
                    ve_m = vec(lambda eng, w_u=w_u, ps_s=ps_s: eng.tensor_mul(
                        st_t[:, 0:w_u], ps_s[:, 0:w_u], tri[:, 0:w_u]),
                        waits=[("pe", pe_sc2), tokC[("st", 0)],
                               ("dH", dH_attn)])
                    tokC["ps3" if u % 2 == 0 else "ps4s"] = ("ve", ve_m)
                    wait_st = [("ve", ve_m)]
                    if lvl >= 5 and lvl not in (7, 8):
                        pe_at = mm(vtok[:, u * NST + h * 65: u * NST + (h + 1) * 65],
                                   st_t[:, 0:w_u],
                                   ps2h.tensor[0:65, (hh * 512) + u * 128: hh * 512 + 512]
                                   if False else
                                   (PS2[0:65, u * 128:512] if hh == 0 else
                                    PS2[0:65, 512 + u * 128:1024]),
                                   start=False, stop=(u == 3),
                                   waits=wait_st, inc=(u == 3))
                        tokC[("st", 0)] = ("pe", pe_at)
                    else:
                        tokC[("st", 0)] = wait_st[0]
                if lvl < 6 or lvl in (7, 8):
                    tokC["ps2" if hh == 0 else "ps2b"] = ("pe", pe_at)
                    continue
                psb = PS1[0:64, 0:512] if hh == 0 else PS1[0:64, 512:1024]
                ve_d1 = vec(lambda eng, hh=hh: eng.tensor_scalar(
                    iotat[64:65, 0:512], PS2[64:65, 512 * hh:512 * hh + 512],
                    1e-6, None, OP.max),
                    waits=[("pe", pe_at), tokC.get("iden", ("pe", 0))])
                ae_ln = act(lambda eng: eng.activation(
                    krelu[64:65, 0:512], iotat[64:65, 0:512], AF.Ln),
                    waits=[("ve", ve_d1)])
                tokC["iden"] = ("ae", ae_ln)
                pe_bc2 = mm(onesT[64:65, 0:64], krelu[64:65, 0:512],
                            psb, start=True, stop=True,
                            waits=[("ae", ae_ln),
                                   tokC["ps1" if hh == 0 else "ps1b"]], inc=True)
                tokC["iotat"] = ("pe", pe_bc2)
                ae_rd = act(lambda eng, psb=psb: eng.activation(
                    krelu[0:64, 0:512], psb, AF.Exp, scale=-1.0),
                    waits=[("pe", pe_bc2), tokC.get("krel", ("pe", 0))])
                ae_n = act(lambda eng, hh=hh: eng.activation(
                    at64[0:64, :], PS2[0:64, 512 * hh:512 * hh + 512], AF.Copy),
                    waits=[("pe", pe_at), tokC.get("at64", ("pe", 0))])
                tokC["ps2" if hh == 0 else "ps2b"] = ("ae", ae_n)
                at_tile = bufA[:, p * 2048 + t0: p * 2048 + t0 + 512]
                if hh == 0:
                    ve_attn_done = vec(lambda eng, at_tile=at_tile: eng.tensor_mul(
                        at_tile[0:64, :], at64[0:64, :], krelu[0:64, 0:512]),
                        waits=[("ae", ae_rd), ("ae", ae_n)])
                    tokC["at64"] = ("ve", ve_attn_done)
                    tokC["ps1"] = ("ae", ae_rd)
                    tokC["krel"] = ("ve", ve_attn_done)
                else:
                    ve_sh = vec(lambda eng: eng.tensor_mul(
                        st_t[0:64, :], at64[0:64, :], krelu[0:64, 0:512]),
                        waits=[("ae", ae_rd), ("ae", ae_n), ("dH", dH_attn)])
                    d_sh3 = dma("sync", at_tile[64:128, :], st_t[0:64, :],
                                sem="dH", waits=[("ve", ve_sh)])
                    dH_attn = d_sh3[1]
                    tokC["at64"] = ("ve", ve_sh)
                    tokC["ps1b"] = ("ae", ae_rd)
                    tokC["krel"] = ("ve", ve_sh)
                    tokC[("st", 0)] = ("ve", ve_sh)
            tokC["qp"] = tokC["at64"]
            tokC["prq"] = tokC[("st", 0)]
            tokC["prk"] = tokC[("st", 0)]
        tokC["sptot"] = ("pe", P.v("pe"))
        tokC["csf"] = tokC["ps0"]
    ve_attn_last = P.v("ve")

    pe_all_b2 = P.v("pe")
    tok_wo = ("pe", pe_all_b2)
    tok_evac = ("ve", 0)
    pe_wo_done = 0
    for dch in range(NK):
        d_wo = col_block_dma(wvsl[:, 0:1024], wo_h, dch * 128, 128, NK,
                             sem="wV2", waits=[tok_wo])
        pe_o = 0
        for k in range(NK):
            for tt in range(4):
                pw = []
                if k == 0 and tt == 0:
                    pw = [d_wo, ("ve", ve_attn_last), ("dH", dH_attn), tok_evac]
                pe_o = mm(wvsl[:, k * 128:(k + 1) * 128],
                          bufA[:, k * 2048 + tt * 512: k * 2048 + (tt + 1) * 512],
                          (PS0 if tt < 2 else PS1)[:, (tt % 2) * 512:(tt % 2) * 512 + 512],
                          start=(k == 0), stop=(k == NK - 1), waits=pw,
                          inc=(k == NK - 1))
        tok_wo = ("pe", pe_o)
        ve_wo = 0
        for tt in range(4):
            ve_wo = vec(lambda eng, dch=dch, tt=tt: eng.scalar_tensor_tensor(
                xr(dch, tt * 512, (tt + 1) * 512),
                (PS0 if tt < 2 else PS1)[:, (tt % 2) * 512:(tt % 2) * 512 + 512],
                bcol1[:, 16 + dch:17 + dch], xr(dch, tt * 512, (tt + 1) * 512),
                OP.add, OP.add), waits=[("pe", pe_o)] if tt == 0 else ())
        tok_evac = ("ve", ve_wo)
        pe_wo_done = pe_o
    ve_wo_done = tok_evac[1]

    if stage == "l1_atdump":
        last = None
        for k in range(NK):
            last = dma("gpsimd", out_h.ap()[k * 128:(k + 1) * 128, :],
                       bufA[:, k * 2048:(k + 1) * 2048],
                       waits=[("ve", ve_wo_done)], sem="dS")
        P.emit("sync", lambda eng: None, waits=[("dS", last[1])])
        emit_all(nc, P)
        es.close()
        return nc
    if stage.startswith("l1_b2"):
        return early_out([("ve", ve_wo_done), ("pe", P.v("pe")),
                          ("ve", P.v("ve")), ("ae", P.v("ae")),
                          ("ge", P.v("ge"))])
    pe_prep0 = pe_wo_done

    ae_tr_tok = ("ae", P.v("ae"))
    q_done = [("pe", pe_prep0)] * 4
    tok_srow = ("pe", 0)
    ve_id2 = ve_id

    def win_scales(b):
        yield lambda: vec(lambda eng, b=b: eng.tensor_scalar(
            sc32[:, b * 2:(b + 1) * 2], sc32[:, b * 2:(b + 1) * 2],
            1e-8, None, OP.max))
        yield lambda: vec(lambda eng, b=b: eng.reciprocal(
            amx32[:, b * 2:(b + 1) * 2], sc32[:, b * 2:(b + 1) * 2]))
        yield lambda: vec(lambda eng, b=b: eng.tensor_scalar(
            sc32[:, b * 2:(b + 1) * 2], amx32[:, b * 2:(b + 1) * 2],
            127.0, None, OP.mult))

    def win_quant(b):
        nonlocal tok_srow, q_done
        wp = wprepD[b % 4]
        c0 = b * 256
        ve_cp = vec(lambda eng, b=b: eng.tensor_copy(
            wpq.bitcast(BF16)[:, 0:2], sc32[:, b * 2:(b + 1) * 2]))
        P.emit("tensor", (lambda eng: eng.transpose(
            PS4.bitcast(BF16)[0:2, 0:128], wpq.bitcast(BF16)[:, 0:2], identb[:])),
            waits=[("ve", ve_cp), tok_srow], incs=[("pe", 1)])
        pe_t2 = P.v("pe")
        ae_sr = act(lambda eng: eng.activation(
            wpq.bitcast(BF16)[0:2, 128:256], PS4.bitcast(BF16)[0:2, 0:128],
            AF.Copy), waits=[("pe", pe_t2)])
        d_fl = dma("sync", srow_t[0:1, 0:256].rearrange("a (k f) -> a k f", f=128),
                   wpq.bitcast(BF16)[0:2, 128:256], sem="pF",
                   waits=[("ae", ae_sr)])
        pe_b3 = mm(ones_rb[0:1, 0:128], srow_t[0:1, 0:256], PS3[:, 0:256],
                   start=True, stop=True, waits=[d_fl, q_done[b % 4]], inc=True)
        tok_srow = ("pe", pe_b3)
        qd = None
        for k in range(NK):
            ve_q1 = vec(lambda eng, k=k, wp=wp: eng.tensor_tensor(
                wpabs[:, 0:256], wp[:, k * 256:(k + 1) * 256], PS3[:, 0:256],
                OP.mult), waits=([("pe", pe_b3)] if k == 0 else []))
            ve_q2 = vec(lambda eng: eng.tensor_scalar(
                wpabs[:, 0:256], wpabs[:, 0:256], MAGIC, -MAGIC, OP.add, OP.add))
            ve_q3 = vec(lambda eng, k=k: eng.tensor_copy(
                wpq.bitcast(BF16)[:, 512:768], wpabs[:, 0:256]),
                waits=[qd] if qd is not None else [])
            qd = dma("sync", winq_d.ap()[k * 128:(k + 1) * 128, c0:c0 + 256],
                     wpq.bitcast(BF16)[:, 512:768], sem="pF",
                     waits=[("ve", ve_q3)])
        q_done[b % 4] = qd
        return qd

    for blk in range(16):
        wp = wprepD[blk % 4]
        c0 = blk * 256
        d_wi = dma("gpsimd",
                   wp[:].rearrange("p (k f) -> p k f", f=256),
                   win_h.ap()[:, c0:c0 + 256].rearrange("(k p) f -> p k f", p=128),
                   sem="pI%d" % (blk % 4),
                   waits=[q_done[blk % 4]])
        sgen = win_scales(blk - 1) if blk > 0 else None
        ve_ab = vec(lambda eng, wp=wp: eng.tensor_tensor(
            wpabs[:, 0:256], wp[:, 0:256], wp[:, 256:512], OP.max),
            waits=[d_wi, ("pe", pe_prep0) if blk == 0 else ("pe", 0)])
        for k in range(2, NK):
            ve_ab = vec(lambda eng, k=k, wp=wp: eng.tensor_tensor(
                wpabs[:, 0:256], wpabs[:, 0:256], wp[:, k * 256:(k + 1) * 256],
                OP.max))
            if sgen and k in (3, 6): next(sgen)()
        ve_ab = vec(lambda eng, wp=wp: eng.tensor_tensor(
            wpq[:, 0:256], wp[:, 0:256], wp[:, 256:512], OP.min))
        for k in range(2, NK):
            ve_ab = vec(lambda eng, k=k, wp=wp: eng.tensor_tensor(
                wpq[:, 0:256], wpq[:, 0:256], wp[:, k * 256:(k + 1) * 256],
                OP.min))
            if sgen and k == 4: next(sgen)()
        ve_ab = vec(lambda eng: eng.tensor_scalar(
            wpq[:, 0:256], wpq[:, 0:256], -1.0, None, OP.mult))
        ve_ab = vec(lambda eng: eng.tensor_tensor(
            wpabs[:, 0:256], wpabs[:, 0:256], wpq[:, 0:256], OP.max))
        ve_ab = vec(lambda eng, blk=blk: eng.tensor_copy(
            wpq.bitcast(BF16)[:, 512:768], wpabs[:, 0:256]),
            waits=[q_done[(blk - 2) % 4]] if blk >= 2 else [])
        for q in range(2):
            P.emit("tensor", (lambda eng, q=q: eng.transpose(
                PS4.bitcast(BF16)[:, q * 128:(q + 1) * 128],
                wpq.bitcast(BF16)[:, 512 + q * 128:512 + (q + 1) * 128],
                identb[:])),
                waits=([("ve", ve_ab), ("ve", ve_id2),
                        ("pe", pe_prep0) if blk == 0 else ("pe", 0),
                        ae_tr_tok] if q == 0 else ()),
                incs=[("pe", 1)])
        pe_tr = P.v("pe")
        ve_mx = 0
        for q in range(2):
            ve_mx = vec(lambda eng, blk=blk, q=q: eng.reduce_max(
                sc32[:, blk * 2 + q:blk * 2 + q + 1],
                PS4.bitcast(BF16)[:, q * 128:(q + 1) * 128], axis=AX.X),
                waits=([("pe", pe_tr)] if q == 0 else ()))
        ae_tr_tok = ("ve", ve_mx)
        if blk > 0:
            win_quant(blk - 1)
    for f in win_scales(15):
        f()
        fence()
        fence()
    win_quant(15)
    ve_rs = vec(lambda eng: eng.reciprocal(rs32[:], sc32[:]))
    fence()
    fence()
    ve_rs = vec(lambda eng: eng.tensor_scalar(
        rs32[:], rs32[:], red[:, 34:35], None, OP.mult))
    pe_prep_done = P.v("pe")
    ae_prep_done = P.v("ae")

    if stage == "l1_prep":
        return early_out([("pe", pe_prep_done), ("ae", ae_prep_done),
                          ("ve", P.v("ve")), (QSEM["sync"], P.v("dS")),
                          ("pF", P.v("pF"))])
    hT = {}
    ae_ffn0 = P.v("ae")
    qdall = list(q_done)
    amC = [amx32, sc32, cmb32, wpq[:, 64:96]]
    ve_xq = 0
    for tl in range(4):
        for k in range(NK):
            ve_x1 = vec(lambda eng, k=k, tl=tl: eng.tensor_scalar(
                wpabs[:], xnv(k, tl * 512, tl * 512 + 512), red[:, 33:34], MAGIC,
                OP.mult, OP.add),
                waits=([("ve", ve_scales), ("pe", pe_prep_done),
                        ("ae", ae_prep_done)] if (tl == 0 and k == 0) else []))
            ve_xq = vec(lambda eng, k=k, tl=tl: eng.tensor_scalar(
                xq4[:, tl * 4096 + k * 512:tl * 4096 + (k + 1) * 512],
                wpabs[:], -MAGIC, None, OP.add))
    for hch in range(NH1):
        sl = hch % 2
        d_w1 = col_block_dma(w1slb[:, sl * 1024:(sl + 1) * 1024], winq_d,
                             hch * 128, 128, NK, sem="fI%d" % sl, q="scalar",
                             waits=([hT[("w", sl)]] if ("w", sl) in hT else qdall))
        pe_h = 0
        for tl in range(4):
            ps = PS0 if tl % 2 == 0 else PS1
            slh = tl % 2
            for k in range(NK):
                pw = []
                if k == 0:
                    pw = [hT.get(("ps", slh), ("ae", ae_ffn0))]
                    if tl == 0:
                        pw += [d_w1, ("ve", ve_xq)]
                pe_h = mm(w1slb[:, sl * 1024 + k * 128: sl * 1024 + (k + 1) * 128],
                          xq4[:, tl * 4096 + k * 512: tl * 4096 + (k + 1) * 512],
                          ps[:, 0:512],
                          start=(k == 0), stop=(k == NK - 1), waits=pw,
                          inc=(k == NK - 1))
            ae_h = act(lambda eng, ps=ps, slh=slh, hch=hch: eng.activation(
                hidt1[:, slh * 512:(slh + 1) * 512], ps[:, 0:512],
                AF.Gelu, scale=rs32[:, hch:hch + 1],
                bias=bcol1[:, 32 + hch:33 + hch]),
                waits=[("pe", pe_h), hT.get(("hd", slh), ("pe", 0)),
                       hT.get(("am", slh), ("pe", 0))])
            hT[("ps", slh)] = ("ae", ae_h)
            ve_hm = vec(lambda eng, slh=slh, hch=hch, tl=tl: eng.reduce_max(
                amC[tl][:, hch:hch + 1], hidt1[:, slh * 512:(slh + 1) * 512],
                axis=AX.X, apply_absolute_value=True), waits=[("ae", ae_h)])
            hT[("am", slh)] = ("ve", ve_hm)
            d_ho = dma("sync",
                       hid_d.ap()[hch * 128:(hch + 1) * 128, tl * 512:(tl + 1) * 512],
                       hidt1[:, slh * 512:(slh + 1) * 512], sem="hd%d" % slh,
                       waits=[("ve", ve_hm)])
            hT[("hd", slh)] = d_ho
        hT[("w", sl)] = ("pe", pe_h)
    ve_fold = fence()
    for tl in range(4):
        ve_fold = vec(lambda eng, tl=tl: eng.reduce_max(
            red[:, 40 + tl:41 + tl], amC[tl][:], axis=AX.X))
    ve_fold = fence()
    ve_fold = vec(lambda eng: eng.reduce_max(
        red[:, 44:45], red[:, 40:44], axis=AX.X))
    ve_fold = fence()
    ve_fb = vec(lambda eng: eng.tensor_copy(summ.bitcast(BF16)[:, 2:3],
                                            red[:, 44:45]))
    P.emit("tensor", (lambda eng: eng.transpose(
        PS4.bitcast(BF16)[0:1, 0:128], summ.bitcast(BF16)[:, 2:3], identb[:])),
        waits=[("ve", ve_fb), tok_srow, ("ae", P.v("ae"))], incs=[("pe", 1)])
    pe_am2 = P.v("pe")
    ve_am3 = vec(lambda eng: eng.reduce_max(
        smax[0:1, 2:3], PS4.bitcast(BF16)[0:1, 0:128], axis=AX.X),
        waits=[("pe", pe_am2)])
    g_a2 = dma("gpsimd", ag2_in.ap()[0:1, 0:1], smax[0:1, 2:3],
               waits=[("ve", ve_am3)], sem="dB")
    P.emit("gpsimd", lambda eng: eng.collective_compute(
        "AllGather", OP.bypass, replica_groups=[list(range(NCORES))],
        ins=[ag2_in.ap()], outs=[ag2_out.ap()]),
        waits=[g_a2], incs=[("cc", 1)])
    cc2 = P.v("cc")
    d_a2 = dma("gpsimd", smax[0:8, 10:11], ag2_out.ap()[:, 0:1], sem="dB",
               waits=[("cc", cc2)], slow=True)
    d_a2f = dma("gpsimd", smax[0:1, 32:40], smax[0:8, 10:11], sem="dB",
                waits=[d_a2], slow=True)
    ve_g2 = vec(lambda eng: eng.reduce_max(
        smax[0:1, 3:4], smax[0:1, 32:40], axis=AX.X), waits=[d_a2f])
    ve_g2 = vec(lambda eng: eng.tensor_scalar(
        smax[0:1, 4:5], smax[0:1, 3:4], 1.0 / 32767.0, None, OP.mult))
    pe_bc3 = mm(ones_row[0:1, 0:128], smax[0:1, 3:5], PS4[:, 0:2],
                start=True, stop=True,
                waits=[("ve", ve_g2), ("ve", ve_am3)], inc=True)
    ve_s2 = vec(lambda eng: eng.reciprocal(red[:, 45:46], PS4[:, 0:1]),
                waits=[("pe", pe_bc3)])
    ve_s2 = vec(lambda eng: eng.tensor_copy(red[:, 47:48], PS4[:, 1:2]))
    ve_s2 = vec(lambda eng: eng.tensor_scalar(
        red[:, 46:47], red[:, 45:46], 32767.0, None, OP.mult))
    ve_cmb = vec(lambda eng: eng.tensor_scalar(
        cmb32[:], rsoC[:], red[:, 47:48], None, OP.mult),
        waits=[("ve", ve_rso)])
    ve_sc2done = ve_cmb

    if stage == "l1_f1":
        return early_out([("ve", ve_sc2done)])
    f2 = {}
    ve_final = 0
    psl = [PS0[:, 0:512], PS0[:, 512:1024], PS1[:, 0:512], PS1[:, 512:1024],
           PS2[:, 0:512], PS2[:, 512:1024], PS3[:, 0:512], PS4[:, 0:512]]
    for tt2 in range(4):
        t0 = tt2 * 512
        pe_d2 = [0] * NK
        for hch in range(NH1):
            sl = hch % 2
            d_w2 = dma("sync", w2sl[:, sl * 1024:(sl + 1) * 1024],
                       woutq_d.ap()[hch * 128:(hch + 1) * 128, :],
                       sem="f2W%d" % sl,
                       waits=[f2.get(("w", sl), ("pe", pe_am2))])
            d_h2 = dma("scalar", hidt[:, sl * 512:(sl + 1) * 512],
                       hid_d.ap()[hch * 128:(hch + 1) * 128, t0:t0 + 512],
                       sem="f2H%d" % sl,
                       waits=[f2.get(("h", sl), ("ve", ve_sc2done))])
            ve_hq = vec(lambda eng, sl=sl: eng.tensor_scalar(
                qtf[:, sl * 512:(sl + 1) * 512],
                hidt[:, sl * 512:(sl + 1) * 512],
                red[:, 46:47], MAGIC, OP.mult, OP.add),
                waits=[d_h2] + ([("ve", ve_sc2done)] if (tt2 == 0 and hch < 2) else []))
            ve_hq2 = vec(lambda eng, sl=sl, hch=hch: eng.tensor_scalar(
                hidt[:, sl * 512:(sl + 1) * 512],
                qtf[:, sl * 512:(sl + 1) * 512],
                -MAGIC, cmb32[:, hch:hch + 1], OP.add, OP.mult))
            for dch in range(NK):
                pw = []
                if dch == 0:
                    pw = [d_w2, ("ve", ve_hq2)]
                    if hch == 0:
                        pw.append(f2.get("ps_all", ("pe", pe_am2)))
                pe_d2[dch] = mm(w2sl[:, sl * 1024 + dch * 128: sl * 1024 + (dch + 1) * 128],
                                hidt[:, sl * 512:(sl + 1) * 512], psl[dch],
                                start=(hch == 0), stop=(hch == NH1 - 1),
                                waits=pw, inc=True)
            f2[("w", sl)] = ("pe", pe_d2[NK - 1])
            f2[("h", sl)] = ("pe", pe_d2[NK - 1])
        for dch in range(NK):
            ve_final = vec(lambda eng, dch=dch, t0=t0: eng.scalar_tensor_tensor(
                xr(dch, t0, t0 + 512), psl[dch], bcol1[:, 24 + dch:25 + dch],
                xr(dch, t0, t0 + 512), OP.add, OP.add),
                waits=[("pe", pe_d2[dch])])
        f2["ps_all"] = ("ve", ve_final)

    last = None
    for k in range(NK):
        last = dma("sync", out_h.ap()[k * 128:(k + 1) * 128, :], xr(k),
                   waits=[("ve", ve_final)])
    P.emit("sync", lambda eng: None, waits=[("dS", last[1])])
    emit_all(nc, P)
    es.close()
    return nc


def emit_all(nc, P):
    with nc.Block() as block:
        @block.sync
        def _(eng):
            for f in P.ops["sync"]:
                f(eng)

        @block.scalar
        def _(eng):
            for f in P.ops["scalar"]:
                f(eng)

        @block.vector
        def _(eng):
            for f in P.ops["vector"]:
                f(eng)

        @block.tensor
        def _(eng):
            for f in P.ops["tensor"]:
                f(eng)

        @block.gpsimd
        def _(eng):
            for f in P.ops["gpsimd"]:
                f(eng)
    return nc


def make_inputs(inp):
    C = np.ascontiguousarray

    def A(v):
        return np.asarray(v)

    x = A(inp["L0_attn_x"])
    shared = {
        "wgT": C(A(inp["L0_attn_Wg"]).T), "wvT": C(A(inp["L0_attn_Wv"]).T),
        "wdT": C(A(inp["L0_attn_Wd"]).T),
        "bcols0": np.concatenate([
            A(inp["L0_attn_bg"]).reshape(8, 128).T,
            A(inp["L0_attn_bv"]).reshape(8, 128).T,
            A(inp["L0_attn_bd"]).reshape(8, 128).T], axis=1).copy(),
        "bcols1": np.concatenate([
            A(inp["L1_attn_bq"]).reshape(8, 128).T,
            A(inp["L1_attn_bk"]).reshape(8, 128).T,
            A(inp["L1_attn_bo"]).reshape(8, 128).T,
            A(inp["L1_ffn_b_out"]).reshape(8, 128).T,
            A(inp["L1_ffn_b_in"]).reshape(32, 128).T], axis=1).copy(),
        "fgT": C(A(inp["L0_ffn_w_gate"]).T), "fuT": C(A(inp["L0_ffn_w_up"]).T),
        "fdT": C(A(inp["L0_ffn_w_down"]).T),
        "wq": C(A(inp["L1_attn_Wq"])), "wk": C(A(inp["L1_attn_Wk"])),
        "wv1": C(A(inp["L1_attn_Wv"])), "wo": C(A(inp["L1_attn_Wo"])),
        "bk": A(inp["L1_attn_bk"]).reshape(1, -1),
        "bv1": A(inp["L1_attn_bv"]).reshape(1, -1),
        "win": C(A(inp["L1_ffn_W_in"])), "wout": C(A(inp["L1_ffn_W_out"])),
    }
    maps = []
    for core in range(NCORES):
        b, j = divmod(core, 4)
        xs = C(x[b, j * T:(j + 1) * T, :].T)
        params = np.zeros((128, 24), np.float32)
        for r in range(NCORES):
            rb, rj = divmod(r, 4)
            params[:, r] = 1.0 if (rb == b and rj < j) else 0.0
            params[:, 9 + r] = 1.0 - params[:, r]
        params[:, 8] = (j * T) * ANG_SCALE
        params[:, 17] = (j * T) * ANG_SCALE + np.pi / 2.0
        m = dict(shared)
        m["x"] = xs
        m["params"] = params
        maps.append(m)
    return maps


def _run(stage, inp):
    maps = make_inputs(inp)
    try:
        nc = build_program(stage)
    except _EarlyStop as e:
        nc, P, es, dD = e.args
        P.emit("sync", lambda eng: None, waits=[("dD", dD)])
        emit_all(nc, P)
        es.close()
    res = run_bass_kernel_spmd(nc, maps, core_ids=list(range(NCORES)))
    out = np.empty((2, 8192, 1024), np.float32)
    for core in range(NCORES):
        b, j = divmod(core, 4)
        out[b, j * T:(j + 1) * T, :] = res.results[core]["out"].T
    return out


def kernel(**inp):
    return _run("full", inp)


# version 15
# speedup vs baseline: 1.2446x; 1.0351x over previous
import numpy as np
from contextlib import ExitStack

import concourse.bass as bass
import concourse.mybir as mybir
from concourse.bass_utils import run_bass_kernel_spmd

F32 = mybir.dt.float32
BF16 = mybir.dt.bfloat16
I32 = mybir.dt.int32
AF = mybir.ActivationFunctionType
OP = mybir.AluOpType
AX = mybir.AxisListType

NCORES = 8
D = 1024
T = 2048
NK = 8
NH = 16
DH = 64
CHUNK = 512
NCH = T // CHUNK
HFF0 = 3072
NH0 = HFF0 // 128
HFF1 = 4096
NH1 = HFF1 // 128
NST = NH * 65
MAGIC = 12582912.0
PI = float(np.pi)
ANG_SCALE = PI / (2.0 * 8192.0)

ENGINES = ("sync", "scalar", "vector", "tensor", "gpsimd")
QSEM = {"sync": "dS", "scalar": "dA", "gpsimd": "dG"}


class Plan:
    def __init__(self):
        self.ops = {e: [] for e in ENGINES}
        self.cnt = {}
        self.sems = {}

    def newsem(self, *names):
        for n in names:
            self.cnt[n] = 0

    def v(self, name):
        return self.cnt[name]

    def emit(self, engine, fn, waits=(), incs=()):
        waits = tuple((s, v) for (s, v) in waits if v > 0)
        for (s, n) in incs:
            self.cnt[s] += n
        def run(eng, _fn=fn, _w=waits, _i=tuple(incs), _sems=self.sems):
            for (s, v) in _w:
                eng.wait_ge(_sems[s], v)
            inst = _fn(eng)
            for (s, n) in _i:
                inst = inst.then_inc(_sems[s], n)

        self.ops[engine].append(run)


class Banks:

    def __init__(self, slots, tok=("pe", 0)):
        self.slots = list(slots)
        self.tok = [tok] * len(slots)
        self.i = 0

    def get(self):
        j = self.i % len(self.slots)
        self.i += 1
        return j, self.slots[j], self.tok[j]

    def set_tok(self, j, tok):
        self.tok[j] = tok


class _EarlyStop(Exception):
    pass


def build_program(stage="full"):
    do_attn = stage in ("full", "l0", "l0_attn", "l0_hin") or stage.startswith("l1")
    do_ffn = stage in ("full", "l0", "l0_ffn", "l0_hid") or stage.startswith("l1")
    nc = bass.Bass("TRN2", target_bir_lowering=False, debug=False)
    P = Plan()
    P.newsem("dS", "dA", "dG", "dB", "pe", "ve", "ae", "ge", "cc", "st")

    x_h = nc.dram_tensor("x", [D, T], F32, kind="ExternalInput")
    out_h = nc.dram_tensor("out", [D, T], F32, kind="ExternalOutput")
    params_h = nc.dram_tensor("params", [128, 24], F32, kind="ExternalInput")
    wgT_h = nc.dram_tensor("wgT", [D, D], F32, kind="ExternalInput")
    wvT_h = nc.dram_tensor("wvT", [D, D], F32, kind="ExternalInput")
    wdT_h = nc.dram_tensor("wdT", [D, D], F32, kind="ExternalInput")
    bcols0_h = nc.dram_tensor("bcols0", [128, 24], F32, kind="ExternalInput")
    bcols1_h = nc.dram_tensor("bcols1", [128, 64], F32, kind="ExternalInput")
    fgT_h = nc.dram_tensor("fgT", [D, HFF0], I32, kind="ExternalInput")
    fuT_h = nc.dram_tensor("fuT", [D, HFF0], I32, kind="ExternalInput")
    fdT_h = nc.dram_tensor("fdT", [HFF0, D], I32, kind="ExternalInput")
    wq_h = nc.dram_tensor("wq", [D, D], F32, kind="ExternalInput")
    wk_h = nc.dram_tensor("wk", [D, D], F32, kind="ExternalInput")
    wv1_h = nc.dram_tensor("wv1", [D, D], F32, kind="ExternalInput")
    wo_h = nc.dram_tensor("wo", [D, D], F32, kind="ExternalInput")
    bk_h = nc.dram_tensor("bk", [1, D], F32, kind="ExternalInput")
    bv1_h = nc.dram_tensor("bv1", [1, D], F32, kind="ExternalInput")
    win_h = nc.dram_tensor("win", [D, HFF1], F32, kind="ExternalInput")
    wout_h = nc.dram_tensor("wout", [HFF1, D], F32, kind="ExternalInput")

    ag0_in = nc.dram_tensor("ag0_in", [1, 128 * 16], F32)
    ag0_out = nc.dram_tensor("ag0_out", [NCORES, 128 * 16], F32)
    ag1_in = nc.dram_tensor("ag1_in", [1, 128 * NST], F32)
    ag1_out = nc.dram_tensor("ag1_out", [NCORES, 128 * NST], F32)
    ag1b_in = nc.dram_tensor("ag1b_in", [1, 8], F32)
    ag1b_out = nc.dram_tensor("ag1b_out", [NCORES, 8], F32)
    ag2_in = nc.dram_tensor("ag2_in", [1, 8], F32)
    ag2_out = nc.dram_tensor("ag2_out", [NCORES, 8], F32)
    winq_d = nc.dram_tensor("winq_d", [D, HFF1], BF16)
    woutq_d = nc.dram_tensor("woutq_d", [HFF1, D], BF16)
    sinrow_d = nc.dram_tensor("sinrow_d", [1, HFF1], F32)
    hid_d = nc.dram_tensor("hid_d", [HFF1, T], BF16)

    es = ExitStack()

    def sbt(name, shape, dtype):
        return es.enter_context(nc.sbuf_tensor(name, shape, dtype))

    xres = sbt("xres", [128, NK * T], F32)
    xn = sbt("xn", [128, NK * T], BF16)
    bufA = sbt("bufA", [128, NK * T], BF16)
    bufB = sbt("bufB", [128, NK * T], BF16)
    ones_row = sbt("ones_row", [1, 128], F32)
    ones_col = sbt("ones_col", [128, 1], F32)
    prm = sbt("prm", [128, 24], F32)
    bcolL0 = sbt("bcolL0", [128, 3 * NK], F32)
    bcol1 = sbt("bcol1", [128, 64], F32)
    tri = sbt("tri", [128, 512], BF16)
    summ = sbt("summ", [128, 16], F32)
    hin = sbt("hin", [128, 8], F32)
    agbuf = sbt("agbuf", [128, 16 * NCORES], F32)
    red = sbt("red", [128, 144], F32)
    smax = sbt("smax", [8, 48], F32)
    identb = sbt("identb", [128, 128], BF16)
    ones_rb = sbt("ones_rb", [1, 128], BF16)

    ARENA_B = 45056
    arena = sbt("arena", [128, ARENA_B // 4], F32)
    abase = nc.lookup_mloc(arena).addr

    def aview(name, shape, dtype, off):
        assert off + shape[-1] * mybir.dt.size(dtype) * int(np.prod(shape[1:-1], initial=1)) <= ARENA_B
        return nc.alloc_sbuf_tensor_at(name, shape, dtype, offset=abase + off)

    KB = 1024
    lnsq = aview("lnsq", [128, 4096], F32, 0)
    lnrw = aview("lnrw", [128, 4096], F32, 16 * KB)
    a_row = aview("a_row", [128, T], F32, 0)
    xc_row = aview("xc_row", [128, T], F32, 8 * KB)
    g_row = aview("g_row", [128, T], F32, 16 * KB)
    vh_row = aview("vh_row", [128, T], F32, 24 * KB)
    wslA = aview("wslA", [128, 6144], BF16, 32 * KB)
    ffn_h = aview("ffn_h", [128, NH0 * 512], BF16, 0)
    ffn_gu = aview("ffn_gu", [128, 4096], BF16, 24 * KB)
    ffn_dn = aview("ffn_dn", [128, 2 * 1536], BF16, 32 * KB)
    gact = aview("gact", [128, 1024], F32, 38 * KB)
    corr_t = aview("corr_t", [128, T], F32, 0)

    PS0 = es.enter_context(nc.psum_tensor("PS0", [128, 1024], F32))
    PS1 = es.enter_context(nc.psum_tensor("PS1", [128, 1024], F32))
    PS2 = es.enter_context(nc.psum_tensor("PS2", [128, 1024], F32))
    PS3 = es.enter_context(nc.psum_tensor("PS3", [128, 512], F32))
    PS4 = es.enter_context(nc.psum_tensor("PS4", [128, 512], F32))

    for s in list(P.cnt.keys()):
        P.sems[s] = es.enter_context(nc.semaphore(s))

    def dma(q, out_ap, in_ap, waits=(), sem=None, slow=False):
        s = sem or QSEM[q]
        if s not in P.cnt:
            P.newsem(s)
            P.sems[s] = es.enter_context(nc.semaphore(s))
        P.emit(q, (lambda eng, o=out_ap, i=in_ap, sl=slow: eng.dma_start(
            out=o, in_=i, allow_slow_non_contiguous=sl)),
               waits=waits, incs=[(s, 16)])
        return (s, P.v(s))

    def mm(lhsT, rhs, out, start, stop, waits=(), inc=False):
        incs = [("pe", 1)] if inc else ()
        P.emit("tensor",
               (lambda eng, l=lhsT, r=rhs, o=out, s=start, p=stop:
                eng.matmul(o, l, r, start=s, stop=p)),
               waits=waits, incs=incs)
        return P.v("pe")

    def act(fn, waits=()):
        P.emit("scalar", fn, waits=waits, incs=[("ae", 1)])
        return P.v("ae")

    def vec(fn, waits=()):
        P.emit("vector", fn, waits=waits, incs=[("ve", 1)])
        return P.v("ve")

    def gp(fn, waits=()):
        P.emit("gpsimd", fn, waits=waits, incs=[("ge", 1)])
        return P.v("ge")

    def col_block_dma(dst_tile, src_h, col0, ncols, nk, waits=(), row0=0,
                      sem=None, q="gpsimd"):
        o = dst_tile.rearrange("p (k f) -> p k f", f=ncols)
        i = src_h.ap()[row0:row0 + nk * 128, col0:col0 + ncols].rearrange(
            "(k p) f -> p k f", p=128)
        return dma(q, o, i, waits, sem=sem)

    def xr(k, t0=0, t1=T):
        return xres[:, k * T + t0: k * T + t1]

    def xnv(k, t0=0, t1=T):
        return xn[:, k * T + t0: k * T + t1]

    setA = Banks([PS0[:, 0:512], PS0[:, 512:1024], PS1[:, 0:512], PS1[:, 512:1024]])
    setB = Banks([PS2[:, 0:512], PS2[:, 512:1024], PS3[:, 0:512], PS4[:, 0:512]])

    xw = []
    for k in range(NK):
        xw.append(dma("sync", xr(k), x_h.ap()[k * 128:(k + 1) * 128, :]))
    dma("sync", prm[:], params_h.ap())
    dma("sync", bcolL0[:], bcols0_h.ap())
    w_const = dma("sync", bcol1[:], bcols1_h.ap())

    vec(lambda eng: eng.memset(ones_row[:], 1.0))
    vec(lambda eng: eng.memset(ones_col[:], 1.0))
    v_tri = vec(lambda eng: eng.memset(tri[:], 1.0))
    g_tri = gp(lambda eng: eng.iota(
        agbuf[:, 0:128], pattern=[[-1, 128]], base=0, channel_multiplier=1,
        allow_small_or_imprecise_dtypes=True), waits=[("ve", v_tri)])
    vec(lambda eng: eng.tensor_single_scalar(
        tri[:, 0:128], agbuf[:, 0:128], 0.0, OP.is_le), waits=[("ge", g_tri)])

    def emit_layernorm(first_waits, stats_tok=("pe", 0)):
        bcR = lnrw[:, 0:512]
        bcM = lnrw[:, 512:1024]
        tmp = lnrw[:, 1024:1536]
        R0 = lnrw[0:1, 1536:2048]
        R1 = lnrw[0:1, 2048:2560]
        R2 = lnrw[0:1, 2560:3072]
        xn_done = 0
        for tt in range(4):
            t0 = tt * 512
            ae_sq = 0
            for k in range(NK):
                w = list(first_waits) if (tt == 0 and k == 0) else []
                ae_sq = act(lambda eng, k=k, t0=t0: eng.activation(
                    lnsq[:, k * 512:(k + 1) * 512], xr(k, t0, t0 + 512),
                    AF.Square), waits=w)
            first_w = [("ae", ae_sq)] + ([stats_tok] if tt == 0 else [])
            for k in range(NK):
                mm(ones_col[:], xr(k, t0, t0 + 512), PS3[0:1, 0:512],
                   start=(k == 0), stop=(k == NK - 1),
                   waits=(first_w if k == 0 else ()))
            pe_s = 0
            for k in range(NK):
                pe_s = mm(ones_col[:], lnsq[:, k * 512:(k + 1) * 512],
                          PS4[0:1, 0:512], start=(k == 0), stop=(k == NK - 1),
                          inc=(k == NK - 1))
            vec(lambda eng, R0=R0: eng.tensor_scalar(
                R0, PS3[0:1, 0:512], 1.0 / D, None, OP.mult),
                waits=[("pe", pe_s)])
            vec(lambda eng, R1=R1: eng.tensor_scalar(
                R1, PS4[0:1, 0:512], 1.0 / D, 1e-5, OP.mult, OP.add))
            vec(lambda eng, R0=R0, R2=R2: eng.scalar_tensor_tensor(
                R2, R0, -1.0, R0, OP.mult, OP.mult))
            ve_var = vec(lambda eng, R1=R1, R2=R2: eng.tensor_add(R1, R1, R2))
            ae_v = act(lambda eng, R1=R1, R2=R2: eng.activation(R2, R1, AF.Sqrt),
                       waits=[("ve", ve_var)])
            vec(lambda eng, R1=R1, R2=R2: eng.reciprocal(R1, R2),
                waits=[("ae", ae_v)])
            ve_r = vec(lambda eng, R0=R0, R1=R1: eng.scalar_tensor_tensor(
                R0, R0, -1.0, R1, OP.mult, OP.mult))
            mm(ones_row[0:1, 0:128], R1, PS4[:, 0:512], start=True, stop=True,
               waits=[("ve", ve_r)])
            pe_b = mm(ones_row[0:1, 0:128], R0, PS3[:, 0:512],
                      start=True, stop=True, inc=True)
            act(lambda eng, bcR=bcR: eng.activation(bcR, PS4[:, 0:512], AF.Copy),
                waits=[("pe", pe_b)])
            ae_b = act(lambda eng, bcM=bcM: eng.activation(bcM, PS3[:, 0:512], AF.Copy))
            for k in range(NK):
                vec(lambda eng, k=k, t0=t0, tmp=tmp, bcR=bcR: eng.tensor_mul(
                    tmp, xr(k, t0, t0 + 512), bcR),
                    waits=([("ae", ae_b)] if k == 0 else []))
                xn_done = vec(lambda eng, k=k, t0=t0, tmp=tmp, bcM=bcM: eng.tensor_add(
                    xnv(k, t0, t0 + 512), tmp, bcM))
        return xn_done

    xn_all = emit_layernorm(list(xw) + [w_const])
    setB.tok[2] = ("ae", P.v("ae"))
    setB.tok[3] = ("ae", P.v("ae"))

    def l0_wslot(m, b):
        c0 = (m * 2 + b) * 1024
        return wslA[:, c0:c0 + 1024]

    wslot_tok = {}
    ve_rows_done = 0
    for e in (range(NK) if do_attn else ()):
        b = e % 2
        dtok = {}
        for m, src in enumerate((wdT_h, wvT_h, wgT_h)):
            dtok[m] = col_block_dma(l0_wslot(m, b), src, e * 128, 128, NK,
                                    waits=[wslot_tok.get((m, b), ("pe", 0))],
                                    sem="wA%d" % (m * 2 + b))
        for m in range(3):
            bs = setA if (e * 3 + m) % 2 == 0 else setB
            wslot = l0_wslot(m, b)
            groups = {}
            for tt in range(4):
                j, ap, tok = bs.get()
                pw = [dtok[m], tok]
                if m == 0 and e == 0:
                    pw.append(("ve", xn_all))
                pe_last = 0
                for k in range(NK):
                    pe_last = mm(wslot[:, k * 128:(k + 1) * 128],
                                 xnv(k, tt * 512, tt * 512 + 512), ap,
                                 start=(k == 0), stop=(k == NK - 1),
                                 waits=(pw if k == 0 else ()),
                                 inc=(k == NK - 1))
                groups[tt] = (pe_last, ap, j)
            wslot_tok[(m, b)] = ("pe", groups[3][0])
            dst = (a_row, vh_row, g_row)[m]
            bofs = (2 * NK, NK, 0)[m]
            for tt in range(4):
                pe_g, ap, j = groups[tt]
                w = [("pe", pe_g)]
                if tt == 0 and m == 0:
                    w.append(("ve", ve_rows_done))
                if m == 1:
                    ve_t = vec(lambda eng, dst=dst, tt=tt, ap=ap, bo=bofs, e=e:
                               eng.tensor_scalar(
                                   dst[:, tt * 512:tt * 512 + 512], ap,
                                   bcolL0[:, bo + e:bo + e + 1], None, OP.add),
                               waits=w)
                    bs.set_tok(j, ("ve", ve_t))
                else:
                    ae_t = act(lambda eng, dst=dst, tt=tt, ap=ap, bo=bofs, e=e:
                               eng.activation(dst[:, tt * 512:tt * 512 + 512], ap,
                                              AF.Sigmoid,
                                              bias=bcolL0[:, bo + e:bo + e + 1]),
                               waits=w)
                    bs.set_tok(j, ("ae", ae_t))
        ae_evacs = P.v("ae")
        vec(lambda eng: eng.tensor_scalar(
            xc_row[:], a_row[:], -1.0, 1.0, OP.mult, OP.add),
            waits=[("ae", ae_evacs)])
        vec(lambda eng: eng.tensor_mul(xc_row[:], xc_row[:], vh_row[:]))
        vec(lambda eng: eng.tensor_tensor_scan(
            vh_row[:], a_row[:], xc_row[:], 0.0, OP.mult, OP.add))
        vec(lambda eng: eng.tensor_tensor_scan(
            xc_row[:], a_row[:], a_row[:], 1.0, OP.mult, OP.bypass))
        vec(lambda eng, e=e: eng.tensor_mul(bufA[:, e * T:(e + 1) * T], vh_row[:], g_row[:]))
        vec(lambda eng, e=e: eng.tensor_mul(
            bufB[:, e * T:(e + 1) * T], xc_row[:], g_row[:]))
        vec(lambda eng, e=e: eng.tensor_copy(summ[:, e:e + 1], xc_row[:, T - 1:T]))
        ve_rows_done = vec(lambda eng, e=e: eng.tensor_copy(
            summ[:, 8 + e:8 + e + 1], vh_row[:, T - 1:T]))
    ve_summary = ve_rows_done
    pe_a1_done = P.v("pe")

    if not do_attn:
        ve_summary = vec(lambda eng: eng.memset(summ[:], 0.0))
    g_ag0 = dma("gpsimd", ag0_in.ap().rearrange("a (p c) -> (a p) c", p=128),
                summ[:], waits=[("ve", ve_summary)])
    P.emit("gpsimd", lambda eng: eng.collective_compute(
        "AllGather", OP.bypass, replica_groups=[list(range(NCORES))],
        ins=[ag0_in.ap()], outs=[ag0_out.ap()]),
        waits=[("dG", g_ag0[1])], incs=[("cc", 1)])
    cc0 = P.v("cc")

    gu_tok = {}
    dn_tok = {}
    tok_g = tok_u = tok_d = ("ae", P.v("ae"))
    ve_ffn = 0
    pe_dn_last = 0
    first_gu_dma = True
    for tp in (range(1) if stage == "l0_hid" else (range(4) if do_ffn else ())):
        t0 = tp * 512
        h_last = 0
        for hch in range(NH0):
            sg = hch % 2
            wg = [gu_tok.get(("g", sg), ("pe", 0))]
            if first_gu_dma:
                wg.append(("ve", ve_summary))
                wg.append(("ve", xn_all))
                first_gu_dma = False
            dg = col_block_dma(ffn_gu[:, sg * 1024:(sg + 1) * 1024], fgT_h,
                               hch * 128, 128, NK, waits=wg, sem="fg%d" % sg)
            du = col_block_dma(ffn_gu[:, (2 + sg) * 1024:(3 + sg) * 1024], fuT_h,
                               hch * 128, 128, NK,
                               waits=[gu_tok.get(("u", sg), ("pe", 0))],
                               sem="fu%d" % sg)
            pw_g = [dg, tok_g]
            pw_u = [du, tok_u]
            if tp == 0 and hch == 0:
                pw_g = pw_g + [("ve", xn_all)]
            pe_g = pe_u = 0
            for k in range(NK):
                pe_g = mm(ffn_gu[:, sg * 1024 + k * 128: sg * 1024 + (k + 1) * 128],
                          xnv(k, t0, t0 + 512), PS0[:, 0:512],
                          start=(k == 0), stop=(k == NK - 1),
                          waits=(pw_g if k == 0 else ()), inc=(k == NK - 1))
            gu_tok[("g", sg)] = ("pe", pe_g)
            for k in range(NK):
                pe_u = mm(ffn_gu[:, (2 + sg) * 1024 + k * 128: (2 + sg) * 1024 + (k + 1) * 128],
                          xnv(k, t0, t0 + 512), PS1[:, 0:512],
                          start=(k == 0), stop=(k == NK - 1),
                          waits=(pw_u if k == 0 else ()), inc=(k == NK - 1))
            gu_tok[("u", sg)] = ("pe", pe_u)
            w_s = [("pe", pe_g), tok_g, ("ve", h_last)]
            if tp > 0 and hch == 0:
                w_s.append(("pe", pe_dn_last))
            ae_s = act(lambda eng: eng.activation(
                gact[:, 0:512], PS0[:, 0:512], AF.Silu, scale=1.0 / 127.0),
                waits=w_s)
            tok_g = ("ae", ae_s)
            h_last = vec(lambda eng, hch=hch: eng.scalar_tensor_tensor(
                ffn_h[:, hch * 512:(hch + 1) * 512], PS1[:, 0:512],
                1.0 / 127.0, gact[:, 0:512], OP.mult, OP.mult),
                waits=[("ae", ae_s), ("pe", pe_u)])
            tok_u = ("ve", h_last)
        for dch in range(NK):
            pe_d = 0
            for half in range(2):
                hb = half
                dn_w = [dn_tok.get(hb, ("pe", pe_a1_done))]
                if hb not in dn_tok:
                    dn_w.append(("ve", ve_summary))
                dd = col_block_dma(ffn_dn[:, hb * 1536:(hb + 1) * 1536], fdT_h,
                                   dch * 128, 128, 12,
                                   row0=half * 1536, waits=dn_w,
                                   sem="fd%d" % hb)
                for k in range(12):
                    kk = half * 12 + k
                    pe_d = mm(ffn_dn[:, hb * 1536 + k * 128: hb * 1536 + (k + 1) * 128],
                              ffn_h[:, kk * 512:(kk + 1) * 512],
                              PS2[:, 0:512],
                              start=(kk == 0), stop=(kk == NH0 - 1),
                              waits=([dd] +
                                     ([("ve", h_last), tok_d] if kk == 0 else []))
                              if k == 0 else (),
                              inc=(k == 11))
                dn_tok[hb] = ("pe", pe_d)
            pe_dn_last = pe_d
            ve_ffn = vec(lambda eng, dch=dch, t0=t0: eng.scalar_tensor_tensor(
                xr(dch, t0, t0 + 512), PS2[:, 0:512], 1.0 / 127.0,
                xr(dch, t0, t0 + 512), OP.mult, OP.add),
                waits=[("pe", pe_d)])
            tok_d = ("ve", ve_ffn)

    P.emit("gpsimd", (lambda eng: eng.dma_start(
        out=agbuf[:].rearrange("p (r c) -> p r c", c=16),
        in_=ag0_out.ap().rearrange("r (p c) -> p r c", p=128))),
        waits=[("cc", cc0)], incs=[("dB", 16)])
    g_back_v = P.v("dB")
    ve_prep = 0
    for r in range(NCORES):
        vec(lambda eng, r=r: eng.tensor_scalar(
            red[:, r * 8:(r + 1) * 8], agbuf[:, r * 16:r * 16 + 8],
            prm[:, r:r + 1], prm[:, 9 + r:10 + r], OP.mult, OP.add),
            waits=([("dB", g_back_v)] if r == 0 else ()))
        ve_prep = vec(lambda eng, r=r: eng.tensor_scalar(
            red[:, 64 + r * 8: 64 + (r + 1) * 8],
            agbuf[:, r * 16 + 8:r * 16 + 16],
            prm[:, r:r + 1], None, OP.mult))
    gp(lambda eng: eng.memset(hin[:], 0.0), waits=[("ve", ve_prep)])
    for r in range(NCORES):
        gp(lambda eng, r=r: eng.tensor_tensor(
            red[:, 128:136], red[:, r * 8:(r + 1) * 8], hin[:], OP.mult))
        gp(lambda eng, r=r: eng.tensor_tensor(
            hin[:], red[:, 128:136], red[:, 64 + r * 8: 64 + (r + 1) * 8], OP.add))
    ge_hin = P.v("ge")
    ve_corr = 0
    for e in (range(NK) if do_attn else ()):
        vec(lambda eng, e=e: eng.scalar_tensor_tensor(
            corr_t[:], bufB[:, e * T:(e + 1) * T], hin[:, e:e + 1],
            bufA[:, e * T:(e + 1) * T], OP.mult, OP.add),
            waits=([("ge", ge_hin), ("ve", ve_ffn)] if e == 0 else ()))
        ve_corr = vec(lambda eng, e=e: eng.tensor_add(
            xr(e), xr(e), corr_t[:]))

    if stage.startswith("l0"):
        last = None
        fin = max(ve_corr, ve_ffn, xn_all)
        if stage == "l0_hin":
            last = dma("gpsimd", out_h.ap()[0:128, 0:128], agbuf[:],
                       waits=[("ve", ve_corr)])
            last = dma("gpsimd", out_h.ap()[0:128, 128:136], hin[:],
                       waits=[("ve", ve_corr)])
            last = dma("gpsimd", out_h.ap()[0:128, 136:152], summ[:],
                       waits=[("ve", ve_corr)])
            last = dma("gpsimd", out_h.ap()[0:128, 152:168], prm[:],
                       waits=[("ve", ve_corr)])
            last = dma("gpsimd", out_h.ap()[0:128, 168:360],
                       bufA[:, 0:192], waits=[("ve", ve_corr)])
            P.emit("gpsimd", lambda eng: None, waits=[("dG", last[1])])
        elif stage == "l0_hid":
            for k in range(6):
                last = dma("gpsimd", out_h.ap()[k * 128:(k + 1) * 128, :],
                           ffn_h[:, k * 2048:(k + 1) * 2048],
                           waits=[("ve", max(ve_ffn, 1))])
            P.emit("gpsimd", lambda eng: None, waits=[("dG", last[1])])
        elif stage == "l0_xn":
            for k in range(NK):
                last = dma("gpsimd", out_h.ap()[k * 128:(k + 1) * 128, :], xnv(k),
                           waits=[("ve", fin)])
            P.emit("gpsimd", lambda eng: None, waits=[("dG", last[1])])
        else:
            for k in range(NK):
                last = dma("sync", out_h.ap()[k * 128:(k + 1) * 128, :], xr(k),
                           waits=[("ve", fin)])
            P.emit("sync", lambda eng: None, waits=[("dS", last[1])])
        emit_all(nc, P)
        es.close()
        return nc

    pe_l0 = P.v("pe")
    ae_l0 = P.v("ae")

    xn_all = emit_layernorm([("ve", ve_corr), ("pe", pe_l0), ("ae", ae_l0)],
                            stats_tok=("ve", ve_corr))
    ae_ln1 = P.v("ae")


    def early_out(waits):
        last = None
        for k in range(NK):
            last = dma("sync", out_h.ap()[k * 128:(k + 1) * 128, :], xr(k),
                       waits=waits)
        P.emit("sync", lambda eng: None, waits=[("dS", last[1])])
        emit_all(nc, P)
        es.close()
        return nc

    brow1 = aview("brow1", [1, 2048], BF16, 0)
    cosfc = aview("cosfc", [128, 512], BF16, 4096)
    sinfc = aview("sinfc", [128, 512], BF16, 5120)
    qp_t = aview("qp_t", [128, 512], BF16, 6272)
    prQc = aview("prQc", [128, 512], BF16, 7296)
    prQs = aview("prQs", [128, 512], BF16, 8320)
    prKc = aview("prKc", [128, 512], BF16, 9344)
    prKs = aview("prKs", [128, 512], BF16, 10368)
    st_t = aview("st_t", [128, 512], BF16, 11392)
    krelu = aview("krelu", [128, 1024], F32, 12416)
    sden = aview("sden", [1, 512], F32, 12416)
    iotat = aview("iotat", [128, 512], F32, 14464)
    vtok = aview("vtok", [128, 4 * NST], BF16, 16512)
    sprefb = aview("sprefb", [128, 4 * NST], BF16, 24832)
    srun1 = aview("srun1", [128, NST], F32, 33152)
    sptot = aview("sptot", [128, NST], BF16, 37312)
    sshift = aview("sshift", [128, NST], BF16, 39392)
    onesT = aview("onesT", [128, 128], F32, 41472)
    wvsl = aview("wvsl", [128, 1024], BF16, 41984)
    at64 = aview("at64", [128, 512], BF16, 44032)
    xq_t = aview("xq_t", [128, 4096], BF16, 0)
    w1sl = aview("w1sl", [128, 2048], BF16, 8192)
    w2sl = aview("w2sl", [128, 2048], BF16, 12288)
    hidt = aview("hidt", [128, 1024], BF16, 16384)
    qtf = aview("qtf", [128, 1024], F32, 18432)
    wprepD = [aview("wprepA", [128, 2048], BF16, 22528),
              aview("wprepB", [128, 2048], BF16, 26624),
              aview("wprepC", [128, 2048], BF16, 30720),
              aview("wprepE", [128, 2048], BF16, 34816)]
    wpabs = aview("wpabs", [128, 512], F32, 38912)
    wpq = aview("wpq", [128, 512], F32, 40960)
    amx32 = agbuf[:, 0:32]
    sc32 = agbuf[:, 32:64]
    rs32 = agbuf[:, 64:96]
    cmb32 = agbuf[:, 96:128]
    srow_t = aview("srow_t", [1, 512], BF16, 43008)
    xq4 = aview("xq4", [128, 16384], BF16, 0)
    w1slb = aview("w1slb", [128, 2048], BF16, 32768)
    hidt1 = aview("hidt1", [128, 1024], BF16, 36864)

    g_id = gp(lambda eng: eng.iota(
        red[:, 0:128] if False else red[:, 16:144], pattern=[[-1, 128]], base=0,
        channel_multiplier=1, allow_small_or_imprecise_dtypes=True))
    ve_id = vec(lambda eng: eng.tensor_single_scalar(
        identb[:], red[:, 16:144], 0.0, OP.is_equal), waits=[("ge", g_id)])
    ve_orb = vec(lambda eng: eng.memset(ones_rb[:], 1.0), waits=[("ve", ve_corr)])

    for k in range(NK):
        vec(lambda eng, k=k: eng.reduce_max(
            red[:, 8 + k:9 + k], xnv(k),
            axis=AX.X, apply_absolute_value=True))
    ve_am = vec(lambda eng: eng.reduce_max(
        summ[:, 0:1], red[:, 8:16], axis=AX.X))
    ve_amb = vec(lambda eng: eng.tensor_copy(tri[:, 0:1] if False else summ.bitcast(BF16)[:, 2:3], summ[:, 0:1]),
                 waits=[("ve", ve_am)])
    P.emit("tensor", (lambda eng: eng.transpose(
        PS4.bitcast(BF16)[0:1, 0:128], summ.bitcast(BF16)[:, 2:3], identb[:])),
        waits=[("ve", ve_amb), ("ve", ve_id)], incs=[("pe", 1)])
    pe_am = P.v("pe")
    ve_am2 = vec(lambda eng: eng.reduce_max(
        smax[0:1, 0:1], PS4.bitcast(BF16)[0:1, 0:128], axis=AX.X),
        waits=[("pe", pe_am)])

    if stage == "l1_ln":
        return early_out([("ve", xn_all), ("ve", ve_am2), ("ve", ve_corr)])
    d_brow = dma("gpsimd", brow1[0:1, 0:D], bk_h.ap(), sem="dR",
                 waits=[("ve", xn_all)])
    d_brow = dma("gpsimd", brow1[0:1, D:2 * D], bv1_h.ap(), sem="dR")
    g_cs = gp(lambda eng: eng.iota(
        iotat[:, 0:16], pattern=[[128, 16]], base=0, channel_multiplier=1,
        allow_small_or_imprecise_dtypes=True), waits=[("ve", xn_all)])
    cscol = red[:, 112:144]
    ae_cs = act(lambda eng: eng.activation(
        cscol[:, 0:16], iotat[:, 0:16], AF.Sin, scale=ANG_SCALE,
        bias=prm[:, 17:18]), waits=[("ge", g_cs)])
    ae_cs = act(lambda eng: eng.activation(
        cscol[:, 16:32], iotat[:, 0:16], AF.Sin, scale=ANG_SCALE,
        bias=prm[:, 8:9]))
    ve_vi = vec(lambda eng: eng.memset(vtok[:], 0.0), waits=[("ve", xn_all)])
    ve_vi = vec(lambda eng: eng.memset(
        vtok[:].rearrange("p (u h) -> p u h", h=65)[:, :, 64:65], 1.0))


    def fence(buf=None):
        b = bufB if buf is None else buf
        return vec(lambda eng, b=b: eng.tensor_copy(b[:, 0:512], b[:, 512:1024]))

    wk_last = wv_last = None
    for k in range(NK):
        wk_last = dma("gpsimd", bufB[:, k * 1024:(k + 1) * 1024],
                      wk_h.ap()[k * 128:(k + 1) * 128, :], sem="wWk",
                      waits=([("ve", ve_corr)] if k == 0 else ()))
    for k in range(NK):
        wv_last = dma("gpsimd", bufB[:, (NK + k) * 1024:(NK + k + 1) * 1024],
                      wv1_h.ap()[k * 128:(k + 1) * 128, :], sem="wWv",
                      waits=([("ve", ve_corr)] if k == 0 else ()))

    w2slB = aview("w2slB", [128, 4096], BF16, 4096)
    rsoC = red[:, 48:80]
    scoC = red[:, 80:112]
    qbufW = [iotat, sshift.bitcast(F32)]
    wo_q = {s: ("ve", xn_all) for s in range(4)}
    wo_a = {}
    wo_g = {}
    wo_qv = {0: 0, 1: 0}

    def wo_load_amax(pair):
        for h in (2 * pair, 2 * pair + 1):
            sl = h % 4
            d_w2 = dma("gpsimd", w2slB[:, sl * 1024:(sl + 1) * 1024],
                       wout_h.ap()[h * 128:(h + 1) * 128, :], sem="pW%d" % sl,
                       waits=[wo_q[sl]])
            ve_a2 = vec(lambda eng, sl=sl, h=h: eng.reduce_max(
                rsoC[:, h:h + 1], w2slB[:, sl * 1024:(sl + 1) * 1024],
                axis=AX.X, apply_absolute_value=True), waits=[d_w2])
            wo_a[h] = ve_a2

    def wo_scales_steps(pair):
        c0, c1 = 2 * pair, 2 * pair + 2

        def s1():
            return vec(lambda eng, c0=c0, c1=c1: eng.tensor_scalar(
                rsoC[:, c0:c1], rsoC[:, c0:c1], 1e-8, None, OP.max))

        def s2():
            return vec(lambda eng, c0=c0, c1=c1: eng.reciprocal(
                scoC[:, c0:c1], rsoC[:, c0:c1]))

        def s3():
            return vec(lambda eng, c0=c0, c1=c1: eng.tensor_scalar(
                rsoC[:, c0:c1], rsoC[:, c0:c1], 1.0 / 127.0, None, OP.mult))

        def s4():
            t = vec(lambda eng, c0=c0, c1=c1: eng.tensor_scalar(
                scoC[:, c0:c1], scoC[:, c0:c1], 127.0, None, OP.mult))
            wo_g[pair] = t
            return t
        return [s1, s2, s3, s4]

    def wo_quant_h(h):
        sl = h % 4
        qb = qbufW[h % 2]
        ae_q1 = act(lambda eng, sl=sl, h=h, qb=qb: eng.activation(
            qb[:, 0:512], w2slB[:, sl * 1024:sl * 1024 + 512],
            AF.Copy, scale=scoC[:, h:h + 1], bias=MAGIC),
            waits=[("ve", wo_qv[h % 2]), ("ve", wo_g[h // 2])])
        ve_w1 = vec(lambda eng, sl=sl, qb=qb: eng.tensor_scalar(
            w2slB[:, sl * 1024:sl * 1024 + 512], qb[:, 0:512], -MAGIC, None,
            OP.add), waits=[("ae", ae_q1)])
        ae_q3 = act(lambda eng, sl=sl, h=h, qb=qb: eng.activation(
            qb[:, 0:512], w2slB[:, sl * 1024 + 512:(sl + 1) * 1024],
            AF.Copy, scale=scoC[:, h:h + 1], bias=MAGIC),
            waits=[("ve", ve_w1)])
        ve_w2 = vec(lambda eng, sl=sl, qb=qb: eng.tensor_scalar(
            w2slB[:, sl * 1024 + 512:(sl + 1) * 1024], qb[:, 0:512], -MAGIC,
            None, OP.add), waits=[("ae", ae_q3)])
        wo_qv[h % 2] = ve_w2
        d_w2o = dma("sync", woutq_d.ap()[h * 128:(h + 1) * 128, :],
                    w2slB[:, sl * 1024:(sl + 1) * 1024], sem="pO%d" % sl,
                    waits=[("ve", ve_w2)])
        wo_q[sl] = d_w2o

    def wkv(which, k):
        base = 0 if which == "k" else NK * 1024
        return bufB[:, base + k * 1024: base + (k + 1) * 1024]

    ve_state_done = 0
    tokB = {("st", 0): ("pe", 0), ("st", 1): ("ve", ve_am2)}
    for c in range(NCH):
        if c == 0:
            ve_sp = vec(lambda eng, c=c: eng.memset(
                sprefb[:, c * NST:(c + 1) * NST], 0.0))
            ve_sp = vec(lambda eng: eng.memset(srun1[:], 0.0))
        else:
            ve_sp = vec(lambda eng, c=c: eng.tensor_copy(
                sprefb[:, c * NST:(c + 1) * NST], srun1[:]),
                waits=[("ve", ve_state_done)])
        for u in range(4):
            tt = c * 4 + u
            t0 = tt * 128
            pe_k = 0
            for k in range(NK):
                for hf in range(2):
                    pw = []
                    if k == 0 and hf == 0:
                        pw = [tokB.get("ps0", ("pe", 0))]
                        if c == 0 and u == 0:
                            pw += [wk_last, ("ve", xn_all)]
                    pe_k = mm(xnv(k, t0, t0 + 128),
                              wkv("k", k)[:, hf * 512:(hf + 1) * 512],
                              PS0[:, hf * 512:hf * 512 + 512],
                              start=(k == 0), stop=False, waits=pw)
            for hf in range(2):
                pe_k = mm(ones_rb[0:1, 0:128], brow1[0:1, hf * 512:(hf + 1) * 512],
                          PS0[:, hf * 512:hf * 512 + 512], start=False, stop=True,
                          waits=([d_brow, ("ve", ve_orb)]
                                 if (c == 0 and u == 0 and hf == 0) else ()),
                          inc=(hf == 1))
            ae_k = act(lambda eng: eng.activation(
                krelu[:, 0:512], PS0[:, 0:512], AF.Relu),
                waits=[("pe", pe_k), tokB.get("krelu", ("pe", 0))])
            ae_k2 = act(lambda eng: eng.activation(
                krelu[:, 512:1024], PS0[:, 512:1024], AF.Relu))
            tokB["ps0"] = ("ae", ae_k2)
            kt = bufA[:, tt * 2048:(tt + 1) * 2048] if False else bufA[:, u * 2048:(u + 1) * 2048]
            ve_kt = 0
            for hf in range(2):
                src = krelu[:, hf * 512:(hf + 1) * 512]
                ve_kt = vec(lambda eng, kt=kt, src=src, hf=hf, tt=tt: eng.tensor_scalar(
                    kt.rearrange("p (h z) -> p h z", z=128)[:, hf * 8:(hf + 1) * 8, 0:64],
                    src.rearrange("p (h z) -> p h z", z=64),
                    cscol[:, tt:tt + 1], None, OP.mult),
                    waits=([("ae", ae_k2), tokB.get("kt", ("pe", 0))] if hf == 0 else ()))
                ve_kt = vec(lambda eng, kt=kt, src=src, hf=hf, tt=tt: eng.tensor_scalar(
                    kt.rearrange("p (h z) -> p h z", z=128)[:, hf * 8:(hf + 1) * 8, 64:128],
                    src.rearrange("p (h z) -> p h z", z=64),
                    cscol[:, 16 + tt:17 + tt], None, OP.mult))
            tokB["krelu"] = ("ve", ve_kt)
            pe_v = 0
            for k in range(NK):
                for hf in range(2):
                    pw = []
                    if k == 0 and hf == 0:
                        pw = [tokB.get("ps1", ("pe", 0))]
                        if c == 0 and u == 0:
                            pw += [wv_last]
                    pe_v = mm(xnv(k, t0, t0 + 128),
                              wkv("v", k)[:, hf * 512:(hf + 1) * 512],
                              PS1[:, hf * 512:hf * 512 + 512],
                              start=(k == 0), stop=False, waits=pw)
            for hf in range(2):
                pe_v = mm(ones_rb[0:1, 0:128], brow1[0:1, D + hf * 512:D + (hf + 1) * 512],
                          PS1[:, hf * 512:hf * 512 + 512], start=False, stop=True,
                          inc=(hf == 1))
            vt = vtok[:, u * NST:(u + 1) * NST]
            ae_vv = 0
            for hf in range(2):
                w0 = []
                if hf == 0:
                    w0 = [("pe", pe_v), tokB.get("vt", ("pe", 0))]
                    if c == 0 and u == 0:
                        w0.append(("ve", ve_vi))
                ae_vv = act(lambda eng, vt=vt, hf=hf: eng.activation(
                    vt.rearrange("p (h z) -> p h z", z=65)[:, hf * 8:(hf + 1) * 8, 0:64],
                    PS1[:, hf * 512:(hf + 1) * 512].rearrange("p (h z) -> p h z", z=64),
                    AF.Copy), waits=w0)
            tokB["ps1"] = ("ae", ae_vv)
            tokB[("ktv", u)] = (("ve", ve_kt), ("ae", ae_vv))
            stw = c * 4 + u
            steps = wo_scales_steps(stw - 1) if stw >= 1 else None
            if steps:
                steps[0]()
            if stw >= 2:
                wo_quant_h(2 * (stw - 2))
            if steps:
                steps[1]()
            if stw >= 2:
                wo_quant_h(2 * (stw - 2) + 1)
            if steps:
                steps[2]()
            wo_load_amax(stw)
            if steps:
                steps[3]()
        for h in range(NH):
            ps = PS3 if h % 2 == 0 else PS4
            pe_s = 0
            for u in range(4):
                kt = bufA[:, u * 2048:(u + 1) * 2048]
                vt = vtok[:, u * NST:(u + 1) * NST]
                pw = []
                if u == 0:
                    pw = [tokB[("st", h % 2)]]
                    if h < 2:
                        pw += [tokB[("ktv", 3)][0], tokB[("ktv", 3)][1]]
                pe_s = mm(kt[:, h * 128:(h + 1) * 128], vt[:, h * 65:(h + 1) * 65],
                          ps[:, 0:65], start=(u == 0), stop=(u == 3),
                          waits=pw, inc=(u == 3))
            ve_state_done = vec(lambda eng, h=h, ps=ps: eng.scalar_tensor_tensor(
                srun1[:, h * 65:(h + 1) * 65], ps[:, 0:65], 1.0,
                srun1[:, h * 65:(h + 1) * 65], OP.mult, OP.add),
                waits=[("pe", pe_s), ("ve", ve_sp) if h == 0 else ("ve", 0)])
            tokB[("st", h % 2)] = ("ve", ve_state_done)
        tokB["vt"] = ("pe", P.v("pe"))
        tokB["kt"] = ("pe", P.v("pe"))

    def fence_k():
        return vec(lambda eng: eng.tensor_copy(krelu[:, 0:512],
                                               krelu[:, 512:1024]))
    steps = wo_scales_steps(15)
    steps[0]()
    fence_k()
    fence_k()
    steps[1]()
    wo_quant_h(28)
    steps[2]()
    wo_quant_h(29)
    steps[3]()
    wo_quant_h(30)
    wo_quant_h(31)
    ve_woutprep = P.v("ve")
    pO_done = [wo_q[s] for s in range(4)]

    g_s1 = dma("gpsimd", ag1_in.ap().rearrange("a (p c) -> (a p) c", p=128),
               srun1[:], waits=[("ve", ve_state_done)], sem="dB")
    g_s2 = dma("gpsimd", ag1b_in.ap()[0:1, 0:1], smax[0:1, 0:1],
               waits=[("ve", ve_am2)], sem="dB")
    P.emit("gpsimd", lambda eng: eng.collective_compute(
        "AllGather", OP.bypass, replica_groups=[list(range(NCORES))],
        ins=[ag1_in.ap()], outs=[ag1_out.ap()]),
        waits=[g_s1, g_s2], incs=[("cc", 1)])
    P.emit("gpsimd", lambda eng: eng.collective_compute(
        "AllGather", OP.bypass, replica_groups=[list(range(NCORES))],
        ins=[ag1b_in.ap()], outs=[ag1b_out.ap()]), incs=[("cc", 1)])
    cc1 = P.v("cc")

    ve_sx = vec(lambda eng: eng.memset(srun1[:], 0.0),
                waits=[("cc", cc1), ("ve", ve_state_done)])
    for r in range(NCORES):
        d_sr = dma("gpsimd", sptot[:],
                   ag1_out.ap()[r:r + 1, :].rearrange("a (p c) -> (a p) c", p=128),
                   sem="dB", waits=[("ve", ve_sx)] if r == 0 else [("ve", ve_sx)])
        ve_m1 = vec(lambda eng, r=r: eng.tensor_scalar(
            sshift[:], sptot[:], prm[:, r:r + 1], None, OP.mult),
            waits=[d_sr])
        ve_sx = vec(lambda eng: eng.tensor_add(srun1[:], srun1[:], sshift[:]))
    ve_scross = ve_sx
    d_sb = dma("gpsimd", smax[0:8, 8:9], ag1b_out.ap()[:, 0:1], sem="dB",
               waits=[("cc", cc1)], slow=True)
    d_fold = dma("gpsimd", smax[0:1, 24:32], smax[0:8, 8:9], sem="dB",
                 waits=[d_sb], slow=True)
    ve_g1 = vec(lambda eng: eng.reduce_max(
        smax[0:1, 16:17], smax[0:1, 24:32], axis=AX.X), waits=[d_fold])
    ve_g1 = vec(lambda eng: eng.tensor_scalar(
        smax[0:1, 17:18], smax[0:1, 16:17], 1.0 / 32767.0, None, OP.mult))
    pe_bc = mm(ones_row[0:1, 0:128], smax[0:1, 16:18], PS4[:, 0:2],
               start=True, stop=True,
               waits=[("ve", ve_g1), ("ve", ve_state_done)], inc=True)
    ve_sc = vec(lambda eng: eng.reciprocal(red[:, 32:33], PS4[:, 0:1]),
                waits=[("pe", pe_bc)])
    ve_sc = vec(lambda eng: eng.tensor_copy(red[:, 34:35], PS4[:, 1:2]))
    ve_sc = vec(lambda eng: eng.tensor_scalar(
        red[:, 33:34], red[:, 32:33], 32767.0, None, OP.mult))
    ve_scales = ve_sc

    if stage == "l1_b1":
        return early_out([("ve", ve_scross), ("ve", ve_scales), ("ve", ve_corr)])
    lvl = 99
    if stage.startswith("l1_b2@"):
        lvl = int(stage.split("@")[1])
    pe_b1 = P.v("pe")
    ve_o1 = vec(lambda eng: eng.memset(onesT[:], 1.0))

    tokC = {"ps4s": ("ve", ve_scales), ("st", 0): ("pe", 0),
            "iotat": ("ve", ve_woutprep),
            "ps0": tokB["ps0"], "ps0b": tokB["ps0"], "ps1": tokB["ps1"],
            "ps1b": tokB["ps1"], "ps3": tokB[("st", 0)],
            "ps2": ("ve", ve_ffn), "ps2b": ("ve", ve_ffn)}
    ve_attn_done = 0
    dH_attn = 0
    for c in range(NCH):
        t0 = c * CHUNK
        ve_tot = vec(lambda eng, c=c: eng.tensor_add(
            sptot[:], sprefb[:, c * NST:(c + 1) * NST], srun1[:]),
            waits=[("ve", ve_scross), tokC.get("sptot", ("pe", 0))])
        d_sh1 = dma("sync", sshift[0:64, :], sptot[64:128, :], sem="dH",
                    waits=[("ve", ve_tot), ("dH", dH_attn)])
        d_sh2 = dma("sync", sshift[64:128, :], sptot[0:64, :], sem="dH")
        g_i2 = gp(lambda eng, c=c: eng.iota(
            iotat[:], pattern=[[1, 512]], base=c * 512, channel_multiplier=0,
            allow_small_or_imprecise_dtypes=True),
            waits=[tokC.get("iotat", ("pe", 0))])
        ae_c1 = act(lambda eng: eng.activation(
            cosfc[:], iotat[:], AF.Sin, scale=ANG_SCALE, bias=prm[:, 17:18]),
            waits=([("ge", g_i2), tokC.get("csf", ("pe", 0))] +
                   (pO_done + [("ve", ve_woutprep)] if c == 0 else [])))
        ae_c2 = act(lambda eng: eng.activation(
            sinfc[:], iotat[:], AF.Sin, scale=ANG_SCALE, bias=prm[:, 8:9]))
        tokC["iotat"] = ("ae", ae_c2)
        for u in range(4):
            tu0 = t0 + u * 128
            pe_v2 = 0
            for k in range(NK):
                for hf in range(2):
                    pw = []
                    if k == 0 and hf == 0:
                        pw = [tokC.get("ps1", ("pe", 0)), tokC.get("ps1b", ("pe", 0))]
                    pe_v2 = mm(xnv(k, tu0, tu0 + 128),
                               wkv("v", k)[:, hf * 512:(hf + 1) * 512],
                               PS1[:, hf * 512:hf * 512 + 512],
                               start=(k == 0), stop=False, waits=pw)
            for hf in range(2):
                pe_v2 = mm(ones_rb[0:1, 0:128],
                           brow1[0:1, D + hf * 512:D + (hf + 1) * 512],
                           PS1[:, hf * 512:hf * 512 + 512], start=False,
                           stop=True, inc=(hf == 1))
            vt = vtok[:, u * NST:(u + 1) * NST]
            ae_vv2 = 0
            for hf in range(2):
                w0 = []
                if hf == 0:
                    w0 = [("pe", pe_v2), tokC.get("vt", ("pe", 0))]
                ae_vv2 = act(lambda eng, vt=vt, hf=hf: eng.activation(
                    vt.rearrange("p (h z) -> p h z", z=65)[:, hf * 8:(hf + 1) * 8, 0:64],
                    PS1[:, hf * 512:(hf + 1) * 512].rearrange("p (h z) -> p h z", z=64),
                    AF.Copy), waits=w0)
            tokC["ps1"] = ("ae", ae_vv2)
        tokC["vprj"] = ("ae", ae_vv2)
        for p in (range(NK) if lvl >= 1 else ()):
            d_wq = col_block_dma(wvsl[:], wq_h, p * 128, 128, NK, sem="wV2",
                                 waits=[tokC.get("wvsl", ("pe", pe_b1))])
            pe_q = 0
            for k in range(NK):
                pw = []
                if k == 0:
                    pw = [tokC.get("ps0", ("pe", 0)), d_wq]
                pe_q = mm(wvsl[:, k * 128:(k + 1) * 128],
                          xnv(k, t0, t0 + CHUNK), PS0[:, 0:512],
                          start=(k == 0), stop=(k == NK - 1), waits=pw,
                          inc=(k == NK - 1))
            tokC["wvsl"] = ("pe", pe_q)
            ae_q = act(lambda eng, p=p: eng.activation(
                qp_t[:], PS0[:, 0:512], AF.Relu, bias=bcol1[:, p:p + 1]),
                waits=[("pe", pe_q), tokC.get("qp", ("pe", 0)), ("ae", ae_c2)])
            ve_qc = vec(lambda eng: eng.tensor_mul(prQc[:], qp_t[:], cosfc[:]),
                        waits=[("ae", ae_q), tokC.get("prq", ("pe", 0))])
            ve_qs = vec(lambda eng: eng.tensor_mul(prQs[:], qp_t[:], sinfc[:]))
            tokC["ps0"] = ("ae", ae_q)
            pe_k2 = 0
            for k in range(NK):
                pw = [tokC.get("ps0b", ("pe", 0))] if k == 0 else []
                pe_k2 = mm(bufB[:, k * 1024 + p * 128: k * 1024 + (p + 1) * 128],
                           xnv(k, t0, t0 + CHUNK), PS0[:, 512:1024],
                           start=(k == 0), stop=(k == NK - 1), waits=pw,
                           inc=(k == NK - 1))
            ae_k3 = act(lambda eng, p=p: eng.activation(
                qp_t[:], PS0[:, 512:1024], AF.Relu, bias=bcol1[:, NK + p:NK + p + 1]),
                waits=[("pe", pe_k2), ("ve", ve_qs)])
            tokC["ps0b"] = ("ae", ae_k3)
            ve_kc = vec(lambda eng: eng.tensor_mul(prKc[:], qp_t[:], cosfc[:]),
                        waits=[("ae", ae_k3), tokC.get("prk", ("pe", 0))])
            ve_ks = vec(lambda eng: eng.tensor_mul(prKs[:], qp_t[:], sinfc[:]))
            for hh in (range(2) if lvl >= 2 else ()):
                h = 2 * p + hh
                pb = hh * 64
                ps2h = PS2[0:65, 0:512] if hh == 0 else PS2[0:65, 512:1024]
                w_int = [tokC.get("ps2" if hh == 0 else "ps2b", ("pe", 0)),
                         ("dH", d_sh2[1]), ("ve", ve_ks), tokC["vprj"]]
                if c == 0 and p == 0 and hh == 0:
                    w_int.append(("ve", ve_o1))
                lhs_cos = sptot[0:64, h * 65:(h + 1) * 65] if hh == 0 \
                    else sshift[64:128, h * 65:(h + 1) * 65]
                lhs_sin = sshift[0:64, h * 65:(h + 1) * 65] if hh == 0 \
                    else sptot[64:128, h * 65:(h + 1) * 65]
                pe_at = mm(lhs_cos, prQc[pb:pb + 64, :], ps2h,
                           start=True, stop=False, waits=w_int)
                pe_at = mm(lhs_sin, prQs[pb:pb + 64, :], ps2h,
                           start=False, stop=(lvl < 5), inc=(lvl < 5))
                for u in (range(4) if lvl >= 3 else ()):
                    w_u = 512 - u * 128
                    ps_s = PS3 if u % 2 == 0 else PS4
                    pe_sc2 = mm(prKc[pb:pb + 64, u * 128:(u + 1) * 128],
                                prQc[pb:pb + 64, u * 128:512],
                                ps_s[:, 0:w_u], start=True, stop=False,
                                waits=[tokC.get("ps3" if u % 2 == 0 else "ps4s",
                                                ("pe", 0))])
                    pe_sc2 = mm(prKs[pb:pb + 64, u * 128:(u + 1) * 128],
                                prQs[pb:pb + 64, u * 128:512],
                                ps_s[:, 0:w_u], start=False, stop=True, inc=True)
                    if lvl == 7:
                        ae_m = act(lambda eng, w_u=w_u: eng.activation(
                            st_t[:, 0:w_u], PS3[:, 0:w_u], AF.Copy),
                            waits=[("pe", pe_sc2)])
                        ve_m = ae_m
                        tokC["ps3"] = ("ae", ae_m)
                        tokC[("st", 0)] = ("ae", ae_m)
                        continue
                    ve_m = vec(lambda eng, w_u=w_u, ps_s=ps_s: eng.tensor_mul(
                        st_t[:, 0:w_u], ps_s[:, 0:w_u], tri[:, 0:w_u]),
                        waits=[("pe", pe_sc2), tokC[("st", 0)],
                               ("dH", dH_attn)])
                    tokC["ps3" if u % 2 == 0 else "ps4s"] = ("ve", ve_m)
                    wait_st = [("ve", ve_m)]
                    if lvl >= 5 and lvl not in (7, 8):
                        pe_at = mm(vtok[:, u * NST + h * 65: u * NST + (h + 1) * 65],
                                   st_t[:, 0:w_u],
                                   ps2h.tensor[0:65, (hh * 512) + u * 128: hh * 512 + 512]
                                   if False else
                                   (PS2[0:65, u * 128:512] if hh == 0 else
                                    PS2[0:65, 512 + u * 128:1024]),
                                   start=False, stop=(u == 3),
                                   waits=wait_st, inc=(u == 3))
                        tokC[("st", 0)] = ("pe", pe_at)
                    else:
                        tokC[("st", 0)] = wait_st[0]
                if lvl < 6 or lvl in (7, 8):
                    tokC["ps2" if hh == 0 else "ps2b"] = ("pe", pe_at)
                    continue
                psb = PS1[0:64, 0:512] if hh == 0 else PS1[0:64, 512:1024]
                ve_d1 = vec(lambda eng, hh=hh: eng.tensor_scalar(
                    iotat[64:65, 0:512], PS2[64:65, 512 * hh:512 * hh + 512],
                    1e-6, None, OP.max),
                    waits=[("pe", pe_at), tokC.get("iden", ("pe", 0))])
                ae_ln = act(lambda eng: eng.activation(
                    krelu[64:65, 0:512], iotat[64:65, 0:512], AF.Ln),
                    waits=[("ve", ve_d1)])
                tokC["iden"] = ("ae", ae_ln)
                pe_bc2 = mm(onesT[64:65, 0:64], krelu[64:65, 0:512],
                            psb, start=True, stop=True,
                            waits=[("ae", ae_ln),
                                   tokC["ps1" if hh == 0 else "ps1b"]], inc=True)
                tokC["iotat"] = ("pe", pe_bc2)
                ae_rd = act(lambda eng, psb=psb: eng.activation(
                    krelu[0:64, 0:512], psb, AF.Exp, scale=-1.0),
                    waits=[("pe", pe_bc2), tokC.get("krel", ("pe", 0))])
                ae_n = act(lambda eng, hh=hh: eng.activation(
                    at64[0:64, :], PS2[0:64, 512 * hh:512 * hh + 512], AF.Copy),
                    waits=[("pe", pe_at), tokC.get("at64", ("pe", 0))])
                tokC["ps2" if hh == 0 else "ps2b"] = ("ae", ae_n)
                at_tile = bufA[:, p * 2048 + t0: p * 2048 + t0 + 512]
                if hh == 0:
                    ve_attn_done = vec(lambda eng, at_tile=at_tile: eng.tensor_mul(
                        at_tile[0:64, :], at64[0:64, :], krelu[0:64, 0:512]),
                        waits=[("ae", ae_rd), ("ae", ae_n)])
                    tokC["at64"] = ("ve", ve_attn_done)
                    tokC["ps1"] = ("ae", ae_rd)
                    tokC["krel"] = ("ve", ve_attn_done)
                else:
                    ve_sh = vec(lambda eng: eng.tensor_mul(
                        st_t[0:64, :], at64[0:64, :], krelu[0:64, 0:512]),
                        waits=[("ae", ae_rd), ("ae", ae_n), ("dH", dH_attn)])
                    d_sh3 = dma("sync", at_tile[64:128, :], st_t[0:64, :],
                                sem="dH", waits=[("ve", ve_sh)])
                    dH_attn = d_sh3[1]
                    tokC["at64"] = ("ve", ve_sh)
                    tokC["ps1b"] = ("ae", ae_rd)
                    tokC["krel"] = ("ve", ve_sh)
                    tokC[("st", 0)] = ("ve", ve_sh)
            tokC["qp"] = tokC["at64"]
            tokC["prq"] = tokC[("st", 0)]
            tokC["prk"] = tokC[("st", 0)]
        tokC["sptot"] = ("pe", P.v("pe"))
        tokC["csf"] = tokC["ps0"]
    ve_attn_last = P.v("ve")

    pe_all_b2 = P.v("pe")
    tok_wo = ("pe", pe_all_b2)
    tok_evac = ("ve", 0)
    pe_wo_done = 0
    for dch in range(NK):
        d_wo = col_block_dma(wvsl[:, 0:1024], wo_h, dch * 128, 128, NK,
                             sem="wV2", waits=[tok_wo])
        pe_o = 0
        for k in range(NK):
            for tt in range(4):
                pw = []
                if k == 0 and tt == 0:
                    pw = [d_wo, ("ve", ve_attn_last), ("dH", dH_attn), tok_evac]
                pe_o = mm(wvsl[:, k * 128:(k + 1) * 128],
                          bufA[:, k * 2048 + tt * 512: k * 2048 + (tt + 1) * 512],
                          (PS0 if tt < 2 else PS1)[:, (tt % 2) * 512:(tt % 2) * 512 + 512],
                          start=(k == 0), stop=(k == NK - 1), waits=pw,
                          inc=(k == NK - 1))
        tok_wo = ("pe", pe_o)
        ve_wo = 0
        for tt in range(4):
            ve_wo = vec(lambda eng, dch=dch, tt=tt: eng.scalar_tensor_tensor(
                xr(dch, tt * 512, (tt + 1) * 512),
                (PS0 if tt < 2 else PS1)[:, (tt % 2) * 512:(tt % 2) * 512 + 512],
                bcol1[:, 16 + dch:17 + dch], xr(dch, tt * 512, (tt + 1) * 512),
                OP.add, OP.add), waits=[("pe", pe_o)] if tt == 0 else ())
        tok_evac = ("ve", ve_wo)
        pe_wo_done = pe_o
    ve_wo_done = tok_evac[1]

    if stage == "l1_atdump":
        last = None
        for k in range(NK):
            last = dma("gpsimd", out_h.ap()[k * 128:(k + 1) * 128, :],
                       bufA[:, k * 2048:(k + 1) * 2048],
                       waits=[("ve", ve_wo_done)], sem="dS")
        P.emit("sync", lambda eng: None, waits=[("dS", last[1])])
        emit_all(nc, P)
        es.close()
        return nc
    if stage.startswith("l1_b2"):
        return early_out([("ve", ve_wo_done), ("pe", P.v("pe")),
                          ("ve", P.v("ve")), ("ae", P.v("ae")),
                          ("ge", P.v("ge"))])
    pe_prep0 = pe_wo_done

    ae_tr_tok = ("ae", P.v("ae"))
    q_done = [("pe", pe_prep0)] * 4
    tok_srow = ("pe", 0)
    ve_id2 = ve_id

    def win_scales(b):
        yield lambda: vec(lambda eng, b=b: eng.tensor_scalar(
            sc32[:, b * 2:(b + 1) * 2], sc32[:, b * 2:(b + 1) * 2],
            1e-8, None, OP.max))
        yield lambda: vec(lambda eng, b=b: eng.reciprocal(
            amx32[:, b * 2:(b + 1) * 2], sc32[:, b * 2:(b + 1) * 2]))
        yield lambda: vec(lambda eng, b=b: eng.tensor_scalar(
            sc32[:, b * 2:(b + 1) * 2], amx32[:, b * 2:(b + 1) * 2],
            127.0, None, OP.mult))

    def win_quant(b):
        nonlocal tok_srow, q_done
        wp = wprepD[b % 4]
        c0 = b * 256
        ve_cp = vec(lambda eng, b=b: eng.tensor_copy(
            wpq.bitcast(BF16)[:, 0:2], sc32[:, b * 2:(b + 1) * 2]))
        P.emit("tensor", (lambda eng: eng.transpose(
            PS4.bitcast(BF16)[0:2, 0:128], wpq.bitcast(BF16)[:, 0:2], identb[:])),
            waits=[("ve", ve_cp), tok_srow], incs=[("pe", 1)])
        pe_t2 = P.v("pe")
        ae_sr = act(lambda eng: eng.activation(
            wpq.bitcast(BF16)[0:2, 128:256], PS4.bitcast(BF16)[0:2, 0:128],
            AF.Copy), waits=[("pe", pe_t2)])
        d_fl = dma("sync", srow_t[0:1, 0:256].rearrange("a (k f) -> a k f", f=128),
                   wpq.bitcast(BF16)[0:2, 128:256], sem="pF",
                   waits=[("ae", ae_sr)])
        pe_b3 = mm(ones_rb[0:1, 0:128], srow_t[0:1, 0:256], PS3[:, 0:256],
                   start=True, stop=True, waits=[d_fl, q_done[b % 4]], inc=True)
        tok_srow = ("pe", pe_b3)
        qd = None
        for k in range(NK):
            ve_q1 = vec(lambda eng, k=k, wp=wp: eng.tensor_tensor(
                wpabs[:, 0:256], wp[:, k * 256:(k + 1) * 256], PS3[:, 0:256],
                OP.mult), waits=([("pe", pe_b3)] if k == 0 else []))
            ve_q2 = vec(lambda eng: eng.tensor_scalar(
                wpabs[:, 0:256], wpabs[:, 0:256], MAGIC, -MAGIC, OP.add, OP.add))
            ve_q3 = vec(lambda eng, k=k: eng.tensor_copy(
                wpq.bitcast(BF16)[:, 512:768], wpabs[:, 0:256]),
                waits=[qd] if qd is not None else [])
            qd = dma("sync", winq_d.ap()[k * 128:(k + 1) * 128, c0:c0 + 256],
                     wpq.bitcast(BF16)[:, 512:768], sem="pF",
                     waits=[("ve", ve_q3)])
        q_done[b % 4] = qd
        return qd

    for blk in range(16):
        wp = wprepD[blk % 4]
        c0 = blk * 256
        d_wi = dma("gpsimd",
                   wp[:].rearrange("p (k f) -> p k f", f=256),
                   win_h.ap()[:, c0:c0 + 256].rearrange("(k p) f -> p k f", p=128),
                   sem="pI%d" % (blk % 4),
                   waits=[q_done[blk % 4]])
        sgen = win_scales(blk - 1) if blk > 0 else None
        ve_ab = vec(lambda eng, wp=wp: eng.tensor_tensor(
            wpabs[:, 0:256], wp[:, 0:256], wp[:, 256:512], OP.max),
            waits=[d_wi, ("pe", pe_prep0) if blk == 0 else ("pe", 0)])
        for k in range(2, NK):
            ve_ab = vec(lambda eng, k=k, wp=wp: eng.tensor_tensor(
                wpabs[:, 0:256], wpabs[:, 0:256], wp[:, k * 256:(k + 1) * 256],
                OP.max))
            if sgen and k in (3, 6): next(sgen)()
        ve_ab = vec(lambda eng, wp=wp: eng.tensor_tensor(
            wpq[:, 0:256], wp[:, 0:256], wp[:, 256:512], OP.min))
        for k in range(2, NK):
            ve_ab = vec(lambda eng, k=k, wp=wp: eng.tensor_tensor(
                wpq[:, 0:256], wpq[:, 0:256], wp[:, k * 256:(k + 1) * 256],
                OP.min))
            if sgen and k == 4: next(sgen)()
        ve_ab = vec(lambda eng: eng.tensor_scalar(
            wpq[:, 0:256], wpq[:, 0:256], -1.0, None, OP.mult))
        ve_ab = vec(lambda eng: eng.tensor_tensor(
            wpabs[:, 0:256], wpabs[:, 0:256], wpq[:, 0:256], OP.max))
        ve_ab = vec(lambda eng, blk=blk: eng.tensor_copy(
            wpq.bitcast(BF16)[:, 512:768], wpabs[:, 0:256]),
            waits=[q_done[(blk - 2) % 4]] if blk >= 2 else [])
        for q in range(2):
            P.emit("tensor", (lambda eng, q=q: eng.transpose(
                PS4.bitcast(BF16)[:, q * 128:(q + 1) * 128],
                wpq.bitcast(BF16)[:, 512 + q * 128:512 + (q + 1) * 128],
                identb[:])),
                waits=([("ve", ve_ab), ("ve", ve_id2),
                        ("pe", pe_prep0) if blk == 0 else ("pe", 0),
                        ae_tr_tok] if q == 0 else ()),
                incs=[("pe", 1)])
        pe_tr = P.v("pe")
        ve_mx = 0
        for q in range(2):
            ve_mx = vec(lambda eng, blk=blk, q=q: eng.reduce_max(
                sc32[:, blk * 2 + q:blk * 2 + q + 1],
                PS4.bitcast(BF16)[:, q * 128:(q + 1) * 128], axis=AX.X),
                waits=([("pe", pe_tr)] if q == 0 else ()))
        ae_tr_tok = ("ve", ve_mx)
        if blk > 0:
            win_quant(blk - 1)
    for f in win_scales(15):
        f()
        fence()
        fence()
    win_quant(15)
    ve_rs = vec(lambda eng: eng.reciprocal(rs32[:], sc32[:]))
    fence()
    fence()
    ve_rs = vec(lambda eng: eng.tensor_scalar(
        rs32[:], rs32[:], red[:, 34:35], None, OP.mult))
    pe_prep_done = P.v("pe")
    ae_prep_done = P.v("ae")

    if stage == "l1_prep":
        return early_out([("pe", pe_prep_done), ("ae", ae_prep_done),
                          ("ve", P.v("ve")), (QSEM["sync"], P.v("dS")),
                          ("pF", P.v("pF"))])
    hT = {}
    ae_ffn0 = P.v("ae")
    qdall = list(q_done)
    amC = [amx32, sc32, cmb32, wpq[:, 64:96]]
    ve_xq = 0
    for tl in range(4):
        for k in range(NK):
            ve_x1 = vec(lambda eng, k=k, tl=tl: eng.tensor_scalar(
                wpabs[:], xnv(k, tl * 512, tl * 512 + 512), red[:, 33:34], MAGIC,
                OP.mult, OP.add),
                waits=([("ve", ve_scales), ("pe", pe_prep_done),
                        ("ae", ae_prep_done)] if (tl == 0 and k == 0) else []))
            ve_xq = vec(lambda eng, k=k, tl=tl: eng.tensor_scalar(
                xq4[:, tl * 4096 + k * 512:tl * 4096 + (k + 1) * 512],
                wpabs[:], -MAGIC, None, OP.add))
    for hch in range(NH1):
        sl = hch % 2
        d_w1 = col_block_dma(w1slb[:, sl * 1024:(sl + 1) * 1024], winq_d,
                             hch * 128, 128, NK, sem="fI%d" % sl, q="scalar",
                             waits=([hT[("w", sl)]] if ("w", sl) in hT else qdall))
        pe_h = 0
        for tl in range(4):
            ps = PS0 if tl % 2 == 0 else PS1
            slh = tl % 2
            for k in range(NK):
                pw = []
                if k == 0:
                    pw = [hT.get(("ps", slh), ("ae", ae_ffn0))]
                    if tl == 0:
                        pw += [d_w1, ("ve", ve_xq)]
                pe_h = mm(w1slb[:, sl * 1024 + k * 128: sl * 1024 + (k + 1) * 128],
                          xq4[:, tl * 4096 + k * 512: tl * 4096 + (k + 1) * 512],
                          ps[:, 0:512],
                          start=(k == 0), stop=(k == NK - 1), waits=pw,
                          inc=(k == NK - 1))
            ae_h = act(lambda eng, ps=ps, slh=slh, hch=hch: eng.activation(
                hidt1[:, slh * 512:(slh + 1) * 512], ps[:, 0:512],
                AF.Gelu, scale=rs32[:, hch:hch + 1],
                bias=bcol1[:, 32 + hch:33 + hch]),
                waits=[("pe", pe_h), hT.get(("hd", slh), ("pe", 0)),
                       hT.get(("am", slh), ("pe", 0))])
            hT[("ps", slh)] = ("ae", ae_h)
            ve_hm = vec(lambda eng, slh=slh, hch=hch, tl=tl: eng.reduce_max(
                amC[tl][:, hch:hch + 1], hidt1[:, slh * 512:(slh + 1) * 512],
                axis=AX.X, apply_absolute_value=True), waits=[("ae", ae_h)])
            hT[("am", slh)] = ("ve", ve_hm)
            d_ho = dma("sync",
                       hid_d.ap()[hch * 128:(hch + 1) * 128, tl * 512:(tl + 1) * 512],
                       hidt1[:, slh * 512:(slh + 1) * 512], sem="hd%d" % slh,
                       waits=[("ve", ve_hm)])
            hT[("hd", slh)] = d_ho
        hT[("w", sl)] = ("pe", pe_h)
    ve_fold = fence()
    for tl in range(4):
        ve_fold = vec(lambda eng, tl=tl: eng.reduce_max(
            red[:, 40 + tl:41 + tl], amC[tl][:], axis=AX.X))
    ve_fold = fence()
    ve_fold = vec(lambda eng: eng.reduce_max(
        red[:, 44:45], red[:, 40:44], axis=AX.X))
    ve_fold = fence()
    ve_fb = vec(lambda eng: eng.tensor_copy(summ.bitcast(BF16)[:, 2:3],
                                            red[:, 44:45]))
    P.emit("tensor", (lambda eng: eng.transpose(
        PS4.bitcast(BF16)[0:1, 0:128], summ.bitcast(BF16)[:, 2:3], identb[:])),
        waits=[("ve", ve_fb), tok_srow, ("ae", P.v("ae"))], incs=[("pe", 1)])
    pe_am2 = P.v("pe")
    ve_am3 = vec(lambda eng: eng.reduce_max(
        smax[0:1, 2:3], PS4.bitcast(BF16)[0:1, 0:128], axis=AX.X),
        waits=[("pe", pe_am2)])
    g_a2 = dma("gpsimd", ag2_in.ap()[0:1, 0:1], smax[0:1, 2:3],
               waits=[("ve", ve_am3)], sem="dB")
    P.emit("gpsimd", lambda eng: eng.collective_compute(
        "AllGather", OP.bypass, replica_groups=[list(range(NCORES))],
        ins=[ag2_in.ap()], outs=[ag2_out.ap()]),
        waits=[g_a2], incs=[("cc", 1)])
    cc2 = P.v("cc")
    d_a2 = dma("gpsimd", smax[0:8, 10:11], ag2_out.ap()[:, 0:1], sem="dB",
               waits=[("cc", cc2)], slow=True)
    d_a2f = dma("gpsimd", smax[0:1, 32:40], smax[0:8, 10:11], sem="dB",
                waits=[d_a2], slow=True)
    ve_g2 = vec(lambda eng: eng.reduce_max(
        smax[0:1, 3:4], smax[0:1, 32:40], axis=AX.X), waits=[d_a2f])
    ve_g2 = vec(lambda eng: eng.tensor_scalar(
        smax[0:1, 4:5], smax[0:1, 3:4], 1.0 / 32767.0, None, OP.mult))
    pe_bc3 = mm(ones_row[0:1, 0:128], smax[0:1, 3:5], PS4[:, 0:2],
                start=True, stop=True,
                waits=[("ve", ve_g2), ("ve", ve_am3)], inc=True)
    ve_s2 = vec(lambda eng: eng.reciprocal(red[:, 45:46], PS4[:, 0:1]),
                waits=[("pe", pe_bc3)])
    ve_s2 = vec(lambda eng: eng.tensor_copy(red[:, 47:48], PS4[:, 1:2]))
    ve_s2 = vec(lambda eng: eng.tensor_scalar(
        red[:, 46:47], red[:, 45:46], 32767.0, None, OP.mult))
    ve_cmb = vec(lambda eng: eng.tensor_scalar(
        cmb32[:], rsoC[:], red[:, 47:48], None, OP.mult),
        waits=[("ve", ve_woutprep)])
    ve_sc2done = ve_cmb

    if stage == "l1_f1":
        return early_out([("ve", ve_sc2done)])
    f2 = {}
    ve_final = 0
    psl = [PS0[:, 0:512], PS0[:, 512:1024], PS1[:, 0:512], PS1[:, 512:1024],
           PS2[:, 0:512], PS2[:, 512:1024], PS3[:, 0:512], PS4[:, 0:512]]
    for tt2 in range(4):
        t0 = tt2 * 512
        pe_d2 = [0] * NK
        for hch in range(NH1):
            sl = hch % 2
            d_w2 = dma("sync", w2sl[:, sl * 1024:(sl + 1) * 1024],
                       woutq_d.ap()[hch * 128:(hch + 1) * 128, :],
                       sem="f2W%d" % sl,
                       waits=[f2.get(("w", sl), ("pe", pe_am2))])
            d_h2 = dma("scalar", hidt[:, sl * 512:(sl + 1) * 512],
                       hid_d.ap()[hch * 128:(hch + 1) * 128, t0:t0 + 512],
                       sem="f2H%d" % sl,
                       waits=[f2.get(("h", sl), ("ve", ve_sc2done))])
            ve_hq = vec(lambda eng, sl=sl: eng.tensor_scalar(
                qtf[:, sl * 512:(sl + 1) * 512],
                hidt[:, sl * 512:(sl + 1) * 512],
                red[:, 46:47], MAGIC, OP.mult, OP.add),
                waits=[d_h2] + ([("ve", ve_sc2done)] if (tt2 == 0 and hch < 2) else []))
            ve_hq2 = vec(lambda eng, sl=sl, hch=hch: eng.tensor_scalar(
                hidt[:, sl * 512:(sl + 1) * 512],
                qtf[:, sl * 512:(sl + 1) * 512],
                -MAGIC, cmb32[:, hch:hch + 1], OP.add, OP.mult))
            for dch in range(NK):
                pw = []
                if dch == 0:
                    pw = [d_w2, ("ve", ve_hq2)]
                    if hch == 0:
                        pw.append(f2.get("ps_all", ("pe", pe_am2)))
                pe_d2[dch] = mm(w2sl[:, sl * 1024 + dch * 128: sl * 1024 + (dch + 1) * 128],
                                hidt[:, sl * 512:(sl + 1) * 512], psl[dch],
                                start=(hch == 0), stop=(hch == NH1 - 1),
                                waits=pw, inc=True)
            f2[("w", sl)] = ("pe", pe_d2[NK - 1])
            f2[("h", sl)] = ("pe", pe_d2[NK - 1])
        for dch in range(NK):
            ve_final = vec(lambda eng, dch=dch, t0=t0: eng.scalar_tensor_tensor(
                xr(dch, t0, t0 + 512), psl[dch], bcol1[:, 24 + dch:25 + dch],
                xr(dch, t0, t0 + 512), OP.add, OP.add),
                waits=[("pe", pe_d2[dch])])
        f2["ps_all"] = ("ve", ve_final)

    last = None
    for k in range(NK):
        last = dma("sync", out_h.ap()[k * 128:(k + 1) * 128, :], xr(k),
                   waits=[("ve", ve_final)])
    P.emit("sync", lambda eng: None, waits=[("dS", last[1])])
    emit_all(nc, P)
    es.close()
    return nc


def emit_all(nc, P):
    with nc.Block() as block:
        @block.sync
        def _(eng):
            for f in P.ops["sync"]:
                f(eng)

        @block.scalar
        def _(eng):
            for f in P.ops["scalar"]:
                f(eng)

        @block.vector
        def _(eng):
            for f in P.ops["vector"]:
                f(eng)

        @block.tensor
        def _(eng):
            for f in P.ops["tensor"]:
                f(eng)

        @block.gpsimd
        def _(eng):
            for f in P.ops["gpsimd"]:
                f(eng)
    return nc


def make_inputs(inp):
    C = np.ascontiguousarray

    def A(v):
        return np.asarray(v)

    x = A(inp["L0_attn_x"])
    shared = {
        "wgT": C(A(inp["L0_attn_Wg"]).T), "wvT": C(A(inp["L0_attn_Wv"]).T),
        "wdT": C(A(inp["L0_attn_Wd"]).T),
        "bcols0": np.concatenate([
            A(inp["L0_attn_bg"]).reshape(8, 128).T,
            A(inp["L0_attn_bv"]).reshape(8, 128).T,
            A(inp["L0_attn_bd"]).reshape(8, 128).T], axis=1).copy(),
        "bcols1": np.concatenate([
            A(inp["L1_attn_bq"]).reshape(8, 128).T,
            A(inp["L1_attn_bk"]).reshape(8, 128).T,
            A(inp["L1_attn_bo"]).reshape(8, 128).T,
            A(inp["L1_ffn_b_out"]).reshape(8, 128).T,
            A(inp["L1_ffn_b_in"]).reshape(32, 128).T], axis=1).copy(),
        "fgT": C(A(inp["L0_ffn_w_gate"]).T), "fuT": C(A(inp["L0_ffn_w_up"]).T),
        "fdT": C(A(inp["L0_ffn_w_down"]).T),
        "wq": C(A(inp["L1_attn_Wq"])), "wk": C(A(inp["L1_attn_Wk"])),
        "wv1": C(A(inp["L1_attn_Wv"])), "wo": C(A(inp["L1_attn_Wo"])),
        "bk": A(inp["L1_attn_bk"]).reshape(1, -1),
        "bv1": A(inp["L1_attn_bv"]).reshape(1, -1),
        "win": C(A(inp["L1_ffn_W_in"])), "wout": C(A(inp["L1_ffn_W_out"])),
    }
    maps = []
    for core in range(NCORES):
        b, j = divmod(core, 4)
        xs = C(x[b, j * T:(j + 1) * T, :].T)
        params = np.zeros((128, 24), np.float32)
        for r in range(NCORES):
            rb, rj = divmod(r, 4)
            params[:, r] = 1.0 if (rb == b and rj < j) else 0.0
            params[:, 9 + r] = 1.0 - params[:, r]
        params[:, 8] = (j * T) * ANG_SCALE
        params[:, 17] = (j * T) * ANG_SCALE + np.pi / 2.0
        m = dict(shared)
        m["x"] = xs
        m["params"] = params
        maps.append(m)
    return maps


def _run(stage, inp):
    maps = make_inputs(inp)
    try:
        nc = build_program(stage)
    except _EarlyStop as e:
        nc, P, es, dD = e.args
        P.emit("sync", lambda eng: None, waits=[("dD", dD)])
        emit_all(nc, P)
        es.close()
    res = run_bass_kernel_spmd(nc, maps, core_ids=list(range(NCORES)))
    out = np.empty((2, 8192, 1024), np.float32)
    for core in range(NCORES):
        b, j = divmod(core, 4)
        out[b, j * T:(j + 1) * T, :] = res.results[core]["out"].T
    return out


def kernel(**inp):
    return _run("full", inp)
